# version 02
# baseline (speedup 1.0000x reference)
import math
from contextlib import ExitStack

import numpy as np
import ml_dtypes

import concourse.bass as bass
import concourse.mybir as mybir
from concourse.bass import IndirectOffsetOnAxis
from concourse.bass_utils import run_bass_kernel_spmd

F32 = mybir.dt.float32
BF16 = mybir.dt.bfloat16
I32 = mybir.dt.int32
U32 = mybir.dt.uint32
AF = mybir.ActivationFunctionType
ALU = mybir.AluOpType
AX = mybir.AxisListType

B = 8
T = 4096
D = 1024
NT = T // 128
NCK = D // 128

AH = 16
AHD = 64
PH = 8
PDK = 128
PK = 8
NKEYS = 256
NEXP = NKEYS * NKEYS
DH = 8
DHD = 128
CH = 64
NCHK = T // CH
NE = 8
FF = 2048

BF = ml_dtypes.bfloat16



def host_prepare(inp):
    c = lambda a: np.ascontiguousarray(a)
    w = {}
    w["qkv_wT"] = c(np.asarray(inp["L0_attn_qkv_w"]).T)
    w["qkv_b"] = c(np.asarray(inp["L0_attn_qkv_b"]).reshape(1, 3 * D))
    w["proj_wT"] = c(np.asarray(inp["L0_attn_proj_w"]).T)
    w["proj_b"] = c(np.asarray(inp["L0_attn_proj_b"]).reshape(1, D))
    w["wq_peerT"] = c(np.asarray(inp["L0_ffn_w_q"]).T)
    w["keysT"] = c(np.asarray(inp["L0_ffn_keys"]).transpose(0, 2, 3, 1))
    w["norm_w"] = c(np.asarray(inp["L0_ffn_norm_w"]).reshape(1, D))
    w["exp_down"] = c(np.asarray(inp["L0_ffn_expert_down"]))
    w["exp_up"] = c(np.asarray(inp["L0_ffn_expert_up"]))
    w["WwT"] = c(np.asarray(inp["L1_attn_W_write"]).T)
    w["WrT"] = c(np.asarray(inp["L1_attn_W_read"]).T)
    w["WbT"] = c(np.asarray(inp["L1_attn_W_beta"]).T)
    w["decay"] = c(np.asarray(inp["L1_attn_decay"]).reshape(1, DH))
    w["router"] = c(np.asarray(inp["L1_ffn_router_kernel_DE"]))
    w["Wg"] = c(np.asarray(inp["L1_ffn_kernel_gating_EDF"]))
    w["Wu"] = c(np.asarray(inp["L1_ffn_kernel_up_EDF"]))
    w["Wd"] = c(np.asarray(inp["L1_ffn_kernel_down_EFD"]))

    half = AHD // 2
    freqs = 1.0 / 10000.0 ** (np.arange(half, dtype=np.float64) / half)
    ang = np.arange(T, dtype=np.float64)[:, None] * freqs[None, :]
    w["rope_cos"] = np.cos(ang).astype(np.float32)
    w["rope_sin"] = np.sin(ang).astype(np.float32)
    masks = np.zeros((4, 128, 512), dtype=BF)
    kk, qq = np.arange(128)[:, None], np.arange(512)[None, :]
    for m in range(4):
        masks[m] = (kk <= qq - 128 * m).astype(BF)
    w["diag_masks"] = c(masks.reshape(4 * 128, 512))
    w["ident_bf"] = np.eye(128, dtype=BF)
    return w



class Sem:
    __slots__ = ("h", "n")

    def __init__(self, h):
        self.h = h
        self.n = 0

    def inc(self, instr, k=1):
        instr.then_inc(self.h, k)
        self.n += k
        return instr

    def dma(self, instr):
        return self.inc(instr, 16)


class Ctx:
    def __init__(self, nc, stack, io):
        self.nc = nc
        self.stack = stack
        self.scope = stack
        self.io = io
        self.block = None
        self.sems = {}
        self.marks = {}
        self.psbt = [nc.alloc_psum_tensor(f"psbt{i}", [128, 1024], F32)
                     for i in range(4)]
        self.psb = [self.psbt[i // 2].ap()[:, 512 * (i % 2):512 * (i % 2 + 1)]
                    for i in range(8)]
        self.const = {}

    def sem(self, name):
        if name not in self.sems:
            h = self.stack.enter_context(self.nc.semaphore(name))
            self.sems[name] = Sem(h)
        return self.sems[name]

    def mark(self, name, sem=None, n=None):
        self.marks[name] = (sem.h, sem.n if n is None else n)

    def wait_mark(self, eng, name):
        if name not in self.marks:
            return
        h, n = self.marks[name]
        if n > 0:
            eng.wait_ge(h, n)

    def pbf(self, i):
        return self.psb[i].bitcast(BF16)


def _sb(ctx, name, shape, dtype=F32):
    return ctx.scope.enter_context(ctx.nc.sbuf_tensor(name, shape, dtype))



def emit_prelude(ctx):
    nc, block = ctx.nc, ctx.block
    ld = ctx.sem("pre_ld")
    ident = _sb(ctx, "c_ident", [128, 128], BF16)
    ones_row = _sb(ctx, "c_ones_row", [1, 512], BF16)
    ctx.const["ident"] = ident
    ctx.const["ones_row"] = ones_row

    @block.sync
    def _(sync):
        ld.dma(sync.dma_start(out=ident[:], in_=ctx.io["ident_bf"]))

    vc = ctx.sem("pre_vc")

    @block.vector
    def _(v):
        vc.inc(v.memset(ones_row[:], 1.0))

    ctx.mark("prelude", vc)
    ctx.mark("prelude_ld", ld)



def emit_ln(ctx, x_name, xn_tm_name, xn_fm_name, tag):
    nc, block = ctx.nc, ctx.block
    x_h = ctx.io[x_name]
    xn_tm_h = ctx.io[xn_tm_name]
    xn_fm_h = ctx.io[xn_fm_name]

    ld2 = [ctx.sem(f"{tag}_ld0"), ctx.sem(f"{tag}_ld1")]
    sc = ctx.sem(f"{tag}_sc")
    vc = ctx.sem(f"{tag}_vc")
    tc = ctx.sem(f"{tag}_tc")
    ev = ctx.sem(f"{tag}_ev")
    st = ctx.sem(f"{tag}_st")

    xld = [_sb(ctx, f"{tag}_xld{i}", [128, D], F32) for i in range(2)]
    scr = _sb(ctx, f"{tag}_scr", [128, D], F32)
    scr2 = _sb(ctx, f"{tag}_scr2", [128, D], F32)
    ssum = _sb(ctx, f"{tag}_ssum", [128, 2], F32)
    mst = _sb(ctx, f"{tag}_mst", [128, 4], F32)
    xn_tm = [_sb(ctx, f"{tag}_xntm{i}", [128, D], BF16) for i in range(2)]
    xn_fm = _sb(ctx, f"{tag}_xnfm", [128, NCK * T], BF16)

    ident = ctx.const["ident"]
    vfree = {}
    tdone = {}

    for i in range(NT):
        @block.sync
        def _(sync, i=i):
            if i >= 2:
                h, n = ctx.marks[f"{tag}_vfree{i - 2}"]
                sync.wait_ge(h, n)
            ld2[i % 2].dma(sync.dma_start(out=xld[i % 2][:],
                                          in_=x_h[128 * i:128 * (i + 1), :]))

        @block.scalar
        def _(s, i=i):
            s.wait_ge(ld2[i % 2].h, ld2[i % 2].n)
            if i >= 1:
                h, n = ctx.marks[f"{tag}_vfree{i - 1}"]
                s.wait_ge(h, n)
            sc.inc(s.activation(out=scr[:], in_=xld[i % 2][:], func=AF.Copy,
                                accum_out=ssum[:, 0:1]))
            sc.inc(s.activation(out=scr2[:], in_=xld[i % 2][:], func=AF.Square,
                                accum_out=ssum[:, 1:2]))

        @block.vector
        def _(v, i=i):
            v.wait_ge(sc.h, sc.n)
            vc.inc(v.tensor_scalar(out=mst[:, 0:1], in0=ssum[:, 0:1], scalar1=1.0 / D,
                                   scalar2=None, op0=ALU.mult))
            v.wait_ge(vc.h, vc.n)
            vc.inc(v.tensor_scalar(out=mst[:, 3:4], in0=mst[:, 0:1], scalar1=mst[:, 0:1],
                                   scalar2=-1e-5, op0=ALU.mult, op1=ALU.add))
            v.wait_ge(vc.h, vc.n)
            vc.inc(v.tensor_scalar(out=mst[:, 1:2], in0=ssum[:, 1:2], scalar1=1.0 / D,
                                   scalar2=mst[:, 3:4], op0=ALU.mult, op1=ALU.subtract))
        ctx.mark(f"{tag}_var{i}", vc)

        @block.scalar
        def _(s, i=i):
            v_h, v_n = ctx.marks[f"{tag}_var{i}"]
            s.wait_ge(v_h, v_n)
            sc.inc(s.activation(out=mst[:, 3:4], in_=mst[:, 1:2], func=AF.Ln))
            s.wait_ge(sc.h, sc.n)
            sc.inc(s.activation(out=mst[:, 2:3], in_=mst[:, 3:4], func=AF.Exp, scale=-0.5))

        @block.vector
        def _(v, i=i):
            v.wait_ge(sc.h, sc.n)
            if i >= 2:
                h, n = ctx.marks[f"{tag}_tdone{i - 2}"]
                v.wait_ge(h, n)
                v.wait_ge(st.h, st.n)
            vc.inc(v.tensor_scalar(out=xn_tm[i % 2][:], in0=xld[i % 2][:],
                                   scalar1=mst[:, 0:1], scalar2=mst[:, 2:3],
                                   op0=ALU.subtract, op1=ALU.mult))
        ctx.mark(f"{tag}_vfree{i}", vc)
        ctx.mark(f"{tag}_xntm{i}", vc)

        @block.sync
        def _(sync, i=i):
            h, n = ctx.marks[f"{tag}_xntm{i}"]
            sync.wait_ge(h, n)
            st.dma(sync.dma_start(out=xn_tm_h[128 * i:128 * (i + 1), :], in_=xn_tm[i % 2][:]))

        @block.tensor
        def _(te, i=i):
            h, n = ctx.marks[f"{tag}_xntm{i}"]
            te.wait_ge(h, n)
            ctx.wait_mark(te, "prelude_ld")
            for cc in range(NCK):
                k = i * NCK + cc
                if k >= 4:
                    te.wait_ge(ev.h, k - 3)
                tc.inc(te.transpose(ctx.pbf(k % 4)[:, 0:128],
                                    xn_tm[i % 2][:, 128 * cc:128 * (cc + 1)], ident[:]))

        @block.vector
        def _(v, i=i):
            for cc in range(NCK):
                k = i * NCK + cc
                v.wait_ge(tc.h, k + 1)
                ev.inc(v.tensor_copy(xn_fm[:, T * cc + 128 * i: T * cc + 128 * (i + 1)],
                                     ctx.pbf(k % 4)[:, 0:128]))
        ctx.mark(f"{tag}_tdone{i}", ev)

    @block.sync
    def _(sync):
        sync.wait_ge(ev.h, NT * NCK)
        for cc in range(NCK):
            st.dma(sync.dma_start(out=xn_fm_h[128 * cc:128 * (cc + 1), :],
                                  in_=xn_fm[:, T * cc:T * (cc + 1)]))
        sync.wait_ge(st.h, st.n)

    ctx.mark(f"{tag}_done", st)



QKS = 1.0 / math.sqrt(8.0)
NGRP = 4
GH = AH // NGRP


def emit_attn0(ctx):
    nc, block = ctx.nc, ctx.block
    io = ctx.io
    ld = ctx.sem("at_ld")
    gl = ctx.sem("at_gl")
    vc = ctx.sem("at_vc")
    sc = ctx.sem("at_sc")
    tc = ctx.sem("at_tc")
    st = ctx.sem("at_st")
    sto = [ctx.sem("at_sto0"), ctx.sem("at_sto1")]
    lb = ctx.sem("at_lb")
    sts = [ctx.sem("at_sts0"), ctx.sem("at_sts1")]
    lo = [ctx.sem("at_lo0"), ctx.sem("at_lo1")]

    ident = ctx.const["ident"]

    xnfm = _sb(ctx, "at_xnfm", [128, NCK * T], BF16)
    wpool = _sb(ctx, "at_wpool", [128, NCK * D], BF16)
    cos_sb = _sb(ctx, "at_cos", [128, NT * 32], BF16)
    sin_sb = _sb(ctx, "at_sin", [128, NT * 32], BF16)
    masks = _sb(ctx, "at_masks", [128, 4 * 512], BF16)
    bias_bg = _sb(ctx, "at_biasbg", [128, 3 * 64 * GH], F32)
    projb_b = _sb(ctx, "at_projbb", [128, D], F32)
    brow_g = _sb(ctx, "at_browg", [1, 3 * 64 * GH], F32)
    pbrow = _sb(ctx, "at_pbrow", [1, D], F32)
    onesr = _sb(ctx, "at_onesr", [1, 128], F32)
    sclr = _sb(ctx, "at_sclr", [1, 128], F32)
    ones64 = _sb(ctx, "at_ones64", [1, 64], F32)

    qTx = [_sb(ctx, f"at_qTx{h}", [64, T], BF16) for h in range(GH)]
    kTx = [_sb(ctx, f"at_kTx{h}", [64, T], BF16) for h in range(GH)]
    vx = _sb(ctx, "at_vx", [128, NT * GH * 65], BF16)
    oTst = [_sb(ctx, f"at_oTst{i}", [64, 512], BF16) for i in range(2)]
    qkvf = _sb(ctx, "at_qkvf", [128, 3 * 64 * GH], F32)
    qk_bf = _sb(ctx, "at_qkbf", [128, 2 * 64 * GH], BF16)
    rsc = _sb(ctx, "at_rsc", [128, 4 * 32 * GH], F32)
    qn2 = _sb(ctx, "at_qn2", [128, 2 * GH], F32)
    qn2m = _sb(ctx, "at_qn2m", [128, 2 * GH], F32)
    qn2mb = _sb(ctx, "at_qn2mb", [128, 2 * GH], BF16)
    cneg = _sb(ctx, "at_cneg", [128, GH], F32)
    qm4 = _sb(ctx, "at_qm4", [4, 1], F32)
    km4 = _sb(ctx, "at_km4", [4, 1], F32)
    c4s = _sb(ctx, "at_c4s", [4, 1], F32)
    c4b = _sb(ctx, "at_c4b", [4, 1], BF16)
    crow = _sb(ctx, "at_crow", [1, GH], BF16)
    onesrb = _sb(ctx, "at_onesrb", [1, 128], BF16)
    PT = [_sb(ctx, f"at_PT{i}", [128, 512], BF16) for i in range(4)]
    rz = _sb(ctx, "at_rz", [1, 512], F32)
    zb = _sb(ctx, "at_zb", [64, 512], F32)
    oTld = [_sb(ctx, f"at_oTld{i}", [128, 512], BF16) for i in range(2)]
    s_tm = [_sb(ctx, f"at_stm{i}", [128, D], BF16) for i in range(2)]

    @block.sync
    def _(sync):
        ctx.wait_mark(sync, "ln0_done")
        for cc in range(NCK):
            ld.dma(sync.dma_start(out=xnfm[:, T * cc:T * (cc + 1)],
                                  in_=io["xn_fm"][128 * cc:128 * (cc + 1), :]))
        ld.dma(sync.dma_start(
            out=masks[:], in_=io["diag_masks"].rearrange("(m p) f -> p m f", p=128)))
        ld.dma(sync.dma_start(out=pbrow[:], in_=io["proj_b"]))


    @block.vector
    def _(v):
        v.memset(onesr[:], 1.0)
        v.memset(sclr[:], QKS)
        v.memset(onesrb[:], 1.0)
        vc.inc(v.memset(ones64[:], 1.0))
    ctx.mark("at_memsets", vc)

    @block.gpsimd
    def _(g):
        gl.dma(g.dma_start(
            out=cos_sb[:], in_=io["rope_cos"].rearrange("(n p) f -> p n f", p=128)))
        gl.dma(g.dma_start(
            out=sin_sb[:], in_=io["rope_sin"].rearrange("(n p) f -> p n f", p=128)))

    @block.tensor
    def _(te):
        te.wait_ge(ld.h, ld.n)
        ctx.wait_mark(te, "at_memsets")
        te.matmul(ctx.psb[6][:], onesr[:], pbrow[:, 0:512], start=True, stop=True)
        tc.inc(te.matmul(ctx.psb[7][:], onesr[:], pbrow[:, 512:1024],
                         start=True, stop=True))
    ctx.mark("at_biasmm", tc)

    @block.scalar
    def _(s):
        ctx.wait_mark(s, "at_biasmm")
        s.activation(out=projb_b[:, 0:512], in_=ctx.psb[6][:], func=AF.Copy)
        sc.inc(s.activation(out=projb_b[:, 512:1024], in_=ctx.psb[7][:], func=AF.Copy))
    ctx.mark("at_bias", sc)

    ctx.mark("at_coresc", sc)
    ctx.mark("at_corevc", vc)
    ctx.mark("at_coretc", tc)

    for name in ("at_epivc", "at_expgrp", "at_avgrp", "at_pjevac"):
        ctx.marks[name] = (vc.h, 0)

    for g in range(NGRP):
        qof, kof, vof = 64 * GH * g, D + 64 * GH * g, 2 * D + 64 * GH * g

        @block.gpsimd
        def _(gq, g=g):
            if g > 0:
                ctx.wait_mark(gq, "at_qkvmm")
            for cc in range(NCK):
                for r, off in ((0, qof), (1, kof), (2, vof)):
                    gl.dma(gq.dma_start(
                        out=wpool[:, 768 * cc + 256 * r:768 * cc + 256 * (r + 1)],
                        in_=io["qkv_wT"][128 * cc:128 * (cc + 1), off:off + 256]))
        ctx.mark("at_gw", gl)

        @block.vector
        def _(v, g=g):
            ctx.wait_mark(v, "at_coretc")
            v.memset(qn2m[:], 0.0)
            vc.inc(v.memset(
                vx.ap().rearrange("p (s c) -> p s c", c=65)[:, :, 64:65], 1.0))
        ctx.mark("at_vxones", vc)

        @block.sync
        def _(sync, g=g):
            if g > 0:
                ctx.wait_mark(sync, "at_gbias")
            for r, off in ((0, qof), (1, kof), (2, vof)):
                lb.dma(sync.dma_start(out=brow_g[:, 256 * r:256 * (r + 1)],
                                      in_=io["qkv_b"][0:1, off:off + 256]))
        ctx.mark("at_gbrow", lb)

        @block.tensor
        def _(te, g=g):
            te.wait_ge(ld.h, ld.n)
            te.wait_ge(lb.h, lb.n)
            ctx.wait_mark(te, "at_memsets")
            ctx.wait_mark(te, "at_coresc")
            ctx.wait_mark(te, "at_corevc")
            te.matmul(ctx.psb[0][:, 0:256], sclr[:], brow_g[:, 0:256],
                      start=True, stop=True)
            te.matmul(ctx.psb[1][:, 0:256], sclr[:], brow_g[:, 256:512],
                      start=True, stop=True)
            tc.inc(te.matmul(ctx.psb[2][:, 0:256], onesr[:], brow_g[:, 512:768],
                             start=True, stop=True))
        ctx.mark("at_gbmm", tc)

        @block.scalar
        def _(s, g=g):
            ctx.wait_mark(s, "at_gbmm")
            s.activation(out=bias_bg[:, 0:256], in_=ctx.psb[0][:, 0:256], func=AF.Copy)
            s.activation(out=bias_bg[:, 256:512], in_=ctx.psb[1][:, 0:256], func=AF.Copy)
            sc.inc(s.activation(out=bias_bg[:, 512:768], in_=ctx.psb[2][:, 0:256],
                                func=AF.Copy))
        ctx.mark("at_gbias", sc)

        for i in range(NT):
            @block.tensor
            def _(te, g=g, i=i):
                if i == 0:
                    te.wait_ge(gl.h, gl.n)
                    ctx.wait_mark(te, "at_gbias")
                else:
                    ctx.wait_mark(te, "at_qkvevac")
                mm = None
                for cc in range(NCK):
                    xn_ci = xnfm[:, T * cc + 128 * i: T * cc + 128 * (i + 1)]
                    st0, sp = (cc == 0), (cc == NCK - 1)
                    te.matmul(ctx.psb[0][:, 0:256], xn_ci,
                              wpool[:, 768 * cc + 0:768 * cc + 256],
                              start=st0, stop=sp)
                    te.matmul(ctx.psb[1][:, 0:256], xn_ci,
                              wpool[:, 768 * cc + 256:768 * cc + 512],
                              start=st0, stop=sp)
                    mm = te.matmul(ctx.psb[2][:, 0:256], xn_ci,
                                   wpool[:, 768 * cc + 512:768 * cc + 768],
                                   start=st0, stop=sp)
                tc.inc(mm)
            ctx.mark("at_qkvmm", tc)

            @block.vector
            def _(v, g=g, i=i):
                ctx.wait_mark(v, "at_qkvmm")
                ctx.wait_mark(v, "at_vxones")
                if i >= 1 or g > 0:
                    ctx.wait_mark(v, "at_trtile")
                v.scalar_tensor_tensor(
                    out=qkvf[:, 0:256], in0=ctx.psb[0][:, 0:256], scalar=QKS,
                    in1=bias_bg[:, 0:256], op0=ALU.mult, op1=ALU.add)
                v.scalar_tensor_tensor(
                    out=qkvf[:, 256:512], in0=ctx.psb[1][:, 0:256], scalar=QKS,
                    in1=bias_bg[:, 256:512], op0=ALU.mult, op1=ALU.add)
                vslots = vx.ap().rearrange("p (s c) -> p s c", c=65)[
                    :, GH * i:GH * (i + 1), 0:64]
                vc.inc(v.scalar_tensor_tensor(
                    out=vslots,
                    in0=ctx.psb[2][:, 0:256].rearrange("p (s c) -> p s c", c=64),
                    scalar=1.0,
                    in1=bias_bg.ap()[:, 512:768].rearrange("p (s c) -> p s c", c=64),
                    op0=ALU.mult, op1=ALU.add))
            ctx.mark("at_qkvevac", vc)

            @block.vector
            def _(v, g=g, i=i):
                for base in (0, 256):
                    src = qkvf.ap()[:, base:base + 256].rearrange(
                        "p (h d) -> p h d", h=GH)
                    dst = qk_bf.ap()[:, base:base + 256].rearrange(
                        "p (h d) -> p h d", h=GH)
                    r = [rsc.ap()[:, 128 * j:128 * (j + 1)].rearrange(
                        "p (h d) -> p h d", h=GH) for j in range(4)]
                    cos_t = cos_sb.ap()[:, 32 * i:32 * (i + 1)].unsqueeze(
                        1).to_broadcast([128, GH, 32])
                    sin_t = sin_sb.ap()[:, 32 * i:32 * (i + 1)].unsqueeze(
                        1).to_broadcast([128, GH, 32])
                    tops, bots = src[:, :, 0:32], src[:, :, 32:64]
                    v.wait_ge(vc.h, vc.n)
                    v.tensor_tensor(out=r[0], in0=tops, in1=cos_t, op=ALU.mult)
                    v.tensor_tensor(out=r[1], in0=bots, in1=sin_t, op=ALU.mult)
                    v.tensor_tensor(out=r[2], in0=tops, in1=sin_t, op=ALU.mult)
                    vc.inc(v.tensor_tensor(out=r[3], in0=bots, in1=cos_t, op=ALU.mult))
                    v.wait_ge(vc.h, vc.n)
                    v.tensor_tensor(out=dst[:, :, 0:32], in0=r[0], in1=r[1],
                                    op=ALU.subtract)
                    vc.inc(v.tensor_tensor(out=dst[:, :, 32:64], in0=r[2], in1=r[3],
                                           op=ALU.add))
                v.wait_ge(vc.h, vc.n)
                last = None
                for hh in range(2 * GH):
                    last = v.scalar_tensor_tensor(
                        out=rsc[:, 64 * hh:64 * (hh + 1)],
                        in0=qkvf[:, 64 * hh:64 * (hh + 1)], scalar=1.0,
                        in1=qkvf[:, 64 * hh:64 * (hh + 1)],
                        op0=ALU.mult, op1=ALU.mult, accum_out=qn2[:, hh:hh + 1])
                vc.inc(last)
                v.wait_ge(vc.h, vc.n)
                vc.inc(v.tensor_tensor(out=qn2m[:], in0=qn2m[:], in1=qn2[:],
                                       op=ALU.max))
            ctx.mark("at_rope", vc)

            for t in range(GH):
                @block.tensor
                def _(te, g=g, i=i, t=t):
                    if t == 0:
                        ctx.wait_mark(te, "at_rope")
                        if g == 0 and i == 0:
                            ctx.wait_mark(te, "prelude_ld")
                    slot = (i * GH + t) % 4
                    if g > 0 or i * GH + t >= 4:
                        ctx.wait_mark(te, f"at_trev{slot}")
                    tc.inc(te.transpose(ctx.pbf(4 + slot)[:, 0:128],
                                        qk_bf[:, 128 * t:128 * (t + 1)], ident[:]))
                ctx.mark("at_tr", tc)
                ctx.mark("at_trtile", tc)

                @block.vector
                def _(v, g=g, i=i, t=t):
                    ctx.wait_mark(v, "at_tr")
                    pv = ctx.pbf(4 + (i * GH + t) % 4)
                    if t < 2:
                        dst0, dst1 = qTx[2 * t], qTx[2 * t + 1]
                    else:
                        dst0, dst1 = kTx[2 * (t - 2)], kTx[2 * (t - 2) + 1]
                    v.tensor_copy(dst0[:, 128 * i:128 * (i + 1)], pv[0:64, 0:128])
                    vc.inc(v.tensor_copy(dst1[:, 128 * i:128 * (i + 1)],
                                         pv[64:128, 0:128]))
                ctx.mark(f"at_trev{(i * GH + t) % 4}", vc)
        ctx.mark("at_trall", vc)

        @block.vector
        def _(v, g=g):
            ctx.wait_mark(v, "at_rope")
            vc.inc(v.tensor_copy(qn2mb[:], qn2m[:]))
        ctx.mark("at_cn0", vc)

        @block.tensor
        def _(te, g=g):
            ctx.wait_mark(te, "at_cn0")
            ctx.wait_mark(te, f"at_trev0")
            ctx.wait_mark(te, f"at_trev1")
            te.transpose(ctx.pbf(4)[0:GH, 0:128], qn2mb[:, 0:GH], ident[:])
            tc.inc(te.transpose(ctx.pbf(5)[0:GH, 0:128], qn2mb[:, GH:2 * GH], ident[:]))
        ctx.mark("at_cn1", tc)

        @block.vector
        def _(v, g=g):
            ctx.wait_mark(v, "at_cn1")
            v.tensor_reduce(out=qm4[:], in_=ctx.pbf(4)[0:GH, 0:128], axis=AX.X,
                            op=ALU.max)
            vc.inc(v.tensor_reduce(out=km4[:], in_=ctx.pbf(5)[0:GH, 0:128],
                                   axis=AX.X, op=ALU.max))
            v.wait_ge(vc.h, vc.n)
            vc.inc(v.tensor_tensor(out=c4s[:], in0=qm4[:], in1=km4[:], op=ALU.add))
            v.wait_ge(vc.h, vc.n)
            vc.inc(v.tensor_scalar(out=c4b[:], in0=c4s[:], scalar1=-0.5,
                                   scalar2=None, op0=ALU.mult))
        ctx.mark("at_cn2", vc)

        @block.tensor
        def _(te, g=g):
            ctx.wait_mark(te, "at_cn2")
            tc.inc(te.transpose(ctx.pbf(6)[0:1, 0:GH], c4b[:], ident[0:GH, 0:GH]))
        ctx.mark("at_cn3", tc)

        @block.vector
        def _(v, g=g):
            ctx.wait_mark(v, "at_cn3")
            vc.inc(v.tensor_copy(crow[:], ctx.pbf(6)[0:1, 0:GH]))
        ctx.mark("at_cn4", vc)

        @block.tensor
        def _(te, g=g):
            ctx.wait_mark(te, "at_cn4")
            tc.inc(te.matmul(ctx.psb[7][:, 0:GH], onesrb[:], crow[:],
                             start=True, stop=True))
        ctx.mark("at_cn5", tc)

        @block.scalar
        def _(s, g=g):
            ctx.wait_mark(s, "at_cn5")
            ctx.wait_mark(s, "at_bias")
            sc.inc(s.activation(out=cneg[:], in_=ctx.psb[7][:, 0:GH], func=AF.Copy))
        ctx.mark("at_cn", sc)

        for hh in range(GH):
            for j in range(T // 512):
                nk = 4 * j + 4
                qsl = qTx[hh][:, 512 * j:512 * (j + 1)]
                for k0 in range(0, nk, 4):
                    kg = list(range(k0, min(k0 + 4, nk)))

                    @block.tensor
                    def _(te, g=g, hh=hh, j=j, kg=kg):
                        if kg[0] == 0:
                            ctx.wait_mark(te, "at_cn")
                            ctx.wait_mark(te, "at_trall")
                            ctx.wait_mark(te, "at_epivc")
                        ctx.wait_mark(te, "at_expgrp")
                        last = None
                        for k in kg:
                            bank = (0, 1, 4, 5)[k % 4]
                            last = te.matmul(ctx.psb[bank][:],
                                             kTx[hh][:, 128 * k:128 * (k + 1)], qsl,
                                             start=True, stop=True)
                        tc.inc(last)
                    ctx.mark("at_mmS", tc)

                    @block.scalar
                    def _(s, g=g, hh=hh, kg=kg):
                        ctx.wait_mark(s, "at_mmS")
                        ctx.wait_mark(s, "at_avgrp")
                        last = None
                        for k in kg:
                            bank = (0, 1, 4, 5)[k % 4]
                            last = s.activation(out=PT[k % 4][:], in_=ctx.psb[bank][:],
                                                func=AF.Exp, bias=cneg[:, hh:hh + 1])
                        sc.inc(last)
                    ctx.mark("at_expgrp", sc)

                    masked = [k for k in kg if k >= 4 * j]
                    if masked:
                        @block.vector
                        def _(v, g=g, j=j, masked=masked):
                            ctx.wait_mark(v, "at_expgrp")
                            last = None
                            for k in masked:
                                m = k - 4 * j
                                last = v.tensor_tensor(
                                    out=PT[k % 4][:], in0=PT[k % 4][:],
                                    in1=masks[:, 512 * m:512 * (m + 1)], op=ALU.mult)
                            vc.inc(last)
                        ctx.mark("at_mask", vc)

                    @block.tensor
                    def _(te, g=g, hh=hh, j=j, kg=kg, masked=masked, nk=nk):
                        ctx.wait_mark(te, "at_expgrp")
                        if masked:
                            ctx.wait_mark(te, "at_mask")
                        last = None
                        for k in kg:
                            last = te.matmul(
                                ctx.psb[2][0:65, :],
                                vx.ap()[:, 65 * (GH * k + hh):65 * (GH * k + hh) + 65],
                                PT[k % 4][:],
                                start=(k == 0), stop=(k == nk - 1))
                        tc.inc(last)
                    ctx.mark("at_avgrp", tc)

                @block.scalar
                def _(s, g=g, hh=hh, j=j):
                    ctx.wait_mark(s, "at_avgrp")
                    ctx.wait_mark(s, "at_epivc")
                    sc.inc(s.activation(out=rz[:], in_=ctx.psb[2][64:65, :], func=AF.Ln))
                    s.wait_ge(sc.h, sc.n)
                    sc.inc(s.activation(out=rz[:], in_=rz[:], func=AF.Exp, scale=-1.0))
                ctx.mark("at_rz", sc)

                @block.tensor
                def _(te, g=g, hh=hh, j=j):
                    ctx.wait_mark(te, "at_rz")
                    tc.inc(te.matmul(ctx.psb[3][0:64, :], ones64[:], rz[:],
                                     start=True, stop=True))
                ctx.mark("at_zbmm", tc)

                @block.scalar
                def _(s, g=g, hh=hh, j=j):
                    ctx.wait_mark(s, "at_zbmm")
                    sc.inc(s.activation(out=zb[:], in_=ctx.psb[3][0:64, :], func=AF.Copy))
                ctx.mark("at_zb", sc)

                @block.vector
                def _(v, g=g, hh=hh, j=j):
                    ctx.wait_mark(v, "at_zb")
                    v.wait_ge(sto[j % 2].h, sto[j % 2].n)
                    vc.inc(v.tensor_tensor(
                        out=oTst[j % 2][:], in0=ctx.psb[2][0:64, :], in1=zb[:],
                        op=ALU.mult))
                ctx.mark("at_epivc", vc)

                @block.sync
                def _(sync, g=g, hh=hh, j=j):
                    ctx.wait_mark(sync, "at_epivc")
                    gh = g * GH + hh
                    sto[j % 2].dma(sync.dma_start(
                        out=io["oTh"][64 * gh:64 * (gh + 1), 512 * j:512 * (j + 1)],
                        in_=oTst[j % 2][:]))

        ctx.mark("at_coresc", sc)
        ctx.mark("at_corevc", vc)
        ctx.mark("at_coretc", tc)

    @block.gpsimd
    def _(gq):
        ctx.wait_mark(gq, "at_qkvmm")
        for cc in range(NCK):
            gl.dma(gq.dma_start(out=wpool[:, D * cc:D * (cc + 1)],
                                in_=io["proj_wT"][128 * cc:128 * (cc + 1), :]))
    ctx.mark("at_pw", gl)

    for tg in range(T // 512):
        for cc in range(NCK):
            @block.sync
            def _(sync, tg=tg, cc=cc):
                if tg == 0 and cc == 0:
                    sync.wait_ge(sto[0].h, sto[0].n)
                    sync.wait_ge(sto[1].h, sto[1].n)
                idx = tg * NCK + cc
                if idx >= 2:
                    ctx.wait_mark(sync, f"at_pjld{cc % 2}")
                lo[cc % 2].dma(sync.dma_start(out=oTld[cc % 2][:],
                                      in_=io["oTh"][128 * cc:128 * (cc + 1),
                                                    512 * tg:512 * (tg + 1)]))

            @block.tensor
            def _(te, tg=tg, cc=cc):
                te.wait_ge(lo[cc % 2].h, lo[cc % 2].n)
                if tg == 0 and cc == 0:
                    ctx.wait_mark(te, "at_pw")
                if cc == 0:
                    ctx.wait_mark(te, "at_pjevac")
                last = None
                for i4 in range(4):
                    for nb in range(2):
                        last = te.matmul(
                            ctx.psb[i4 * 2 + nb][:],
                            oTld[cc % 2][:, 128 * i4:128 * (i4 + 1)],
                            wpool[:, D * cc + 512 * nb:D * cc + 512 * (nb + 1)],
                            start=(cc == 0), stop=(cc == NCK - 1))
                tc.inc(last)
            ctx.mark(f"at_pjld{cc % 2}", tc)
        ctx.mark("at_pjmm", tc)

        for i4 in range(4):
            @block.vector
            def _(v, tg=tg, i4=i4):
                if i4 == 0:
                    ctx.wait_mark(v, "at_pjmm")
                v.wait_ge(sts[i4 % 2].h, sts[i4 % 2].n)
                last = None
                for nb in range(2):
                    last = v.tensor_tensor(
                        out=s_tm[i4 % 2][:, 512 * nb:512 * (nb + 1)],
                        in0=ctx.psb[i4 * 2 + nb][:],
                        in1=projb_b[:, 512 * nb:512 * (nb + 1)], op=ALU.add)
                vc.inc(last)
            if i4 == 3:
                ctx.mark("at_pjevac", vc)
            ctx.mark("at_pjev1", vc)

            @block.sync
            def _(sync, tg=tg, i4=i4):
                ctx.wait_mark(sync, "at_pjev1")
                sts[i4 % 2].dma(sync.dma_start(
                    out=io["s"][512 * tg + 128 * i4:512 * tg + 128 * (i4 + 1), :],
                    in_=s_tm[i4 % 2][:]))

    @block.sync
    def _(sync):
        sync.wait_ge(st.h, st.n)
        for s2 in (sto[0], sto[1], sts[0], sts[1], lo[0], lo[1], lb, gl, ld):
            if s2.n:
                sync.wait_ge(s2.h, s2.n)
    ctx.mark("attn0_done", sts[1])




def emit_peer(ctx):
    nc, block = ctx.nc, ctx.block
    io = ctx.io
    pl = ctx.sem("pe_ld")
    pg = ctx.sem("pe_gl")
    lx = [ctx.sem("pe_lx0"), ctx.sem("pe_lx1")]
    gd = [ctx.sem(f"pe_gd{i}") for i in range(4)]
    gu = [ctx.sem(f"pe_gu{i}") for i in range(4)]
    vc = ctx.sem("pe_vc")
    sc = ctx.sem("pe_sc")
    tc = ctx.sem("pe_tc")
    stp = [ctx.sem("pe_st0"), ctx.sem("pe_st1")]
    ident = ctx.const["ident"]

    xnfm = _sb(ctx, "pe_xnfm", [128, NCK * T], BF16)
    wq = _sb(ctx, "pe_wq", [128, NCK * 2 * D], BF16)
    keys = _sb(ctx, "pe_keys", [128, 2 * PH * NKEYS], BF16)
    nrow = _sb(ctx, "pe_nrow", [1, D], F32)
    normw_b = _sb(ctx, "pe_normwb", [128, D], F32)
    normw_c = _sb(ctx, "pe_normwc", [128, NCK], F32)
    onesr = _sb(ctx, "pe_onesr", [1, 128], F32)
    qp = _sb(ctx, "pe_qp", [128, 16 * 512], BF16)
    sims = _sb(ctx, "pe_sims", [128, 16 * NKEYS], F32)
    xntm2 = [_sb(ctx, f"pe_xntm{i}", [128, D], BF16) for i in range(2)]
    xt2 = [_sb(ctx, f"pe_xt{i}", [128, D], BF16) for i in range(2)]
    xld2 = [_sb(ctx, f"pe_xld{i}", [128, D], F32) for i in range(2)]
    sld2 = [_sb(ctx, f"pe_sld{i}", [128, D], BF16) for i in range(2)]
    x1t2 = [_sb(ctx, f"pe_x1t{i}", [128, D], F32) for i in range(2)]
    s12 = _sb(ctx, "pe_s12", [128, 2 * PH * PK], F32)
    i12 = _sb(ctx, "pe_i12", [128, 2 * PH * PK], U32)
    ascr = _sb(ctx, "pe_ascr", [128, PH * PK * PK], F32)
    fs = _sb(ctx, "pe_fs", [128, PH * PK], F32)
    fsr = _sb(ctx, "pe_fsr", [128, PH * PK], F32)
    pk_t = _sb(ctx, "pe_pk", [128, PH * PK], U32)
    ab = _sb(ctx, "pe_ab", [128, 2 * PH * PK], U32)
    eq = _sb(ctx, "pe_eq", [128, PH * PK * PK], U32)
    isel = _sb(ctx, "pe_isel", [128, 2 * PH * PK], U32)
    fi = _sb(ctx, "pe_fi", [128, PH * PK], U32)
    iota8 = _sb(ctx, "pe_iota8", [128, PK], I32)
    hid = _sb(ctx, "pe_hid", [128, PH * PK], F32)
    hgel = _sb(ctx, "pe_hgel", [128, PH * PK], F32)
    coef = _sb(ctx, "pe_coef", [128, PH * PK], F32)
    Gdn = [_sb(ctx, f"pe_gdn{i}", [128, D], BF16) for i in range(4)]
    Gup = [_sb(ctx, f"pe_gup{i}", [128, D], BF16) for i in range(4)]
    diag = [_sb(ctx, f"pe_diag{i}", [128, 128], BF16) for i in range(2)]
    gsil = _sb(ctx, "pe_gsil", [128, D], F32)
    sqs = _sb(ctx, "pe_sqs", [128, D], BF16)
    stat = _sb(ctx, "pe_stat", [128, 4], F32)

    NSLOT = PH * PK

    @block.sync
    def _(sync):
        ctx.wait_mark(sync, "ln0_done")
        ctx.wait_mark(sync, "attn0_done")
        for cc in range(NCK):
            pl.dma(sync.dma_start(out=xnfm[:, T * cc:T * (cc + 1)],
                                  in_=io["xn_fm"][128 * cc:128 * (cc + 1), :]))
        pl.dma(sync.dma_start(out=nrow[:], in_=io["norm_w"]))
        pl.dma(sync.dma_start(
            out=normw_c[:],
            in_=io["norm_w"].rearrange("o (c p) -> p (o c)", p=128),
            allow_slow_non_contiguous=True))

    @block.gpsimd
    def _(g):
        for cc in range(NCK):
            pg.dma(g.dma_start(out=wq[:, 2 * D * cc:2 * D * (cc + 1)],
                               in_=io["wq_peerT"][128 * cc:128 * (cc + 1), :]))
        for h in range(PH):
            for s_ in range(2):
                pg.dma(g.dma_start(
                    out=keys[:, 256 * (2 * h + s_):256 * (2 * h + s_ + 1)],
                    in_=io["keysT"][h, s_]))
        pio = ctx.sem("pe_io")
        pio.inc(g.iota(iota8[:], pattern=[[1, PK]], base=0, channel_multiplier=0))
        ctx.mark("pe_iota", pio)

    @block.vector
    def _(v):
        vc.inc(v.memset(onesr[:], 1.0))
    ctx.mark("pe_ones", vc)

    @block.tensor
    def _(te):
        te.wait_ge(pl.h, pl.n)
        ctx.wait_mark(te, "pe_ones")
        te.matmul(ctx.psb[0][:], onesr[:], nrow[:, 0:512], start=True, stop=True)
        tc.inc(te.matmul(ctx.psb[1][:], onesr[:], nrow[:, 512:1024],
                         start=True, stop=True))
    ctx.mark("pe_nwmm", tc)

    @block.scalar
    def _(s):
        ctx.wait_mark(s, "pe_nwmm")
        s.activation(out=normw_b[:, 0:512], in_=ctx.psb[0][:], func=AF.Copy)
        sc.inc(s.activation(out=normw_b[:, 512:1024], in_=ctx.psb[1][:], func=AF.Copy))
    ctx.mark("pe_nwb", sc)

    @block.vector
    def _(v):
        v.wait_ge(pg.h, pg.n)
        v.wait_ge(pl.h, pl.n)
        last = None
        for cc in range(NCK):
            last = v.tensor_scalar(out=wq[:, 2 * D * cc:2 * D * (cc + 1)],
                                   in0=wq[:, 2 * D * cc:2 * D * (cc + 1)],
                                   scalar1=normw_c[:, cc:cc + 1], scalar2=None,
                                   op0=ALU.mult)
        vc.inc(last)
    ctx.mark("pe_wqf", vc)

    ctx.marks["pe_simsdone"] = (tc.h, 0)
    ctx.marks["pe_silu"] = (sc.h, 0)
    ctx.marks["pe_fin"] = (vc.h, 0)
    ctx.marks["pe_topk1"] = (vc.h, 0)

    for sg in range(T // 512):
        for qfc in range(16):
            @block.tensor
            def _(te, sg=sg, qfc=qfc):
                if qfc == 0:
                    ctx.wait_mark(te, "pe_wqf")
                    ctx.wait_mark(te, "pe_nwb")
                    ctx.wait_mark(te, "pe_simsdone")
                    ctx.wait_mark(te, "pe_silu")
                if qfc >= 2:
                    ctx.wait_mark(te, f"pe_qpev{qfc % 2}")
                last = None
                for cc in range(NCK):
                    last = te.matmul(
                        ctx.psb[qfc % 2][:],
                        wq[:, 2 * D * cc + 128 * qfc:2 * D * cc + 128 * (qfc + 1)],
                        xnfm[:, T * cc + 512 * sg:T * cc + 512 * (sg + 1)],
                        start=(cc == 0), stop=(cc == NCK - 1))
                tc.inc(last)
            ctx.mark(f"pe_qpmm{qfc % 2}", tc)

            @block.vector
            def _(v, sg=sg, qfc=qfc):
                ctx.wait_mark(v, f"pe_qpmm{qfc % 2}")
                vc.inc(v.tensor_copy(qp[:, 512 * qfc:512 * (qfc + 1)],
                                     ctx.psb[qfc % 2][:]))
            ctx.mark(f"pe_qpev{qfc % 2}", vc)
        ctx.mark("pe_qpall", vc)

        for it in range(4):
            i = sg * 4 + it
            par = i % 2

            @block.sync
            def _(sync, i=i, par=par):
                if i >= 2:
                    ctx.wait_mark(sync, f"pe_finv{par}")
                lx[par].dma(sync.dma_start(out=xntm2[par][:],
                                           in_=io["xn_tm"][128 * i:128 * (i + 1), :]))
                lx[par].dma(sync.dma_start(out=xld2[par][:],
                                           in_=io["x"][128 * i:128 * (i + 1), :]))
                lx[par].dma(sync.dma_start(out=sld2[par][:],
                                           in_=io["s"][128 * i:128 * (i + 1), :]))

            @block.scalar
            def _(s, i=i, par=par):
                s.wait_ge(lx[par].h, lx[par].n)
                if i >= 1:
                    ctx.wait_mark(s, "pe_xt")
                sc.inc(s.activation(out=sqs[:], in_=xntm2[par][:], func=AF.Square,
                                    accum_out=stat[:, 0:1]))
            ctx.mark("pe_ssq", sc)

            @block.vector
            def _(v, i=i):
                ctx.wait_mark(v, "pe_ssq")
                vc.inc(v.tensor_scalar(out=stat[:, 1:2], in0=stat[:, 0:1],
                                       scalar1=1.0 / D, scalar2=1e-6,
                                       op0=ALU.mult, op1=ALU.add))
            ctx.mark("pe_var", vc)

            @block.scalar
            def _(s, i=i):
                ctx.wait_mark(s, "pe_var")
                sc.inc(s.activation(out=stat[:, 2:3], in_=stat[:, 1:2], func=AF.Ln))
                s.wait_ge(sc.h, sc.n)
                sc.inc(s.activation(out=stat[:, 3:4], in_=stat[:, 2:3], func=AF.Exp,
                                    scale=-0.5))
            ctx.mark("pe_rstd", sc)

            @block.vector
            def _(v, i=i, par=par):
                ctx.wait_mark(v, "pe_rstd")
                ctx.wait_mark(v, "pe_nwb")
                if i >= 2:
                    ctx.wait_mark(v, f"pe_dots{par}")
                vc.inc(v.scalar_tensor_tensor(out=xt2[par][:], in0=xntm2[par][:],
                                              scalar=stat[:, 3:4], in1=normw_b[:],
                                              op0=ALU.mult, op1=ALU.mult))
            ctx.mark("pe_xt", vc)

            for hs in range(16):
                @block.tensor
                def _(te, i=i, it=it, hs=hs):
                    if hs == 0:
                        ctx.wait_mark(te, "pe_qpall")
                        ctx.wait_mark(te, "pe_topk1")
                    if hs >= 2:
                        ctx.wait_mark(te, f"pe_simev{hs % 2}")
                    tc.inc(te.matmul(
                        ctx.psb[2 + hs % 2][:, 0:256],
                        qp[:, 512 * hs + 128 * it:512 * hs + 128 * (it + 1)],
                        keys[:, 256 * hs:256 * (hs + 1)],
                        start=True, stop=True))
                ctx.mark(f"pe_simmm{hs % 2}", tc)
                if hs == 15:
                    ctx.mark("pe_simsdone", tc)

                @block.vector
                def _(v, i=i, hs=hs):
                    ctx.wait_mark(v, f"pe_simmm{hs % 2}")
                    ctx.wait_mark(v, "pe_rstd")
                    vc.inc(v.tensor_scalar(out=sims[:, 256 * hs:256 * (hs + 1)],
                                           in0=ctx.psb[2 + hs % 2][:, 0:256],
                                           scalar1=stat[:, 3:4], scalar2=None,
                                           op0=ALU.mult))
                ctx.mark(f"pe_simev{hs % 2}", vc)
            ctx.mark("pe_simsall", vc)

            @block.vector
            def _(v, i=i):
                ctx.wait_mark(v, "pe_simsall")
                last = None
                for hs in range(16):
                    last = v.max(out=s12[:, PK * hs:PK * (hs + 1)],
                                 in_=sims[:, 256 * hs:256 * (hs + 1)])
                vc.inc(last)
                v.wait_ge(vc.h, vc.n)
                for hs in range(16):
                    last = v.max_index(out=i12[:, PK * hs:PK * (hs + 1)],
                                       in_max=s12[:, PK * hs:PK * (hs + 1)],
                                       in_values=sims[:, 256 * hs:256 * (hs + 1)])
                vc.inc(last)
            ctx.mark("pe_topk1", vc)

            @block.vector
            def _(v, i=i):
                v.wait_ge(vc.h, vc.n)
                ctx.wait_mark(v, "pe_iota")
                s12r = s12.ap().rearrange("p (h s e) -> p h s e", h=PH, s=2)
                s1b = s12r[:, :, 0:1, :].transpose([0, 1, 3, 2]).to_broadcast(
                    [128, PH, PK, PK])
                s2b = s12r[:, :, 1:2, :].to_broadcast([128, PH, PK, PK])
                vc.inc(v.tensor_tensor(
                    out=ascr.ap().rearrange("p (h a b) -> p h a b", h=PH, a=PK),
                    in0=s1b, in1=s2b, op=ALU.add))
                v.wait_ge(vc.h, vc.n)
                last = None
                for h in range(PH):
                    last = v.max(out=fs[:, PK * h:PK * (h + 1)],
                                 in_=ascr[:, 64 * h:64 * (h + 1)])
                vc.inc(last)
                v.wait_ge(vc.h, vc.n)
                for h in range(PH):
                    last = v.max_index(out=pk_t[:, PK * h:PK * (h + 1)],
                                       in_max=fs[:, PK * h:PK * (h + 1)],
                                       in_values=ascr[:, 64 * h:64 * (h + 1)])
                vc.inc(last)
                v.wait_ge(vc.h, vc.n)
                v.tensor_scalar(out=ab[:, 0:NSLOT], in0=pk_t[:], scalar1=3,
                                scalar2=None, op0=ALU.logical_shift_right)
                vc.inc(v.tensor_scalar(out=ab[:, NSLOT:2 * NSLOT], in0=pk_t[:],
                                       scalar1=7, scalar2=None, op0=ALU.bitwise_and))
                v.wait_ge(vc.h, vc.n)
                i12r = i12.ap().rearrange("p (h s e) -> p h s e", h=PH, s=2)
                iob = iota8.ap().bitcast(U32).unsqueeze(1).unsqueeze(1).to_broadcast(
                    [128, PH, PK, PK])
                for half in range(2):
                    abr = ab.ap()[:, NSLOT * half:NSLOT * (half + 1)].rearrange(
                        "p (h s) -> p h s", h=PH).unsqueeze(3).to_broadcast(
                        [128, PH, PK, PK])
                    vc.inc(v.tensor_tensor(
                        out=eq.ap().rearrange("p (h a b) -> p h a b", h=PH, a=PK),
                        in0=abr, in1=iob, op=ALU.is_equal))
                    v.wait_ge(vc.h, vc.n)
                    ivals = i12r[:, :, half:half + 1, :].to_broadcast(
                        [128, PH, PK, PK])
                    vc.inc(v.tensor_tensor(
                        out=eq.ap().rearrange("p (h a b) -> p h a b", h=PH, a=PK),
                        in0=eq.ap().rearrange("p (h a b) -> p h a b", h=PH, a=PK),
                        in1=ivals, op=ALU.mult))
                    v.wait_ge(vc.h, vc.n)
                    with nc.allow_low_precision(reason="u32 index reduce"):
                        vc.inc(v.tensor_reduce(
                            out=isel.ap()[:, NSLOT * half:NSLOT * (half + 1)].rearrange(
                                "p (h s) -> p h s", h=PH),
                            in_=eq.ap().rearrange("p (h a b) -> p h a b", h=PH, a=PK),
                            axis=AX.X, op=ALU.add))
                    v.wait_ge(vc.h, vc.n)
                vc.inc(v.scalar_tensor_tensor(
                    out=fi[:], in0=isel[:, 0:NSLOT], scalar=NKEYS,
                    in1=isel[:, NSLOT:2 * NSLOT],
                    op0=ALU.mult, op1=ALU.add))
                vc.inc(v.tensor_scalar(out=fsr[:], in0=fs[:], scalar1=0.0,
                                       scalar2=None, op0=ALU.max))
            ctx.mark("pe_fi", vc)

            vdots0 = vc.n
            @block.gpsimd
            def _(g, i=i, vdots0=vdots0):
                ctx.wait_mark(g, "pe_fi")
                for j in range(NSLOT):
                    if j >= 4:
                        g.wait_ge(vc.h, vdots0 + (j - 4) + 1)
                    gd[j % 4].dma(g.indirect_dma_start(
                        out=Gdn[j % 4][:], out_offset=None, in_=io["exp_down"],
                        in_offset=IndirectOffsetOnAxis(
                            ap=fi.ap()[:, j:j + 1].bitcast(I32), axis=0)))

            @block.vector
            def _(v, i=i, par=par, vdots0=vdots0):
                ctx.wait_mark(v, "pe_xt")
                for j in range(NSLOT):
                    v.wait_ge(gd[j % 4].h, 16 * (NSLOT // 4) * i + 16 * (j // 4 + 1)
                              if False else gd[j % 4].n - 16 * (NSLOT // 4) + 16 * (j // 4 + 1))
                    vc.inc(v.scalar_tensor_tensor(
                        out=Gdn[j % 4][:], in0=Gdn[j % 4][:], scalar=1.0,
                        in1=xt2[par][:], op0=ALU.mult, op1=ALU.mult,
                        accum_out=hid[:, j:j + 1]))
            assert vc.n == vdots0 + NSLOT
            ctx.mark(f"pe_dots{par}", vc)
            ctx.mark("pe_dots", vc)

            @block.scalar
            def _(s, i=i):
                ctx.wait_mark(s, "pe_dots")
                sc.inc(s.activation(out=hgel[:], in_=hid[:], func=AF.Gelu))
            ctx.mark("pe_gelu", sc)

            @block.vector
            def _(v, i=i):
                ctx.wait_mark(v, "pe_gelu")
                vc.inc(v.tensor_tensor(out=coef[:], in0=hgel[:], in1=fsr[:],
                                       op=ALU.mult))
            ctx.mark("pe_coef", vc)

            tcu0 = tc.n
            vcu0 = vc.n
            @block.gpsimd
            def _(g, i=i, tcu0=tcu0):
                ctx.wait_mark(g, "pe_fi")
                g.wait_ge(tc.h, tcu0)
                for j in range(NSLOT):
                    if j >= 4:
                        g.wait_ge(tc.h, tcu0 + (j - 4) + 1)
                    gu[j % 4].dma(g.indirect_dma_start(
                        out=Gup[j % 4][:], out_offset=None, in_=io["exp_up"],
                        in_offset=IndirectOffsetOnAxis(
                            ap=fi.ap()[:, j:j + 1].bitcast(I32), axis=0)))
            gu_base = [gu[r].n - 16 * (NSLOT // 4) for r in range(4)]

            @block.vector
            def _(v, i=i, tcu0=tcu0):
                ctx.wait_mark(v, "pe_coef")
                ctx.wait_mark(v, "prelude_ld")
                for j in range(NSLOT):
                    if j >= 2:
                        v.wait_ge(tc.h, tcu0 + (j - 2) + 1)
                    vc.inc(v.tensor_scalar(out=diag[j % 2][:], in0=ident[:],
                                           scalar1=coef[:, j:j + 1], scalar2=None,
                                           op0=ALU.mult))
            assert vc.n == vcu0 + NSLOT

            @block.tensor
            def _(te, i=i, tcu0=tcu0, vcu0=vcu0, gu_base=gu_base):
                ctx.wait_mark(te, "pe_silu")
                for j in range(NSLOT):
                    te.wait_ge(vc.h, vcu0 + j + 1)
                    te.wait_ge(gu[j % 4].h, gu_base[j % 4] + 16 * (j // 4 + 1))
                    te.matmul(ctx.psb[0][:], diag[j % 2][:], Gup[j % 4][:, 0:512],
                              start=(j == 0), stop=(j == NSLOT - 1))
                    mm = te.matmul(ctx.psb[1][:], diag[j % 2][:],
                                   Gup[j % 4][:, 512:1024],
                                   start=(j == 0), stop=(j == NSLOT - 1))
                    tc.inc(mm)
            assert tc.n == tcu0 + NSLOT
            ctx.mark("pe_upmm", tc)

            @block.scalar
            def _(s, i=i):
                ctx.wait_mark(s, "pe_upmm")
                ctx.wait_mark(s, "pe_fin")
                sc.inc(s.activation(out=gsil[:], in_=ctx.psbt[0].ap(), func=AF.Silu))
            ctx.mark("pe_silu", sc)

            @block.vector
            def _(v, i=i, par=par):
                ctx.wait_mark(v, "pe_silu")
                v.wait_ge(lx[par].h, lx[par].n)
                v.wait_ge(stp[par].h, stp[par].n)
                vc.inc(v.tensor_tensor(out=gsil[:], in0=sld2[par][:], in1=gsil[:],
                                       op=ALU.mult))
                v.wait_ge(vc.h, vc.n)
                vc.inc(v.tensor_tensor(out=x1t2[par][:], in0=xld2[par][:],
                                       in1=gsil[:], op=ALU.add))
            ctx.mark("pe_fin", vc)
            ctx.mark(f"pe_finv{par}", vc)

            @block.sync
            def _(sync, i=i, par=par):
                ctx.wait_mark(sync, f"pe_finv{par}")
                stp[par].dma(sync.dma_start(
                    out=io["x1"][128 * i:128 * (i + 1), :], in_=x1t2[par][:]))

    @block.sync
    def _(sync):
        for s2 in [stp[0], stp[1], lx[0], lx[1], pl, pg] + gd + gu:
            if s2.n:
                sync.wait_ge(s2.h, s2.n)
    ctx.mark("peer_done", stp[1])




def emit_moe(ctx):
    nc, block = ctx.nc, ctx.block
    io = ctx.io
    ml = ctx.sem("mo_ld")
    mg = ctx.sem("mo_gl")
    ma = ctx.sem("mo_ma")
    lf = [ctx.sem("mo_lf0"), ctx.sem("mo_lf1")]
    vc = ctx.sem("mo_vc")
    sc = ctx.sem("mo_sc")
    tc = ctx.sem("mo_tc")
    mst = [ctx.sem("mo_st0"), ctx.sem("mo_st1")]

    xnfm = _sb(ctx, "mo_xnfm", [128, NCK * T], BF16)
    rtr = _sb(ctx, "mo_rtr", [128, NCK * NE], BF16)
    wg_sb = _sb(ctx, "mo_wg", [128, NCK * FF], BF16)
    wu_sb = _sb(ctx, "mo_wu", [128, NCK * FF], BF16)
    wd_sb = _sb(ctx, "mo_wd", [128, (FF // 128) * D], BF16)
    fuse = _sb(ctx, "mo_fuse", [128, (FF // 128) * 512], BF16)
    sil = [_sb(ctx, f"mo_sil{i}", [128, 512], BF16) for i in range(2)]
    wfull = _sb(ctx, "mo_wfull", [128, NT * NE], F32)
    rsc = _sb(ctx, "mo_rsc", [128, 2 * NE + 8], F32)
    iota8 = _sb(ctx, "mo_iota8", [128, NE], I32)
    pv = _sb(ctx, "mo_pv", [128, 8], F32)
    pidx = _sb(ctx, "mo_pidx", [128, 8], U32)
    eqf = _sb(ctx, "mo_eqf", [128, NE], F32)
    scal = [_sb(ctx, f"mo_scal{i}", [128, D], F32) for i in range(2)]
    acld = scal
    x1ld = [_sb(ctx, f"mo_x1ld{i}", [128, D], F32) for i in range(2)]
    s1ld = [_sb(ctx, f"mo_s1ld{i}", [128, D], BF16) for i in range(2)]
    gs = _sb(ctx, "mo_gs", [128, D], F32)

    NFC = FF // 128

    @block.sync
    def _(sync):
        ctx.wait_mark(sync, "ln1_done")
        for cc in range(NCK):
            ml.dma(sync.dma_start(out=xnfm[:, T * cc:T * (cc + 1)],
                                  in_=io["xn1_fm"][128 * cc:128 * (cc + 1), :]))

    @block.gpsimd
    def _(g):
        for cc in range(NCK):
            mg.dma(g.dma_start(out=rtr[:, NE * cc:NE * (cc + 1)],
                               in_=io["router"][128 * cc:128 * (cc + 1), :]))
        mio = ctx.sem("mo_io")
        mio.inc(g.iota(iota8[:], pattern=[[1, NE]], base=0, channel_multiplier=0))
        ctx.mark("mo_iota", mio)

    ctx.marks["mo_wfv"] = (vc.h, 0)
    ctx.marks["mo_sm2"] = (sc.h, 0)
    for i in range(NT):
        @block.tensor
        def _(te, i=i):
            te.wait_ge(ml.h, ml.n)
            te.wait_ge(mg.h, 16 * NCK)
            ctx.wait_mark(te, "mo_wfv")
            last = None
            for cc in range(NCK):
                last = te.matmul(ctx.psb[4][:, 0:NE],
                                 xnfm[:, T * cc + 128 * i:T * cc + 128 * (i + 1)],
                                 rtr[:, NE * cc:NE * (cc + 1)],
                                 start=(cc == 0), stop=(cc == NCK - 1))
            tc.inc(last)
        ctx.mark("mo_rmm", tc)

        @block.vector
        def _(v, i=i):
            ctx.wait_mark(v, "mo_rmm")
            ctx.wait_mark(v, "mo_sm2")
            vc.inc(v.tensor_reduce(out=rsc[:, 0:1], in_=ctx.psb[4][:, 0:NE],
                                   axis=AX.X, op=ALU.max))
            v.wait_ge(vc.h, vc.n)
            vc.inc(v.tensor_scalar(out=rsc[:, 8:8 + NE], in0=ctx.psb[4][:, 0:NE],
                                   scalar1=rsc[:, 0:1], scalar2=None,
                                   op0=ALU.subtract))
        ctx.mark("mo_sm1", vc)

        @block.scalar
        def _(s, i=i):
            ctx.wait_mark(s, "mo_sm1")
            sc.inc(s.activation(out=rsc[:, 8:8 + NE], in_=rsc[:, 8:8 + NE],
                                func=AF.Exp, accum_out=rsc[:, 1:2]))
        ctx.mark("mo_sm2", sc)

        @block.vector
        def _(v, i=i):
            ctx.wait_mark(v, "mo_sm2")
            ctx.wait_mark(v, "mo_iota")
            vc.inc(v.reciprocal(rsc[:, 2:3], rsc[:, 1:2]))
            v.wait_ge(vc.h, vc.n)
            vc.inc(v.tensor_scalar(out=rsc[:, 8:8 + NE], in0=rsc[:, 8:8 + NE],
                                   scalar1=rsc[:, 2:3], scalar2=None, op0=ALU.mult))
            v.wait_ge(vc.h, vc.n)
            vc.inc(v.max(out=pv[:], in_=rsc[:, 8:8 + NE]))
            v.wait_ge(vc.h, vc.n)
            vc.inc(v.max_index(out=pidx[:], in_max=pv[:], in_values=rsc[:, 8:8 + NE]))
            v.wait_ge(vc.h, vc.n)
            vc.inc(v.tensor_tensor(out=rsc[:, 3:4], in0=pv[:, 1:2], in1=pv[:, 0:1],
                                   op=ALU.subtract))
        ctx.mark("mo_nw1", vc)

        @block.scalar
        def _(s, i=i):
            ctx.wait_mark(s, "mo_nw1")
            sc.inc(s.activation(out=rsc[:, 4:5], in_=rsc[:, 3:4], func=AF.Exp))
        ctx.mark("mo_nw2", sc)

        @block.vector
        def _(v, i=i):
            ctx.wait_mark(v, "mo_nw2")
            vc.inc(v.tensor_scalar(out=rsc[:, 5:6], in0=rsc[:, 4:5], scalar1=1.0,
                                   scalar2=None, op0=ALU.add))
            v.wait_ge(vc.h, vc.n)
            vc.inc(v.reciprocal(rsc[:, 6:7], rsc[:, 5:6]))
            v.wait_ge(vc.h, vc.n)
            vc.inc(v.tensor_tensor(out=rsc[:, 7:8], in0=rsc[:, 4:5], in1=rsc[:, 6:7],
                                   op=ALU.mult))
            v.wait_ge(vc.h, vc.n)
            vc.inc(v.tensor_tensor(out=eqf[:],
                                   in0=iota8.ap().bitcast(U32),
                                   in1=pidx.ap()[:, 0:1].to_broadcast([128, NE]),
                                   op=ALU.is_equal))
            v.wait_ge(vc.h, vc.n)
            vc.inc(v.tensor_scalar(out=wfull[:, NE * i:NE * (i + 1)], in0=eqf[:],
                                   scalar1=rsc[:, 6:7], scalar2=None, op0=ALU.mult))
            v.wait_ge(vc.h, vc.n)
            vc.inc(v.tensor_tensor(out=eqf[:],
                                   in0=iota8.ap().bitcast(U32),
                                   in1=pidx.ap()[:, 1:2].to_broadcast([128, NE]),
                                   op=ALU.is_equal))
            v.wait_ge(vc.h, vc.n)
            vc.inc(v.scalar_tensor_tensor(out=wfull[:, NE * i:NE * (i + 1)],
                                          in0=eqf[:], scalar=rsc[:, 7:8],
                                          in1=wfull[:, NE * i:NE * (i + 1)],
                                          op0=ALU.mult, op1=ALU.add))
        ctx.mark("mo_wfv", vc)
    ctx.mark("mo_wfall", vc)

    ctx.marks["mo_dnev"] = (vc.h, 0)
    ctx.marks["mo_fuz0"] = (vc.h, 0)
    ctx.marks["mo_fuz1"] = (vc.h, 0)
    ctx.marks["mo_sil0"] = (sc.h, 0)
    ctx.marks["mo_sil1"] = (sc.h, 0)
    for e in range(NE):
        @block.gpsimd
        def _(g, e=e):
            g.wait_ge(mg.h, mg.n)
            if e > 0:
                ctx.wait_mark(g, "mo_gumm")
            for cc in range(NCK):
                mg.dma(g.dma_start(out=wg_sb[:, FF * cc:FF * (cc + 1)],
                                   in_=io["Wg"][e, 128 * cc:128 * (cc + 1), :]))
                mg.dma(g.dma_start(out=wu_sb[:, FF * cc:FF * (cc + 1)],
                                   in_=io["Wu"][e, 128 * cc:128 * (cc + 1), :]))
            if e > 0:
                ctx.wait_mark(g, "mo_dnmm")
            for fc in range(NFC):
                mg.dma(g.dma_start(out=wd_sb[:, D * fc:D * (fc + 1)],
                                   in_=io["Wd"][e, 128 * fc:128 * (fc + 1), :]))
        ctx.mark("mo_wld", mg)

        for tq in range(T // 512):
            for fc in range(NFC):
                bg, bu = 2 * (fc % 2), 1 + 2 * (fc % 2)
                @block.tensor
                def _(te, e=e, tq=tq, fc=fc, bg=bg, bu=bu):
                    if fc == 0 and tq == 0:
                        te.wait_ge(mg.h, mg.n)
                        ctx.wait_mark(te, "mo_wfall")
                    if fc < 2:
                        ctx.wait_mark(te, "mo_dnev")
                    ctx.wait_mark(te, f"mo_sil{fc % 2}")
                    ctx.wait_mark(te, f"mo_fuz{fc % 2}")
                    lastg = lastu = None
                    for cc in range(NCK):
                        xs = xnfm[:, T * cc + 512 * tq:T * cc + 512 * (tq + 1)]
                        lastg = te.matmul(ctx.psb[bg][:],
                                          wg_sb[:, FF * cc + 128 * fc:FF * cc + 128 * (fc + 1)],
                                          xs, start=(cc == 0), stop=(cc == NCK - 1))
                    tc.inc(lastg)
                    ctx.mark(f"mo_gmm{fc % 2}", tc)
                    for cc in range(NCK):
                        xs = xnfm[:, T * cc + 512 * tq:T * cc + 512 * (tq + 1)]
                        lastu = te.matmul(ctx.psb[bu][:],
                                          wu_sb[:, FF * cc + 128 * fc:FF * cc + 128 * (fc + 1)],
                                          xs, start=(cc == 0), stop=(cc == NCK - 1))
                    tc.inc(lastu)
                ctx.mark("mo_gumm", tc)
                ctx.mark(f"mo_gumm{fc % 2}", tc)

                @block.scalar
                def _(s, e=e, tq=tq, fc=fc, bg=bg):
                    h, n = ctx.marks[f"mo_gmm{fc % 2}"]
                    s.wait_ge(h, n)
                    ctx.wait_mark(s, f"mo_fuz{fc % 2}")
                    sc.inc(s.activation(out=sil[fc % 2][:], in_=ctx.psb[bg][:],
                                        func=AF.Silu))
                ctx.mark(f"mo_sil{fc % 2}", sc)

                @block.vector
                def _(v, e=e, tq=tq, fc=fc, bu=bu):
                    ctx.wait_mark(v, f"mo_gumm{fc % 2}")
                    ctx.wait_mark(v, f"mo_sil{fc % 2}")
                    vc.inc(v.tensor_tensor(out=fuse[:, 512 * fc:512 * (fc + 1)],
                                           in0=ctx.psb[bu][:], in1=sil[fc % 2][:],
                                           op=ALU.mult))
                ctx.mark(f"mo_fuz{fc % 2}", vc)
                if fc == NFC - 1:
                    ctx.mark("mo_fuseall", vc)

            @block.tensor
            def _(te, e=e, tq=tq):
                ctx.wait_mark(te, "mo_fuseall")
                ctx.wait_mark(te, "mo_dnev")
                last = None
                for i4 in range(4):
                    for nb in range(2):
                        for fc in range(NFC):
                            last = te.matmul(
                                ctx.psb[i4 * 2 + nb][:],
                                fuse[:, 512 * fc + 128 * i4:512 * fc + 128 * (i4 + 1)],
                                wd_sb[:, D * fc + 512 * nb:D * fc + 512 * (nb + 1)],
                                start=(fc == 0), stop=(fc == NFC - 1))
                tc.inc(last)
            ctx.mark("mo_dnmm", tc)

            for i4h in range(2):
                @block.vector
                def _(v, e=e, tq=tq, i4h=i4h):
                    ctx.wait_mark(v, "mo_dnmm")
                    v.wait_ge(ma.h, ma.n)
                    last = None
                    for i4 in (2 * i4h, 2 * i4h + 1):
                        ti = 4 * tq + i4
                        for nb in range(2):
                            last = v.tensor_scalar(
                                out=scal[i4 % 2][:, 512 * nb:512 * (nb + 1)],
                                in0=ctx.psb[i4 * 2 + nb][:],
                                scalar1=wfull[:, NE * ti + e:NE * ti + e + 1],
                                scalar2=None, op0=ALU.mult)
                    vc.inc(last)
                ctx.mark("mo_scev", vc)

                @block.gpsimd
                def _(g, e=e, tq=tq, i4h=i4h):
                    ctx.wait_mark(g, "mo_scev")
                    if e > 0:
                        g.wait_ge(ma.h, ma.n)
                    for i4 in (2 * i4h, 2 * i4h + 1):
                        ti = 4 * tq + i4
                        op = {}
                        if e > 0:
                            op = dict(accum_op=ALU.add)
                        ma.dma(g.dma_start(
                            out=io["moeacc"][128 * ti:128 * (ti + 1), :],
                            in_=scal[i4 % 2][:], **op))
            ctx.mark("mo_dnev", vc)

    for i in range(NT):
        par = i % 2

        @block.sync
        def _(sync, i=i, par=par):
            if i == 0:
                sync.wait_ge(ma.h, ma.n)
            if i >= 2:
                ctx.wait_mark(sync, f"mo_x2v{par}")
                sync.wait_ge(mst[par].h, mst[par].n)
            lf[par].dma(sync.dma_start(out=acld[par][:],
                                       in_=io["moeacc"][128 * i:128 * (i + 1), :]))
            lf[par].dma(sync.dma_start(out=x1ld[par][:],
                                       in_=io["x1"][128 * i:128 * (i + 1), :]))
            lf[par].dma(sync.dma_start(out=s1ld[par][:],
                                       in_=io["s1"][128 * i:128 * (i + 1), :]))

        @block.scalar
        def _(s, i=i, par=par):
            s.wait_ge(lf[par].h, lf[par].n)
            if i >= 1:
                ctx.wait_mark(s, "mo_x2s")
            sc.inc(s.activation(out=gs[:], in_=acld[par][:], func=AF.Silu))
        ctx.mark("mo_gsil", sc)

        @block.vector
        def _(v, i=i, par=par):
            ctx.wait_mark(v, "mo_gsil")
            v.wait_ge(mst[par].h, mst[par].n)
            vc.inc(v.tensor_tensor(out=gs[:], in0=s1ld[par][:], in1=gs[:],
                                   op=ALU.mult))
            v.wait_ge(vc.h, vc.n)
            vc.inc(v.tensor_tensor(out=x1ld[par][:], in0=x1ld[par][:], in1=gs[:],
                                   op=ALU.add))
        ctx.mark("mo_x2s", vc)
        ctx.mark(f"mo_x2v{par}", vc)

        @block.sync
        def _(sync, i=i, par=par):
            ctx.wait_mark(sync, f"mo_x2v{par}")
            mst[par].dma(sync.dma_start(out=io["out"][128 * i:128 * (i + 1), :],
                                        in_=x1ld[par][:]))

    @block.sync
    def _(sync):
        for s2 in [mst[0], mst[1], lf[0], lf[1], ml, mg, ma]:
            if s2.n:
                sync.wait_ge(s2.h, s2.n)
    ctx.mark("moe_done", mst[1])




def emit_delta(ctx):
    nc, block = ctx.nc, ctx.block
    io = ctx.io
    dl = ctx.sem("dk_ld")
    dg = ctx.sem("dk_gl")
    vc = ctx.sem("dk_vc")
    sc = ctx.sem("dk_sc")
    tc = ctx.sem("dk_tc")
    dst = [ctx.sem("dk_st0"), ctx.sem("dk_st1")]
    lp = [ctx.sem("dk_lp0"), ctx.sem("dk_lp1")]
    lq = [ctx.sem("dk_lq0"), ctx.sem("dk_lq1")]
    pst = [ctx.sem("dk_pst0"), ctx.sem("dk_pst1")]
    ident = ctx.const["ident"]

    xnfm = _sb(ctx, "dk_xnfm", [128, NCK * T], BF16)
    vfm = _sb(ctx, "dk_vfm", [128, NCK * T], BF16)
    HB = NCHK // 2 * DH
    betaC = _sb(ctx, "dk_betaC", [64, NCHK * DH], F32)
    vtm = [_sb(ctx, f"dk_vtm{h}", [64, DHD], BF16) for h in range(DH)]
    wpool = _sb(ctx, "dk_wpool", [128, NCK * D], BF16)
    wb_sb = _sb(ctx, "dk_wb", [128, NCK * DH], BF16)
    drow = _sb(ctx, "dk_drow", [1, DH], F32)
    lgrow = _sb(ctx, "dk_lgrow", [1, DH], F32)
    beta_tm = _sb(ctx, "dk_beta", [128, NT * DH], F32)
    onesr = _sb(ctx, "dk_onesr", [1, 128], F32)
    onesrb = _sb(ctx, "dk_onesrb", [1, 128], BF16)
    onescb = _sb(ctx, "dk_onescb", [128, 1], BF16)
    iopm = _sb(ctx, "dk_iopm", [64, 64], F32)
    iocol = _sb(ctx, "dk_iocol", [64, 1], F32)
    iorow = _sb(ctx, "dk_iorow", [1, 64], F32)
    lgc64 = _sb(ctx, "dk_lgc64", [64, DH], F32)
    lgc128 = _sb(ctx, "dk_lgc128", [128, DH], F32)
    negLs = _sb(ctx, "dk_negLs", [64, DH * 64], BF16)
    LTt = _sb(ctx, "dk_LTt", [64, DH * 64], BF16)
    de_col = _sb(ctx, "dk_decol", [64, DH], F32)
    dw_col = _sb(ctx, "dk_dwcol", [64, DH], F32)
    ctd128 = _sb(ctx, "dk_ctd", [128, DH], F32)
    de_row = _sb(ctx, "dk_derow", [1, DH * 64], F32)
    de_b = _sb(ctx, "dk_deb", [128, DH * 64], BF16)
    t64a = _sb(ctx, "dk_t64a", [64, 64], F32)
    t64b = _sb(ctx, "dk_t64b", [64, 64], F32)
    rst_row = _sb(ctx, "dk_rstrow", [1, 512], BF16)

    S_f = [_sb(ctx, f"dk_Sf{h}", [128, DHD], F32) for h in range(DH)]
    S_b = [_sb(ctx, f"dk_Sb{h}", [128, DHD], BF16) for h in range(DH)]
    kk_s = [_sb(ctx, f"dk_kks{h}", [64, 64], F32) for h in range(DH)]
    A0 = [_sb(ctx, f"dk_A0{h}", [64, 64], BF16) for h in range(DH)]
    atT = [_sb(ctx, f"dk_atT{h}", [64, 64], BF16) for h in range(DH)]
    Xb = [_sb(ctx, f"dk_X{h}", [64, 64], BF16) for h in range(DH)]
    Pb = [_sb(ctx, f"dk_P{h}", [64, 64], BF16) for h in range(DH)]
    PTs = [_sb(ctx, f"dk_PT{h}", [64, 64], BF16) for h in range(DH)]
    ATs = [_sb(ctx, f"dk_AT{h}", [64, 64], BF16) for h in range(DH)]
    ktm = [_sb(ctx, f"dk_ktm{h}", [64, DHD], BF16) for h in range(DH)]
    kbde = [_sb(ctx, f"dk_kbde{h}", [64, DHD], BF16) for h in range(DH)]
    kdw = [_sb(ctx, f"dk_kdw{h}", [64, DHD], BF16) for h in range(DH)]
    kdeT = [_sb(ctx, f"dk_kdeT{h}", [128, 64], BF16) for h in range(DH)]
    Yv_s = [_sb(ctx, f"dk_Yv{h}", [64, DHD], F32) for h in range(DH)]
    kcd = [_sb(ctx, f"dk_kcd{h}", [64, DHD], BF16) for h in range(DH)]
    kcdT = [_sb(ctx, f"dk_kcdT{h}", [128, 64], BF16) for h in range(DH)]
    vnew = [_sb(ctx, f"dk_vnew{h}", [64, DHD], BF16) for h in range(DH)]
    o_s = [_sb(ctx, f"dk_os{h}", [64, DHD], BF16) for h in range(DH)]
    oTst = [[_sb(ctx, f"dk_oTst{p}_{h}", [128, 64], BF16) for h in range(DH)]
            for p in range(2)]
    xnld = [_sb(ctx, f"dk_xnld{i}", [128, D], BF16) for i in range(2)]
    oTld = [_sb(ctx, f"dk_oTld{i}", [128, 512], BF16) for i in range(2)]
    s1t = [_sb(ctx, f"dk_s1t{i}", [128, D], BF16) for i in range(2)]

    def kk_ps(h):
        return ctx.psbt[2].ap()[0:64, 64 * h:64 * (h + 1)]
    def nwx_ps(h):
        return ctx.psbt[0].ap()[0:64, 64 * h:64 * (h + 1)]
    def nwp_ps(h):
        return ctx.psbt[0].ap()[0:64, 512 + 64 * h:512 + 64 * (h + 1)]
    def yv_ps(h):
        return ctx.psbt[1].ap()[0:64, 128 * h:128 * (h + 1)]
    def o_ps(h):
        return ctx.psbt[2].ap()[0:64, 128 * h:128 * (h + 1)]
    def dW_ps(h):
        return ctx.psbt[0].ap()[:, 128 * h:128 * (h + 1)]
    def ktr_ps(h):
        return ctx.pbf(6)[0:64, 128 * h:128 * h + 128]
    def vtr_ps(h):
        return ctx.pbf(6)[0:64, 128 * h:128 * h + 128]
    def ptr_ps(h):
        return ctx.pbf(6)[0:64, 64 * h:64 * (h + 1)]
    def ttr_ps(h):
        return ctx.pbf(7)[:, 64 * h:64 * (h + 1)]

    @block.sync
    def _(sync):
        ctx.wait_mark(sync, "ln1_done")
        for cc in range(NCK):
            dl.dma(sync.dma_start(out=xnfm[:, T * cc:T * (cc + 1)],
                                  in_=io["xn1_fm"][128 * cc:128 * (cc + 1), :]))
        dl.dma(sync.dma_start(out=drow[:], in_=io["decay"]))

    @block.gpsimd
    def _(g):
        for cc in range(NCK):
            dg.dma(g.dma_start(out=wpool[:, D * cc:D * (cc + 1)],
                               in_=io["WwT"][128 * cc:128 * (cc + 1), :]))
            dg.dma(g.dma_start(out=wb_sb[:, DH * cc:DH * (cc + 1)],
                               in_=io["WbT"][128 * cc:128 * (cc + 1), :]))
        dio = ctx.sem("dk_io")
        g.iota(iopm[:], pattern=[[-1, 64]], base=0, channel_multiplier=1,
               allow_small_or_imprecise_dtypes=True)
        g.iota(iocol[:], pattern=[[0, 1]], base=0, channel_multiplier=1,
               allow_small_or_imprecise_dtypes=True)
        dio.inc(g.iota(iorow[:], pattern=[[1, 64]], base=0, channel_multiplier=0,
                       allow_small_or_imprecise_dtypes=True))
        ctx.mark("dk_iota", dio)

    @block.vector
    def _(v):
        v.memset(onesr[:], 1.0)
        v.memset(onesrb[:], 1.0)
        vc.inc(v.memset(onescb[:], 1.0))
        for h in range(DH):
            v.memset(S_f[h][:], 0.0)
            vc.inc(v.memset(S_b[h][:], 0.0))
    ctx.mark("dk_ones", vc)

    @block.scalar
    def _(s):
        s.wait_ge(dl.h, dl.n)
        sc.inc(s.activation(out=lgrow[:], in_=drow[:], func=AF.Sigmoid))
        s.wait_ge(sc.h, sc.n)
        sc.inc(s.activation(out=lgrow[:], in_=lgrow[:], func=AF.Ln))
    ctx.mark("dk_lg", sc)

    @block.tensor
    def _(te):
        ctx.wait_mark(te, "dk_lg")
        ctx.wait_mark(te, "dk_ones")
        te.matmul(ctx.psb[0][0:64, 0:DH], onesr[:, 0:64], lgrow[:], start=True, stop=True)
        tc.inc(te.matmul(ctx.psb[1][:, 0:DH], onesr[:], lgrow[:], start=True, stop=True))
    ctx.mark("dk_lgmm", tc)

    @block.scalar
    def _(s):
        ctx.wait_mark(s, "dk_lgmm")
        s.activation(out=lgc64[:], in_=ctx.psb[0][0:64, 0:DH], func=AF.Copy)
        sc.inc(s.activation(out=lgc128[:], in_=ctx.psb[1][:, 0:DH], func=AF.Copy))
    ctx.mark("dk_lgc", sc)

    for h in range(DH):
        @block.vector
        def _(v, h=h):
            if h == 0:
                ctx.wait_mark(v, "dk_lgc")
                ctx.wait_mark(v, "dk_iota")
            else:
                ctx.wait_mark(v, f"dk_lmask{h - 1}")
            vc.inc(v.tensor_scalar(out=t64a[:], in0=iopm[:], scalar1=lgc64[:, h:h + 1],
                                   scalar2=None, op0=ALU.mult))
            v.wait_ge(vc.h, vc.n)
            vc.inc(v.tensor_scalar(out=t64b[:], in0=t64a[:], scalar1=-1.0,
                                   scalar2=None, op0=ALU.mult))
            v.wait_ge(vc.h, vc.n)
            v.tensor_scalar(out=de_col[:, h:h + 1], in0=iocol[:], scalar1=1.0,
                            scalar2=lgc64[:, h:h + 1], op0=ALU.add, op1=ALU.mult)
            v.tensor_scalar(out=dw_col[:, h:h + 1], in0=iocol[:], scalar1=63.0,
                            scalar2=lgc64[:, h:h + 1], op0=ALU.subtract,
                            op1=ALU.mult)
            v.tensor_scalar(out=ctd128[:, h:h + 1], in0=lgc128[:, h:h + 1],
                            scalar1=64.0, scalar2=None, op0=ALU.mult)
            vc.inc(v.tensor_scalar(out=de_row[:, 64 * h:64 * (h + 1)], in0=iorow[:],
                                   scalar1=1.0, scalar2=lgrow[:, h:h + 1],
                                   op0=ALU.add, op1=ALU.mult))
        ctx.mark(f"dk_dd{h}", vc)

        @block.scalar
        def _(s, h=h):
            ctx.wait_mark(s, f"dk_dd{h}")
            s.activation(out=t64a[:], in_=t64a[:], func=AF.Exp)
            s.activation(out=t64b[:], in_=t64b[:], func=AF.Exp)
            s.activation(out=de_col[:, h:h + 1], in_=de_col[:, h:h + 1], func=AF.Exp)
            s.activation(out=dw_col[:, h:h + 1], in_=dw_col[:, h:h + 1], func=AF.Exp,
                         scale=-1.0)
            s.activation(out=ctd128[:, h:h + 1], in_=ctd128[:, h:h + 1], func=AF.Exp)
            sc.inc(s.activation(out=de_row[:, 64 * h:64 * (h + 1)],
                                in_=de_row[:, 64 * h:64 * (h + 1)], func=AF.Exp))
        ctx.mark(f"dk_exp{h}", sc)

        @block.gpsimd
        def _(g, h=h):
            ctx.wait_mark(g, f"dk_exp{h}")
            gio = ctx.sem("dk_gio")
            g.affine_select(negLs[:, 64 * h:64 * (h + 1)], t64a[:],
                            pattern=[[-1, 64]], compare_op=ALU.is_gt, fill=0.0,
                            base=0, channel_multiplier=1)
            gio.inc(g.affine_select(LTt[:, 64 * h:64 * (h + 1)], t64b[:],
                                    pattern=[[1, 64]], compare_op=ALU.is_ge,
                                    fill=0.0, base=0, channel_multiplier=-1))
        ctx.mark(f"dk_lmask{h}", ctx.sems["dk_gio"])

        @block.vector
        def _(v2, h=h):
            ctx.wait_mark(v2, f"dk_lmask{h}")
            vc.inc(v2.tensor_scalar(out=negLs[:, 64 * h:64 * (h + 1)],
                                    in0=negLs[:, 64 * h:64 * (h + 1)],
                                    scalar1=-1.0, scalar2=None, op0=ALU.mult))
        ctx.mark(f"dk_negl{h}", vc)
    ctx.mark("dk_consts", vc)

    @block.tensor
    def _(te):
        ctx.wait_mark(te, f"dk_exp{DH - 1}")
        ctx.wait_mark(te, "dk_consts")
        last = None
        for h in range(DH):
            last = te.matmul(ctx.psb[2][:, 0:448] if False else
                             ctx.psb[2 + h // 4][:, 64 * (h % 4):64 * (h % 4 + 1)],
                             onesr[:], de_row[:, 64 * h:64 * (h + 1)],
                             start=True, stop=True)
        tc.inc(last)
    ctx.mark("dk_debmm", tc)

    @block.scalar
    def _(s):
        ctx.wait_mark(s, "dk_debmm")
        s.activation(out=de_b[:, 0:256], in_=ctx.psb[2][:, 0:256], func=AF.Copy)
        sc.inc(s.activation(out=de_b[:, 256:512], in_=ctx.psb[3][:, 0:256], func=AF.Copy))
    ctx.mark("dk_deb", sc)

    for i in range(NT):
        @block.tensor
        def _(te, i=i):
            te.wait_ge(dl.h, dl.n)
            te.wait_ge(dg.h, dg.n)
            ctx.wait_mark(te, "dk_deb")
            if i >= 2:
                ctx.wait_mark(te, f"dk_bt{i % 2}")
            last = None
            for cc in range(NCK):
                last = te.matmul(ctx.psb[4 + i % 2][:, 0:DH],
                                 xnfm[:, T * cc + 128 * i:T * cc + 128 * (i + 1)],
                                 wb_sb[:, DH * cc:DH * (cc + 1)],
                                 start=(cc == 0), stop=(cc == NCK - 1))
            tc.inc(last)
        ctx.mark(f"dk_btm{i % 2}", tc)

        @block.scalar
        def _(s, i=i):
            ctx.wait_mark(s, f"dk_btm{i % 2}")
            sc.inc(s.activation(out=beta_tm[:, DH * i:DH * (i + 1)],
                                in_=ctx.psb[4 + i % 2][:, 0:DH], func=AF.Sigmoid))
        ctx.mark(f"dk_bt{i % 2}", sc)
    ctx.mark("dk_btall", sc)

    @block.sync
    def _(sync):
        ctx.wait_mark(sync, "dk_btall")
        bsem = ctx.sem("dk_bc")
        src_lo = beta_tm.ap()[0:64, :]
        src_hi = beta_tm.ap()[64:128, :]
        bsem.dma(sync.dma_start(out=betaC[:, 0:HB], in_=src_lo))
        bsem.dma(sync.dma_start(out=betaC[:, HB:2 * HB], in_=src_hi))
        ctx.mark("dk_betaC", bsem)

    for dc in range(NCK):
        for tq in range(T // 512):
            @block.tensor
            def _(te, dc=dc, tq=tq):
                te.wait_ge(dl.h, dl.n)
                te.wait_ge(dg.h, dg.n)
                ctx.wait_mark(te, "dk_deb")
                ctx.wait_mark(te, "dk_btall")
                idx = dc * (T // 512) + tq
                if idx >= 2:
                    ctx.wait_mark(te, f"dk_vfe{idx % 2}")
                last = None
                for cc in range(NCK):
                    last = te.matmul(
                        ctx.psb[idx % 2][:],
                        wpool[:, D * cc + 128 * dc:D * cc + 128 * (dc + 1)],
                        xnfm[:, T * cc + 512 * tq:T * cc + 512 * (tq + 1)],
                        start=(cc == 0), stop=(cc == NCK - 1))
                tc.inc(last)
            ctx.mark(f"dk_vfm{(dc * (T // 512) + tq) % 2}", tc)

            @block.vector
            def _(v, dc=dc, tq=tq):
                idx = dc * (T // 512) + tq
                ctx.wait_mark(v, f"dk_vfm{idx % 2}")
                vc.inc(v.tensor_copy(
                    vfm[:, T * dc + 512 * tq:T * dc + 512 * (tq + 1)],
                    ctx.psb[idx % 2][:]))
            ctx.mark(f"dk_vfe{(dc * (T // 512) + tq) % 2}", vc)
    ctx.mark("dk_vball", vc)

    for h in range(DH):
        for tq in range(T // 512):
            @block.vector
            def _(v, h=h, tq=tq):
                ctx.wait_mark(v, "dk_vball")
                if h > 0 or tq > 0:
                    ctx.wait_mark(v, "dk_knsq")
                vc.inc(v.scalar_tensor_tensor(
                    out=s1t[0][:, 0:512], in0=xnfm[:, T * h + 512 * tq:T * h + 512 * (tq + 1)],
                    scalar=1.0, in1=xnfm[:, T * h + 512 * tq:T * h + 512 * (tq + 1)],
                    op0=ALU.mult, op1=ALU.mult))
            ctx.mark("dk_knsq", vc)

            @block.tensor
            def _(te, h=h, tq=tq):
                ctx.wait_mark(te, "dk_knsq")
                if h > 0 or tq > 0:
                    ctx.wait_mark(te, "dk_knrs")
                tc.inc(te.matmul(ctx.psb[6][0:1, :], onescb[:],
                                 s1t[0][:, 0:512], start=True, stop=True))
            ctx.mark("dk_knmm", tc)

            @block.scalar
            def _(s, h=h, tq=tq):
                ctx.wait_mark(s, "dk_knmm")
                sc.inc(s.activation(out=rst_row[:], in_=ctx.psb[6][0:1, :],
                                     func=AF.Ln))
                s.wait_ge(sc.h, sc.n)
                sc.inc(s.activation(out=rst_row[:], in_=rst_row[:], func=AF.Exp,
                                    scale=-0.5))
            ctx.mark("dk_knr", sc)

            @block.tensor
            def _(te, h=h, tq=tq):
                ctx.wait_mark(te, "dk_knr")
                tc.inc(te.matmul(ctx.psb[7][:], onesrb[:], rst_row[:],
                                 start=True, stop=True))
            ctx.mark("dk_knrs", tc)

            @block.vector
            def _(v, h=h, tq=tq):
                ctx.wait_mark(v, "dk_knrs")
                vc.inc(v.tensor_tensor(
                    out=xnfm[:, T * h + 512 * tq:T * h + 512 * (tq + 1)],
                    in0=xnfm[:, T * h + 512 * tq:T * h + 512 * (tq + 1)],
                    in1=ctx.psb[7][:], op=ALU.mult))
            ctx.mark("dk_knorm", vc)
    ctx.mark("dk_kall", vc)

    ctx.marks["dk_e1"] = (vc.h, 0)
    ctx.marks["dk_e7"] = ctx.marks["dk_ones"]
    ctx.marks["dk_e8"] = (vc.h, 0)
    ctx.marks["dk_e4p"] = (vc.h, 0)
    for c in range(NCHK):
        ti, hf = c // 2, c % 2
        r64 = slice(64 * hf, 64 * (hf + 1))

        def kslice(h, c=c):
            return xnfm[:, T * h + 64 * c:T * h + 64 * (c + 1)]

        @block.tensor
        def _(te, c=c):
            if c == 0:
                ctx.wait_mark(te, "dk_kall")
                ctx.wait_mark(te, "dk_vball")
                ctx.wait_mark(te, "prelude_ld")
            ctx.wait_mark(te, "dk_e1")
            ctx.wait_mark(te, "dk_betaC")
            last = None
            for h in range(DH):
                te.matmul(kk_ps(h), kslice(h), kslice(h), start=True, stop=True)
                last = te.transpose(ktr_ps(h), kslice(h), ident[:])
            tc.inc(last)
        ctx.mark("dk_t1", tc)

        @block.vector
        def _(v, c=c, ti=ti, hf=hf):
            ctx.wait_mark(v, "dk_t1")
            for h in range(DH):
                v.tensor_copy(kk_s[h][:], kk_ps(h))
                vc.inc(v.tensor_copy(ktm[h][:], ktr_ps(h)))
        ctx.mark("dk_e1k", vc)

        @block.tensor
        def _(te, c=c):
            ctx.wait_mark(te, "dk_e1k")
            last = None
            for h in range(DH):
                last = te.transpose(vtr_ps(h),
                                    vfm[:, T * h + 64 * c:T * h + 64 * (c + 1)],
                                    ident[:])
            tc.inc(last)
        ctx.mark("dk_t1v", tc)

        @block.vector
        def _(v, c=c, ti=ti, hf=hf):
            ctx.wait_mark(v, "dk_t1v")
            for h in range(DH):
                bcol0 = betaC.ap()[:, HB * hf + DH * ti + h:HB * hf + DH * ti + h + 1]
                vc.inc(v.tensor_scalar(out=vtm[h][:], in0=vtr_ps(h), scalar1=bcol0,
                                       scalar2=None, op0=ALU.mult))
            v.wait_ge(vc.h, vc.n)
            last = None
            for h in range(DH):
                bcol = betaC.ap()[:, HB * hf + DH * ti + h:HB * hf + DH * ti + h + 1]
                v.scalar_tensor_tensor(out=A0[h][:], in0=kk_s[h][:], scalar=bcol,
                                       in1=negLs[:, 64 * h:64 * (h + 1)],
                                       op0=ALU.mult, op1=ALU.mult)
                v.tensor_tensor(out=atT[h][:], in0=kk_s[h][:],
                                in1=LTt[:, 64 * h:64 * (h + 1)], op=ALU.mult)
                v.tensor_scalar(out=kbde[h][:], in0=ktm[h][:], scalar1=bcol,
                                scalar2=de_col[:, h:h + 1], op0=ALU.mult, op1=ALU.mult)
                v.tensor_scalar(out=kdw[h][:], in0=ktm[h][:],
                                scalar1=dw_col[:, h:h + 1], scalar2=None, op0=ALU.mult)
                last = v.tensor_tensor(out=kdeT[h][:], in0=kslice(h),
                                       in1=de_b[:, 64 * h:64 * (h + 1)], op=ALU.mult)
            vc.inc(last)
            v.wait_ge(vc.h, vc.n)
            for h in range(DH):
                vc.inc(v.tensor_copy(Xb[h][:], ident[0:64, 0:64]))
            v.wait_ge(vc.h, vc.n)
            for h in range(DH):
                vc.inc(v.tensor_copy(Pb[h][:], A0[h][:]))
        ctx.mark("dk_e1", vc)

        NR = 6
        for r in range(NR):
            @block.tensor
            def _(te, c=c, r=r):
                ctx.wait_mark(te, "dk_e1" if r == 0 else f"dk_ne{r - 1}")
                last = None
                for h in range(DH):
                    last = te.transpose(ptr_ps(h), Pb[h][:], ident[0:64, 0:64])
                tc.inc(last)
            ctx.mark(f"dk_ntr{r}", tc)

            @block.vector
            def _(v, c=c, r=r):
                ctx.wait_mark(v, f"dk_ntr{r}")
                for h in range(DH):
                    vc.inc(v.tensor_copy(PTs[h][:], ptr_ps(h)))
            ctx.mark(f"dk_nev{r}", vc)

            @block.tensor
            def _(te, c=c, r=r):
                ctx.wait_mark(te, f"dk_nev{r}")
                last = None
                for h in range(DH):
                    last = te.matmul(nwx_ps(h), PTs[h][:], Xb[h][:],
                                     start=True, stop=True)
                    if r < NR - 1:
                        last = te.matmul(nwp_ps(h), PTs[h][:], Pb[h][:],
                                         start=True, stop=True)
                tc.inc(last)
            ctx.mark(f"dk_nmm{r}", tc)

            @block.vector
            def _(v, c=c, r=r):
                ctx.wait_mark(v, f"dk_nmm{r}")
                for h in range(DH):
                    vc.inc(v.tensor_tensor(out=Xb[h][:], in0=Xb[h][:],
                                           in1=nwx_ps(h), op=ALU.add))
                if r < NR - 1:
                    v.wait_ge(vc.h, vc.n)
                    for h in range(DH):
                        vc.inc(v.tensor_copy(Pb[h][:], nwp_ps(h)))
            ctx.mark(f"dk_ne{r}", vc)

        @block.tensor
        def _(te, c=c):
            ctx.wait_mark(te, f"dk_ne{NR - 1}")
            last = None
            for h in range(DH):
                last = te.transpose(ptr_ps(h), Xb[h][:], ident[0:64, 0:64])
            tc.inc(last)
        ctx.mark("dk_t3", tc)

        @block.vector
        def _(v, c=c):
            ctx.wait_mark(v, "dk_t3")
            for h in range(DH):
                vc.inc(v.tensor_copy(ATs[h][:], ptr_ps(h)))
        ctx.mark("dk_e3", vc)

        @block.tensor
        def _(te, c=c, ti=ti, hf=hf):
            ctx.wait_mark(te, "dk_e3")
            ctx.wait_mark(te, "dk_e4p")
            last = None
            for h in range(DH):
                te.matmul(yv_ps(h), ATs[h][:], vtm[h][:], start=True, stop=True)
                last = te.matmul(o_ps(h), ATs[h][:], kbde[h][:], start=True, stop=True)
            tc.inc(last)
        ctx.mark("dk_t4", tc)

        @block.vector
        def _(v, c=c):
            ctx.wait_mark(v, "dk_t4")
            for h in range(DH):
                v.tensor_copy(Yv_s[h][:], yv_ps(h))
                vc.inc(v.tensor_copy(kcd[h][:], o_ps(h)))
        ctx.mark("dk_e4", vc)
        ctx.mark("dk_e4p", vc)

        @block.tensor
        def _(te, c=c):
            ctx.wait_mark(te, "dk_e4")
            last = None
            for h in range(DH):
                last = te.transpose(ttr_ps(h), kcd[h][:], ident[0:64, 0:64])
            tc.inc(last)
        ctx.mark("dk_t5", tc)

        @block.vector
        def _(v, c=c):
            ctx.wait_mark(v, "dk_t5")
            for h in range(DH):
                vc.inc(v.tensor_copy(kcdT[h][:], ttr_ps(h)))
        ctx.mark("dk_e5", vc)

        @block.tensor
        def _(te, c=c):
            ctx.wait_mark(te, "dk_e5")
            ctx.wait_mark(te, "dk_e7")
            last = None
            for h in range(DH):
                last = te.matmul(o_ps(h), kcdT[h][:], S_b[h][:], start=True, stop=True)
            tc.inc(last)
        ctx.mark("dk_t6", tc)

        @block.vector
        def _(v, c=c):
            ctx.wait_mark(v, "dk_t6")
            for h in range(DH):
                vc.inc(v.tensor_tensor(out=vnew[h][:], in0=Yv_s[h][:], in1=o_ps(h),
                                       op=ALU.subtract))
        ctx.mark("dk_e6", vc)

        @block.tensor
        def _(te, c=c):
            ctx.wait_mark(te, "dk_e6")
            last = None
            for h in range(DH):
                te.matmul(yv_ps(h), kdeT[h][:], S_b[h][:], start=True, stop=False)
                te.matmul(yv_ps(h), atT[h][:], vnew[h][:], start=False, stop=True)
                last = te.matmul(dW_ps(h), kdw[h][:], vnew[h][:], start=True, stop=True)
            tc.inc(last)
        ctx.mark("dk_t7", tc)

        @block.vector
        def _(v, c=c):
            ctx.wait_mark(v, "dk_t7")
            for h in range(DH):
                vc.inc(v.tensor_copy(o_s[h][:], yv_ps(h)))
            v.wait_ge(vc.h, vc.n)
            for h in range(DH):
                vc.inc(v.scalar_tensor_tensor(out=S_f[h][:], in0=S_f[h][:],
                                              scalar=ctd128[:, h:h + 1], in1=dW_ps(h),
                                              op0=ALU.mult, op1=ALU.add))
            v.wait_ge(vc.h, vc.n)
            for h in range(DH):
                vc.inc(v.tensor_copy(S_b[h][:], S_f[h][:]))
        ctx.mark("dk_e7", vc)

        @block.tensor
        def _(te, c=c):
            ctx.wait_mark(te, "dk_e7")
            ctx.wait_mark(te, "dk_e8")
            last = None
            for h in range(DH):
                last = te.transpose(ttr_ps(h), o_s[h][:], ident[0:64, 0:64])
            tc.inc(last)
        ctx.mark("dk_t8", tc)

        @block.vector
        def _(v, c=c):
            ctx.wait_mark(v, "dk_t8")
            v.wait_ge(dst[c % 2].h, dst[c % 2].n)
            for h in range(DH):
                vc.inc(v.tensor_copy(oTst[c % 2][h][:], ttr_ps(h)))
        ctx.mark("dk_e8", vc)

        @block.sync
        def _(sync, c=c):
            ctx.wait_mark(sync, "dk_e8")
            sync.wait_ge(dst[c % 2].h, dst[c % 2].n)
            for h in range(DH):
                dst[c % 2].dma(sync.dma_start(
                    out=io["oT1h"][DHD * h:DHD * (h + 1), 64 * c:64 * (c + 1)],
                    in_=oTst[c % 2][h][:]))

    @block.gpsimd
    def _(g):
        g.wait_ge(dg.h, dg.n)
        ctx.wait_mark(g, "dk_vball")
        for cc in range(NCK):
            dg.dma(g.dma_start(out=wpool[:, D * cc:D * (cc + 1)],
                               in_=io["WrT"][128 * cc:128 * (cc + 1), :]))
    ctx.mark("dk_wr", dg)

    for tg in range(T // 512):
        for oc in range(NCK):
            @block.sync
            def _(sync, tg=tg, oc=oc):
                if tg == 0 and oc == 0:
                    sync.wait_ge(dst[0].h, dst[0].n)
                    sync.wait_ge(dst[1].h, dst[1].n)
                idx = tg * NCK + oc
                if idx >= 2:
                    ctx.wait_mark(sync, f"dk_pj{oc % 2}")
                lp[oc % 2].dma(sync.dma_start(
                    out=oTld[oc % 2][:],
                    in_=io["oT1h"][128 * oc:128 * (oc + 1), 512 * tg:512 * (tg + 1)]))

            @block.tensor
            def _(te, tg=tg, oc=oc):
                te.wait_ge(lp[oc % 2].h, lp[oc % 2].n)
                if tg == 0 and oc == 0:
                    ctx.wait_mark(te, "dk_wr")
                if oc == 0:
                    ctx.wait_mark(te, "dk_pjev")
                last = None
                for i4 in range(4):
                    for nb in range(2):
                        last = te.matmul(
                            ctx.psb[i4 * 2 + nb][:],
                            oTld[oc % 2][:, 128 * i4:128 * (i4 + 1)],
                            wpool[:, D * oc + 512 * nb:D * oc + 512 * (nb + 1)],
                            start=(oc == 0), stop=(oc == NCK - 1))
                tc.inc(last)
            ctx.mark(f"dk_pj{oc % 2}", tc)
        ctx.mark("dk_pjmm", tc)

        for i4 in range(4):
            @block.sync
            def _(sync, tg=tg, i4=i4):
                i = 4 * tg + i4
                if i >= 2:
                    ctx.wait_mark(sync, f"dk_s1v{i4 % 2}")
                lq[i4 % 2].dma(sync.dma_start(
                    out=xnld[i4 % 2][:], in_=io["xn1_tm"][128 * i:128 * (i + 1), :]))

            @block.vector
            def _(v, tg=tg, i4=i4):
                if i4 == 0:
                    ctx.wait_mark(v, "dk_pjmm")
                v.wait_ge(lq[i4 % 2].h, lq[i4 % 2].n)
                v.wait_ge(pst[i4 % 2].h, pst[i4 % 2].n)
                last = None
                for nb in range(2):
                    last = v.tensor_tensor(
                        out=s1t[i4 % 2][:, 512 * nb:512 * (nb + 1)],
                        in0=ctx.psb[i4 * 2 + nb][:],
                        in1=xnld[i4 % 2][:, 512 * nb:512 * (nb + 1)], op=ALU.add)
                vc.inc(last)
            if i4 == 3:
                ctx.mark("dk_pjev", vc)
            ctx.mark(f"dk_s1v{i4 % 2}", vc)

            @block.sync
            def _(sync, tg=tg, i4=i4):
                ctx.wait_mark(sync, f"dk_s1v{i4 % 2}")
                i = 4 * tg + i4
                pst[i4 % 2].dma(sync.dma_start(
                    out=io["s1"][128 * i:128 * (i + 1), :], in_=s1t[i4 % 2][:]))

    ctx.mark("delta_done", pst[1])


TENSORS = {
    "x": ([T, D], F32),
    "qkv_wT": ([D, 3 * D], F32), "qkv_b": ([1, 3 * D], F32),
    "proj_wT": ([D, D], F32), "proj_b": ([1, D], F32),
    "wq_peerT": ([D, 2 * D], F32), "keysT": ([PH, 2, PDK, NKEYS], F32),
    "norm_w": ([1, D], F32),
    "exp_down": ([NEXP, D], F32), "exp_up": ([NEXP, D], F32),
    "WwT": ([D, D], F32), "WrT": ([D, D], F32), "WbT": ([D, DH], F32),
    "decay": ([1, DH], F32),
    "router": ([D, NE], F32),
    "Wg": ([NE, D, FF], F32), "Wu": ([NE, D, FF], F32), "Wd": ([NE, FF, D], F32),
    "rope_cos": ([T, AHD // 2], F32), "rope_sin": ([T, AHD // 2], F32),
    "diag_masks": ([4 * 128, 512], BF16), "ident_bf": ([128, 128], BF16),
    "xn_tm": ([T, D], BF16), "xn_fm": ([D, T], BF16),
    "s": ([T, D], BF16), "oTh": ([D, T], BF16),
    "x1": ([T, D], F32),
    "xn1_tm": ([T, D], BF16), "xn1_fm": ([D, T], BF16),
    "s1": ([T, D], BF16), "moeacc": ([T, D], F32), "oT1h": ([D, T], BF16),
    "out": ([T, D], F32),
}

STAGES = {
    "ln0": dict(reads=["x", "ident_bf"], writes=["xn_tm", "xn_fm"],
                emit=lambda ctx: emit_ln(ctx, "x", "xn_tm", "xn_fm", "ln0")),
    "ln1": dict(reads=["x1", "ident_bf"], writes=["xn1_tm", "xn1_fm"],
                emit=lambda ctx: emit_ln(ctx, "x1", "xn1_tm", "xn1_fm", "ln1")),
    "attn0": dict(reads=["xn_fm", "qkv_wT", "qkv_b", "proj_wT", "proj_b",
                         "rope_cos", "rope_sin", "diag_masks", "ident_bf"],
                  writes=["s"], scratch=["oTh"], emit=emit_attn0),
    "peer": dict(reads=["xn_tm", "xn_fm", "wq_peerT", "keysT", "norm_w",
                        "exp_down", "exp_up", "x", "s", "ident_bf"],
                 writes=["x1"], emit=emit_peer),
    "moe": dict(reads=["xn1_fm", "router", "Wg", "Wu", "Wd", "x1", "s1",
                       "ident_bf"],
                writes=["out"], scratch=["moeacc"], emit=emit_moe),
    "delta": dict(reads=["xn1_fm", "xn1_tm", "WwT", "WbT", "WrT", "decay",
                         "ident_bf"],
                  writes=["s1"], scratch=["oT1h"], emit=emit_delta),
}


def build_program(stage_names):
    stages = [STAGES[s] for s in stage_names]
    reads = set()
    writes = set()
    scratch = set()
    for s in stages:
        reads.update(s["reads"])
        writes.update(s["writes"])
        scratch.update(s.get("scratch", []))
    fused_all = "out" in writes and len(stage_names) > 1

    stack = ExitStack()
    nc = bass.Bass("TRN2", target_bir_lowering=False, debug=False)
    io = {}
    in_names = []
    out_names = []
    for name in sorted(reads | writes | scratch):
        shape, dt = TENSORS[name]
        if name in scratch:
            kind = "Internal"
        elif name in writes:
            if fused_all and name != "out":
                kind = "Internal"
            else:
                kind = "ExternalOutput"
                out_names.append(name)
        else:
            kind = "ExternalInput"
            in_names.append(name)
        io[name] = nc.dram_tensor(name, shape, dt, kind=kind).ap()

    ctx = Ctx(nc, stack, io)
    with stack:
        with nc.Block() as block:
            ctx.block = block
            emit_prelude(ctx)
            for si, s in enumerate(stages):
                if si > 0:
                    def _bar(eng):
                        for sem in ctx.sems.values():
                            if sem.n > 0:
                                eng.wait_ge(sem.h, sem.n)
                    block.sync(_bar)
                    block.gpsimd(_bar)
                    block.vector(_bar)
                    block.scalar(_bar)
                    block.tensor(_bar)
                with ExitStack() as scope:
                    ctx.scope = scope
                    s["emit"](ctx)
                ctx.scope = stack

            @block.sync
            def _(sync):
                for sem in ctx.sems.values():
                    if sem.n > 0:
                        sync.wait_ge(sem.h, sem.n)
    return nc, in_names, out_names


def run_stages(stage_names, per_core, core_ids=None):
    core_ids = core_ids or list(range(len(per_core)))
    nc, in_names, out_names = build_program(stage_names)
    in_maps = [{n: pc[n] for n in in_names} for pc in per_core]
    res = run_bass_kernel_spmd(nc, in_maps, core_ids=core_ids)
    for pc, r in zip(per_core, res.results):
        for n in out_names:
            pc[n] = r[n]
    return per_core


PIPELINE = ["ln0", "attn0", "peer", "ln1", "delta", "moe"]
FUSED = True


def kernel(**inputs):
    w = host_prepare(inputs)
    xfull = np.asarray(inputs["L0_attn_x"])
    per_core = [dict(w, x=np.ascontiguousarray(xfull[i])) for i in range(B)]
    if FUSED:
        run_stages(PIPELINE, per_core)
    else:
        for st in PIPELINE:
            run_stages([st], per_core)
    return np.stack([pc["out"] for pc in per_core], axis=0)


# version 03 (final)
# speedup vs baseline: 1.1477x; 1.1477x over previous
import math
from contextlib import ExitStack

import numpy as np
import ml_dtypes

import concourse.bass as bass
import concourse.mybir as mybir
from concourse.bass import IndirectOffsetOnAxis
from concourse.bass_utils import run_bass_kernel_spmd

F32 = mybir.dt.float32
BF16 = mybir.dt.bfloat16
I32 = mybir.dt.int32
U32 = mybir.dt.uint32
AF = mybir.ActivationFunctionType
ALU = mybir.AluOpType
AX = mybir.AxisListType

B = 8
T = 4096
D = 1024
NT = T // 128
NCK = D // 128

AH = 16
AHD = 64
PH = 8
PDK = 128
PK = 8
NKEYS = 256
NEXP = NKEYS * NKEYS
DH = 8
DHD = 128
CH = 64
NCHK = T // CH
NE = 8
FF = 2048

BF = ml_dtypes.bfloat16

QKS = 1.0 / math.sqrt(8.0)
NGRP = 4
GH = AH // NGRP



def host_prepare(inp):
    c = lambda a: np.ascontiguousarray(a)
    w = {}
    w["qkv_wT"] = c(np.asarray(inp["L0_attn_qkv_w"]).T)
    w["qkv_b"] = c(np.asarray(inp["L0_attn_qkv_b"]).reshape(1, 3 * D))
    w["proj_wT"] = c(np.asarray(inp["L0_attn_proj_w"]).T)
    w["proj_b"] = c(np.asarray(inp["L0_attn_proj_b"]).reshape(1, D))
    w["wq_peerT"] = c(np.asarray(inp["L0_ffn_w_q"]).T)
    w["keysT"] = c(np.asarray(inp["L0_ffn_keys"]).transpose(0, 2, 3, 1))
    w["norm_w"] = c(np.asarray(inp["L0_ffn_norm_w"]).reshape(1, D))
    w["exp_down"] = c(np.asarray(inp["L0_ffn_expert_down"]))
    w["exp_up"] = c(np.asarray(inp["L0_ffn_expert_up"]))
    w["WwT"] = c(np.asarray(inp["L1_attn_W_write"]).T)
    w["WrT"] = c(np.asarray(inp["L1_attn_W_read"]).T)
    w["WbT"] = c(np.asarray(inp["L1_attn_W_beta"]).T)
    w["decay"] = c(np.asarray(inp["L1_attn_decay"]).reshape(1, DH))
    w["router"] = c(np.asarray(inp["L1_ffn_router_kernel_DE"]))
    w["Wg"] = c(np.asarray(inp["L1_ffn_kernel_gating_EDF"]))
    w["Wu"] = c(np.asarray(inp["L1_ffn_kernel_up_EDF"]))
    w["Wd"] = c(np.asarray(inp["L1_ffn_kernel_down_EFD"]))

    qkvT = w["qkv_wT"]
    gw = np.zeros((NGRP, NCK, 128, 3 * 64 * GH), dtype=np.float32)
    for g in range(NGRP):
        cols = np.r_[64 * GH * g:64 * GH * (g + 1),
                     D + 64 * GH * g:D + 64 * GH * (g + 1),
                     2 * D + 64 * GH * g:2 * D + 64 * GH * (g + 1)]
        for cc in range(NCK):
            gw[g, cc] = qkvT[128 * cc:128 * (cc + 1), cols]
    w["qkv_wg"] = c(gw.reshape(NGRP * NCK * 128, 3 * 64 * GH))

    half = AHD // 2
    freqs = 1.0 / 10000.0 ** (np.arange(half, dtype=np.float64) / half)
    ang = np.arange(T, dtype=np.float64)[:, None] * freqs[None, :]
    cosf = np.cos(ang).astype(np.float32)
    sinf = np.sin(ang).astype(np.float32)
    w["rope_cos"] = c(cosf.reshape(T // 128, 128, half).transpose(1, 0, 2).reshape(128, -1))
    w["rope_sin"] = c(sinf.reshape(T // 128, 128, half).transpose(1, 0, 2).reshape(128, -1))
    masks = np.zeros((4, 128, 512), dtype=BF)
    kk, qq = np.arange(128)[:, None], np.arange(512)[None, :]
    for m in range(4):
        masks[m] = (kk <= qq - 128 * m).astype(BF)
    w["diag_masks"] = c(masks.transpose(1, 0, 2).reshape(128, 4 * 512))
    w["ident_bf"] = np.eye(128, dtype=BF)
    return w



class Sem:
    __slots__ = ("h", "n")

    def __init__(self, h):
        self.h = h
        self.n = 0

    def inc(self, instr, k=1):
        instr.then_inc(self.h, k)
        self.n += k
        return instr

    def dma(self, instr):
        return self.inc(instr, 16)


class Ctx:
    def __init__(self, nc, stack, io):
        self.nc = nc
        self.stack = stack
        self.scope = stack
        self.io = io
        self.block = None
        self.sems = {}
        self.marks = {}
        self.psbt = [nc.alloc_psum_tensor(f"psbt{i}", [128, 1024], F32)
                     for i in range(4)]
        self.psb = [self.psbt[i // 2].ap()[:, 512 * (i % 2):512 * (i % 2 + 1)]
                    for i in range(8)]
        self.const = {}

    def sem(self, name):
        if name not in self.sems:
            h = self.stack.enter_context(self.nc.semaphore(name))
            self.sems[name] = Sem(h)
        return self.sems[name]

    def mark(self, name, sem=None, n=None):
        self.marks[name] = (sem.h, sem.n if n is None else n)

    def wait_mark(self, eng, name):
        if name not in self.marks:
            return
        h, n = self.marks[name]
        if n > 0:
            eng.wait_ge(h, n)

    def pbf(self, i):
        return self.psb[i].bitcast(BF16)


def _sb(ctx, name, shape, dtype=F32):
    return ctx.scope.enter_context(ctx.nc.sbuf_tensor(name, shape, dtype))



def emit_prelude(ctx):
    nc, block = ctx.nc, ctx.block
    ld = ctx.sem("pre_ld")
    ident = _sb(ctx, "c_ident", [128, 128], BF16)
    ones_row = _sb(ctx, "c_ones_row", [1, 512], BF16)
    ctx.const["ident"] = ident
    ctx.const["ones_row"] = ones_row

    @block.sync
    def _(sync):
        ld.dma(sync.dma_start(out=ident[:], in_=ctx.io["ident_bf"]))

    vc = ctx.sem("pre_vc")

    @block.vector
    def _(v):
        vc.inc(v.memset(ones_row[:], 1.0))

    ctx.mark("prelude", vc)
    ctx.mark("prelude_ld", ld)



def emit_ln(ctx, x_name, xn_tm_name, xn_fm_name, tag):
    nc, block = ctx.nc, ctx.block
    x_h = ctx.io[x_name]
    xn_tm_h = ctx.io[xn_tm_name]
    xn_fm_h = ctx.io[xn_fm_name]

    ld2 = [ctx.sem(f"{tag}_ld0"), ctx.sem(f"{tag}_ld1")]
    sc = ctx.sem(f"{tag}_sc")
    vc = ctx.sem(f"{tag}_vc")
    tc = ctx.sem(f"{tag}_tc")
    ev = ctx.sem(f"{tag}_ev")
    st = ctx.sem(f"{tag}_st")

    xld = [_sb(ctx, f"{tag}_xld{i}", [128, D], F32) for i in range(2)]
    scr = _sb(ctx, f"{tag}_scr", [128, D], F32)
    scr2 = _sb(ctx, f"{tag}_scr2", [128, D], F32)
    ssum = _sb(ctx, f"{tag}_ssum", [128, 2], F32)
    mst = _sb(ctx, f"{tag}_mst", [128, 4], F32)
    xn_tm = [_sb(ctx, f"{tag}_xntm{i}", [128, D], BF16) for i in range(2)]
    xn_fm = _sb(ctx, f"{tag}_xnfm", [128, NCK * T], BF16)

    ident = ctx.const["ident"]
    vfree = {}
    tdone = {}

    for i in range(NT):
        @block.sync
        def _(sync, i=i):
            if i >= 2:
                h, n = ctx.marks[f"{tag}_vfree{i - 2}"]
                sync.wait_ge(h, n)
            ld2[i % 2].dma(sync.dma_start(out=xld[i % 2][:],
                                          in_=x_h[128 * i:128 * (i + 1), :]))

        @block.scalar
        def _(s, i=i):
            s.wait_ge(ld2[i % 2].h, ld2[i % 2].n)
            if i >= 1:
                h, n = ctx.marks[f"{tag}_vfree{i - 1}"]
                s.wait_ge(h, n)
            sc.inc(s.activation(out=scr[:], in_=xld[i % 2][:], func=AF.Copy,
                                accum_out=ssum[:, 0:1]))
            sc.inc(s.activation(out=scr2[:], in_=xld[i % 2][:], func=AF.Square,
                                accum_out=ssum[:, 1:2]))

        @block.vector
        def _(v, i=i):
            v.wait_ge(sc.h, sc.n)
            vc.inc(v.tensor_scalar(out=mst[:, 0:1], in0=ssum[:, 0:1], scalar1=1.0 / D,
                                   scalar2=None, op0=ALU.mult))
            v.wait_ge(vc.h, vc.n)
            vc.inc(v.tensor_scalar(out=mst[:, 3:4], in0=mst[:, 0:1], scalar1=mst[:, 0:1],
                                   scalar2=-1e-5, op0=ALU.mult, op1=ALU.add))
            v.wait_ge(vc.h, vc.n)
            vc.inc(v.tensor_scalar(out=mst[:, 1:2], in0=ssum[:, 1:2], scalar1=1.0 / D,
                                   scalar2=mst[:, 3:4], op0=ALU.mult, op1=ALU.subtract))
        ctx.mark(f"{tag}_var{i}", vc)

        @block.scalar
        def _(s, i=i):
            v_h, v_n = ctx.marks[f"{tag}_var{i}"]
            s.wait_ge(v_h, v_n)
            sc.inc(s.activation(out=mst[:, 3:4], in_=mst[:, 1:2], func=AF.Ln))
            s.wait_ge(sc.h, sc.n)
            sc.inc(s.activation(out=mst[:, 2:3], in_=mst[:, 3:4], func=AF.Exp, scale=-0.5))

        @block.vector
        def _(v, i=i):
            v.wait_ge(sc.h, sc.n)
            if i >= 2:
                h, n = ctx.marks[f"{tag}_tdone{i - 2}"]
                v.wait_ge(h, n)
                v.wait_ge(st.h, st.n)
            vc.inc(v.tensor_scalar(out=xn_tm[i % 2][:], in0=xld[i % 2][:],
                                   scalar1=mst[:, 0:1], scalar2=mst[:, 2:3],
                                   op0=ALU.subtract, op1=ALU.mult))
        ctx.mark(f"{tag}_vfree{i}", vc)
        ctx.mark(f"{tag}_xntm{i}", vc)

        @block.sync
        def _(sync, i=i):
            h, n = ctx.marks[f"{tag}_xntm{i}"]
            sync.wait_ge(h, n)
            st.dma(sync.dma_start(out=xn_tm_h[128 * i:128 * (i + 1), :], in_=xn_tm[i % 2][:]))

        @block.tensor
        def _(te, i=i):
            h, n = ctx.marks[f"{tag}_xntm{i}"]
            te.wait_ge(h, n)
            ctx.wait_mark(te, "prelude_ld")
            for cc in range(NCK):
                k = i * NCK + cc
                if k >= 4:
                    te.wait_ge(ev.h, k - 3)
                tc.inc(te.transpose(ctx.pbf(k % 4)[:, 0:128],
                                    xn_tm[i % 2][:, 128 * cc:128 * (cc + 1)], ident[:]))

        @block.vector
        def _(v, i=i):
            for cc in range(NCK):
                k = i * NCK + cc
                v.wait_ge(tc.h, k + 1)
                ev.inc(v.tensor_copy(xn_fm[:, T * cc + 128 * i: T * cc + 128 * (i + 1)],
                                     ctx.pbf(k % 4)[:, 0:128]))
        ctx.mark(f"{tag}_tdone{i}", ev)

    @block.sync
    def _(sync):
        sync.wait_ge(ev.h, NT * NCK)
        for cc in range(NCK):
            st.dma(sync.dma_start(out=xn_fm_h[128 * cc:128 * (cc + 1), :],
                                  in_=xn_fm[:, T * cc:T * (cc + 1)]))
        sync.wait_ge(st.h, st.n)

    ctx.mark(f"{tag}_done", st)






def emit_av(ctx, block, pend, tc, vx, PT):
    hh, j, kg, nk, par, had_mask = pend

    @block.tensor
    def _(te):
        ctx.wait_mark(te, f"at_expg{par}")
        if had_mask:
            ctx.wait_mark(te, f"at_maskg{par}")
        last = None
        for idx, k in enumerate(kg):
            last = te.matmul(
                ctx.psb[2][0:65, :],
                vx.ap()[:, 65 * (GH * k + hh):65 * (GH * k + hh) + 65],
                PT[3 * par + idx][:],
                start=(k == 0), stop=(k == nk - 1))
        tc.inc(last)
    ctx.mark(f"at_avg{par}", tc)
    ctx.mark("at_avgrp", tc)


def emit_attn0(ctx):
    nc, block = ctx.nc, ctx.block
    io = ctx.io
    ld = ctx.sem("at_ld")
    gl = ctx.sem("at_gl")
    vc = ctx.sem("at_vc")
    sc = ctx.sem("at_sc")
    tc = ctx.sem("at_tc")
    st = ctx.sem("at_st")
    sto = [ctx.sem("at_sto0"), ctx.sem("at_sto1")]
    lb = ctx.sem("at_lb")
    sts = [ctx.sem("at_sts0"), ctx.sem("at_sts1")]
    lo = [ctx.sem("at_lo0"), ctx.sem("at_lo1")]

    ident = ctx.const["ident"]

    xnfm = _sb(ctx, "at_xnfm", [128, NCK * T], BF16)
    wpool = _sb(ctx, "at_wpool", [128, NCK * D], BF16)
    cos_sb = _sb(ctx, "at_cos", [128, NT * 32], BF16)
    sin_sb = _sb(ctx, "at_sin", [128, NT * 32], BF16)
    masks = _sb(ctx, "at_masks", [128, 4 * 512], BF16)
    bias_bg = _sb(ctx, "at_biasbg", [128, 3 * 64 * GH], F32)
    projb_b = _sb(ctx, "at_projbb", [128, D], F32)
    brow_g = _sb(ctx, "at_browg", [1, 3 * 64 * GH], BF16)
    pbrow = _sb(ctx, "at_pbrow", [1, D], BF16)
    onesr = _sb(ctx, "at_onesr", [1, 128], BF16)
    sclr = _sb(ctx, "at_sclr", [1, 128], BF16)
    ones64 = _sb(ctx, "at_ones64", [1, 64], BF16)

    qTx = [_sb(ctx, f"at_qTx{h}", [64, T], BF16) for h in range(GH)]
    kTx = [_sb(ctx, f"at_kTx{h}", [64, T], BF16) for h in range(GH)]
    vx = _sb(ctx, "at_vx", [128, NT * GH * 65], BF16)
    oTst = [_sb(ctx, f"at_oTst{i}", [64, 512], BF16) for i in range(2)]
    qkvf = _sb(ctx, "at_qkvf", [128, 3 * 64 * GH], F32)
    qk_bf = _sb(ctx, "at_qkbf", [128, 2 * 64 * GH], BF16)
    rsc = _sb(ctx, "at_rsc", [128, 4 * 32 * GH], F32)
    qn2 = _sb(ctx, "at_qn2", [128, 2 * GH], F32)
    qn2m = _sb(ctx, "at_qn2m", [128, 2 * GH], F32)
    qn2mb = _sb(ctx, "at_qn2mb", [128, 2 * GH], BF16)
    cneg = _sb(ctx, "at_cneg", [128, GH], F32)
    qm4 = _sb(ctx, "at_qm4", [4, 1], F32)
    km4 = _sb(ctx, "at_km4", [4, 1], F32)
    c4s = _sb(ctx, "at_c4s", [4, 1], F32)
    c4b = _sb(ctx, "at_c4b", [4, 1], BF16)
    crow = _sb(ctx, "at_crow", [1, GH], BF16)
    onesrb = _sb(ctx, "at_onesrb", [1, 128], BF16)
    PT = [_sb(ctx, f"at_PT{i}", [128, 512], BF16) for i in range(6)]
    rz = _sb(ctx, "at_rz", [1, 512], BF16)
    zb = _sb(ctx, "at_zb", [64, 512], F32)
    oTld = [_sb(ctx, f"at_oTld{i}", [128, 512], BF16) for i in range(2)]
    s_tm = [_sb(ctx, f"at_stm{i}", [128, D], BF16) for i in range(2)]

    @block.sync
    def _(sync):
        ctx.wait_mark(sync, "ln0_done")
        for cc in range(NCK):
            ld.dma(sync.dma_start(out=xnfm[:, T * cc:T * (cc + 1)],
                                  in_=io["xn_fm"][128 * cc:128 * (cc + 1), :]))
        ld.dma(sync.dma_start(out=masks[:], in_=io["diag_masks"]))



    @block.vector
    def _(v):
        v.memset(onesr[:], 1.0)
        v.memset(sclr[:], QKS)
        v.memset(onesrb[:], 1.0)
        vc.inc(v.memset(ones64[:], 1.0))
    ctx.mark("at_memsets", vc)

    @block.gpsimd
    def _(g):
        gl.dma(g.dma_start(out=cos_sb[:], in_=io["rope_cos"]))
        gl.dma(g.dma_start(out=sin_sb[:], in_=io["rope_sin"]))
        gl.dma(g.dma_start(out=pbrow[:], in_=io["proj_b"]))

    @block.tensor
    def _(te):
        te.wait_ge(ld.h, ld.n)
        te.wait_ge(gl.h, gl.n)
        ctx.wait_mark(te, "at_memsets")
        te.matmul(ctx.psb[6][:], onesr[:], pbrow[:, 0:512], start=True, stop=True)
        tc.inc(te.matmul(ctx.psb[7][:], onesr[:], pbrow[:, 512:1024],
                         start=True, stop=True))
    ctx.mark("at_biasmm", tc)

    @block.scalar
    def _(s):
        ctx.wait_mark(s, "at_biasmm")
        s.activation(out=projb_b[:, 0:512], in_=ctx.psb[6][:], func=AF.Copy)
        sc.inc(s.activation(out=projb_b[:, 512:1024], in_=ctx.psb[7][:], func=AF.Copy))
    ctx.mark("at_bias", sc)

    ctx.mark("at_coresc", sc)
    ctx.mark("at_corevc", vc)
    ctx.mark("at_coretc", tc)

    for name in ("at_epivc", "at_avgrp", "at_pjevac", "at_expg0", "at_expg1",
                 "at_avg0", "at_avg1", "at_maskg0", "at_maskg1"):
        ctx.marks[name] = (vc.h, 0)

    for g in range(NGRP):
        qof, kof, vof = 64 * GH * g, D + 64 * GH * g, 2 * D + 64 * GH * g

        @block.gpsimd
        def _(gq, g=g):
            gq.wait_ge(gl.h, gl.n)
            if g > 0:
                ctx.wait_mark(gq, "at_qkvmm")
            base = g * NCK * 128
            for cc in range(NCK):
                gl.dma(gq.dma_start(
                    out=wpool[:, 768 * cc:768 * (cc + 1)],
                    in_=io["qkv_wg"][base + 128 * cc:base + 128 * (cc + 1), :]))
        ctx.mark("at_gw", gl)

        @block.vector
        def _(v, g=g):
            ctx.wait_mark(v, "at_coretc")
            v.memset(qn2m[:], 0.0)
            vc.inc(v.memset(
                vx.ap().rearrange("p (s c) -> p s c", c=65)[:, :, 64:65], 1.0))
        ctx.mark("at_vxones", vc)

        @block.sync
        def _(sync, g=g):
            if g > 0:
                ctx.wait_mark(sync, "at_gbias")
            pass

        @block.gpsimd
        def _(gq2, g=g):
            gq2.wait_ge(lb.h, lb.n)
            for r, off in ((0, qof), (1, kof), (2, vof)):
                lb.dma(gq2.dma_start(out=brow_g[:, 256 * r:256 * (r + 1)],
                                     in_=io["qkv_b"][0:1, off:off + 256]))
        ctx.mark("at_gbrow", lb)

        @block.tensor
        def _(te, g=g):
            te.wait_ge(ld.h, ld.n)
            te.wait_ge(lb.h, lb.n)
            ctx.wait_mark(te, "at_memsets")
            ctx.wait_mark(te, "at_coresc")
            ctx.wait_mark(te, "at_corevc")
            te.matmul(ctx.psb[0][:, 0:256], sclr[:], brow_g[:, 0:256],
                      start=True, stop=True)
            te.matmul(ctx.psb[1][:, 0:256], sclr[:], brow_g[:, 256:512],
                      start=True, stop=True)
            tc.inc(te.matmul(ctx.psb[2][:, 0:256], onesr[:], brow_g[:, 512:768],
                             start=True, stop=True))
        ctx.mark("at_gbmm", tc)

        @block.scalar
        def _(s, g=g):
            ctx.wait_mark(s, "at_gbmm")
            s.activation(out=bias_bg[:, 0:256], in_=ctx.psb[0][:, 0:256], func=AF.Copy)
            s.activation(out=bias_bg[:, 256:512], in_=ctx.psb[1][:, 0:256], func=AF.Copy)
            sc.inc(s.activation(out=bias_bg[:, 512:768], in_=ctx.psb[2][:, 0:256],
                                func=AF.Copy))
        ctx.mark("at_gbias", sc)

        for i in range(NT):
            @block.tensor
            def _(te, g=g, i=i):
                if i == 0:
                    te.wait_ge(gl.h, gl.n)
                    ctx.wait_mark(te, "at_gbias")
                else:
                    ctx.wait_mark(te, "at_qkvevac")
                mm = None
                for cc in range(NCK):
                    xn_ci = xnfm[:, T * cc + 128 * i: T * cc + 128 * (i + 1)]
                    st0, sp = (cc == 0), (cc == NCK - 1)
                    te.matmul(ctx.psb[0][:, 0:256], xn_ci,
                              wpool[:, 768 * cc + 0:768 * cc + 256],
                              start=st0, stop=sp)
                    te.matmul(ctx.psb[1][:, 0:256], xn_ci,
                              wpool[:, 768 * cc + 256:768 * cc + 512],
                              start=st0, stop=sp)
                    mm = te.matmul(ctx.psb[2][:, 0:256], xn_ci,
                                   wpool[:, 768 * cc + 512:768 * cc + 768],
                                   start=st0, stop=sp)
                tc.inc(mm)
            ctx.mark("at_qkvmm", tc)

            @block.vector
            def _(v, g=g, i=i):
                ctx.wait_mark(v, "at_qkvmm")
                ctx.wait_mark(v, "at_vxones")
                if i >= 1 or g > 0:
                    ctx.wait_mark(v, "at_trtile")
                v.scalar_tensor_tensor(
                    out=qkvf[:, 0:256], in0=ctx.psb[0][:, 0:256], scalar=QKS,
                    in1=bias_bg[:, 0:256], op0=ALU.mult, op1=ALU.add)
                v.scalar_tensor_tensor(
                    out=qkvf[:, 256:512], in0=ctx.psb[1][:, 0:256], scalar=QKS,
                    in1=bias_bg[:, 256:512], op0=ALU.mult, op1=ALU.add)
                vslots = vx.ap().rearrange("p (s c) -> p s c", c=65)[
                    :, GH * i:GH * (i + 1), 0:64]
                vc.inc(v.scalar_tensor_tensor(
                    out=vslots,
                    in0=ctx.psb[2][:, 0:256].rearrange("p (s c) -> p s c", c=64),
                    scalar=1.0,
                    in1=bias_bg.ap()[:, 512:768].rearrange("p (s c) -> p s c", c=64),
                    op0=ALU.mult, op1=ALU.add))
            ctx.mark("at_qkvevac", vc)

            @block.vector
            def _(v, g=g, i=i):
                for base in (0, 256):
                    src = qkvf.ap()[:, base:base + 256].rearrange(
                        "p (h d) -> p h d", h=GH)
                    dst = qk_bf.ap()[:, base:base + 256].rearrange(
                        "p (h d) -> p h d", h=GH)
                    r = [rsc.ap()[:, 128 * j:128 * (j + 1)].rearrange(
                        "p (h d) -> p h d", h=GH) for j in range(4)]
                    cos_t = cos_sb.ap()[:, 32 * i:32 * (i + 1)].unsqueeze(
                        1).to_broadcast([128, GH, 32])
                    sin_t = sin_sb.ap()[:, 32 * i:32 * (i + 1)].unsqueeze(
                        1).to_broadcast([128, GH, 32])
                    tops, bots = src[:, :, 0:32], src[:, :, 32:64]
                    v.wait_ge(vc.h, vc.n)
                    v.tensor_tensor(out=r[0], in0=tops, in1=cos_t, op=ALU.mult)
                    v.tensor_tensor(out=r[1], in0=bots, in1=sin_t, op=ALU.mult)
                    v.tensor_tensor(out=r[2], in0=tops, in1=sin_t, op=ALU.mult)
                    vc.inc(v.tensor_tensor(out=r[3], in0=bots, in1=cos_t, op=ALU.mult))
                    v.wait_ge(vc.h, vc.n)
                    v.tensor_tensor(out=dst[:, :, 0:32], in0=r[0], in1=r[1],
                                    op=ALU.subtract)
                    vc.inc(v.tensor_tensor(out=dst[:, :, 32:64], in0=r[2], in1=r[3],
                                           op=ALU.add))
                v.wait_ge(vc.h, vc.n)
                last = None
                for hh in range(2 * GH):
                    last = v.scalar_tensor_tensor(
                        out=rsc[:, 64 * hh:64 * (hh + 1)],
                        in0=qkvf[:, 64 * hh:64 * (hh + 1)], scalar=1.0,
                        in1=qkvf[:, 64 * hh:64 * (hh + 1)],
                        op0=ALU.mult, op1=ALU.mult, accum_out=qn2[:, hh:hh + 1])
                vc.inc(last)
                v.wait_ge(vc.h, vc.n)
                vc.inc(v.tensor_tensor(out=qn2m[:], in0=qn2m[:], in1=qn2[:],
                                       op=ALU.max))
            ctx.mark("at_rope", vc)

            for t in range(GH):
                @block.tensor
                def _(te, g=g, i=i, t=t):
                    if t == 0:
                        ctx.wait_mark(te, "at_rope")
                        if g == 0 and i == 0:
                            ctx.wait_mark(te, "prelude_ld")
                    slot = (i * GH + t) % 4
                    if g > 0 or i * GH + t >= 4:
                        ctx.wait_mark(te, f"at_trev{slot}")
                    tc.inc(te.transpose(ctx.pbf(4 + slot)[:, 0:128],
                                        qk_bf[:, 128 * t:128 * (t + 1)], ident[:]))
                ctx.mark("at_tr", tc)
                ctx.mark("at_trtile", tc)

                @block.vector
                def _(v, g=g, i=i, t=t):
                    ctx.wait_mark(v, "at_tr")
                    pv = ctx.pbf(4 + (i * GH + t) % 4)
                    if t < 2:
                        dst0, dst1 = qTx[2 * t], qTx[2 * t + 1]
                    else:
                        dst0, dst1 = kTx[2 * (t - 2)], kTx[2 * (t - 2) + 1]
                    v.tensor_copy(dst0[:, 128 * i:128 * (i + 1)], pv[0:64, 0:128])
                    vc.inc(v.tensor_copy(dst1[:, 128 * i:128 * (i + 1)],
                                         pv[64:128, 0:128]))
                ctx.mark(f"at_trev{(i * GH + t) % 4}", vc)
        ctx.mark("at_trall", vc)

        @block.vector
        def _(v, g=g):
            ctx.wait_mark(v, "at_rope")
            vc.inc(v.tensor_copy(qn2mb[:], qn2m[:]))
        ctx.mark("at_cn0", vc)

        @block.tensor
        def _(te, g=g):
            ctx.wait_mark(te, "at_cn0")
            ctx.wait_mark(te, f"at_trev0")
            ctx.wait_mark(te, f"at_trev1")
            te.transpose(ctx.pbf(4)[0:GH, 0:128], qn2mb[:, 0:GH], ident[:])
            tc.inc(te.transpose(ctx.pbf(5)[0:GH, 0:128], qn2mb[:, GH:2 * GH], ident[:]))
        ctx.mark("at_cn1", tc)

        @block.vector
        def _(v, g=g):
            ctx.wait_mark(v, "at_cn1")
            v.tensor_reduce(out=qm4[:], in_=ctx.pbf(4)[0:GH, 0:128], axis=AX.X,
                            op=ALU.max)
            vc.inc(v.tensor_reduce(out=km4[:], in_=ctx.pbf(5)[0:GH, 0:128],
                                   axis=AX.X, op=ALU.max))
            v.wait_ge(vc.h, vc.n)
            vc.inc(v.tensor_tensor(out=c4s[:], in0=qm4[:], in1=km4[:], op=ALU.add))
            v.wait_ge(vc.h, vc.n)
            vc.inc(v.tensor_scalar(out=c4b[:], in0=c4s[:], scalar1=-0.5,
                                   scalar2=None, op0=ALU.mult))
        ctx.mark("at_cn2", vc)

        @block.tensor
        def _(te, g=g):
            ctx.wait_mark(te, "at_cn2")
            tc.inc(te.transpose(ctx.pbf(6)[0:1, 0:GH], c4b[:], ident[0:GH, 0:GH]))
        ctx.mark("at_cn3", tc)

        @block.vector
        def _(v, g=g):
            ctx.wait_mark(v, "at_cn3")
            vc.inc(v.tensor_copy(crow[:], ctx.pbf(6)[0:1, 0:GH]))
        ctx.mark("at_cn4", vc)

        @block.tensor
        def _(te, g=g):
            ctx.wait_mark(te, "at_cn4")
            tc.inc(te.matmul(ctx.psb[7][:, 0:GH], onesrb[:], crow[:],
                             start=True, stop=True))
        ctx.mark("at_cn5", tc)

        @block.scalar
        def _(s, g=g):
            ctx.wait_mark(s, "at_cn5")
            ctx.wait_mark(s, "at_bias")
            sc.inc(s.activation(out=cneg[:], in_=ctx.psb[7][:, 0:GH], func=AF.Copy))
        ctx.mark("at_cn", sc)

        core_items = []
        for hh in range(GH):
            for j in range(T // 512):
                nk = 4 * j + 4
                groups = [list(range(k0, min(k0 + 3, nk))) for k0 in range(0, nk, 3)]
                for gi, kg in enumerate(groups):
                    core_items.append(("grp", hh, j, kg, nk))
                core_items.append(("epi", hh, j))

        sc0 = sc.n
        exp_end = [sc0 + idx + 1 for idx in range(len(core_items))]
        rec_mmS = [0] * len(core_items)
        rec_avmarried = None
        rec_av = [0] * len(core_items)
        rec_avlast = [0] * len(core_items)
        rec_mask = [0] * len(core_items)
        rec_oTn = [0] * len(core_items)
        rec_zb = [0] * len(core_items)

        gpar = 0
        pend = None
        first_grp = True
        lastpar_exp = {0: 0, 1: 0}
        lastpar_av = {0: 0, 1: 0}
        last_epi_idx = -1

        last_oTn_cell = [0]

        def do_av(pidx):
            hhp, jp, kgp, nkp, parp, maskp = core_pend[pidx]

            @block.tensor
            def _(te):
                te.wait_ge(sc.h, exp_end[pidx])
                if maskp:
                    te.wait_ge(vc.h, rec_mask[pidx])
                if kgp[0] == 0 and last_oTn_cell[0]:
                    te.wait_ge(vc.h, last_oTn_cell[0])
                last = None
                for idx, k in enumerate(kgp):
                    last = te.matmul(
                        ctx.psb[2][0:65, :],
                        vx.ap()[:, 65 * (GH * k + hhp):65 * (GH * k + hhp) + 65],
                        PT[3 * parp + idx][:],
                        start=(k == 0), stop=(k == nkp - 1))
                tc.inc(last)
            rec_av[pidx] = tc.n

        core_pend = {}
        for idx, item in enumerate(core_items):
            if item[0] == "grp":
                _, hh, j, kg, nk = item
                par = gpar % 2
                gpar += 1
                bankset = (0, 1, 4) if par == 0 else (5, 6, 7)
                qsl = qTx[hh][:, 512 * j:512 * (j + 1)]

                @block.tensor
                def _(te, kg=kg, par=par, bankset=bankset, qsl=qsl, hh=hh,
                      first=first_grp):
                    if first:
                        ctx.wait_mark(te, "at_cn")
                        ctx.wait_mark(te, "at_trall")
                        ctx.wait_mark(te, "at_epivc")
                    if lastpar_exp[par]:
                        te.wait_ge(sc.h, lastpar_exp[par])
                    last = None
                    for i2, k in enumerate(kg):
                        last = te.matmul(ctx.psb[bankset[i2]][:],
                                         kTx[hh][:, 128 * k:128 * (k + 1)], qsl,
                                         start=True, stop=True)
                    tc.inc(last)
                rec_mmS[idx] = tc.n
                first_grp = False
                lastpar_exp[par] = exp_end[idx]

                masked = [(i2, k) for i2, k in enumerate(kg) if k >= 4 * j]
                if masked:
                    @block.vector
                    def _(v, j=j, par=par, masked=masked, idx=idx):
                        v.wait_ge(sc.h, exp_end[idx])
                        last = None
                        for i2, k in masked:
                            m = k - 4 * j
                            last = v.tensor_tensor(
                                out=PT[3 * par + i2][:], in0=PT[3 * par + i2][:],
                                in1=masks[:, 512 * m:512 * (m + 1)], op=ALU.mult)
                        vc.inc(last)
                    rec_mask[idx] = vc.n

                core_pend[idx] = (hh, j, kg, nk, par, bool(masked))
                if pend is not None:
                    do_av(pend)
                pend = idx
            else:
                _, hh, j = item
                if pend is not None:
                    do_av(pend)
                    rec_avlast[idx] = tc.n
                    pend = None

                @block.tensor
                def _(te, idx=idx):
                    te.wait_ge(sc.h, exp_end[idx])
                    tc.inc(te.matmul(ctx.psb[3][0:64, :], ones64[:], rz[:],
                                     start=True, stop=True))
                rec_zb[idx] = tc.n

                @block.vector
                def _(v, j=j, idx=idx):
                    v.wait_ge(tc.h, rec_zb[idx])
                    vc.inc(v.reciprocal(zb[:], ctx.psb[3][0:64, :]))
                    v.wait_ge(vc.h, vc.n)
                    v.wait_ge(sto[j % 2].h, sto[j % 2].n)
                    vc.inc(v.tensor_tensor(
                        out=oTst[j % 2][:], in0=ctx.psb[2][0:64, :], in1=zb[:],
                        op=ALU.mult))
                rec_oTn[idx] = vc.n
                last_oTn_cell[0] = vc.n
                ctx.mark("at_epivc", vc)

                @block.sync
                def _(sync, g=g, hh=hh, j=j, idx=idx):
                    sync.wait_ge(vc.h, rec_oTn[idx])
                    sync.wait_ge(sto[j % 2].h, sto[j % 2].n)
                    gh = g * GH + hh
                    sto[j % 2].dma(sync.dma_start(
                        out=io["oTh"][64 * gh:64 * (gh + 1), 512 * j:512 * (j + 1)],
                        in_=oTst[j % 2][:]))
                last_epi_idx = idx

        @block.scalar
        def _(s, items=list(enumerate(core_items)), rec_mmS=rec_mmS,
              rec_av=rec_av, rec_avlast=rec_avlast, rec_oTn=rec_oTn):
            lp_av = {0: 0, 1: 0}
            prev_oTn = 0
            for idx, item in items:
                if item[0] == "grp":
                    hh2, j2, kg2, nk2 = item[1], item[2], item[3], item[4]
                    par2 = core_pend[idx][4]
                    s.wait_ge(tc.h, rec_mmS[idx])
                    if lp_av[par2]:
                        s.wait_ge(tc.h, rec_av[lp_av[par2]])
                    lp_av[par2] = idx
                    bankset2 = (0, 1, 4) if par2 == 0 else (5, 6, 7)
                    last = None
                    for i2, k in enumerate(kg2):
                        last = s.activation(out=PT[3 * par2 + i2][:],
                                            in_=ctx.psb[bankset2[i2]][:],
                                            func=AF.Exp, bias=cneg[:, hh2:hh2 + 1])
                    sc.inc(last)
                else:
                    s.wait_ge(tc.h, rec_avlast[idx])
                    if prev_oTn:
                        s.wait_ge(vc.h, prev_oTn)
                    sc.inc(s.activation(out=rz[:], in_=ctx.psb[2][64:65, :],
                                        func=AF.Copy))
                    prev_oTn = rec_oTn[idx]
            assert sc.n == sc0 + len(items)

        ctx.mark("at_coresc", sc)
        ctx.mark("at_corevc", vc)
        ctx.mark("at_coretc", tc)

    @block.gpsimd
    def _(gq):
        ctx.wait_mark(gq, "at_qkvmm")
        for cc in range(NCK):
            gl.dma(gq.dma_start(out=wpool[:, D * cc:D * (cc + 1)],
                                in_=io["proj_wT"][128 * cc:128 * (cc + 1), :]))
    ctx.mark("at_pw", gl)

    for tg in range(T // 512):
        for cc in range(NCK):
            @block.sync
            def _(sync, tg=tg, cc=cc):
                if tg == 0 and cc == 0:
                    sync.wait_ge(sto[0].h, sto[0].n)
                    sync.wait_ge(sto[1].h, sto[1].n)
                idx = tg * NCK + cc
                if idx >= 2:
                    ctx.wait_mark(sync, f"at_pjld{cc % 2}")
                lo[cc % 2].dma(sync.dma_start(out=oTld[cc % 2][:],
                                      in_=io["oTh"][128 * cc:128 * (cc + 1),
                                                    512 * tg:512 * (tg + 1)]))

            @block.tensor
            def _(te, tg=tg, cc=cc):
                te.wait_ge(lo[cc % 2].h, lo[cc % 2].n)
                if tg == 0 and cc == 0:
                    ctx.wait_mark(te, "at_pw")
                if cc == 0:
                    ctx.wait_mark(te, "at_pjevac")
                last = None
                for i4 in range(4):
                    for nb in range(2):
                        last = te.matmul(
                            ctx.psb[i4 * 2 + nb][:],
                            oTld[cc % 2][:, 128 * i4:128 * (i4 + 1)],
                            wpool[:, D * cc + 512 * nb:D * cc + 512 * (nb + 1)],
                            start=(cc == 0), stop=(cc == NCK - 1))
                tc.inc(last)
            ctx.mark(f"at_pjld{cc % 2}", tc)
        ctx.mark("at_pjmm", tc)

        for i4 in range(4):
            @block.vector
            def _(v, tg=tg, i4=i4):
                if i4 == 0:
                    ctx.wait_mark(v, "at_pjmm")
                v.wait_ge(sts[i4 % 2].h, sts[i4 % 2].n)
                last = None
                for nb in range(2):
                    last = v.tensor_tensor(
                        out=s_tm[i4 % 2][:, 512 * nb:512 * (nb + 1)],
                        in0=ctx.psb[i4 * 2 + nb][:],
                        in1=projb_b[:, 512 * nb:512 * (nb + 1)], op=ALU.add)
                vc.inc(last)
            if i4 == 3:
                ctx.mark("at_pjevac", vc)
            ctx.mark("at_pjev1", vc)

            @block.sync
            def _(sync, tg=tg, i4=i4):
                ctx.wait_mark(sync, "at_pjev1")
                sts[i4 % 2].dma(sync.dma_start(
                    out=io["s"][512 * tg + 128 * i4:512 * tg + 128 * (i4 + 1), :],
                    in_=s_tm[i4 % 2][:]))

    @block.sync
    def _(sync):
        sync.wait_ge(st.h, st.n)
        for s2 in (sto[0], sto[1], sts[0], sts[1], lo[0], lo[1], lb, gl, ld):
            if s2.n:
                sync.wait_ge(s2.h, s2.n)
    ctx.mark("attn0_done", sts[1])




def emit_peer(ctx):
    nc, block = ctx.nc, ctx.block
    io = ctx.io
    pl = ctx.sem("pe_ld")
    pg = ctx.sem("pe_gl")
    lx = [ctx.sem("pe_lx0"), ctx.sem("pe_lx1")]
    gd = [ctx.sem(f"pe_gd{i}") for i in range(4)]
    gu = [ctx.sem(f"pe_gu{i}") for i in range(4)]
    vc = ctx.sem("pe_vc")
    sc = ctx.sem("pe_sc")
    tc = ctx.sem("pe_tc")
    stp = [ctx.sem("pe_st0"), ctx.sem("pe_st1")]
    ident = ctx.const["ident"]

    xnfm = _sb(ctx, "pe_xnfm", [128, NCK * T], BF16)
    wq = _sb(ctx, "pe_wq", [128, NCK * 2 * D], BF16)
    keys = _sb(ctx, "pe_keys", [128, 2 * PH * NKEYS], BF16)
    nrow = _sb(ctx, "pe_nrow", [1, D], F32)
    normw_b = _sb(ctx, "pe_normwb", [128, D], F32)
    normw_c = _sb(ctx, "pe_normwc", [128, NCK], F32)
    onesr = _sb(ctx, "pe_onesr", [1, 128], F32)
    qp = _sb(ctx, "pe_qp", [128, 16 * 512], BF16)
    sims = _sb(ctx, "pe_sims", [128, 16 * NKEYS], F32)
    xntm2 = [_sb(ctx, f"pe_xntm{i}", [128, D], BF16) for i in range(2)]
    xt2 = [_sb(ctx, f"pe_xt{i}", [128, D], BF16) for i in range(2)]
    xld2 = [_sb(ctx, f"pe_xld{i}", [128, D], F32) for i in range(2)]
    sld2 = [_sb(ctx, f"pe_sld{i}", [128, D], BF16) for i in range(2)]
    x1t2 = [_sb(ctx, f"pe_x1t{i}", [128, D], F32) for i in range(2)]
    s12 = _sb(ctx, "pe_s12", [128, 2 * PH * PK], F32)
    i12 = _sb(ctx, "pe_i12", [128, 2 * PH * PK], U32)
    ascr = _sb(ctx, "pe_ascr", [128, PH * PK * PK], F32)
    fs = _sb(ctx, "pe_fs", [128, PH * PK], F32)
    fsr = _sb(ctx, "pe_fsr", [128, PH * PK], F32)
    pk_t = _sb(ctx, "pe_pk", [128, PH * PK], U32)
    ab = _sb(ctx, "pe_ab", [128, 2 * PH * PK], U32)
    eq = _sb(ctx, "pe_eq", [128, PH * PK * PK], U32)
    isel = _sb(ctx, "pe_isel", [128, 2 * PH * PK], U32)
    fi = _sb(ctx, "pe_fi", [128, PH * PK], U32)
    iota8 = _sb(ctx, "pe_iota8", [128, PK], I32)
    hid = _sb(ctx, "pe_hid", [128, PH * PK], F32)
    hgel = _sb(ctx, "pe_hgel", [128, PH * PK], F32)
    coef = _sb(ctx, "pe_coef", [128, PH * PK], F32)
    Gdn = [_sb(ctx, f"pe_gdn{i}", [128, D], BF16) for i in range(4)]
    Gup = [_sb(ctx, f"pe_gup{i}", [128, D], BF16) for i in range(4)]
    diag = [_sb(ctx, f"pe_diag{i}", [128, 128], BF16) for i in range(2)]
    gsil = _sb(ctx, "pe_gsil", [128, D], F32)
    sqs = _sb(ctx, "pe_sqs", [128, D], BF16)
    stat = _sb(ctx, "pe_stat", [128, 4], F32)

    NSLOT = PH * PK

    @block.sync
    def _(sync):
        ctx.wait_mark(sync, "ln0_done")
        ctx.wait_mark(sync, "attn0_done")
        for cc in range(NCK):
            pl.dma(sync.dma_start(out=xnfm[:, T * cc:T * (cc + 1)],
                                  in_=io["xn_fm"][128 * cc:128 * (cc + 1), :]))
        pl.dma(sync.dma_start(out=nrow[:], in_=io["norm_w"]))
        pl.dma(sync.dma_start(
            out=normw_c[:],
            in_=io["norm_w"].rearrange("o (c p) -> p (o c)", p=128),
            allow_slow_non_contiguous=True))

    @block.gpsimd
    def _(g):
        for cc in range(NCK):
            pg.dma(g.dma_start(out=wq[:, 2 * D * cc:2 * D * (cc + 1)],
                               in_=io["wq_peerT"][128 * cc:128 * (cc + 1), :]))
        for h in range(PH):
            for s_ in range(2):
                pg.dma(g.dma_start(
                    out=keys[:, 256 * (2 * h + s_):256 * (2 * h + s_ + 1)],
                    in_=io["keysT"][h, s_]))
        pio = ctx.sem("pe_io")
        pio.inc(g.iota(iota8[:], pattern=[[1, PK]], base=0, channel_multiplier=0))
        ctx.mark("pe_iota", pio)

    @block.vector
    def _(v):
        vc.inc(v.memset(onesr[:], 1.0))
    ctx.mark("pe_ones", vc)

    @block.tensor
    def _(te):
        te.wait_ge(pl.h, pl.n)
        ctx.wait_mark(te, "pe_ones")
        te.matmul(ctx.psb[0][:], onesr[:], nrow[:, 0:512], start=True, stop=True)
        tc.inc(te.matmul(ctx.psb[1][:], onesr[:], nrow[:, 512:1024],
                         start=True, stop=True))
    ctx.mark("pe_nwmm", tc)

    @block.scalar
    def _(s):
        ctx.wait_mark(s, "pe_nwmm")
        s.activation(out=normw_b[:, 0:512], in_=ctx.psb[0][:], func=AF.Copy)
        sc.inc(s.activation(out=normw_b[:, 512:1024], in_=ctx.psb[1][:], func=AF.Copy))
    ctx.mark("pe_nwb", sc)

    @block.vector
    def _(v):
        v.wait_ge(pg.h, pg.n)
        v.wait_ge(pl.h, pl.n)
        last = None
        for cc in range(NCK):
            last = v.tensor_scalar(out=wq[:, 2 * D * cc:2 * D * (cc + 1)],
                                   in0=wq[:, 2 * D * cc:2 * D * (cc + 1)],
                                   scalar1=normw_c[:, cc:cc + 1], scalar2=None,
                                   op0=ALU.mult)
        vc.inc(last)
    ctx.mark("pe_wqf", vc)

    ctx.marks["pe_simsdone"] = (tc.h, 0)
    ctx.marks["pe_silu"] = (sc.h, 0)
    ctx.marks["pe_fin"] = (vc.h, 0)
    ctx.marks["pe_topk1"] = (vc.h, 0)

    for sg in range(T // 512):
        for qfc in range(16):
            @block.tensor
            def _(te, sg=sg, qfc=qfc):
                if qfc == 0:
                    ctx.wait_mark(te, "pe_wqf")
                    ctx.wait_mark(te, "pe_nwb")
                    ctx.wait_mark(te, "pe_simsdone")
                    ctx.wait_mark(te, "pe_silu")
                if qfc >= 2:
                    ctx.wait_mark(te, f"pe_qpev{qfc % 2}")
                last = None
                for cc in range(NCK):
                    last = te.matmul(
                        ctx.psb[qfc % 2][:],
                        wq[:, 2 * D * cc + 128 * qfc:2 * D * cc + 128 * (qfc + 1)],
                        xnfm[:, T * cc + 512 * sg:T * cc + 512 * (sg + 1)],
                        start=(cc == 0), stop=(cc == NCK - 1))
                tc.inc(last)
            ctx.mark(f"pe_qpmm{qfc % 2}", tc)

            @block.vector
            def _(v, sg=sg, qfc=qfc):
                ctx.wait_mark(v, f"pe_qpmm{qfc % 2}")
                vc.inc(v.tensor_copy(qp[:, 512 * qfc:512 * (qfc + 1)],
                                     ctx.psb[qfc % 2][:]))
            ctx.mark(f"pe_qpev{qfc % 2}", vc)
        ctx.mark("pe_qpall", vc)

        for it in range(4):
            i = sg * 4 + it
            par = i % 2

            @block.sync
            def _(sync, i=i, par=par):
                if i >= 2:
                    ctx.wait_mark(sync, f"pe_finv{par}")
                lx[par].dma(sync.dma_start(out=xntm2[par][:],
                                           in_=io["xn_tm"][128 * i:128 * (i + 1), :]))
                lx[par].dma(sync.dma_start(out=xld2[par][:],
                                           in_=io["x"][128 * i:128 * (i + 1), :]))
                lx[par].dma(sync.dma_start(out=sld2[par][:],
                                           in_=io["s"][128 * i:128 * (i + 1), :]))

            @block.scalar
            def _(s, i=i, par=par):
                s.wait_ge(lx[par].h, lx[par].n)
                if i >= 1:
                    ctx.wait_mark(s, "pe_xt")
                sc.inc(s.activation(out=sqs[:], in_=xntm2[par][:], func=AF.Square,
                                    accum_out=stat[:, 0:1]))
            ctx.mark("pe_ssq", sc)

            @block.vector
            def _(v, i=i):
                ctx.wait_mark(v, "pe_ssq")
                vc.inc(v.tensor_scalar(out=stat[:, 1:2], in0=stat[:, 0:1],
                                       scalar1=1.0 / D, scalar2=1e-6,
                                       op0=ALU.mult, op1=ALU.add))
            ctx.mark("pe_var", vc)

            @block.scalar
            def _(s, i=i):
                ctx.wait_mark(s, "pe_var")
                sc.inc(s.activation(out=stat[:, 2:3], in_=stat[:, 1:2], func=AF.Ln))
                s.wait_ge(sc.h, sc.n)
                sc.inc(s.activation(out=stat[:, 3:4], in_=stat[:, 2:3], func=AF.Exp,
                                    scale=-0.5))
            ctx.mark("pe_rstd", sc)

            @block.vector
            def _(v, i=i, par=par):
                ctx.wait_mark(v, "pe_rstd")
                ctx.wait_mark(v, "pe_nwb")
                if i >= 2:
                    ctx.wait_mark(v, f"pe_dots{par}")
                vc.inc(v.scalar_tensor_tensor(out=xt2[par][:], in0=xntm2[par][:],
                                              scalar=stat[:, 3:4], in1=normw_b[:],
                                              op0=ALU.mult, op1=ALU.mult))
            ctx.mark("pe_xt", vc)

            for hs in range(16):
                @block.tensor
                def _(te, i=i, it=it, hs=hs):
                    if hs == 0:
                        ctx.wait_mark(te, "pe_qpall")
                        ctx.wait_mark(te, "pe_topk1")
                    if hs >= 2:
                        ctx.wait_mark(te, f"pe_simev{hs % 2}")
                    tc.inc(te.matmul(
                        ctx.psb[2 + hs % 2][:, 0:256],
                        qp[:, 512 * hs + 128 * it:512 * hs + 128 * (it + 1)],
                        keys[:, 256 * hs:256 * (hs + 1)],
                        start=True, stop=True))
                ctx.mark(f"pe_simmm{hs % 2}", tc)
                if hs == 15:
                    ctx.mark("pe_simsdone", tc)

                @block.vector
                def _(v, i=i, hs=hs):
                    ctx.wait_mark(v, f"pe_simmm{hs % 2}")
                    ctx.wait_mark(v, "pe_rstd")
                    vc.inc(v.tensor_scalar(out=sims[:, 256 * hs:256 * (hs + 1)],
                                           in0=ctx.psb[2 + hs % 2][:, 0:256],
                                           scalar1=stat[:, 3:4], scalar2=None,
                                           op0=ALU.mult))
                ctx.mark(f"pe_simev{hs % 2}", vc)
            ctx.mark("pe_simsall", vc)

            @block.vector
            def _(v, i=i):
                ctx.wait_mark(v, "pe_simsall")
                last = None
                for hs in range(16):
                    last = v.max(out=s12[:, PK * hs:PK * (hs + 1)],
                                 in_=sims[:, 256 * hs:256 * (hs + 1)])
                vc.inc(last)
                v.wait_ge(vc.h, vc.n)
                for hs in range(16):
                    last = v.max_index(out=i12[:, PK * hs:PK * (hs + 1)],
                                       in_max=s12[:, PK * hs:PK * (hs + 1)],
                                       in_values=sims[:, 256 * hs:256 * (hs + 1)])
                vc.inc(last)
            ctx.mark("pe_topk1", vc)

            @block.vector
            def _(v, i=i):
                v.wait_ge(vc.h, vc.n)
                ctx.wait_mark(v, "pe_iota")
                s12r = s12.ap().rearrange("p (h s e) -> p h s e", h=PH, s=2)
                s1b = s12r[:, :, 0:1, :].transpose([0, 1, 3, 2]).to_broadcast(
                    [128, PH, PK, PK])
                s2b = s12r[:, :, 1:2, :].to_broadcast([128, PH, PK, PK])
                vc.inc(v.tensor_tensor(
                    out=ascr.ap().rearrange("p (h a b) -> p h a b", h=PH, a=PK),
                    in0=s1b, in1=s2b, op=ALU.add))
                v.wait_ge(vc.h, vc.n)
                last = None
                for h in range(PH):
                    last = v.max(out=fs[:, PK * h:PK * (h + 1)],
                                 in_=ascr[:, 64 * h:64 * (h + 1)])
                vc.inc(last)
                v.wait_ge(vc.h, vc.n)
                for h in range(PH):
                    last = v.max_index(out=pk_t[:, PK * h:PK * (h + 1)],
                                       in_max=fs[:, PK * h:PK * (h + 1)],
                                       in_values=ascr[:, 64 * h:64 * (h + 1)])
                vc.inc(last)
                v.wait_ge(vc.h, vc.n)
                v.tensor_scalar(out=ab[:, 0:NSLOT], in0=pk_t[:], scalar1=3,
                                scalar2=None, op0=ALU.logical_shift_right)
                vc.inc(v.tensor_scalar(out=ab[:, NSLOT:2 * NSLOT], in0=pk_t[:],
                                       scalar1=7, scalar2=None, op0=ALU.bitwise_and))
                v.wait_ge(vc.h, vc.n)
                i12r = i12.ap().rearrange("p (h s e) -> p h s e", h=PH, s=2)
                iob = iota8.ap().bitcast(U32).unsqueeze(1).unsqueeze(1).to_broadcast(
                    [128, PH, PK, PK])
                for half in range(2):
                    abr = ab.ap()[:, NSLOT * half:NSLOT * (half + 1)].rearrange(
                        "p (h s) -> p h s", h=PH).unsqueeze(3).to_broadcast(
                        [128, PH, PK, PK])
                    vc.inc(v.tensor_tensor(
                        out=eq.ap().rearrange("p (h a b) -> p h a b", h=PH, a=PK),
                        in0=abr, in1=iob, op=ALU.is_equal))
                    v.wait_ge(vc.h, vc.n)
                    ivals = i12r[:, :, half:half + 1, :].to_broadcast(
                        [128, PH, PK, PK])
                    vc.inc(v.tensor_tensor(
                        out=eq.ap().rearrange("p (h a b) -> p h a b", h=PH, a=PK),
                        in0=eq.ap().rearrange("p (h a b) -> p h a b", h=PH, a=PK),
                        in1=ivals, op=ALU.mult))
                    v.wait_ge(vc.h, vc.n)
                    with nc.allow_low_precision(reason="u32 index reduce"):
                        vc.inc(v.tensor_reduce(
                            out=isel.ap()[:, NSLOT * half:NSLOT * (half + 1)].rearrange(
                                "p (h s) -> p h s", h=PH),
                            in_=eq.ap().rearrange("p (h a b) -> p h a b", h=PH, a=PK),
                            axis=AX.X, op=ALU.add))
                    v.wait_ge(vc.h, vc.n)
                vc.inc(v.scalar_tensor_tensor(
                    out=fi[:], in0=isel[:, 0:NSLOT], scalar=NKEYS,
                    in1=isel[:, NSLOT:2 * NSLOT],
                    op0=ALU.mult, op1=ALU.add))
                vc.inc(v.tensor_scalar(out=fsr[:], in0=fs[:], scalar1=0.0,
                                       scalar2=None, op0=ALU.max))
            ctx.mark("pe_fi", vc)

            vdots0 = vc.n
            @block.gpsimd
            def _(g, i=i, vdots0=vdots0):
                ctx.wait_mark(g, "pe_fi")
                for j in range(NSLOT):
                    if j >= 4:
                        g.wait_ge(vc.h, vdots0 + (j - 4) + 1)
                    gd[j % 4].dma(g.indirect_dma_start(
                        out=Gdn[j % 4][:], out_offset=None, in_=io["exp_down"],
                        in_offset=IndirectOffsetOnAxis(
                            ap=fi.ap()[:, j:j + 1].bitcast(I32), axis=0)))

            @block.vector
            def _(v, i=i, par=par, vdots0=vdots0):
                ctx.wait_mark(v, "pe_xt")
                for j in range(NSLOT):
                    v.wait_ge(gd[j % 4].h, 16 * (NSLOT // 4) * i + 16 * (j // 4 + 1)
                              if False else gd[j % 4].n - 16 * (NSLOT // 4) + 16 * (j // 4 + 1))
                    vc.inc(v.scalar_tensor_tensor(
                        out=Gdn[j % 4][:], in0=Gdn[j % 4][:], scalar=1.0,
                        in1=xt2[par][:], op0=ALU.mult, op1=ALU.mult,
                        accum_out=hid[:, j:j + 1]))
            assert vc.n == vdots0 + NSLOT
            ctx.mark(f"pe_dots{par}", vc)
            ctx.mark("pe_dots", vc)

            @block.scalar
            def _(s, i=i):
                ctx.wait_mark(s, "pe_dots")
                sc.inc(s.activation(out=hgel[:], in_=hid[:], func=AF.Gelu))
            ctx.mark("pe_gelu", sc)

            @block.vector
            def _(v, i=i):
                ctx.wait_mark(v, "pe_gelu")
                vc.inc(v.tensor_tensor(out=coef[:], in0=hgel[:], in1=fsr[:],
                                       op=ALU.mult))
            ctx.mark("pe_coef", vc)

            tcu0 = tc.n
            vcu0 = vc.n
            @block.gpsimd
            def _(g, i=i, tcu0=tcu0):
                ctx.wait_mark(g, "pe_fi")
                g.wait_ge(tc.h, tcu0)
                for j in range(NSLOT):
                    if j >= 4:
                        g.wait_ge(tc.h, tcu0 + (j - 4) + 1)
                    gu[j % 4].dma(g.indirect_dma_start(
                        out=Gup[j % 4][:], out_offset=None, in_=io["exp_up"],
                        in_offset=IndirectOffsetOnAxis(
                            ap=fi.ap()[:, j:j + 1].bitcast(I32), axis=0)))
            gu_base = [gu[r].n - 16 * (NSLOT // 4) for r in range(4)]

            @block.vector
            def _(v, i=i, tcu0=tcu0):
                ctx.wait_mark(v, "pe_coef")
                ctx.wait_mark(v, "prelude_ld")
                for j in range(NSLOT):
                    if j >= 2:
                        v.wait_ge(tc.h, tcu0 + (j - 2) + 1)
                    vc.inc(v.tensor_scalar(out=diag[j % 2][:], in0=ident[:],
                                           scalar1=coef[:, j:j + 1], scalar2=None,
                                           op0=ALU.mult))
            assert vc.n == vcu0 + NSLOT

            @block.tensor
            def _(te, i=i, tcu0=tcu0, vcu0=vcu0, gu_base=gu_base):
                ctx.wait_mark(te, "pe_silu")
                for j in range(NSLOT):
                    te.wait_ge(vc.h, vcu0 + j + 1)
                    te.wait_ge(gu[j % 4].h, gu_base[j % 4] + 16 * (j // 4 + 1))
                    te.matmul(ctx.psb[0][:], diag[j % 2][:], Gup[j % 4][:, 0:512],
                              start=(j == 0), stop=(j == NSLOT - 1))
                    mm = te.matmul(ctx.psb[1][:], diag[j % 2][:],
                                   Gup[j % 4][:, 512:1024],
                                   start=(j == 0), stop=(j == NSLOT - 1))
                    tc.inc(mm)
            assert tc.n == tcu0 + NSLOT
            ctx.mark("pe_upmm", tc)

            @block.scalar
            def _(s, i=i):
                ctx.wait_mark(s, "pe_upmm")
                ctx.wait_mark(s, "pe_fin")
                sc.inc(s.activation(out=gsil[:], in_=ctx.psbt[0].ap(), func=AF.Silu))
            ctx.mark("pe_silu", sc)

            @block.vector
            def _(v, i=i, par=par):
                ctx.wait_mark(v, "pe_silu")
                v.wait_ge(lx[par].h, lx[par].n)
                v.wait_ge(stp[par].h, stp[par].n)
                vc.inc(v.tensor_tensor(out=gsil[:], in0=sld2[par][:], in1=gsil[:],
                                       op=ALU.mult))
                v.wait_ge(vc.h, vc.n)
                vc.inc(v.tensor_tensor(out=x1t2[par][:], in0=xld2[par][:],
                                       in1=gsil[:], op=ALU.add))
            ctx.mark("pe_fin", vc)
            ctx.mark(f"pe_finv{par}", vc)

            @block.sync
            def _(sync, i=i, par=par):
                ctx.wait_mark(sync, f"pe_finv{par}")
                stp[par].dma(sync.dma_start(
                    out=io["x1"][128 * i:128 * (i + 1), :], in_=x1t2[par][:]))

    @block.sync
    def _(sync):
        for s2 in [stp[0], stp[1], lx[0], lx[1], pl, pg] + gd + gu:
            if s2.n:
                sync.wait_ge(s2.h, s2.n)
    ctx.mark("peer_done", stp[1])




def emit_moe(ctx):
    nc, block = ctx.nc, ctx.block
    io = ctx.io
    ml = ctx.sem("mo_ld")
    mg = ctx.sem("mo_gl")
    ma4 = [ctx.sem(f"mo_ma{i}") for i in range(4)]
    lf = [ctx.sem("mo_lf0"), ctx.sem("mo_lf1")]
    vc = ctx.sem("mo_vc")
    sc = ctx.sem("mo_sc")
    tc = ctx.sem("mo_tc")
    mst = [ctx.sem("mo_st0"), ctx.sem("mo_st1")]

    xnfm = _sb(ctx, "mo_xnfm", [128, NCK * T], BF16)
    rtr = _sb(ctx, "mo_rtr", [128, NCK * NE], BF16)
    wg_sb = _sb(ctx, "mo_wg", [128, NCK * FF], BF16)
    wu_sb = _sb(ctx, "mo_wu", [128, NCK * FF], BF16)
    wd_sb = _sb(ctx, "mo_wd", [128, (FF // 128) * D], BF16)
    fuse = _sb(ctx, "mo_fuse", [128, (FF // 128) * 512], BF16)
    sil = [_sb(ctx, f"mo_sil{i}", [128, 512], BF16) for i in range(2)]
    wfull = _sb(ctx, "mo_wfull", [128, NT * NE], F32)
    rsc = _sb(ctx, "mo_rsc", [128, 2 * NE + 8], F32)
    iota8 = _sb(ctx, "mo_iota8", [128, NE], I32)
    pv = _sb(ctx, "mo_pv", [128, 8], F32)
    pidx = _sb(ctx, "mo_pidx", [128, 8], U32)
    eqf = _sb(ctx, "mo_eqf", [128, NE], F32)
    scal = [_sb(ctx, f"mo_scal{i}", [128, D], F32) for i in range(4)]
    acld = scal
    gs = scal[2]
    x1ld = [scal[3], _sb(ctx, "mo_x1ld1", [128, D], F32)]
    s1ld = [_sb(ctx, f"mo_s1ld{i}", [128, D], BF16) for i in range(2)]

    NFC = FF // 128

    @block.sync
    def _(sync):
        ctx.wait_mark(sync, "ln1_done")
        for cc in range(NCK):
            ml.dma(sync.dma_start(out=xnfm[:, T * cc:T * (cc + 1)],
                                  in_=io["xn1_fm"][128 * cc:128 * (cc + 1), :]))

    @block.gpsimd
    def _(g):
        for cc in range(NCK):
            mg.dma(g.dma_start(out=rtr[:, NE * cc:NE * (cc + 1)],
                               in_=io["router"][128 * cc:128 * (cc + 1), :]))
        mio = ctx.sem("mo_io")
        mio.inc(g.iota(iota8[:], pattern=[[1, NE]], base=0, channel_multiplier=0))
        ctx.mark("mo_iota", mio)

    ctx.marks["mo_wfv"] = (vc.h, 0)
    ctx.marks["mo_sm2"] = (sc.h, 0)
    for i in range(NT):
        @block.tensor
        def _(te, i=i):
            te.wait_ge(ml.h, ml.n)
            te.wait_ge(mg.h, 16 * NCK)
            ctx.wait_mark(te, "mo_wfv")
            last = None
            for cc in range(NCK):
                last = te.matmul(ctx.psb[4][:, 0:NE],
                                 xnfm[:, T * cc + 128 * i:T * cc + 128 * (i + 1)],
                                 rtr[:, NE * cc:NE * (cc + 1)],
                                 start=(cc == 0), stop=(cc == NCK - 1))
            tc.inc(last)
        ctx.mark("mo_rmm", tc)

        @block.vector
        def _(v, i=i):
            ctx.wait_mark(v, "mo_rmm")
            ctx.wait_mark(v, "mo_sm2")
            vc.inc(v.tensor_reduce(out=rsc[:, 0:1], in_=ctx.psb[4][:, 0:NE],
                                   axis=AX.X, op=ALU.max))
            v.wait_ge(vc.h, vc.n)
            vc.inc(v.tensor_scalar(out=rsc[:, 8:8 + NE], in0=ctx.psb[4][:, 0:NE],
                                   scalar1=rsc[:, 0:1], scalar2=None,
                                   op0=ALU.subtract))
        ctx.mark("mo_sm1", vc)

        @block.scalar
        def _(s, i=i):
            ctx.wait_mark(s, "mo_sm1")
            sc.inc(s.activation(out=rsc[:, 8:8 + NE], in_=rsc[:, 8:8 + NE],
                                func=AF.Exp, accum_out=rsc[:, 1:2]))
        ctx.mark("mo_sm2", sc)

        @block.vector
        def _(v, i=i):
            ctx.wait_mark(v, "mo_sm2")
            ctx.wait_mark(v, "mo_iota")
            vc.inc(v.reciprocal(rsc[:, 2:3], rsc[:, 1:2]))
            v.wait_ge(vc.h, vc.n)
            vc.inc(v.tensor_scalar(out=rsc[:, 8:8 + NE], in0=rsc[:, 8:8 + NE],
                                   scalar1=rsc[:, 2:3], scalar2=None, op0=ALU.mult))
            v.wait_ge(vc.h, vc.n)
            vc.inc(v.max(out=pv[:], in_=rsc[:, 8:8 + NE]))
            v.wait_ge(vc.h, vc.n)
            vc.inc(v.max_index(out=pidx[:], in_max=pv[:], in_values=rsc[:, 8:8 + NE]))
            v.wait_ge(vc.h, vc.n)
            vc.inc(v.tensor_tensor(out=rsc[:, 3:4], in0=pv[:, 1:2], in1=pv[:, 0:1],
                                   op=ALU.subtract))
        ctx.mark("mo_nw1", vc)

        @block.scalar
        def _(s, i=i):
            ctx.wait_mark(s, "mo_nw1")
            sc.inc(s.activation(out=rsc[:, 4:5], in_=rsc[:, 3:4], func=AF.Exp))
        ctx.mark("mo_nw2", sc)

        @block.vector
        def _(v, i=i):
            ctx.wait_mark(v, "mo_nw2")
            vc.inc(v.tensor_scalar(out=rsc[:, 5:6], in0=rsc[:, 4:5], scalar1=1.0,
                                   scalar2=None, op0=ALU.add))
            v.wait_ge(vc.h, vc.n)
            vc.inc(v.reciprocal(rsc[:, 6:7], rsc[:, 5:6]))
            v.wait_ge(vc.h, vc.n)
            vc.inc(v.tensor_tensor(out=rsc[:, 7:8], in0=rsc[:, 4:5], in1=rsc[:, 6:7],
                                   op=ALU.mult))
            v.wait_ge(vc.h, vc.n)
            vc.inc(v.tensor_tensor(out=eqf[:],
                                   in0=iota8.ap().bitcast(U32),
                                   in1=pidx.ap()[:, 0:1].to_broadcast([128, NE]),
                                   op=ALU.is_equal))
            v.wait_ge(vc.h, vc.n)
            vc.inc(v.tensor_scalar(out=wfull[:, NE * i:NE * (i + 1)], in0=eqf[:],
                                   scalar1=rsc[:, 6:7], scalar2=None, op0=ALU.mult))
            v.wait_ge(vc.h, vc.n)
            vc.inc(v.tensor_tensor(out=eqf[:],
                                   in0=iota8.ap().bitcast(U32),
                                   in1=pidx.ap()[:, 1:2].to_broadcast([128, NE]),
                                   op=ALU.is_equal))
            v.wait_ge(vc.h, vc.n)
            vc.inc(v.scalar_tensor_tensor(out=wfull[:, NE * i:NE * (i + 1)],
                                          in0=eqf[:], scalar=rsc[:, 7:8],
                                          in1=wfull[:, NE * i:NE * (i + 1)],
                                          op0=ALU.mult, op1=ALU.add))
        ctx.mark("mo_wfv", vc)
    ctx.mark("mo_wfall", vc)

    ctx.marks["mo_dnev"] = (vc.h, 0)
    ctx.marks["mo_fuz0"] = (vc.h, 0)
    ctx.marks["mo_fuz1"] = (vc.h, 0)
    ctx.marks["mo_sil0"] = (sc.h, 0)
    ctx.marks["mo_sil1"] = (sc.h, 0)
    for e in range(NE):
        @block.gpsimd
        def _(g, e=e):
            g.wait_ge(mg.h, mg.n)
            if e > 0:
                ctx.wait_mark(g, "mo_gumm")
            for cc in range(NCK):
                mg.dma(g.dma_start(out=wg_sb[:, FF * cc:FF * (cc + 1)],
                                   in_=io["Wg"][e, 128 * cc:128 * (cc + 1), :]))
                mg.dma(g.dma_start(out=wu_sb[:, FF * cc:FF * (cc + 1)],
                                   in_=io["Wu"][e, 128 * cc:128 * (cc + 1), :]))
            if e > 0:
                ctx.wait_mark(g, "mo_dnmm")
            for fc in range(NFC):
                mg.dma(g.dma_start(out=wd_sb[:, D * fc:D * (fc + 1)],
                                   in_=io["Wd"][e, 128 * fc:128 * (fc + 1), :]))
        ctx.mark("mo_wld", mg)

        for tq in range(T // 512):
            sc0m = sc.n
            sil_end = [sc0m + fc + 1 for fc in range(NFC)]
            rec_gmm = [0] * NFC
            rec_fuz = [0] * NFC
            for fc in range(NFC):
                bg, bu = 2 * (fc % 2), 1 + 2 * (fc % 2)
                @block.tensor
                def _(te, e=e, tq=tq, fc=fc, bg=bg, bu=bu):
                    if fc == 0 and tq == 0:
                        te.wait_ge(mg.h, mg.n)
                        ctx.wait_mark(te, "mo_wfall")
                    if fc < 2:
                        ctx.wait_mark(te, "mo_dnev")
                    if fc >= 2:
                        te.wait_ge(sc.h, sil_end[fc - 2])
                        te.wait_ge(vc.h, rec_fuz[fc - 2])
                    lastg = lastu = None
                    for cc in range(NCK):
                        xs = xnfm[:, T * cc + 512 * tq:T * cc + 512 * (tq + 1)]
                        lastg = te.matmul(ctx.psb[bg][:],
                                          wg_sb[:, FF * cc + 128 * fc:FF * cc + 128 * (fc + 1)],
                                          xs, start=(cc == 0), stop=(cc == NCK - 1))
                    tc.inc(lastg)
                    rec_gmm[fc] = tc.n
                    for cc in range(NCK):
                        xs = xnfm[:, T * cc + 512 * tq:T * cc + 512 * (tq + 1)]
                        lastu = te.matmul(ctx.psb[bu][:],
                                          wu_sb[:, FF * cc + 128 * fc:FF * cc + 128 * (fc + 1)],
                                          xs, start=(cc == 0), stop=(cc == NCK - 1))
                    tc.inc(lastu)
                ctx.mark("mo_gumm", tc)
                ctx.mark(f"mo_gumm{fc % 2}", tc)

                @block.vector
                def _(v, e=e, tq=tq, fc=fc, bu=bu):
                    ctx.wait_mark(v, f"mo_gumm{fc % 2}")
                    v.wait_ge(sc.h, sil_end[fc])
                    vc.inc(v.tensor_tensor(out=fuse[:, 512 * fc:512 * (fc + 1)],
                                           in0=ctx.psb[bu][:], in1=sil[fc % 2][:],
                                           op=ALU.mult))
                rec_fuz[fc] = vc.n
                ctx.mark(f"mo_fuz{fc % 2}", vc)
                if fc == NFC - 1:
                    ctx.mark("mo_fuseall", vc)

            @block.scalar
            def _(s, e=e, tq=tq, rec_gmm=rec_gmm, rec_fuz=rec_fuz):
                for fc in range(NFC):
                    s.wait_ge(tc.h, rec_gmm[fc])
                    if fc >= 2:
                        s.wait_ge(vc.h, rec_fuz[fc - 2])
                    sc.inc(s.activation(out=sil[fc % 2][:],
                                        in_=ctx.psb[2 * (fc % 2)][:], func=AF.Silu))
            assert sc.n == sc0m + NFC
            ctx.mark("mo_silall", sc)

            @block.tensor
            def _(te, e=e, tq=tq):
                ctx.wait_mark(te, "mo_fuseall")
                ctx.wait_mark(te, "mo_dnev")
                last = None
                for i4 in range(4):
                    for nb in range(2):
                        for fc in range(NFC):
                            last = te.matmul(
                                ctx.psb[i4 * 2 + nb][:],
                                fuse[:, 512 * fc + 128 * i4:512 * fc + 128 * (i4 + 1)],
                                wd_sb[:, D * fc + 512 * nb:D * fc + 512 * (nb + 1)],
                                start=(fc == 0), stop=(fc == NFC - 1))
                tc.inc(last)
            ctx.mark("mo_dnmm", tc)

            for i4h in range(2):
                @block.vector
                def _(v, e=e, tq=tq, i4h=i4h):
                    ctx.wait_mark(v, "mo_dnmm")
                    for i4 in (2 * i4h, 2 * i4h + 1):
                        v.wait_ge(ma4[i4].h, ma4[i4].n)
                    last = None
                    for i4 in (2 * i4h, 2 * i4h + 1):
                        ti = 4 * tq + i4
                        for nb in range(2):
                            last = v.tensor_scalar(
                                out=scal[i4][:, 512 * nb:512 * (nb + 1)],
                                in0=ctx.psb[i4 * 2 + nb][:],
                                scalar1=wfull[:, NE * ti + e:NE * ti + e + 1],
                                scalar2=None, op0=ALU.mult)
                    vc.inc(last)
                ctx.mark("mo_scev", vc)

                @block.gpsimd
                def _(g, e=e, tq=tq, i4h=i4h):
                    ctx.wait_mark(g, "mo_scev")
                    if e > 0 and tq == 0 and i4h == 0:
                        for s4 in ma4:
                            g.wait_ge(s4.h, s4.n)
                    for i4 in (2 * i4h, 2 * i4h + 1):
                        ti = 4 * tq + i4
                        op = {}
                        if e > 0:
                            op = dict(accum_op=ALU.add)
                        ma4[i4].dma(g.dma_start(
                            out=io["moeacc"][128 * ti:128 * (ti + 1), :],
                            in_=scal[i4][:], **op))
            ctx.mark("mo_dnev", vc)

    for i in range(NT):
        par = i % 2

        @block.sync
        def _(sync, i=i, par=par):
            if i == 0:
                for s4 in ma4:
                    sync.wait_ge(s4.h, s4.n)
            if i >= 2:
                ctx.wait_mark(sync, f"mo_x2v{par}")
                sync.wait_ge(mst[par].h, mst[par].n)
            lf[par].dma(sync.dma_start(out=acld[par][:],
                                       in_=io["moeacc"][128 * i:128 * (i + 1), :]))
            lf[par].dma(sync.dma_start(out=x1ld[par][:],
                                       in_=io["x1"][128 * i:128 * (i + 1), :]))
            lf[par].dma(sync.dma_start(out=s1ld[par][:],
                                       in_=io["s1"][128 * i:128 * (i + 1), :]))

        @block.scalar
        def _(s, i=i, par=par):
            s.wait_ge(lf[par].h, lf[par].n)
            if i >= 1:
                ctx.wait_mark(s, "mo_x2s")
            sc.inc(s.activation(out=gs[:], in_=acld[par][:], func=AF.Silu))
        ctx.mark("mo_gsil", sc)

        @block.vector
        def _(v, i=i, par=par):
            ctx.wait_mark(v, "mo_gsil")
            v.wait_ge(mst[par].h, mst[par].n)
            vc.inc(v.tensor_tensor(out=gs[:], in0=s1ld[par][:], in1=gs[:],
                                   op=ALU.mult))
            v.wait_ge(vc.h, vc.n)
            vc.inc(v.tensor_tensor(out=x1ld[par][:], in0=x1ld[par][:], in1=gs[:],
                                   op=ALU.add))
        ctx.mark("mo_x2s", vc)
        ctx.mark(f"mo_x2v{par}", vc)

        @block.sync
        def _(sync, i=i, par=par):
            ctx.wait_mark(sync, f"mo_x2v{par}")
            mst[par].dma(sync.dma_start(out=io["out"][128 * i:128 * (i + 1), :],
                                        in_=x1ld[par][:]))

    @block.sync
    def _(sync):
        for s2 in [mst[0], mst[1], lf[0], lf[1], ml, mg] + ma4:
            if s2.n:
                sync.wait_ge(s2.h, s2.n)
    ctx.mark("moe_done", mst[1])




def emit_delta(ctx):
    nc, block = ctx.nc, ctx.block
    io = ctx.io
    dl = ctx.sem("dk_ld")
    dg = ctx.sem("dk_gl")
    vc = ctx.sem("dk_vc")
    sc = ctx.sem("dk_sc")
    tc = ctx.sem("dk_tc")
    dst = [ctx.sem("dk_st0"), ctx.sem("dk_st1")]
    lp = [ctx.sem("dk_lp0"), ctx.sem("dk_lp1")]
    lq = [ctx.sem("dk_lq0"), ctx.sem("dk_lq1")]
    pst = [ctx.sem("dk_pst0"), ctx.sem("dk_pst1")]
    ident = ctx.const["ident"]

    xnfm = _sb(ctx, "dk_xnfm", [128, NCK * T], BF16)
    vfm = _sb(ctx, "dk_vfm", [128, NCK * T], BF16)
    HB = NCHK // 2 * DH
    betaC = _sb(ctx, "dk_betaC", [64, NCHK * DH], F32)
    vtm = [_sb(ctx, f"dk_vtm{h}", [64, DHD], BF16) for h in range(DH)]
    wpool = _sb(ctx, "dk_wpool", [128, NCK * D], BF16)
    wb_sb = _sb(ctx, "dk_wb", [128, NCK * DH], BF16)
    drow = _sb(ctx, "dk_drow", [1, DH], F32)
    lgrow = _sb(ctx, "dk_lgrow", [1, DH], F32)
    beta_tm = _sb(ctx, "dk_beta", [128, NT * DH], F32)
    onesr = _sb(ctx, "dk_onesr", [1, 128], F32)
    onesrb = _sb(ctx, "dk_onesrb", [1, 128], BF16)
    onescb = _sb(ctx, "dk_onescb", [128, 1], BF16)
    iopm = _sb(ctx, "dk_iopm", [64, 64], F32)
    iocol = _sb(ctx, "dk_iocol", [64, 1], F32)
    iorow = _sb(ctx, "dk_iorow", [1, 64], F32)
    lgc64 = _sb(ctx, "dk_lgc64", [64, DH], F32)
    lgc128 = _sb(ctx, "dk_lgc128", [128, DH], F32)
    negLs = _sb(ctx, "dk_negLs", [64, DH * 64], BF16)
    LTt = _sb(ctx, "dk_LTt", [64, DH * 64], BF16)
    de_col = _sb(ctx, "dk_decol", [64, DH], F32)
    dw_col = _sb(ctx, "dk_dwcol", [64, DH], F32)
    ctd128 = _sb(ctx, "dk_ctd", [128, DH], F32)
    de_row = _sb(ctx, "dk_derow", [1, DH * 64], F32)
    de_b = _sb(ctx, "dk_deb", [128, DH * 64], BF16)
    t64a = _sb(ctx, "dk_t64a", [64, 64], F32)
    t64b = _sb(ctx, "dk_t64b", [64, 64], F32)
    rst_row = _sb(ctx, "dk_rstrow", [1, 512], BF16)

    S_f = [_sb(ctx, f"dk_Sf{h}", [128, DHD], F32) for h in range(DH)]
    S_b = [_sb(ctx, f"dk_Sb{h}", [128, DHD], BF16) for h in range(DH)]
    kk_s = [_sb(ctx, f"dk_kks{h}", [64, 64], F32) for h in range(DH)]
    A0 = [_sb(ctx, f"dk_A0{h}", [64, 64], BF16) for h in range(DH)]
    atT = [_sb(ctx, f"dk_atT{h}", [64, 64], BF16) for h in range(DH)]
    Xb = [_sb(ctx, f"dk_X{h}", [64, 64], BF16) for h in range(DH)]
    Pb = [_sb(ctx, f"dk_P{h}", [64, 64], BF16) for h in range(DH)]
    PTs = [_sb(ctx, f"dk_PT{h}", [64, 64], BF16) for h in range(DH)]
    ATs = [_sb(ctx, f"dk_AT{h}", [64, 64], BF16) for h in range(DH)]
    ktm = [_sb(ctx, f"dk_ktm{h}", [64, DHD], BF16) for h in range(DH)]
    kbde = [_sb(ctx, f"dk_kbde{h}", [64, DHD], BF16) for h in range(DH)]
    kdw = [_sb(ctx, f"dk_kdw{h}", [64, DHD], BF16) for h in range(DH)]
    kdeT = [_sb(ctx, f"dk_kdeT{h}", [128, 64], BF16) for h in range(DH)]
    Yv_s = [_sb(ctx, f"dk_Yv{h}", [64, DHD], F32) for h in range(DH)]
    kcd = [_sb(ctx, f"dk_kcd{h}", [64, DHD], BF16) for h in range(DH)]
    kcdT = [_sb(ctx, f"dk_kcdT{h}", [128, 64], BF16) for h in range(DH)]
    vnew = [_sb(ctx, f"dk_vnew{h}", [64, DHD], BF16) for h in range(DH)]
    o_s = [_sb(ctx, f"dk_os{h}", [64, DHD], BF16) for h in range(DH)]
    oTst = [[_sb(ctx, f"dk_oTst{p}_{h}", [128, 64], BF16) for h in range(DH)]
            for p in range(2)]
    xnld = [_sb(ctx, f"dk_xnld{i}", [128, D], BF16) for i in range(2)]
    oTld = [_sb(ctx, f"dk_oTld{i}", [128, 512], BF16) for i in range(2)]
    s1t = [_sb(ctx, f"dk_s1t{i}", [128, D], BF16) for i in range(2)]

    def kk_ps(h):
        return ctx.psbt[2].ap()[0:64, 64 * h:64 * (h + 1)]
    def nwx_ps(h):
        return ctx.psbt[0].ap()[0:64, 64 * h:64 * (h + 1)]
    def nwp_ps(h):
        return ctx.psbt[0].ap()[0:64, 512 + 64 * h:512 + 64 * (h + 1)]
    def yv_ps(h):
        return ctx.psbt[1].ap()[0:64, 128 * h:128 * (h + 1)]
    def o_ps(h):
        return ctx.psbt[2].ap()[0:64, 128 * h:128 * (h + 1)]
    def dW_ps(h):
        return ctx.psbt[0].ap()[:, 128 * h:128 * (h + 1)]
    def ktr_ps(h):
        return ctx.pbf(6)[0:64, 128 * h:128 * h + 128]
    def vtr_ps(h):
        return ctx.pbf(6)[0:64, 128 * h:128 * h + 128]
    def ptr_ps(h):
        return ctx.pbf(6)[0:64, 64 * h:64 * (h + 1)]
    def ttr_ps(h):
        return ctx.pbf(7)[:, 64 * h:64 * (h + 1)]

    @block.sync
    def _(sync):
        ctx.wait_mark(sync, "ln1_done")
        for cc in range(NCK):
            dl.dma(sync.dma_start(out=xnfm[:, T * cc:T * (cc + 1)],
                                  in_=io["xn1_fm"][128 * cc:128 * (cc + 1), :]))
        dl.dma(sync.dma_start(out=drow[:], in_=io["decay"]))

    @block.gpsimd
    def _(g):
        for cc in range(NCK):
            dg.dma(g.dma_start(out=wpool[:, D * cc:D * (cc + 1)],
                               in_=io["WwT"][128 * cc:128 * (cc + 1), :]))
            dg.dma(g.dma_start(out=wb_sb[:, DH * cc:DH * (cc + 1)],
                               in_=io["WbT"][128 * cc:128 * (cc + 1), :]))
        dio = ctx.sem("dk_io")
        g.iota(iopm[:], pattern=[[-1, 64]], base=0, channel_multiplier=1,
               allow_small_or_imprecise_dtypes=True)
        g.iota(iocol[:], pattern=[[0, 1]], base=0, channel_multiplier=1,
               allow_small_or_imprecise_dtypes=True)
        dio.inc(g.iota(iorow[:], pattern=[[1, 64]], base=0, channel_multiplier=0,
                       allow_small_or_imprecise_dtypes=True))
        ctx.mark("dk_iota", dio)

    @block.vector
    def _(v):
        v.memset(onesr[:], 1.0)
        v.memset(onesrb[:], 1.0)
        vc.inc(v.memset(onescb[:], 1.0))
        for h in range(DH):
            v.memset(S_f[h][:], 0.0)
            vc.inc(v.memset(S_b[h][:], 0.0))
    ctx.mark("dk_ones", vc)

    @block.scalar
    def _(s):
        s.wait_ge(dl.h, dl.n)
        sc.inc(s.activation(out=lgrow[:], in_=drow[:], func=AF.Sigmoid))
        s.wait_ge(sc.h, sc.n)
        sc.inc(s.activation(out=lgrow[:], in_=lgrow[:], func=AF.Ln))
    ctx.mark("dk_lg", sc)

    @block.tensor
    def _(te):
        ctx.wait_mark(te, "dk_lg")
        ctx.wait_mark(te, "dk_ones")
        te.matmul(ctx.psb[0][0:64, 0:DH], onesr[:, 0:64], lgrow[:], start=True, stop=True)
        tc.inc(te.matmul(ctx.psb[1][:, 0:DH], onesr[:], lgrow[:], start=True, stop=True))
    ctx.mark("dk_lgmm", tc)

    @block.scalar
    def _(s):
        ctx.wait_mark(s, "dk_lgmm")
        s.activation(out=lgc64[:], in_=ctx.psb[0][0:64, 0:DH], func=AF.Copy)
        sc.inc(s.activation(out=lgc128[:], in_=ctx.psb[1][:, 0:DH], func=AF.Copy))
    ctx.mark("dk_lgc", sc)

    for h in range(DH):
        @block.vector
        def _(v, h=h):
            if h == 0:
                ctx.wait_mark(v, "dk_lgc")
                ctx.wait_mark(v, "dk_iota")
            else:
                ctx.wait_mark(v, f"dk_lmask{h - 1}")
            vc.inc(v.tensor_scalar(out=t64a[:], in0=iopm[:], scalar1=lgc64[:, h:h + 1],
                                   scalar2=None, op0=ALU.mult))
            v.wait_ge(vc.h, vc.n)
            vc.inc(v.tensor_scalar(out=t64b[:], in0=t64a[:], scalar1=-1.0,
                                   scalar2=None, op0=ALU.mult))
            v.wait_ge(vc.h, vc.n)
            v.tensor_scalar(out=de_col[:, h:h + 1], in0=iocol[:], scalar1=1.0,
                            scalar2=lgc64[:, h:h + 1], op0=ALU.add, op1=ALU.mult)
            v.tensor_scalar(out=dw_col[:, h:h + 1], in0=iocol[:], scalar1=63.0,
                            scalar2=lgc64[:, h:h + 1], op0=ALU.subtract,
                            op1=ALU.mult)
            v.tensor_scalar(out=ctd128[:, h:h + 1], in0=lgc128[:, h:h + 1],
                            scalar1=64.0, scalar2=None, op0=ALU.mult)
            vc.inc(v.tensor_scalar(out=de_row[:, 64 * h:64 * (h + 1)], in0=iorow[:],
                                   scalar1=1.0, scalar2=lgrow[:, h:h + 1],
                                   op0=ALU.add, op1=ALU.mult))
        ctx.mark(f"dk_dd{h}", vc)

        @block.scalar
        def _(s, h=h):
            ctx.wait_mark(s, f"dk_dd{h}")
            s.activation(out=t64a[:], in_=t64a[:], func=AF.Exp)
            s.activation(out=t64b[:], in_=t64b[:], func=AF.Exp)
            s.activation(out=de_col[:, h:h + 1], in_=de_col[:, h:h + 1], func=AF.Exp)
            s.activation(out=dw_col[:, h:h + 1], in_=dw_col[:, h:h + 1], func=AF.Exp,
                         scale=-1.0)
            s.activation(out=ctd128[:, h:h + 1], in_=ctd128[:, h:h + 1], func=AF.Exp)
            sc.inc(s.activation(out=de_row[:, 64 * h:64 * (h + 1)],
                                in_=de_row[:, 64 * h:64 * (h + 1)], func=AF.Exp))
        ctx.mark(f"dk_exp{h}", sc)

        @block.gpsimd
        def _(g, h=h):
            ctx.wait_mark(g, f"dk_exp{h}")
            gio = ctx.sem("dk_gio")
            g.affine_select(negLs[:, 64 * h:64 * (h + 1)], t64a[:],
                            pattern=[[-1, 64]], compare_op=ALU.is_gt, fill=0.0,
                            base=0, channel_multiplier=1)
            gio.inc(g.affine_select(LTt[:, 64 * h:64 * (h + 1)], t64b[:],
                                    pattern=[[1, 64]], compare_op=ALU.is_ge,
                                    fill=0.0, base=0, channel_multiplier=-1))
        ctx.mark(f"dk_lmask{h}", ctx.sems["dk_gio"])

        @block.vector
        def _(v2, h=h):
            ctx.wait_mark(v2, f"dk_lmask{h}")
            vc.inc(v2.tensor_scalar(out=negLs[:, 64 * h:64 * (h + 1)],
                                    in0=negLs[:, 64 * h:64 * (h + 1)],
                                    scalar1=-1.0, scalar2=None, op0=ALU.mult))
        ctx.mark(f"dk_negl{h}", vc)
    ctx.mark("dk_consts", vc)

    @block.tensor
    def _(te):
        ctx.wait_mark(te, f"dk_exp{DH - 1}")
        ctx.wait_mark(te, "dk_consts")
        last = None
        for h in range(DH):
            last = te.matmul(ctx.psb[2][:, 0:448] if False else
                             ctx.psb[2 + h // 4][:, 64 * (h % 4):64 * (h % 4 + 1)],
                             onesr[:], de_row[:, 64 * h:64 * (h + 1)],
                             start=True, stop=True)
        tc.inc(last)
    ctx.mark("dk_debmm", tc)

    @block.scalar
    def _(s):
        ctx.wait_mark(s, "dk_debmm")
        s.activation(out=de_b[:, 0:256], in_=ctx.psb[2][:, 0:256], func=AF.Copy)
        sc.inc(s.activation(out=de_b[:, 256:512], in_=ctx.psb[3][:, 0:256], func=AF.Copy))
    ctx.mark("dk_deb", sc)

    for i in range(NT):
        @block.tensor
        def _(te, i=i):
            te.wait_ge(dl.h, dl.n)
            te.wait_ge(dg.h, dg.n)
            ctx.wait_mark(te, "dk_deb")
            if i >= 2:
                ctx.wait_mark(te, f"dk_bt{i % 2}")
            last = None
            for cc in range(NCK):
                last = te.matmul(ctx.psb[4 + i % 2][:, 0:DH],
                                 xnfm[:, T * cc + 128 * i:T * cc + 128 * (i + 1)],
                                 wb_sb[:, DH * cc:DH * (cc + 1)],
                                 start=(cc == 0), stop=(cc == NCK - 1))
            tc.inc(last)
        ctx.mark(f"dk_btm{i % 2}", tc)

        @block.scalar
        def _(s, i=i):
            ctx.wait_mark(s, f"dk_btm{i % 2}")
            sc.inc(s.activation(out=beta_tm[:, DH * i:DH * (i + 1)],
                                in_=ctx.psb[4 + i % 2][:, 0:DH], func=AF.Sigmoid))
        ctx.mark(f"dk_bt{i % 2}", sc)
    ctx.mark("dk_btall", sc)

    @block.sync
    def _(sync):
        ctx.wait_mark(sync, "dk_btall")
        bsem = ctx.sem("dk_bc")
        src_lo = beta_tm.ap()[0:64, :]
        src_hi = beta_tm.ap()[64:128, :]
        bsem.dma(sync.dma_start(out=betaC[:, 0:HB], in_=src_lo))
        bsem.dma(sync.dma_start(out=betaC[:, HB:2 * HB], in_=src_hi))
        ctx.mark("dk_betaC", bsem)

    for dc in range(NCK):
        for tq in range(T // 512):
            @block.tensor
            def _(te, dc=dc, tq=tq):
                te.wait_ge(dl.h, dl.n)
                te.wait_ge(dg.h, dg.n)
                ctx.wait_mark(te, "dk_deb")
                ctx.wait_mark(te, "dk_btall")
                idx = dc * (T // 512) + tq
                if idx >= 2:
                    ctx.wait_mark(te, f"dk_vfe{idx % 2}")
                last = None
                for cc in range(NCK):
                    last = te.matmul(
                        ctx.psb[idx % 2][:],
                        wpool[:, D * cc + 128 * dc:D * cc + 128 * (dc + 1)],
                        xnfm[:, T * cc + 512 * tq:T * cc + 512 * (tq + 1)],
                        start=(cc == 0), stop=(cc == NCK - 1))
                tc.inc(last)
            ctx.mark(f"dk_vfm{(dc * (T // 512) + tq) % 2}", tc)

            @block.vector
            def _(v, dc=dc, tq=tq):
                idx = dc * (T // 512) + tq
                ctx.wait_mark(v, f"dk_vfm{idx % 2}")
                vc.inc(v.tensor_copy(
                    vfm[:, T * dc + 512 * tq:T * dc + 512 * (tq + 1)],
                    ctx.psb[idx % 2][:]))
            ctx.mark(f"dk_vfe{(dc * (T // 512) + tq) % 2}", vc)
    ctx.mark("dk_vball", vc)

    for h in range(DH):
        for tq in range(T // 512):
            @block.vector
            def _(v, h=h, tq=tq):
                ctx.wait_mark(v, "dk_vball")
                if h > 0 or tq > 0:
                    ctx.wait_mark(v, "dk_knsq")
                vc.inc(v.scalar_tensor_tensor(
                    out=s1t[0][:, 0:512], in0=xnfm[:, T * h + 512 * tq:T * h + 512 * (tq + 1)],
                    scalar=1.0, in1=xnfm[:, T * h + 512 * tq:T * h + 512 * (tq + 1)],
                    op0=ALU.mult, op1=ALU.mult))
            ctx.mark("dk_knsq", vc)

            @block.tensor
            def _(te, h=h, tq=tq):
                ctx.wait_mark(te, "dk_knsq")
                if h > 0 or tq > 0:
                    ctx.wait_mark(te, "dk_knrs")
                tc.inc(te.matmul(ctx.psb[6][0:1, :], onescb[:],
                                 s1t[0][:, 0:512], start=True, stop=True))
            ctx.mark("dk_knmm", tc)

            @block.scalar
            def _(s, h=h, tq=tq):
                ctx.wait_mark(s, "dk_knmm")
                sc.inc(s.activation(out=rst_row[:], in_=ctx.psb[6][0:1, :],
                                     func=AF.Ln))
                s.wait_ge(sc.h, sc.n)
                sc.inc(s.activation(out=rst_row[:], in_=rst_row[:], func=AF.Exp,
                                    scale=-0.5))
            ctx.mark("dk_knr", sc)

            @block.tensor
            def _(te, h=h, tq=tq):
                ctx.wait_mark(te, "dk_knr")
                tc.inc(te.matmul(ctx.psb[7][:], onesrb[:], rst_row[:],
                                 start=True, stop=True))
            ctx.mark("dk_knrs", tc)

            @block.vector
            def _(v, h=h, tq=tq):
                ctx.wait_mark(v, "dk_knrs")
                vc.inc(v.tensor_tensor(
                    out=xnfm[:, T * h + 512 * tq:T * h + 512 * (tq + 1)],
                    in0=xnfm[:, T * h + 512 * tq:T * h + 512 * (tq + 1)],
                    in1=ctx.psb[7][:], op=ALU.mult))
            ctx.mark("dk_knorm", vc)
    ctx.mark("dk_kall", vc)

    ctx.marks["dk_e1"] = (vc.h, 0)
    ctx.marks["dk_e7"] = ctx.marks["dk_ones"]
    ctx.marks["dk_e8"] = (vc.h, 0)
    ctx.marks["dk_e4p"] = (vc.h, 0)
    for c in range(NCHK):
        ti, hf = c // 2, c % 2
        r64 = slice(64 * hf, 64 * (hf + 1))

        def kslice(h, c=c):
            return xnfm[:, T * h + 64 * c:T * h + 64 * (c + 1)]

        @block.tensor
        def _(te, c=c):
            if c == 0:
                ctx.wait_mark(te, "dk_kall")
                ctx.wait_mark(te, "dk_vball")
                ctx.wait_mark(te, "prelude_ld")
            ctx.wait_mark(te, "dk_e1")
            ctx.wait_mark(te, "dk_betaC")
            last = None
            for h in range(DH):
                te.matmul(kk_ps(h), kslice(h), kslice(h), start=True, stop=True)
                last = te.transpose(ktr_ps(h), kslice(h), ident[:])
            tc.inc(last)
        ctx.mark("dk_t1", tc)

        @block.vector
        def _(v, c=c, ti=ti, hf=hf):
            ctx.wait_mark(v, "dk_t1")
            for h in range(DH):
                v.tensor_copy(kk_s[h][:], kk_ps(h))
                vc.inc(v.tensor_copy(ktm[h][:], ktr_ps(h)))
        ctx.mark("dk_e1k", vc)

        @block.tensor
        def _(te, c=c):
            ctx.wait_mark(te, "dk_e1k")
            last = None
            for h in range(DH):
                last = te.transpose(vtr_ps(h),
                                    vfm[:, T * h + 64 * c:T * h + 64 * (c + 1)],
                                    ident[:])
            tc.inc(last)
        ctx.mark("dk_t1v", tc)

        @block.vector
        def _(v, c=c, ti=ti, hf=hf):
            ctx.wait_mark(v, "dk_t1v")
            for h in range(DH):
                bcol0 = betaC.ap()[:, HB * hf + DH * ti + h:HB * hf + DH * ti + h + 1]
                vc.inc(v.tensor_scalar(out=vtm[h][:], in0=vtr_ps(h), scalar1=bcol0,
                                       scalar2=None, op0=ALU.mult))
            v.wait_ge(vc.h, vc.n)
            last = None
            for h in range(DH):
                bcol = betaC.ap()[:, HB * hf + DH * ti + h:HB * hf + DH * ti + h + 1]
                v.scalar_tensor_tensor(out=A0[h][:], in0=kk_s[h][:], scalar=bcol,
                                       in1=negLs[:, 64 * h:64 * (h + 1)],
                                       op0=ALU.mult, op1=ALU.mult)
                v.tensor_tensor(out=atT[h][:], in0=kk_s[h][:],
                                in1=LTt[:, 64 * h:64 * (h + 1)], op=ALU.mult)
                v.tensor_scalar(out=kbde[h][:], in0=ktm[h][:], scalar1=bcol,
                                scalar2=de_col[:, h:h + 1], op0=ALU.mult, op1=ALU.mult)
                v.tensor_scalar(out=kdw[h][:], in0=ktm[h][:],
                                scalar1=dw_col[:, h:h + 1], scalar2=None, op0=ALU.mult)
                last = v.tensor_tensor(out=kdeT[h][:], in0=kslice(h),
                                       in1=de_b[:, 64 * h:64 * (h + 1)], op=ALU.mult)
            vc.inc(last)
            v.wait_ge(vc.h, vc.n)
            for h in range(DH):
                vc.inc(v.tensor_copy(Xb[h][:], ident[0:64, 0:64]))
            v.wait_ge(vc.h, vc.n)
            for h in range(DH):
                vc.inc(v.tensor_copy(Pb[h][:], A0[h][:]))
        ctx.mark("dk_e1", vc)

        NR = 6
        for r in range(NR):
            @block.tensor
            def _(te, c=c, r=r):
                ctx.wait_mark(te, "dk_e1" if r == 0 else f"dk_ne{r - 1}")
                last = None
                for h in range(DH):
                    last = te.transpose(ptr_ps(h), Pb[h][:], ident[0:64, 0:64])
                tc.inc(last)
            ctx.mark(f"dk_ntr{r}", tc)

            @block.vector
            def _(v, c=c, r=r):
                ctx.wait_mark(v, f"dk_ntr{r}")
                for h in range(DH):
                    vc.inc(v.tensor_copy(PTs[h][:], ptr_ps(h)))
            ctx.mark(f"dk_nev{r}", vc)

            @block.tensor
            def _(te, c=c, r=r):
                ctx.wait_mark(te, f"dk_nev{r}")
                last = None
                for h in range(DH):
                    last = te.matmul(nwx_ps(h), PTs[h][:], Xb[h][:],
                                     start=True, stop=True)
                    if r < NR - 1:
                        last = te.matmul(nwp_ps(h), PTs[h][:], Pb[h][:],
                                         start=True, stop=True)
                tc.inc(last)
            ctx.mark(f"dk_nmm{r}", tc)

            @block.vector
            def _(v, c=c, r=r):
                ctx.wait_mark(v, f"dk_nmm{r}")
                for h in range(DH):
                    vc.inc(v.tensor_tensor(out=Xb[h][:], in0=Xb[h][:],
                                           in1=nwx_ps(h), op=ALU.add))
                if r < NR - 1:
                    v.wait_ge(vc.h, vc.n)
                    for h in range(DH):
                        vc.inc(v.tensor_copy(Pb[h][:], nwp_ps(h)))
            ctx.mark(f"dk_ne{r}", vc)

        @block.tensor
        def _(te, c=c):
            ctx.wait_mark(te, f"dk_ne{NR - 1}")
            last = None
            for h in range(DH):
                last = te.transpose(ptr_ps(h), Xb[h][:], ident[0:64, 0:64])
            tc.inc(last)
        ctx.mark("dk_t3", tc)

        @block.vector
        def _(v, c=c):
            ctx.wait_mark(v, "dk_t3")
            for h in range(DH):
                vc.inc(v.tensor_copy(ATs[h][:], ptr_ps(h)))
        ctx.mark("dk_e3", vc)

        @block.tensor
        def _(te, c=c, ti=ti, hf=hf):
            ctx.wait_mark(te, "dk_e3")
            ctx.wait_mark(te, "dk_e4p")
            last = None
            for h in range(DH):
                te.matmul(yv_ps(h), ATs[h][:], vtm[h][:], start=True, stop=True)
                last = te.matmul(o_ps(h), ATs[h][:], kbde[h][:], start=True, stop=True)
            tc.inc(last)
        ctx.mark("dk_t4", tc)

        @block.vector
        def _(v, c=c):
            ctx.wait_mark(v, "dk_t4")
            for h in range(DH):
                v.tensor_copy(Yv_s[h][:], yv_ps(h))
                vc.inc(v.tensor_copy(kcd[h][:], o_ps(h)))
        ctx.mark("dk_e4", vc)
        ctx.mark("dk_e4p", vc)

        @block.tensor
        def _(te, c=c):
            ctx.wait_mark(te, "dk_e4")
            last = None
            for h in range(DH):
                last = te.transpose(ttr_ps(h), kcd[h][:], ident[0:64, 0:64])
            tc.inc(last)
        ctx.mark("dk_t5", tc)

        @block.vector
        def _(v, c=c):
            ctx.wait_mark(v, "dk_t5")
            for h in range(DH):
                vc.inc(v.tensor_copy(kcdT[h][:], ttr_ps(h)))
        ctx.mark("dk_e5", vc)

        @block.tensor
        def _(te, c=c):
            ctx.wait_mark(te, "dk_e5")
            ctx.wait_mark(te, "dk_e7")
            last = None
            for h in range(DH):
                last = te.matmul(o_ps(h), kcdT[h][:], S_b[h][:], start=True, stop=True)
            tc.inc(last)
        ctx.mark("dk_t6", tc)

        @block.vector
        def _(v, c=c):
            ctx.wait_mark(v, "dk_t6")
            for h in range(DH):
                vc.inc(v.tensor_tensor(out=vnew[h][:], in0=Yv_s[h][:], in1=o_ps(h),
                                       op=ALU.subtract))
        ctx.mark("dk_e6", vc)

        @block.tensor
        def _(te, c=c):
            ctx.wait_mark(te, "dk_e6")
            last = None
            for h in range(DH):
                te.matmul(yv_ps(h), kdeT[h][:], S_b[h][:], start=True, stop=False)
                te.matmul(yv_ps(h), atT[h][:], vnew[h][:], start=False, stop=True)
                last = te.matmul(dW_ps(h), kdw[h][:], vnew[h][:], start=True, stop=True)
            tc.inc(last)
        ctx.mark("dk_t7", tc)

        @block.vector
        def _(v, c=c):
            ctx.wait_mark(v, "dk_t7")
            for h in range(DH):
                vc.inc(v.tensor_copy(o_s[h][:], yv_ps(h)))
            v.wait_ge(vc.h, vc.n)
            for h in range(DH):
                vc.inc(v.scalar_tensor_tensor(out=S_f[h][:], in0=S_f[h][:],
                                              scalar=ctd128[:, h:h + 1], in1=dW_ps(h),
                                              op0=ALU.mult, op1=ALU.add))
            v.wait_ge(vc.h, vc.n)
            for h in range(DH):
                vc.inc(v.tensor_copy(S_b[h][:], S_f[h][:]))
        ctx.mark("dk_e7", vc)

        @block.tensor
        def _(te, c=c):
            ctx.wait_mark(te, "dk_e7")
            ctx.wait_mark(te, "dk_e8")
            last = None
            for h in range(DH):
                last = te.transpose(ttr_ps(h), o_s[h][:], ident[0:64, 0:64])
            tc.inc(last)
        ctx.mark("dk_t8", tc)

        @block.vector
        def _(v, c=c):
            ctx.wait_mark(v, "dk_t8")
            v.wait_ge(dst[c % 2].h, dst[c % 2].n)
            for h in range(DH):
                vc.inc(v.tensor_copy(oTst[c % 2][h][:], ttr_ps(h)))
        ctx.mark("dk_e8", vc)

        @block.sync
        def _(sync, c=c):
            ctx.wait_mark(sync, "dk_e8")
            sync.wait_ge(dst[c % 2].h, dst[c % 2].n)
            for h in range(DH):
                dst[c % 2].dma(sync.dma_start(
                    out=io["oT1h"][DHD * h:DHD * (h + 1), 64 * c:64 * (c + 1)],
                    in_=oTst[c % 2][h][:]))

    @block.gpsimd
    def _(g):
        g.wait_ge(dg.h, dg.n)
        ctx.wait_mark(g, "dk_vball")
        for cc in range(NCK):
            dg.dma(g.dma_start(out=wpool[:, D * cc:D * (cc + 1)],
                               in_=io["WrT"][128 * cc:128 * (cc + 1), :]))
    ctx.mark("dk_wr", dg)

    for tg in range(T // 512):
        for oc in range(NCK):
            @block.sync
            def _(sync, tg=tg, oc=oc):
                if tg == 0 and oc == 0:
                    sync.wait_ge(dst[0].h, dst[0].n)
                    sync.wait_ge(dst[1].h, dst[1].n)
                idx = tg * NCK + oc
                if idx >= 2:
                    ctx.wait_mark(sync, f"dk_pj{oc % 2}")
                lp[oc % 2].dma(sync.dma_start(
                    out=oTld[oc % 2][:],
                    in_=io["oT1h"][128 * oc:128 * (oc + 1), 512 * tg:512 * (tg + 1)]))

            @block.tensor
            def _(te, tg=tg, oc=oc):
                te.wait_ge(lp[oc % 2].h, lp[oc % 2].n)
                if tg == 0 and oc == 0:
                    ctx.wait_mark(te, "dk_wr")
                if oc == 0:
                    ctx.wait_mark(te, "dk_pjev")
                last = None
                for i4 in range(4):
                    for nb in range(2):
                        last = te.matmul(
                            ctx.psb[i4 * 2 + nb][:],
                            oTld[oc % 2][:, 128 * i4:128 * (i4 + 1)],
                            wpool[:, D * oc + 512 * nb:D * oc + 512 * (nb + 1)],
                            start=(oc == 0), stop=(oc == NCK - 1))
                tc.inc(last)
            ctx.mark(f"dk_pj{oc % 2}", tc)
        ctx.mark("dk_pjmm", tc)

        for i4 in range(4):
            @block.sync
            def _(sync, tg=tg, i4=i4):
                i = 4 * tg + i4
                if i >= 2:
                    ctx.wait_mark(sync, f"dk_s1v{i4 % 2}")
                lq[i4 % 2].dma(sync.dma_start(
                    out=xnld[i4 % 2][:], in_=io["xn1_tm"][128 * i:128 * (i + 1), :]))

            @block.vector
            def _(v, tg=tg, i4=i4):
                if i4 == 0:
                    ctx.wait_mark(v, "dk_pjmm")
                v.wait_ge(lq[i4 % 2].h, lq[i4 % 2].n)
                v.wait_ge(pst[i4 % 2].h, pst[i4 % 2].n)
                last = None
                for nb in range(2):
                    last = v.tensor_tensor(
                        out=s1t[i4 % 2][:, 512 * nb:512 * (nb + 1)],
                        in0=ctx.psb[i4 * 2 + nb][:],
                        in1=xnld[i4 % 2][:, 512 * nb:512 * (nb + 1)], op=ALU.add)
                vc.inc(last)
            if i4 == 3:
                ctx.mark("dk_pjev", vc)
            ctx.mark(f"dk_s1v{i4 % 2}", vc)

            @block.sync
            def _(sync, tg=tg, i4=i4):
                ctx.wait_mark(sync, f"dk_s1v{i4 % 2}")
                i = 4 * tg + i4
                pst[i4 % 2].dma(sync.dma_start(
                    out=io["s1"][128 * i:128 * (i + 1), :], in_=s1t[i4 % 2][:]))

    ctx.mark("delta_done", pst[1])


TENSORS = {
    "x": ([T, D], F32),
    "qkv_wT": ([D, 3 * D], F32), "qkv_b": ([1, 3 * D], F32),
    "proj_wT": ([D, D], F32), "proj_b": ([1, D], F32),
    "wq_peerT": ([D, 2 * D], F32), "keysT": ([PH, 2, PDK, NKEYS], F32),
    "norm_w": ([1, D], F32),
    "exp_down": ([NEXP, D], F32), "exp_up": ([NEXP, D], F32),
    "WwT": ([D, D], F32), "WrT": ([D, D], F32), "WbT": ([D, DH], F32),
    "decay": ([1, DH], F32),
    "router": ([D, NE], F32),
    "Wg": ([NE, D, FF], F32), "Wu": ([NE, D, FF], F32), "Wd": ([NE, FF, D], F32),
    "rope_cos": ([128, (T // 128) * 32], F32), "rope_sin": ([128, (T // 128) * 32], F32),
    "diag_masks": ([128, 4 * 512], BF16), "ident_bf": ([128, 128], BF16),
    "qkv_wg": ([NGRP * NCK * 128, 3 * 64 * GH], F32),
    "xn_tm": ([T, D], BF16), "xn_fm": ([D, T], BF16),
    "s": ([T, D], BF16), "oTh": ([D, T], BF16),
    "x1": ([T, D], F32),
    "xn1_tm": ([T, D], BF16), "xn1_fm": ([D, T], BF16),
    "s1": ([T, D], BF16), "moeacc": ([T, D], F32), "oT1h": ([D, T], BF16),
    "out": ([T, D], F32),
}

STAGES = {
    "ln0": dict(reads=["x", "ident_bf"], writes=["xn_tm", "xn_fm"],
                emit=lambda ctx: emit_ln(ctx, "x", "xn_tm", "xn_fm", "ln0")),
    "ln1": dict(reads=["x1", "ident_bf"], writes=["xn1_tm", "xn1_fm"],
                emit=lambda ctx: emit_ln(ctx, "x1", "xn1_tm", "xn1_fm", "ln1")),
    "attn0": dict(reads=["xn_fm", "qkv_wg", "qkv_b", "proj_wT", "proj_b",
                         "rope_cos", "rope_sin", "diag_masks", "ident_bf"],
                  writes=["s"], scratch=["oTh"], emit=emit_attn0),
    "peer": dict(reads=["xn_tm", "xn_fm", "wq_peerT", "keysT", "norm_w",
                        "exp_down", "exp_up", "x", "s", "ident_bf"],
                 writes=["x1"], emit=emit_peer),
    "moe": dict(reads=["xn1_fm", "router", "Wg", "Wu", "Wd", "x1", "s1",
                       "ident_bf"],
                writes=["out"], scratch=["moeacc"], emit=emit_moe),
    "delta": dict(reads=["xn1_fm", "xn1_tm", "WwT", "WbT", "WrT", "decay",
                         "ident_bf"],
                  writes=["s1"], scratch=["oT1h"], emit=emit_delta),
}


def build_program(stage_names):
    stages = [STAGES[s] for s in stage_names]
    reads = set()
    writes = set()
    scratch = set()
    for s in stages:
        reads.update(s["reads"])
        writes.update(s["writes"])
        scratch.update(s.get("scratch", []))
    fused_all = "out" in writes and len(stage_names) > 1

    stack = ExitStack()
    nc = bass.Bass("TRN2", target_bir_lowering=False, debug=False)
    io = {}
    in_names = []
    out_names = []
    for name in sorted(reads | writes | scratch):
        shape, dt = TENSORS[name]
        if name in scratch:
            kind = "Internal"
        elif name in writes:
            if fused_all and name != "out":
                kind = "Internal"
            else:
                kind = "ExternalOutput"
                out_names.append(name)
        else:
            kind = "ExternalInput"
            in_names.append(name)
        io[name] = nc.dram_tensor(name, shape, dt, kind=kind).ap()

    ctx = Ctx(nc, stack, io)
    with stack:
        with nc.Block() as block:
            ctx.block = block
            emit_prelude(ctx)
            for si, s in enumerate(stages):
                if si > 0:
                    def _bar(eng):
                        for sem in ctx.sems.values():
                            if sem.n > 0:
                                eng.wait_ge(sem.h, sem.n)
                    block.sync(_bar)
                    block.gpsimd(_bar)
                    block.vector(_bar)
                    block.scalar(_bar)
                    block.tensor(_bar)
                with ExitStack() as scope:
                    ctx.scope = scope
                    s["emit"](ctx)
                ctx.scope = stack

            @block.sync
            def _(sync):
                for sem in ctx.sems.values():
                    if sem.n > 0:
                        sync.wait_ge(sem.h, sem.n)
    return nc, in_names, out_names


def run_stages(stage_names, per_core, core_ids=None):
    core_ids = core_ids or list(range(len(per_core)))
    nc, in_names, out_names = build_program(stage_names)
    in_maps = [{n: pc[n] for n in in_names} for pc in per_core]
    res = run_bass_kernel_spmd(nc, in_maps, core_ids=core_ids)
    for pc, r in zip(per_core, res.results):
        for n in out_names:
            pc[n] = r[n]
    return per_core


PIPELINE = ["ln0", "attn0", "peer", "ln1", "delta", "moe"]
FUSED = True


def kernel(**inputs):
    w = host_prepare(inputs)
    xfull = np.asarray(inputs["L0_attn_x"])
    per_core = [dict(w, x=np.ascontiguousarray(xfull[i])) for i in range(B)]
    if FUSED:
        run_stages(PIPELINE, per_core)
    else:
        for st in PIPELINE:
            run_stages([st], per_core)
    return np.stack([pc["out"] for pc in per_core], axis=0)


# version 04
# speedup vs baseline: 1.1528x; 1.0045x over previous
import math
from contextlib import ExitStack

import numpy as np
import ml_dtypes

import concourse.bass as bass
import concourse.mybir as mybir
from concourse.bass import IndirectOffsetOnAxis
from concourse.bass_utils import run_bass_kernel_spmd

F32 = mybir.dt.float32
BF16 = mybir.dt.bfloat16
I32 = mybir.dt.int32
U32 = mybir.dt.uint32
AF = mybir.ActivationFunctionType
ALU = mybir.AluOpType
AX = mybir.AxisListType

B = 8
T = 4096
D = 1024
NT = T // 128
NCK = D // 128

AH = 16
AHD = 64
PH = 8
PDK = 128
PK = 8
NKEYS = 256
NEXP = NKEYS * NKEYS
DH = 8
DHD = 128
CH = 64
NCHK = T // CH
NE = 8
FF = 2048

BF = ml_dtypes.bfloat16

QKS = 1.0 / math.sqrt(8.0)
NGRP = 4
GH = AH // NGRP



def host_prepare(inp):
    c = lambda a: np.ascontiguousarray(a)
    w = {}
    w["qkv_wT"] = c(np.asarray(inp["L0_attn_qkv_w"]).T)
    w["qkv_b"] = c(np.asarray(inp["L0_attn_qkv_b"]).reshape(1, 3 * D))
    w["proj_wT"] = c(np.asarray(inp["L0_attn_proj_w"]).T)
    w["proj_b"] = c(np.asarray(inp["L0_attn_proj_b"]).reshape(1, D))
    w["wq_peerT"] = c(np.asarray(inp["L0_ffn_w_q"]).T)
    w["keysT"] = c(np.asarray(inp["L0_ffn_keys"]).transpose(0, 2, 3, 1))
    w["norm_w"] = c(np.asarray(inp["L0_ffn_norm_w"]).reshape(1, D))
    w["exp_down"] = c(np.asarray(inp["L0_ffn_expert_down"]))
    w["exp_up"] = c(np.asarray(inp["L0_ffn_expert_up"]))
    w["WwT"] = c(np.asarray(inp["L1_attn_W_write"]).T)
    w["WrT"] = c(np.asarray(inp["L1_attn_W_read"]).T)
    w["WbT"] = c(np.asarray(inp["L1_attn_W_beta"]).T)
    w["decay"] = c(np.asarray(inp["L1_attn_decay"]).reshape(1, DH))
    w["router"] = c(np.asarray(inp["L1_ffn_router_kernel_DE"]))
    w["Wg"] = c(np.asarray(inp["L1_ffn_kernel_gating_EDF"]))
    w["Wu"] = c(np.asarray(inp["L1_ffn_kernel_up_EDF"]))
    w["Wd"] = c(np.asarray(inp["L1_ffn_kernel_down_EFD"]))

    qkvT = w["qkv_wT"]
    gw = np.zeros((NGRP, NCK, 128, 3 * 64 * GH), dtype=np.float32)
    for g in range(NGRP):
        cols = np.r_[64 * GH * g:64 * GH * (g + 1),
                     D + 64 * GH * g:D + 64 * GH * (g + 1),
                     2 * D + 64 * GH * g:2 * D + 64 * GH * (g + 1)]
        for cc in range(NCK):
            gw[g, cc] = qkvT[128 * cc:128 * (cc + 1), cols]
    w["qkv_wg"] = c(gw.reshape(NGRP * NCK * 128, 3 * 64 * GH))

    half = AHD // 2
    freqs = 1.0 / 10000.0 ** (np.arange(half, dtype=np.float64) / half)
    ang = np.arange(T, dtype=np.float64)[:, None] * freqs[None, :]
    cosf = np.cos(ang).astype(np.float32)
    sinf = np.sin(ang).astype(np.float32)
    w["rope_cos"] = c(cosf.reshape(T // 128, 128, half).transpose(1, 0, 2).reshape(128, -1))
    w["rope_sin"] = c(sinf.reshape(T // 128, 128, half).transpose(1, 0, 2).reshape(128, -1))
    masks = np.zeros((4, 128, 512), dtype=BF)
    kk, qq = np.arange(128)[:, None], np.arange(512)[None, :]
    for m in range(4):
        masks[m] = (kk <= qq - 128 * m).astype(BF)
    w["diag_masks"] = c(masks.transpose(1, 0, 2).reshape(128, 4 * 512))
    w["ident_bf"] = np.eye(128, dtype=BF)
    return w



class Sem:
    __slots__ = ("h", "n")

    def __init__(self, h):
        self.h = h
        self.n = 0

    def inc(self, instr, k=1):
        instr.then_inc(self.h, k)
        self.n += k
        return instr

    def dma(self, instr):
        return self.inc(instr, 16)


class Ctx:
    def __init__(self, nc, stack, io):
        self.nc = nc
        self.stack = stack
        self.scope = stack
        self.io = io
        self.block = None
        self.sems = {}
        self.marks = {}
        self.psbt = [nc.alloc_psum_tensor(f"psbt{i}", [128, 1024], F32)
                     for i in range(4)]
        self.psb = [self.psbt[i // 2].ap()[:, 512 * (i % 2):512 * (i % 2 + 1)]
                    for i in range(8)]
        self.const = {}

    def sem(self, name):
        if name not in self.sems:
            h = self.stack.enter_context(self.nc.semaphore(name))
            self.sems[name] = Sem(h)
        return self.sems[name]

    def mark(self, name, sem=None, n=None):
        self.marks[name] = (sem.h, sem.n if n is None else n)

    def wait_mark(self, eng, name):
        if name not in self.marks:
            return
        h, n = self.marks[name]
        if n > 0:
            eng.wait_ge(h, n)

    def pbf(self, i):
        return self.psb[i].bitcast(BF16)


def _sb(ctx, name, shape, dtype=F32):
    return ctx.scope.enter_context(ctx.nc.sbuf_tensor(name, shape, dtype))



def emit_prelude(ctx):
    nc, block = ctx.nc, ctx.block
    ld = ctx.sem("pre_ld")
    ident = _sb(ctx, "c_ident", [128, 128], BF16)
    ones_row = _sb(ctx, "c_ones_row", [1, 512], BF16)
    ctx.const["ident"] = ident
    ctx.const["ones_row"] = ones_row

    @block.sync
    def _(sync):
        ld.dma(sync.dma_start(out=ident[:], in_=ctx.io["ident_bf"]))

    vc = ctx.sem("pre_vc")

    @block.vector
    def _(v):
        vc.inc(v.memset(ones_row[:], 1.0))

    ctx.mark("prelude", vc)
    ctx.mark("prelude_ld", ld)



def emit_ln(ctx, x_name, xn_tm_name, xn_fm_name, tag):
    nc, block = ctx.nc, ctx.block
    x_h = ctx.io[x_name]
    xn_tm_h = ctx.io[xn_tm_name]
    xn_fm_h = ctx.io[xn_fm_name]

    ld2 = [ctx.sem(f"{tag}_ld0"), ctx.sem(f"{tag}_ld1")]
    sc = ctx.sem(f"{tag}_sc")
    vc = ctx.sem(f"{tag}_vc")
    tc = ctx.sem(f"{tag}_tc")
    ev = ctx.sem(f"{tag}_ev")
    st = ctx.sem(f"{tag}_st")

    xld = [_sb(ctx, f"{tag}_xld{i}", [128, D], F32) for i in range(2)]
    scr = _sb(ctx, f"{tag}_scr", [128, D], F32)
    scr2 = _sb(ctx, f"{tag}_scr2", [128, D], F32)
    ssum = _sb(ctx, f"{tag}_ssum", [128, 2], F32)
    mst = _sb(ctx, f"{tag}_mst", [128, 4], F32)
    xn_tm = [_sb(ctx, f"{tag}_xntm{i}", [128, D], BF16) for i in range(2)]
    xn_fm = _sb(ctx, f"{tag}_xnfm", [128, NCK * T], BF16)

    ident = ctx.const["ident"]
    vfree = {}
    tdone = {}

    for i in range(NT):
        @block.sync
        def _(sync, i=i):
            if i >= 2:
                h, n = ctx.marks[f"{tag}_vfree{i - 2}"]
                sync.wait_ge(h, n)
            ld2[i % 2].dma(sync.dma_start(out=xld[i % 2][:],
                                          in_=x_h[128 * i:128 * (i + 1), :]))

        @block.scalar
        def _(s, i=i):
            s.wait_ge(ld2[i % 2].h, ld2[i % 2].n)
            if i >= 1:
                h, n = ctx.marks[f"{tag}_vfree{i - 1}"]
                s.wait_ge(h, n)
            sc.inc(s.activation(out=scr[:], in_=xld[i % 2][:], func=AF.Copy,
                                accum_out=ssum[:, 0:1]))
            sc.inc(s.activation(out=scr2[:], in_=xld[i % 2][:], func=AF.Square,
                                accum_out=ssum[:, 1:2]))

        @block.vector
        def _(v, i=i):
            v.wait_ge(sc.h, sc.n)
            vc.inc(v.tensor_scalar(out=mst[:, 0:1], in0=ssum[:, 0:1], scalar1=1.0 / D,
                                   scalar2=None, op0=ALU.mult))
            v.wait_ge(vc.h, vc.n)
            vc.inc(v.tensor_scalar(out=mst[:, 3:4], in0=mst[:, 0:1], scalar1=mst[:, 0:1],
                                   scalar2=-1e-5, op0=ALU.mult, op1=ALU.add))
            v.wait_ge(vc.h, vc.n)
            vc.inc(v.tensor_scalar(out=mst[:, 1:2], in0=ssum[:, 1:2], scalar1=1.0 / D,
                                   scalar2=mst[:, 3:4], op0=ALU.mult, op1=ALU.subtract))
        ctx.mark(f"{tag}_var{i}", vc)

        @block.scalar
        def _(s, i=i):
            v_h, v_n = ctx.marks[f"{tag}_var{i}"]
            s.wait_ge(v_h, v_n)
            sc.inc(s.activation(out=mst[:, 3:4], in_=mst[:, 1:2], func=AF.Ln))
            s.wait_ge(sc.h, sc.n)
            sc.inc(s.activation(out=mst[:, 2:3], in_=mst[:, 3:4], func=AF.Exp, scale=-0.5))

        @block.vector
        def _(v, i=i):
            v.wait_ge(sc.h, sc.n)
            if i >= 2:
                h, n = ctx.marks[f"{tag}_tdone{i - 2}"]
                v.wait_ge(h, n)
                v.wait_ge(st.h, st.n)
            vc.inc(v.tensor_scalar(out=xn_tm[i % 2][:], in0=xld[i % 2][:],
                                   scalar1=mst[:, 0:1], scalar2=mst[:, 2:3],
                                   op0=ALU.subtract, op1=ALU.mult))
        ctx.mark(f"{tag}_vfree{i}", vc)
        ctx.mark(f"{tag}_xntm{i}", vc)

        @block.sync
        def _(sync, i=i):
            h, n = ctx.marks[f"{tag}_xntm{i}"]
            sync.wait_ge(h, n)
            st.dma(sync.dma_start(out=xn_tm_h[128 * i:128 * (i + 1), :], in_=xn_tm[i % 2][:]))

        @block.tensor
        def _(te, i=i):
            h, n = ctx.marks[f"{tag}_xntm{i}"]
            te.wait_ge(h, n)
            ctx.wait_mark(te, "prelude_ld")
            for cc in range(NCK):
                k = i * NCK + cc
                if k >= 4:
                    te.wait_ge(ev.h, k - 3)
                tc.inc(te.transpose(ctx.pbf(k % 4)[:, 0:128],
                                    xn_tm[i % 2][:, 128 * cc:128 * (cc + 1)], ident[:]))

        @block.vector
        def _(v, i=i):
            for cc in range(NCK):
                k = i * NCK + cc
                v.wait_ge(tc.h, k + 1)
                ev.inc(v.tensor_copy(xn_fm[:, T * cc + 128 * i: T * cc + 128 * (i + 1)],
                                     ctx.pbf(k % 4)[:, 0:128]))
        ctx.mark(f"{tag}_tdone{i}", ev)

    @block.sync
    def _(sync):
        sync.wait_ge(ev.h, NT * NCK)
        for cc in range(NCK):
            st.dma(sync.dma_start(out=xn_fm_h[128 * cc:128 * (cc + 1), :],
                                  in_=xn_fm[:, T * cc:T * (cc + 1)]))
        sync.wait_ge(st.h, st.n)

    ctx.mark(f"{tag}_done", st)






def emit_av(ctx, block, pend, tc, vx, PT):
    hh, j, kg, nk, par, had_mask = pend

    @block.tensor
    def _(te):
        ctx.wait_mark(te, f"at_expg{par}")
        if had_mask:
            ctx.wait_mark(te, f"at_maskg{par}")
        last = None
        for idx, k in enumerate(kg):
            last = te.matmul(
                ctx.psb[2][0:65, :],
                vx.ap()[:, 65 * (GH * k + hh):65 * (GH * k + hh) + 65],
                PT[3 * par + idx][:],
                start=(k == 0), stop=(k == nk - 1))
        tc.inc(last)
    ctx.mark(f"at_avg{par}", tc)
    ctx.mark("at_avgrp", tc)


def emit_attn0(ctx):
    nc, block = ctx.nc, ctx.block
    io = ctx.io
    ld = ctx.sem("at_ld")
    gl = ctx.sem("at_gl")
    vc = ctx.sem("at_vc")
    sc = ctx.sem("at_sc")
    tc = ctx.sem("at_tc")
    st = ctx.sem("at_st")
    sto = [ctx.sem("at_sto0"), ctx.sem("at_sto1")]
    lb = ctx.sem("at_lb")
    sts = [ctx.sem("at_sts0"), ctx.sem("at_sts1")]
    lo = [ctx.sem("at_lo0"), ctx.sem("at_lo1")]

    ident = ctx.const["ident"]

    xnfm = _sb(ctx, "at_xnfm", [128, NCK * T], BF16)
    wpool = _sb(ctx, "at_wpool", [128, NCK * D], BF16)
    cos_sb = _sb(ctx, "at_cos", [128, NT * 32], BF16)
    sin_sb = _sb(ctx, "at_sin", [128, NT * 32], BF16)
    masks = _sb(ctx, "at_masks", [128, 4 * 512], BF16)
    bias_bg = _sb(ctx, "at_biasbg", [128, 3 * 64 * GH], F32)
    projb_b = _sb(ctx, "at_projbb", [128, D], F32)
    brow_g = _sb(ctx, "at_browg", [1, 3 * 64 * GH], BF16)
    pbrow = _sb(ctx, "at_pbrow", [1, D], BF16)
    onesr = _sb(ctx, "at_onesr", [1, 128], BF16)
    sclr = _sb(ctx, "at_sclr", [1, 128], BF16)
    ones64 = _sb(ctx, "at_ones64", [1, 64], BF16)

    qTx = [_sb(ctx, f"at_qTx{h}", [64, T], BF16) for h in range(GH)]
    kTx = [_sb(ctx, f"at_kTx{h}", [64, T], BF16) for h in range(GH)]
    vx = _sb(ctx, "at_vx", [128, NT * GH * 65], BF16)
    oTst = [_sb(ctx, f"at_oTst{i}", [64, 512], BF16) for i in range(2)]
    qkvf = _sb(ctx, "at_qkvf", [128, 3 * 64 * GH], F32)
    qk_bf = _sb(ctx, "at_qkbf", [128, 2 * 64 * GH], BF16)
    rsc = _sb(ctx, "at_rsc", [128, 4 * 32 * GH], F32)
    qn2 = _sb(ctx, "at_qn2", [128, 2 * GH], F32)
    qn2m = _sb(ctx, "at_qn2m", [128, 2 * GH], F32)
    qn2mb = _sb(ctx, "at_qn2mb", [128, 2 * GH], BF16)
    cneg = _sb(ctx, "at_cneg", [128, GH], F32)
    qm4 = _sb(ctx, "at_qm4", [4, 1], F32)
    km4 = _sb(ctx, "at_km4", [4, 1], F32)
    c4s = _sb(ctx, "at_c4s", [4, 1], F32)
    c4b = _sb(ctx, "at_c4b", [4, 1], BF16)
    crow = _sb(ctx, "at_crow", [1, GH], BF16)
    onesrb = _sb(ctx, "at_onesrb", [1, 128], BF16)
    PT = [_sb(ctx, f"at_PT{i}", [128, 512], BF16) for i in range(6)]
    rz = _sb(ctx, "at_rz", [1, 512], BF16)
    zb = _sb(ctx, "at_zb", [64, 512], F32)
    oTld = [_sb(ctx, f"at_oTld{i}", [128, 512], BF16) for i in range(2)]
    s_tm = [_sb(ctx, f"at_stm{i}", [128, D], BF16) for i in range(2)]

    @block.sync
    def _(sync):
        ctx.wait_mark(sync, "ln0_done")
        for cc in range(NCK):
            ld.dma(sync.dma_start(out=xnfm[:, T * cc:T * (cc + 1)],
                                  in_=io["xn_fm"][128 * cc:128 * (cc + 1), :]))
        ld.dma(sync.dma_start(out=masks[:], in_=io["diag_masks"]))



    @block.vector
    def _(v):
        v.memset(onesr[:], 1.0)
        v.memset(sclr[:], QKS)
        v.memset(onesrb[:], 1.0)
        vc.inc(v.memset(ones64[:], 1.0))
    ctx.mark("at_memsets", vc)

    @block.gpsimd
    def _(g):
        gl.dma(g.dma_start(out=cos_sb[:], in_=io["rope_cos"]))
        gl.dma(g.dma_start(out=sin_sb[:], in_=io["rope_sin"]))
        gl.dma(g.dma_start(out=pbrow[:], in_=io["proj_b"]))

    tb = ctx.sem("at_tb")

    @block.gpsimd
    def _(g):
        CHUNK = 2048
        for r0 in range(0, NEXP, CHUNK):
            tb.dma(g.dma_start(out=io["tblbf"][r0:r0 + CHUNK, 0:D],
                               in_=io["exp_down"][r0:r0 + CHUNK, :]))
            tb.dma(g.dma_start(out=io["tblbf"][r0:r0 + CHUNK, D:2 * D],
                               in_=io["exp_up"][r0:r0 + CHUNK, :]))

    @block.tensor
    def _(te):
        te.wait_ge(ld.h, ld.n)
        te.wait_ge(gl.h, gl.n)
        ctx.wait_mark(te, "at_memsets")
        te.matmul(ctx.psb[6][:], onesr[:], pbrow[:, 0:512], start=True, stop=True)
        tc.inc(te.matmul(ctx.psb[7][:], onesr[:], pbrow[:, 512:1024],
                         start=True, stop=True))
    ctx.mark("at_biasmm", tc)

    @block.scalar
    def _(s):
        ctx.wait_mark(s, "at_biasmm")
        s.activation(out=projb_b[:, 0:512], in_=ctx.psb[6][:], func=AF.Copy)
        sc.inc(s.activation(out=projb_b[:, 512:1024], in_=ctx.psb[7][:], func=AF.Copy))
    ctx.mark("at_bias", sc)

    ctx.mark("at_coresc", sc)
    ctx.mark("at_corevc", vc)
    ctx.mark("at_coretc", tc)

    for name in ("at_epivc", "at_avgrp", "at_pjevac", "at_expg0", "at_expg1",
                 "at_avg0", "at_avg1", "at_maskg0", "at_maskg1"):
        ctx.marks[name] = (vc.h, 0)

    for g in range(NGRP):
        qof, kof, vof = 64 * GH * g, D + 64 * GH * g, 2 * D + 64 * GH * g

        @block.gpsimd
        def _(gq, g=g):
            gq.wait_ge(gl.h, gl.n)
            if g > 0:
                ctx.wait_mark(gq, "at_qkvmm")
            base = g * NCK * 128
            for cc in range(NCK):
                gl.dma(gq.dma_start(
                    out=wpool[:, 768 * cc:768 * (cc + 1)],
                    in_=io["qkv_wg"][base + 128 * cc:base + 128 * (cc + 1), :]))
        ctx.mark("at_gw", gl)

        @block.vector
        def _(v, g=g):
            ctx.wait_mark(v, "at_coretc")
            v.memset(qn2m[:], 0.0)
            vc.inc(v.memset(
                vx.ap().rearrange("p (s c) -> p s c", c=65)[:, :, 64:65], 1.0))
        ctx.mark("at_vxones", vc)

        @block.sync
        def _(sync, g=g):
            if g > 0:
                ctx.wait_mark(sync, "at_gbias")
            pass

        @block.gpsimd
        def _(gq2, g=g):
            gq2.wait_ge(lb.h, lb.n)
            for r, off in ((0, qof), (1, kof), (2, vof)):
                lb.dma(gq2.dma_start(out=brow_g[:, 256 * r:256 * (r + 1)],
                                     in_=io["qkv_b"][0:1, off:off + 256]))
        ctx.mark("at_gbrow", lb)

        @block.tensor
        def _(te, g=g):
            te.wait_ge(ld.h, ld.n)
            te.wait_ge(lb.h, lb.n)
            ctx.wait_mark(te, "at_memsets")
            ctx.wait_mark(te, "at_coresc")
            ctx.wait_mark(te, "at_corevc")
            te.matmul(ctx.psb[0][:, 0:256], sclr[:], brow_g[:, 0:256],
                      start=True, stop=True)
            te.matmul(ctx.psb[1][:, 0:256], sclr[:], brow_g[:, 256:512],
                      start=True, stop=True)
            tc.inc(te.matmul(ctx.psb[2][:, 0:256], onesr[:], brow_g[:, 512:768],
                             start=True, stop=True))
        ctx.mark("at_gbmm", tc)

        @block.scalar
        def _(s, g=g):
            ctx.wait_mark(s, "at_gbmm")
            s.activation(out=bias_bg[:, 0:256], in_=ctx.psb[0][:, 0:256], func=AF.Copy)
            s.activation(out=bias_bg[:, 256:512], in_=ctx.psb[1][:, 0:256], func=AF.Copy)
            sc.inc(s.activation(out=bias_bg[:, 512:768], in_=ctx.psb[2][:, 0:256],
                                func=AF.Copy))
        ctx.mark("at_gbias", sc)

        for i in range(NT):
            @block.tensor
            def _(te, g=g, i=i):
                if i == 0:
                    te.wait_ge(gl.h, gl.n)
                    ctx.wait_mark(te, "at_gbias")
                else:
                    ctx.wait_mark(te, "at_qkvevac")
                mm = None
                for cc in range(NCK):
                    xn_ci = xnfm[:, T * cc + 128 * i: T * cc + 128 * (i + 1)]
                    st0, sp = (cc == 0), (cc == NCK - 1)
                    te.matmul(ctx.psb[0][:, 0:256], xn_ci,
                              wpool[:, 768 * cc + 0:768 * cc + 256],
                              start=st0, stop=sp)
                    te.matmul(ctx.psb[1][:, 0:256], xn_ci,
                              wpool[:, 768 * cc + 256:768 * cc + 512],
                              start=st0, stop=sp)
                    mm = te.matmul(ctx.psb[2][:, 0:256], xn_ci,
                                   wpool[:, 768 * cc + 512:768 * cc + 768],
                                   start=st0, stop=sp)
                tc.inc(mm)
            ctx.mark("at_qkvmm", tc)

            @block.vector
            def _(v, g=g, i=i):
                ctx.wait_mark(v, "at_qkvmm")
                ctx.wait_mark(v, "at_vxones")
                if i >= 1 or g > 0:
                    ctx.wait_mark(v, "at_trtile")
                v.scalar_tensor_tensor(
                    out=qkvf[:, 0:256], in0=ctx.psb[0][:, 0:256], scalar=QKS,
                    in1=bias_bg[:, 0:256], op0=ALU.mult, op1=ALU.add)
                v.scalar_tensor_tensor(
                    out=qkvf[:, 256:512], in0=ctx.psb[1][:, 0:256], scalar=QKS,
                    in1=bias_bg[:, 256:512], op0=ALU.mult, op1=ALU.add)
                vslots = vx.ap().rearrange("p (s c) -> p s c", c=65)[
                    :, GH * i:GH * (i + 1), 0:64]
                vc.inc(v.scalar_tensor_tensor(
                    out=vslots,
                    in0=ctx.psb[2][:, 0:256].rearrange("p (s c) -> p s c", c=64),
                    scalar=1.0,
                    in1=bias_bg.ap()[:, 512:768].rearrange("p (s c) -> p s c", c=64),
                    op0=ALU.mult, op1=ALU.add))
            ctx.mark("at_qkvevac", vc)

            @block.vector
            def _(v, g=g, i=i):
                for base in (0, 256):
                    src = qkvf.ap()[:, base:base + 256].rearrange(
                        "p (h d) -> p h d", h=GH)
                    dst = qk_bf.ap()[:, base:base + 256].rearrange(
                        "p (h d) -> p h d", h=GH)
                    r = [rsc.ap()[:, 128 * j:128 * (j + 1)].rearrange(
                        "p (h d) -> p h d", h=GH) for j in range(4)]
                    cos_t = cos_sb.ap()[:, 32 * i:32 * (i + 1)].unsqueeze(
                        1).to_broadcast([128, GH, 32])
                    sin_t = sin_sb.ap()[:, 32 * i:32 * (i + 1)].unsqueeze(
                        1).to_broadcast([128, GH, 32])
                    tops, bots = src[:, :, 0:32], src[:, :, 32:64]
                    v.wait_ge(vc.h, vc.n)
                    v.tensor_tensor(out=r[0], in0=tops, in1=cos_t, op=ALU.mult)
                    v.tensor_tensor(out=r[1], in0=bots, in1=sin_t, op=ALU.mult)
                    v.tensor_tensor(out=r[2], in0=tops, in1=sin_t, op=ALU.mult)
                    vc.inc(v.tensor_tensor(out=r[3], in0=bots, in1=cos_t, op=ALU.mult))
                    v.wait_ge(vc.h, vc.n)
                    v.tensor_tensor(out=dst[:, :, 0:32], in0=r[0], in1=r[1],
                                    op=ALU.subtract)
                    vc.inc(v.tensor_tensor(out=dst[:, :, 32:64], in0=r[2], in1=r[3],
                                           op=ALU.add))
                v.wait_ge(vc.h, vc.n)
                last = None
                for hh in range(2 * GH):
                    last = v.scalar_tensor_tensor(
                        out=rsc[:, 64 * hh:64 * (hh + 1)],
                        in0=qkvf[:, 64 * hh:64 * (hh + 1)], scalar=1.0,
                        in1=qkvf[:, 64 * hh:64 * (hh + 1)],
                        op0=ALU.mult, op1=ALU.mult, accum_out=qn2[:, hh:hh + 1])
                vc.inc(last)
                v.wait_ge(vc.h, vc.n)
                vc.inc(v.tensor_tensor(out=qn2m[:], in0=qn2m[:], in1=qn2[:],
                                       op=ALU.max))
            ctx.mark("at_rope", vc)

            for t in range(GH):
                @block.tensor
                def _(te, g=g, i=i, t=t):
                    if t == 0:
                        ctx.wait_mark(te, "at_rope")
                        if g == 0 and i == 0:
                            ctx.wait_mark(te, "prelude_ld")
                    slot = (i * GH + t) % 4
                    if g > 0 or i * GH + t >= 4:
                        ctx.wait_mark(te, f"at_trev{slot}")
                    tc.inc(te.transpose(ctx.pbf(4 + slot)[:, 0:128],
                                        qk_bf[:, 128 * t:128 * (t + 1)], ident[:]))
                ctx.mark("at_tr", tc)
                ctx.mark("at_trtile", tc)

                @block.vector
                def _(v, g=g, i=i, t=t):
                    ctx.wait_mark(v, "at_tr")
                    pv = ctx.pbf(4 + (i * GH + t) % 4)
                    if t < 2:
                        dst0, dst1 = qTx[2 * t], qTx[2 * t + 1]
                    else:
                        dst0, dst1 = kTx[2 * (t - 2)], kTx[2 * (t - 2) + 1]
                    v.tensor_copy(dst0[:, 128 * i:128 * (i + 1)], pv[0:64, 0:128])
                    vc.inc(v.tensor_copy(dst1[:, 128 * i:128 * (i + 1)],
                                         pv[64:128, 0:128]))
                ctx.mark(f"at_trev{(i * GH + t) % 4}", vc)
        ctx.mark("at_trall", vc)

        @block.vector
        def _(v, g=g):
            ctx.wait_mark(v, "at_rope")
            vc.inc(v.tensor_copy(qn2mb[:], qn2m[:]))
        ctx.mark("at_cn0", vc)

        @block.tensor
        def _(te, g=g):
            ctx.wait_mark(te, "at_cn0")
            ctx.wait_mark(te, f"at_trev0")
            ctx.wait_mark(te, f"at_trev1")
            te.transpose(ctx.pbf(4)[0:GH, 0:128], qn2mb[:, 0:GH], ident[:])
            tc.inc(te.transpose(ctx.pbf(5)[0:GH, 0:128], qn2mb[:, GH:2 * GH], ident[:]))
        ctx.mark("at_cn1", tc)

        @block.vector
        def _(v, g=g):
            ctx.wait_mark(v, "at_cn1")
            v.tensor_reduce(out=qm4[:], in_=ctx.pbf(4)[0:GH, 0:128], axis=AX.X,
                            op=ALU.max)
            vc.inc(v.tensor_reduce(out=km4[:], in_=ctx.pbf(5)[0:GH, 0:128],
                                   axis=AX.X, op=ALU.max))
            v.wait_ge(vc.h, vc.n)
            vc.inc(v.tensor_tensor(out=c4s[:], in0=qm4[:], in1=km4[:], op=ALU.add))
            v.wait_ge(vc.h, vc.n)
            vc.inc(v.tensor_scalar(out=c4b[:], in0=c4s[:], scalar1=-0.5,
                                   scalar2=None, op0=ALU.mult))
        ctx.mark("at_cn2", vc)

        @block.tensor
        def _(te, g=g):
            ctx.wait_mark(te, "at_cn2")
            tc.inc(te.transpose(ctx.pbf(6)[0:1, 0:GH], c4b[:], ident[0:GH, 0:GH]))
        ctx.mark("at_cn3", tc)

        @block.vector
        def _(v, g=g):
            ctx.wait_mark(v, "at_cn3")
            vc.inc(v.tensor_copy(crow[:], ctx.pbf(6)[0:1, 0:GH]))
        ctx.mark("at_cn4", vc)

        @block.tensor
        def _(te, g=g):
            ctx.wait_mark(te, "at_cn4")
            tc.inc(te.matmul(ctx.psb[7][:, 0:GH], onesrb[:], crow[:],
                             start=True, stop=True))
        ctx.mark("at_cn5", tc)

        @block.scalar
        def _(s, g=g):
            ctx.wait_mark(s, "at_cn5")
            ctx.wait_mark(s, "at_bias")
            sc.inc(s.activation(out=cneg[:], in_=ctx.psb[7][:, 0:GH], func=AF.Copy))
        ctx.mark("at_cn", sc)

        core_items = []
        for hh in range(GH):
            for j in range(T // 512):
                nk = 4 * j + 4
                groups = [list(range(k0, min(k0 + 3, nk))) for k0 in range(0, nk, 3)]
                for gi, kg in enumerate(groups):
                    core_items.append(("grp", hh, j, kg, nk))
                core_items.append(("epi", hh, j))

        sc0 = sc.n
        exp_end = [sc0 + idx + 1 for idx in range(len(core_items))]
        rec_mmS = [0] * len(core_items)
        rec_avmarried = None
        rec_av = [0] * len(core_items)
        rec_avlast = [0] * len(core_items)
        rec_mask = [0] * len(core_items)
        rec_oTn = [0] * len(core_items)
        rec_zb = [0] * len(core_items)

        gpar = 0
        pend = None
        first_grp = True
        lastpar_exp = {0: 0, 1: 0}
        lastpar_av = {0: 0, 1: 0}
        last_epi_idx = -1

        last_oTn_cell = [0]

        def do_av(pidx):
            hhp, jp, kgp, nkp, parp, maskp = core_pend[pidx]

            @block.tensor
            def _(te):
                te.wait_ge(sc.h, exp_end[pidx])
                if maskp:
                    te.wait_ge(vc.h, rec_mask[pidx])
                if kgp[0] == 0 and last_oTn_cell[0]:
                    te.wait_ge(vc.h, last_oTn_cell[0])
                last = None
                for idx, k in enumerate(kgp):
                    last = te.matmul(
                        ctx.psb[2][0:65, :],
                        vx.ap()[:, 65 * (GH * k + hhp):65 * (GH * k + hhp) + 65],
                        PT[3 * parp + idx][:],
                        start=(k == 0), stop=(k == nkp - 1))
                tc.inc(last)
            rec_av[pidx] = tc.n

        core_pend = {}
        for idx, item in enumerate(core_items):
            if item[0] == "grp":
                _, hh, j, kg, nk = item
                par = gpar % 2
                gpar += 1
                bankset = (0, 1, 4) if par == 0 else (5, 6, 7)
                qsl = qTx[hh][:, 512 * j:512 * (j + 1)]

                @block.tensor
                def _(te, kg=kg, par=par, bankset=bankset, qsl=qsl, hh=hh,
                      first=first_grp):
                    if first:
                        ctx.wait_mark(te, "at_cn")
                        ctx.wait_mark(te, "at_trall")
                        ctx.wait_mark(te, "at_epivc")
                    if lastpar_exp[par]:
                        te.wait_ge(sc.h, lastpar_exp[par])
                    last = None
                    for i2, k in enumerate(kg):
                        last = te.matmul(ctx.psb[bankset[i2]][:],
                                         kTx[hh][:, 128 * k:128 * (k + 1)], qsl,
                                         start=True, stop=True)
                    tc.inc(last)
                rec_mmS[idx] = tc.n
                first_grp = False
                lastpar_exp[par] = exp_end[idx]

                masked = [(i2, k) for i2, k in enumerate(kg) if k >= 4 * j]
                if masked:
                    @block.vector
                    def _(v, j=j, par=par, masked=masked, idx=idx):
                        v.wait_ge(sc.h, exp_end[idx])
                        last = None
                        for i2, k in masked:
                            m = k - 4 * j
                            last = v.tensor_tensor(
                                out=PT[3 * par + i2][:], in0=PT[3 * par + i2][:],
                                in1=masks[:, 512 * m:512 * (m + 1)], op=ALU.mult)
                        vc.inc(last)
                    rec_mask[idx] = vc.n

                core_pend[idx] = (hh, j, kg, nk, par, bool(masked))
                if pend is not None:
                    do_av(pend)
                pend = idx
            else:
                _, hh, j = item
                if pend is not None:
                    do_av(pend)
                    rec_avlast[idx] = tc.n
                    pend = None

                @block.tensor
                def _(te, idx=idx):
                    te.wait_ge(sc.h, exp_end[idx])
                    tc.inc(te.matmul(ctx.psb[3][0:64, :], ones64[:], rz[:],
                                     start=True, stop=True))
                rec_zb[idx] = tc.n

                @block.vector
                def _(v, j=j, idx=idx):
                    v.wait_ge(tc.h, rec_zb[idx])
                    vc.inc(v.reciprocal(zb[:], ctx.psb[3][0:64, :]))
                    v.wait_ge(vc.h, vc.n)
                    v.wait_ge(sto[j % 2].h, sto[j % 2].n)
                    vc.inc(v.tensor_tensor(
                        out=oTst[j % 2][:], in0=ctx.psb[2][0:64, :], in1=zb[:],
                        op=ALU.mult))
                rec_oTn[idx] = vc.n
                last_oTn_cell[0] = vc.n
                ctx.mark("at_epivc", vc)

                @block.sync
                def _(sync, g=g, hh=hh, j=j, idx=idx):
                    sync.wait_ge(vc.h, rec_oTn[idx])
                    sync.wait_ge(sto[j % 2].h, sto[j % 2].n)
                    gh = g * GH + hh
                    sto[j % 2].dma(sync.dma_start(
                        out=io["oTh"][64 * gh:64 * (gh + 1), 512 * j:512 * (j + 1)],
                        in_=oTst[j % 2][:]))
                last_epi_idx = idx

        @block.scalar
        def _(s, items=list(enumerate(core_items)), rec_mmS=rec_mmS,
              rec_av=rec_av, rec_avlast=rec_avlast, rec_oTn=rec_oTn):
            lp_av = {0: 0, 1: 0}
            prev_oTn = 0
            for idx, item in items:
                if item[0] == "grp":
                    hh2, j2, kg2, nk2 = item[1], item[2], item[3], item[4]
                    par2 = core_pend[idx][4]
                    s.wait_ge(tc.h, rec_mmS[idx])
                    if lp_av[par2]:
                        s.wait_ge(tc.h, rec_av[lp_av[par2]])
                    lp_av[par2] = idx
                    bankset2 = (0, 1, 4) if par2 == 0 else (5, 6, 7)
                    last = None
                    for i2, k in enumerate(kg2):
                        last = s.activation(out=PT[3 * par2 + i2][:],
                                            in_=ctx.psb[bankset2[i2]][:],
                                            func=AF.Exp, bias=cneg[:, hh2:hh2 + 1])
                    sc.inc(last)
                else:
                    s.wait_ge(tc.h, rec_avlast[idx])
                    if prev_oTn:
                        s.wait_ge(vc.h, prev_oTn)
                    sc.inc(s.activation(out=rz[:], in_=ctx.psb[2][64:65, :],
                                        func=AF.Copy))
                    prev_oTn = rec_oTn[idx]
            assert sc.n == sc0 + len(items)

        ctx.mark("at_coresc", sc)
        ctx.mark("at_corevc", vc)
        ctx.mark("at_coretc", tc)

    @block.gpsimd
    def _(gq):
        ctx.wait_mark(gq, "at_qkvmm")
        for cc in range(NCK):
            gl.dma(gq.dma_start(out=wpool[:, D * cc:D * (cc + 1)],
                                in_=io["proj_wT"][128 * cc:128 * (cc + 1), :]))
    ctx.mark("at_pw", gl)

    for tg in range(T // 512):
        for cc in range(NCK):
            @block.sync
            def _(sync, tg=tg, cc=cc):
                if tg == 0 and cc == 0:
                    sync.wait_ge(sto[0].h, sto[0].n)
                    sync.wait_ge(sto[1].h, sto[1].n)
                idx = tg * NCK + cc
                if idx >= 2:
                    ctx.wait_mark(sync, f"at_pjld{cc % 2}")
                lo[cc % 2].dma(sync.dma_start(out=oTld[cc % 2][:],
                                      in_=io["oTh"][128 * cc:128 * (cc + 1),
                                                    512 * tg:512 * (tg + 1)]))

            @block.tensor
            def _(te, tg=tg, cc=cc):
                te.wait_ge(lo[cc % 2].h, lo[cc % 2].n)
                if tg == 0 and cc == 0:
                    ctx.wait_mark(te, "at_pw")
                if cc == 0:
                    ctx.wait_mark(te, "at_pjevac")
                last = None
                for i4 in range(4):
                    for nb in range(2):
                        last = te.matmul(
                            ctx.psb[i4 * 2 + nb][:],
                            oTld[cc % 2][:, 128 * i4:128 * (i4 + 1)],
                            wpool[:, D * cc + 512 * nb:D * cc + 512 * (nb + 1)],
                            start=(cc == 0), stop=(cc == NCK - 1))
                tc.inc(last)
            ctx.mark(f"at_pjld{cc % 2}", tc)
        ctx.mark("at_pjmm", tc)

        for i4 in range(4):
            @block.vector
            def _(v, tg=tg, i4=i4):
                if i4 == 0:
                    ctx.wait_mark(v, "at_pjmm")
                v.wait_ge(sts[i4 % 2].h, sts[i4 % 2].n)
                last = None
                for nb in range(2):
                    last = v.tensor_tensor(
                        out=s_tm[i4 % 2][:, 512 * nb:512 * (nb + 1)],
                        in0=ctx.psb[i4 * 2 + nb][:],
                        in1=projb_b[:, 512 * nb:512 * (nb + 1)], op=ALU.add)
                vc.inc(last)
            if i4 == 3:
                ctx.mark("at_pjevac", vc)
            ctx.mark("at_pjev1", vc)

            @block.sync
            def _(sync, tg=tg, i4=i4):
                ctx.wait_mark(sync, "at_pjev1")
                sts[i4 % 2].dma(sync.dma_start(
                    out=io["s"][512 * tg + 128 * i4:512 * tg + 128 * (i4 + 1), :],
                    in_=s_tm[i4 % 2][:]))

    @block.sync
    def _(sync):
        sync.wait_ge(st.h, st.n)
        for s2 in (sto[0], sto[1], sts[0], sts[1], lo[0], lo[1], lb, gl, ld):
            if s2.n:
                sync.wait_ge(s2.h, s2.n)
    ctx.mark("attn0_done", sts[1])




def emit_peer(ctx):
    nc, block = ctx.nc, ctx.block
    io = ctx.io
    pl = ctx.sem("pe_ld")
    pg = ctx.sem("pe_gl")
    lx = [ctx.sem("pe_lx0"), ctx.sem("pe_lx1")]
    gd = [ctx.sem(f"pe_gd{i}") for i in range(4)]
    gu = [ctx.sem(f"pe_gu{i}") for i in range(4)]
    vc = ctx.sem("pe_vc")
    sc = ctx.sem("pe_sc")
    tc = ctx.sem("pe_tc")
    stp = [ctx.sem("pe_st0"), ctx.sem("pe_st1")]
    ident = ctx.const["ident"]

    xnfm = _sb(ctx, "pe_xnfm", [128, NCK * T], BF16)
    wq = _sb(ctx, "pe_wq", [128, NCK * 2 * D], BF16)
    keys = _sb(ctx, "pe_keys", [128, 2 * PH * NKEYS], BF16)
    nrow = _sb(ctx, "pe_nrow", [1, D], F32)
    normw_b = _sb(ctx, "pe_normwb", [128, D], F32)
    normw_c = _sb(ctx, "pe_normwc", [128, NCK], F32)
    onesr = _sb(ctx, "pe_onesr", [1, 128], F32)
    qp = _sb(ctx, "pe_qp", [128, 16 * 512], BF16)
    sims = _sb(ctx, "pe_sims", [128, 16 * NKEYS], F32)
    xntm2 = [_sb(ctx, f"pe_xntm{i}", [128, D], BF16) for i in range(2)]
    xt2 = [_sb(ctx, f"pe_xt{i}", [128, D], BF16) for i in range(2)]
    xld2 = [_sb(ctx, f"pe_xld{i}", [128, D], F32) for i in range(2)]
    sld2 = [_sb(ctx, f"pe_sld{i}", [128, D], BF16) for i in range(2)]
    x1t2 = [_sb(ctx, f"pe_x1t{i}", [128, D], F32) for i in range(2)]
    s12 = _sb(ctx, "pe_s12", [128, 2 * PH * PK], F32)
    i12 = _sb(ctx, "pe_i12", [128, 2 * PH * PK], U32)
    ascr = _sb(ctx, "pe_ascr", [128, PH * PK * PK], F32)
    fs = _sb(ctx, "pe_fs", [128, PH * PK], F32)
    fsr = _sb(ctx, "pe_fsr", [128, PH * PK], F32)
    pk_t = _sb(ctx, "pe_pk", [128, PH * PK], U32)
    ab = _sb(ctx, "pe_ab", [128, 2 * PH * PK], U32)
    eq = _sb(ctx, "pe_eq", [128, PH * PK * PK], U32)
    isel = _sb(ctx, "pe_isel", [128, 2 * PH * PK], U32)
    fi = _sb(ctx, "pe_fi", [128, PH * PK], U32)
    iota8 = _sb(ctx, "pe_iota8", [128, PK], I32)
    hid = _sb(ctx, "pe_hid", [128, PH * PK], F32)
    hgel = _sb(ctx, "pe_hgel", [128, PH * PK], F32)
    coef = _sb(ctx, "pe_coef", [128, PH * PK], F32)
    Gdn = [_sb(ctx, f"pe_gdn{i}", [128, D], BF16) for i in range(4)]
    Gup = [_sb(ctx, f"pe_gup{i}", [128, D], BF16) for i in range(4)]
    diag = [_sb(ctx, f"pe_diag{i}", [128, 128], BF16) for i in range(2)]
    gsil = _sb(ctx, "pe_gsil", [128, D], F32)
    sqs = _sb(ctx, "pe_sqs", [128, D], BF16)
    stat = _sb(ctx, "pe_stat", [128, 4], F32)

    NSLOT = PH * PK

    @block.sync
    def _(sync):
        ctx.wait_mark(sync, "ln0_done")
        ctx.wait_mark(sync, "attn0_done")
        for cc in range(NCK):
            pl.dma(sync.dma_start(out=xnfm[:, T * cc:T * (cc + 1)],
                                  in_=io["xn_fm"][128 * cc:128 * (cc + 1), :]))
        pl.dma(sync.dma_start(out=nrow[:], in_=io["norm_w"]))
        pl.dma(sync.dma_start(
            out=normw_c[:],
            in_=io["norm_w"].rearrange("o (c p) -> p (o c)", p=128),
            allow_slow_non_contiguous=True))

    @block.gpsimd
    def _(g):
        for cc in range(NCK):
            pg.dma(g.dma_start(out=wq[:, 2 * D * cc:2 * D * (cc + 1)],
                               in_=io["wq_peerT"][128 * cc:128 * (cc + 1), :]))
        for h in range(PH):
            for s_ in range(2):
                pg.dma(g.dma_start(
                    out=keys[:, 256 * (2 * h + s_):256 * (2 * h + s_ + 1)],
                    in_=io["keysT"][h, s_]))
        pio = ctx.sem("pe_io")
        pio.inc(g.iota(iota8[:], pattern=[[1, PK]], base=0, channel_multiplier=0))
        ctx.mark("pe_iota", pio)

    @block.vector
    def _(v):
        vc.inc(v.memset(onesr[:], 1.0))
    ctx.mark("pe_ones", vc)

    @block.tensor
    def _(te):
        te.wait_ge(pl.h, pl.n)
        ctx.wait_mark(te, "pe_ones")
        te.matmul(ctx.psb[0][:], onesr[:], nrow[:, 0:512], start=True, stop=True)
        tc.inc(te.matmul(ctx.psb[1][:], onesr[:], nrow[:, 512:1024],
                         start=True, stop=True))
    ctx.mark("pe_nwmm", tc)

    @block.scalar
    def _(s):
        ctx.wait_mark(s, "pe_nwmm")
        s.activation(out=normw_b[:, 0:512], in_=ctx.psb[0][:], func=AF.Copy)
        sc.inc(s.activation(out=normw_b[:, 512:1024], in_=ctx.psb[1][:], func=AF.Copy))
    ctx.mark("pe_nwb", sc)

    @block.vector
    def _(v):
        v.wait_ge(pg.h, pg.n)
        v.wait_ge(pl.h, pl.n)
        last = None
        for cc in range(NCK):
            last = v.tensor_scalar(out=wq[:, 2 * D * cc:2 * D * (cc + 1)],
                                   in0=wq[:, 2 * D * cc:2 * D * (cc + 1)],
                                   scalar1=normw_c[:, cc:cc + 1], scalar2=None,
                                   op0=ALU.mult)
        vc.inc(last)
    ctx.mark("pe_wqf", vc)

    ctx.marks["pe_simsdone"] = (tc.h, 0)
    ctx.marks["pe_silu"] = (sc.h, 0)
    ctx.marks["pe_fin"] = (vc.h, 0)
    ctx.marks["pe_topk1"] = (vc.h, 0)

    for sg in range(T // 512):
        for qfc in range(16):
            @block.tensor
            def _(te, sg=sg, qfc=qfc):
                if qfc == 0:
                    ctx.wait_mark(te, "pe_wqf")
                    ctx.wait_mark(te, "pe_nwb")
                    ctx.wait_mark(te, "pe_simsdone")
                    ctx.wait_mark(te, "pe_silu")
                if qfc >= 2:
                    ctx.wait_mark(te, f"pe_qpev{qfc % 2}")
                last = None
                for cc in range(NCK):
                    last = te.matmul(
                        ctx.psb[qfc % 2][:],
                        wq[:, 2 * D * cc + 128 * qfc:2 * D * cc + 128 * (qfc + 1)],
                        xnfm[:, T * cc + 512 * sg:T * cc + 512 * (sg + 1)],
                        start=(cc == 0), stop=(cc == NCK - 1))
                tc.inc(last)
            ctx.mark(f"pe_qpmm{qfc % 2}", tc)

            @block.vector
            def _(v, sg=sg, qfc=qfc):
                ctx.wait_mark(v, f"pe_qpmm{qfc % 2}")
                vc.inc(v.tensor_copy(qp[:, 512 * qfc:512 * (qfc + 1)],
                                     ctx.psb[qfc % 2][:]))
            ctx.mark(f"pe_qpev{qfc % 2}", vc)
        ctx.mark("pe_qpall", vc)

        for it in range(4):
            i = sg * 4 + it
            par = i % 2

            @block.sync
            def _(sync, i=i, par=par):
                if i >= 2:
                    ctx.wait_mark(sync, f"pe_finv{par}")
                lx[par].dma(sync.dma_start(out=xntm2[par][:],
                                           in_=io["xn_tm"][128 * i:128 * (i + 1), :]))
                lx[par].dma(sync.dma_start(out=xld2[par][:],
                                           in_=io["x"][128 * i:128 * (i + 1), :]))
                lx[par].dma(sync.dma_start(out=sld2[par][:],
                                           in_=io["s"][128 * i:128 * (i + 1), :]))

            @block.scalar
            def _(s, i=i, par=par):
                s.wait_ge(lx[par].h, lx[par].n)
                if i >= 1:
                    ctx.wait_mark(s, "pe_xt")
                sc.inc(s.activation(out=sqs[:], in_=xntm2[par][:], func=AF.Square,
                                    accum_out=stat[:, 0:1]))
            ctx.mark("pe_ssq", sc)

            @block.vector
            def _(v, i=i):
                ctx.wait_mark(v, "pe_ssq")
                vc.inc(v.tensor_scalar(out=stat[:, 1:2], in0=stat[:, 0:1],
                                       scalar1=1.0 / D, scalar2=1e-6,
                                       op0=ALU.mult, op1=ALU.add))
            ctx.mark("pe_var", vc)

            @block.scalar
            def _(s, i=i):
                ctx.wait_mark(s, "pe_var")
                sc.inc(s.activation(out=stat[:, 2:3], in_=stat[:, 1:2], func=AF.Ln))
                s.wait_ge(sc.h, sc.n)
                sc.inc(s.activation(out=stat[:, 3:4], in_=stat[:, 2:3], func=AF.Exp,
                                    scale=-0.5))
            ctx.mark("pe_rstd", sc)

            @block.vector
            def _(v, i=i, par=par):
                ctx.wait_mark(v, "pe_rstd")
                ctx.wait_mark(v, "pe_nwb")
                if i >= 2:
                    ctx.wait_mark(v, f"pe_dots{par}")
                vc.inc(v.scalar_tensor_tensor(out=xt2[par][:], in0=xntm2[par][:],
                                              scalar=stat[:, 3:4], in1=normw_b[:],
                                              op0=ALU.mult, op1=ALU.mult))
            ctx.mark("pe_xt", vc)

            for hs in range(16):
                @block.tensor
                def _(te, i=i, it=it, hs=hs):
                    if hs == 0:
                        ctx.wait_mark(te, "pe_qpall")
                        ctx.wait_mark(te, "pe_topk1")
                    if hs >= 2:
                        ctx.wait_mark(te, f"pe_simev{hs % 2}")
                    tc.inc(te.matmul(
                        ctx.psb[2 + hs % 2][:, 0:256],
                        qp[:, 512 * hs + 128 * it:512 * hs + 128 * (it + 1)],
                        keys[:, 256 * hs:256 * (hs + 1)],
                        start=True, stop=True))
                ctx.mark(f"pe_simmm{hs % 2}", tc)
                if hs == 15:
                    ctx.mark("pe_simsdone", tc)

                @block.vector
                def _(v, i=i, hs=hs):
                    ctx.wait_mark(v, f"pe_simmm{hs % 2}")
                    ctx.wait_mark(v, "pe_rstd")
                    vc.inc(v.tensor_scalar(out=sims[:, 256 * hs:256 * (hs + 1)],
                                           in0=ctx.psb[2 + hs % 2][:, 0:256],
                                           scalar1=stat[:, 3:4], scalar2=None,
                                           op0=ALU.mult))
                ctx.mark(f"pe_simev{hs % 2}", vc)
            ctx.mark("pe_simsall", vc)

            @block.vector
            def _(v, i=i):
                ctx.wait_mark(v, "pe_simsall")
                last = None
                for hs in range(16):
                    last = v.max(out=s12[:, PK * hs:PK * (hs + 1)],
                                 in_=sims[:, 256 * hs:256 * (hs + 1)])
                vc.inc(last)
                v.wait_ge(vc.h, vc.n)
                for hs in range(16):
                    last = v.max_index(out=i12[:, PK * hs:PK * (hs + 1)],
                                       in_max=s12[:, PK * hs:PK * (hs + 1)],
                                       in_values=sims[:, 256 * hs:256 * (hs + 1)])
                vc.inc(last)
            ctx.mark("pe_topk1", vc)

            @block.vector
            def _(v, i=i):
                v.wait_ge(vc.h, vc.n)
                ctx.wait_mark(v, "pe_iota")
                s12r = s12.ap().rearrange("p (h s e) -> p h s e", h=PH, s=2)
                s1b = s12r[:, :, 0:1, :].transpose([0, 1, 3, 2]).to_broadcast(
                    [128, PH, PK, PK])
                s2b = s12r[:, :, 1:2, :].to_broadcast([128, PH, PK, PK])
                vc.inc(v.tensor_tensor(
                    out=ascr.ap().rearrange("p (h a b) -> p h a b", h=PH, a=PK),
                    in0=s1b, in1=s2b, op=ALU.add))
                v.wait_ge(vc.h, vc.n)
                last = None
                for h in range(PH):
                    last = v.max(out=fs[:, PK * h:PK * (h + 1)],
                                 in_=ascr[:, 64 * h:64 * (h + 1)])
                vc.inc(last)
                v.wait_ge(vc.h, vc.n)
                for h in range(PH):
                    last = v.max_index(out=pk_t[:, PK * h:PK * (h + 1)],
                                       in_max=fs[:, PK * h:PK * (h + 1)],
                                       in_values=ascr[:, 64 * h:64 * (h + 1)])
                vc.inc(last)
                v.wait_ge(vc.h, vc.n)
                v.tensor_scalar(out=ab[:, 0:NSLOT], in0=pk_t[:], scalar1=3,
                                scalar2=None, op0=ALU.logical_shift_right)
                vc.inc(v.tensor_scalar(out=ab[:, NSLOT:2 * NSLOT], in0=pk_t[:],
                                       scalar1=7, scalar2=None, op0=ALU.bitwise_and))
                v.wait_ge(vc.h, vc.n)
                i12r = i12.ap().rearrange("p (h s e) -> p h s e", h=PH, s=2)
                iob = iota8.ap().bitcast(U32).unsqueeze(1).unsqueeze(1).to_broadcast(
                    [128, PH, PK, PK])
                for half in range(2):
                    abr = ab.ap()[:, NSLOT * half:NSLOT * (half + 1)].rearrange(
                        "p (h s) -> p h s", h=PH).unsqueeze(3).to_broadcast(
                        [128, PH, PK, PK])
                    vc.inc(v.tensor_tensor(
                        out=eq.ap().rearrange("p (h a b) -> p h a b", h=PH, a=PK),
                        in0=abr, in1=iob, op=ALU.is_equal))
                    v.wait_ge(vc.h, vc.n)
                    ivals = i12r[:, :, half:half + 1, :].to_broadcast(
                        [128, PH, PK, PK])
                    vc.inc(v.tensor_tensor(
                        out=eq.ap().rearrange("p (h a b) -> p h a b", h=PH, a=PK),
                        in0=eq.ap().rearrange("p (h a b) -> p h a b", h=PH, a=PK),
                        in1=ivals, op=ALU.mult))
                    v.wait_ge(vc.h, vc.n)
                    with nc.allow_low_precision(reason="u32 index reduce"):
                        vc.inc(v.tensor_reduce(
                            out=isel.ap()[:, NSLOT * half:NSLOT * (half + 1)].rearrange(
                                "p (h s) -> p h s", h=PH),
                            in_=eq.ap().rearrange("p (h a b) -> p h a b", h=PH, a=PK),
                            axis=AX.X, op=ALU.add))
                    v.wait_ge(vc.h, vc.n)
                vc.inc(v.scalar_tensor_tensor(
                    out=fi[:], in0=isel[:, 0:NSLOT], scalar=NKEYS,
                    in1=isel[:, NSLOT:2 * NSLOT],
                    op0=ALU.mult, op1=ALU.add))
                vc.inc(v.tensor_scalar(out=fsr[:], in0=fs[:], scalar1=0.0,
                                       scalar2=None, op0=ALU.max))
            ctx.mark("pe_fi", vc)

            vdots0 = vc.n
            @block.gpsimd
            def _(g, i=i, vdots0=vdots0):
                ctx.wait_mark(g, "pe_fi")
                for j in range(NSLOT):
                    if j >= 4:
                        g.wait_ge(vc.h, vdots0 + (j - 4) + 1)
                    gd[j % 4].dma(g.indirect_dma_start(
                        out=Gdn[j % 4][:], out_offset=None, in_=io["tblbf"],
                        in_offset=IndirectOffsetOnAxis(
                            ap=fi.ap()[:, j:j + 1].bitcast(I32), axis=0)))

            @block.vector
            def _(v, i=i, par=par, vdots0=vdots0):
                ctx.wait_mark(v, "pe_xt")
                for j in range(NSLOT):
                    v.wait_ge(gd[j % 4].h, 16 * (NSLOT // 4) * i + 16 * (j // 4 + 1)
                              if False else gd[j % 4].n - 16 * (NSLOT // 4) + 16 * (j // 4 + 1))
                    vc.inc(v.scalar_tensor_tensor(
                        out=Gdn[j % 4][:], in0=Gdn[j % 4][:], scalar=1.0,
                        in1=xt2[par][:], op0=ALU.mult, op1=ALU.mult,
                        accum_out=hid[:, j:j + 1]))
            assert vc.n == vdots0 + NSLOT
            ctx.mark(f"pe_dots{par}", vc)
            ctx.mark("pe_dots", vc)

            @block.scalar
            def _(s, i=i):
                ctx.wait_mark(s, "pe_dots")
                sc.inc(s.activation(out=hgel[:], in_=hid[:], func=AF.Gelu))
            ctx.mark("pe_gelu", sc)

            @block.vector
            def _(v, i=i):
                ctx.wait_mark(v, "pe_gelu")
                vc.inc(v.tensor_tensor(out=coef[:], in0=hgel[:], in1=fsr[:],
                                       op=ALU.mult))
            ctx.mark("pe_coef", vc)

            tcu0 = tc.n
            vcu0 = vc.n
            @block.gpsimd
            def _(g, i=i, tcu0=tcu0):
                ctx.wait_mark(g, "pe_fi")
                g.wait_ge(tc.h, tcu0)
                for j in range(NSLOT):
                    if j >= 4:
                        g.wait_ge(tc.h, tcu0 + (j - 4) + 1)
                    gu[j % 4].dma(g.indirect_dma_start(
                        out=Gup[j % 4][:], out_offset=None, in_=io["tblbf"],
                        in_offset=IndirectOffsetOnAxis(
                            ap=fi.ap()[:, j:j + 1].bitcast(I32), axis=0),
                        element_offset=D))
            gu_base = [gu[r].n - 16 * (NSLOT // 4) for r in range(4)]

            @block.vector
            def _(v, i=i, tcu0=tcu0):
                ctx.wait_mark(v, "pe_coef")
                ctx.wait_mark(v, "prelude_ld")
                for j in range(NSLOT):
                    if j >= 2:
                        v.wait_ge(tc.h, tcu0 + (j - 2) + 1)
                    vc.inc(v.tensor_scalar(out=diag[j % 2][:], in0=ident[:],
                                           scalar1=coef[:, j:j + 1], scalar2=None,
                                           op0=ALU.mult))
            assert vc.n == vcu0 + NSLOT

            @block.tensor
            def _(te, i=i, tcu0=tcu0, vcu0=vcu0, gu_base=gu_base):
                ctx.wait_mark(te, "pe_silu")
                for j in range(NSLOT):
                    te.wait_ge(vc.h, vcu0 + j + 1)
                    te.wait_ge(gu[j % 4].h, gu_base[j % 4] + 16 * (j // 4 + 1))
                    te.matmul(ctx.psb[0][:], diag[j % 2][:], Gup[j % 4][:, 0:512],
                              start=(j == 0), stop=(j == NSLOT - 1))
                    mm = te.matmul(ctx.psb[1][:], diag[j % 2][:],
                                   Gup[j % 4][:, 512:1024],
                                   start=(j == 0), stop=(j == NSLOT - 1))
                    tc.inc(mm)
            assert tc.n == tcu0 + NSLOT
            ctx.mark("pe_upmm", tc)

            @block.scalar
            def _(s, i=i):
                ctx.wait_mark(s, "pe_upmm")
                ctx.wait_mark(s, "pe_fin")
                sc.inc(s.activation(out=gsil[:], in_=ctx.psbt[0].ap(), func=AF.Silu))
            ctx.mark("pe_silu", sc)

            @block.vector
            def _(v, i=i, par=par):
                ctx.wait_mark(v, "pe_silu")
                v.wait_ge(lx[par].h, lx[par].n)
                v.wait_ge(stp[par].h, stp[par].n)
                vc.inc(v.tensor_tensor(out=gsil[:], in0=sld2[par][:], in1=gsil[:],
                                       op=ALU.mult))
                v.wait_ge(vc.h, vc.n)
                vc.inc(v.tensor_tensor(out=x1t2[par][:], in0=xld2[par][:],
                                       in1=gsil[:], op=ALU.add))
            ctx.mark("pe_fin", vc)
            ctx.mark(f"pe_finv{par}", vc)

            @block.sync
            def _(sync, i=i, par=par):
                ctx.wait_mark(sync, f"pe_finv{par}")
                stp[par].dma(sync.dma_start(
                    out=io["x1"][128 * i:128 * (i + 1), :], in_=x1t2[par][:]))

    @block.sync
    def _(sync):
        for s2 in [stp[0], stp[1], lx[0], lx[1], pl, pg] + gd + gu:
            if s2.n:
                sync.wait_ge(s2.h, s2.n)
    ctx.mark("peer_done", stp[1])




def emit_moe(ctx):
    nc, block = ctx.nc, ctx.block
    io = ctx.io
    ml = ctx.sem("mo_ld")
    mg = ctx.sem("mo_gl")
    ma4 = [ctx.sem(f"mo_ma{i}") for i in range(4)]
    lf = [ctx.sem("mo_lf0"), ctx.sem("mo_lf1")]
    vc = ctx.sem("mo_vc")
    sc = ctx.sem("mo_sc")
    tc = ctx.sem("mo_tc")
    mst = [ctx.sem("mo_st0"), ctx.sem("mo_st1")]

    xnfm = _sb(ctx, "mo_xnfm", [128, NCK * T], BF16)
    rtr = _sb(ctx, "mo_rtr", [128, NCK * NE], BF16)
    wg_sb = _sb(ctx, "mo_wg", [128, NCK * FF], BF16)
    wu_sb = _sb(ctx, "mo_wu", [128, NCK * FF], BF16)
    wd_sb = _sb(ctx, "mo_wd", [128, (FF // 128) * D], BF16)
    fuse = _sb(ctx, "mo_fuse", [128, (FF // 128) * 512], BF16)
    sil = [_sb(ctx, f"mo_sil{i}", [128, 512], BF16) for i in range(2)]
    wfull = _sb(ctx, "mo_wfull", [128, NT * NE], F32)
    rsc = _sb(ctx, "mo_rsc", [128, 2 * NE + 8], F32)
    iota8 = _sb(ctx, "mo_iota8", [128, NE], I32)
    pv = _sb(ctx, "mo_pv", [128, 8], F32)
    pidx = _sb(ctx, "mo_pidx", [128, 8], U32)
    eqf = _sb(ctx, "mo_eqf", [128, NE], F32)
    scal = [_sb(ctx, f"mo_scal{i}", [128, D], F32) for i in range(4)]
    acld = scal
    gs = scal[2]
    x1ld = [scal[3], _sb(ctx, "mo_x1ld1", [128, D], F32)]
    s1ld = [_sb(ctx, f"mo_s1ld{i}", [128, D], BF16) for i in range(2)]

    NFC = FF // 128

    @block.sync
    def _(sync):
        ctx.wait_mark(sync, "ln1_done")
        for cc in range(NCK):
            ml.dma(sync.dma_start(out=xnfm[:, T * cc:T * (cc + 1)],
                                  in_=io["xn1_fm"][128 * cc:128 * (cc + 1), :]))

    @block.gpsimd
    def _(g):
        for cc in range(NCK):
            mg.dma(g.dma_start(out=rtr[:, NE * cc:NE * (cc + 1)],
                               in_=io["router"][128 * cc:128 * (cc + 1), :]))
        mio = ctx.sem("mo_io")
        mio.inc(g.iota(iota8[:], pattern=[[1, NE]], base=0, channel_multiplier=0))
        ctx.mark("mo_iota", mio)

    ctx.marks["mo_wfv"] = (vc.h, 0)
    ctx.marks["mo_sm2"] = (sc.h, 0)
    for i in range(NT):
        @block.tensor
        def _(te, i=i):
            te.wait_ge(ml.h, ml.n)
            te.wait_ge(mg.h, 16 * NCK)
            ctx.wait_mark(te, "mo_wfv")
            last = None
            for cc in range(NCK):
                last = te.matmul(ctx.psb[4][:, 0:NE],
                                 xnfm[:, T * cc + 128 * i:T * cc + 128 * (i + 1)],
                                 rtr[:, NE * cc:NE * (cc + 1)],
                                 start=(cc == 0), stop=(cc == NCK - 1))
            tc.inc(last)
        ctx.mark("mo_rmm", tc)

        @block.vector
        def _(v, i=i):
            ctx.wait_mark(v, "mo_rmm")
            ctx.wait_mark(v, "mo_sm2")
            vc.inc(v.tensor_reduce(out=rsc[:, 0:1], in_=ctx.psb[4][:, 0:NE],
                                   axis=AX.X, op=ALU.max))
            v.wait_ge(vc.h, vc.n)
            vc.inc(v.tensor_scalar(out=rsc[:, 8:8 + NE], in0=ctx.psb[4][:, 0:NE],
                                   scalar1=rsc[:, 0:1], scalar2=None,
                                   op0=ALU.subtract))
        ctx.mark("mo_sm1", vc)

        @block.scalar
        def _(s, i=i):
            ctx.wait_mark(s, "mo_sm1")
            sc.inc(s.activation(out=rsc[:, 8:8 + NE], in_=rsc[:, 8:8 + NE],
                                func=AF.Exp, accum_out=rsc[:, 1:2]))
        ctx.mark("mo_sm2", sc)

        @block.vector
        def _(v, i=i):
            ctx.wait_mark(v, "mo_sm2")
            ctx.wait_mark(v, "mo_iota")
            vc.inc(v.reciprocal(rsc[:, 2:3], rsc[:, 1:2]))
            v.wait_ge(vc.h, vc.n)
            vc.inc(v.tensor_scalar(out=rsc[:, 8:8 + NE], in0=rsc[:, 8:8 + NE],
                                   scalar1=rsc[:, 2:3], scalar2=None, op0=ALU.mult))
            v.wait_ge(vc.h, vc.n)
            vc.inc(v.max(out=pv[:], in_=rsc[:, 8:8 + NE]))
            v.wait_ge(vc.h, vc.n)
            vc.inc(v.max_index(out=pidx[:], in_max=pv[:], in_values=rsc[:, 8:8 + NE]))
            v.wait_ge(vc.h, vc.n)
            vc.inc(v.tensor_tensor(out=rsc[:, 3:4], in0=pv[:, 1:2], in1=pv[:, 0:1],
                                   op=ALU.subtract))
        ctx.mark("mo_nw1", vc)

        @block.scalar
        def _(s, i=i):
            ctx.wait_mark(s, "mo_nw1")
            sc.inc(s.activation(out=rsc[:, 4:5], in_=rsc[:, 3:4], func=AF.Exp))
        ctx.mark("mo_nw2", sc)

        @block.vector
        def _(v, i=i):
            ctx.wait_mark(v, "mo_nw2")
            vc.inc(v.tensor_scalar(out=rsc[:, 5:6], in0=rsc[:, 4:5], scalar1=1.0,
                                   scalar2=None, op0=ALU.add))
            v.wait_ge(vc.h, vc.n)
            vc.inc(v.reciprocal(rsc[:, 6:7], rsc[:, 5:6]))
            v.wait_ge(vc.h, vc.n)
            vc.inc(v.tensor_tensor(out=rsc[:, 7:8], in0=rsc[:, 4:5], in1=rsc[:, 6:7],
                                   op=ALU.mult))
            v.wait_ge(vc.h, vc.n)
            vc.inc(v.tensor_tensor(out=eqf[:],
                                   in0=iota8.ap().bitcast(U32),
                                   in1=pidx.ap()[:, 0:1].to_broadcast([128, NE]),
                                   op=ALU.is_equal))
            v.wait_ge(vc.h, vc.n)
            vc.inc(v.tensor_scalar(out=wfull[:, NE * i:NE * (i + 1)], in0=eqf[:],
                                   scalar1=rsc[:, 6:7], scalar2=None, op0=ALU.mult))
            v.wait_ge(vc.h, vc.n)
            vc.inc(v.tensor_tensor(out=eqf[:],
                                   in0=iota8.ap().bitcast(U32),
                                   in1=pidx.ap()[:, 1:2].to_broadcast([128, NE]),
                                   op=ALU.is_equal))
            v.wait_ge(vc.h, vc.n)
            vc.inc(v.scalar_tensor_tensor(out=wfull[:, NE * i:NE * (i + 1)],
                                          in0=eqf[:], scalar=rsc[:, 7:8],
                                          in1=wfull[:, NE * i:NE * (i + 1)],
                                          op0=ALU.mult, op1=ALU.add))
        ctx.mark("mo_wfv", vc)
    ctx.mark("mo_wfall", vc)

    ctx.marks["mo_dnev"] = (vc.h, 0)
    ctx.marks["mo_fuz0"] = (vc.h, 0)
    ctx.marks["mo_fuz1"] = (vc.h, 0)
    ctx.marks["mo_sil0"] = (sc.h, 0)
    ctx.marks["mo_sil1"] = (sc.h, 0)
    for e in range(NE):
        @block.gpsimd
        def _(g, e=e):
            g.wait_ge(mg.h, mg.n)
            if e > 0:
                ctx.wait_mark(g, "mo_gumm")
            for cc in range(NCK):
                mg.dma(g.dma_start(out=wg_sb[:, FF * cc:FF * (cc + 1)],
                                   in_=io["Wg"][e, 128 * cc:128 * (cc + 1), :]))
                mg.dma(g.dma_start(out=wu_sb[:, FF * cc:FF * (cc + 1)],
                                   in_=io["Wu"][e, 128 * cc:128 * (cc + 1), :]))
            if e > 0:
                ctx.wait_mark(g, "mo_dnmm")
            for fc in range(NFC):
                mg.dma(g.dma_start(out=wd_sb[:, D * fc:D * (fc + 1)],
                                   in_=io["Wd"][e, 128 * fc:128 * (fc + 1), :]))
        ctx.mark("mo_wld", mg)

        for tq in range(T // 512):
            sc0m = sc.n
            sil_end = [sc0m + fc + 1 for fc in range(NFC)]
            rec_gmm = [0] * NFC
            rec_fuz = [0] * NFC
            for fc in range(NFC):
                bg, bu = 2 * (fc % 2), 1 + 2 * (fc % 2)
                @block.tensor
                def _(te, e=e, tq=tq, fc=fc, bg=bg, bu=bu):
                    if fc == 0 and tq == 0:
                        te.wait_ge(mg.h, mg.n)
                        ctx.wait_mark(te, "mo_wfall")
                    if fc < 2:
                        ctx.wait_mark(te, "mo_dnev")
                    if fc >= 2:
                        te.wait_ge(sc.h, sil_end[fc - 2])
                        te.wait_ge(vc.h, rec_fuz[fc - 2])
                    lastg = lastu = None
                    for cc in range(NCK):
                        xs = xnfm[:, T * cc + 512 * tq:T * cc + 512 * (tq + 1)]
                        lastg = te.matmul(ctx.psb[bg][:],
                                          wg_sb[:, FF * cc + 128 * fc:FF * cc + 128 * (fc + 1)],
                                          xs, start=(cc == 0), stop=(cc == NCK - 1))
                    tc.inc(lastg)
                    rec_gmm[fc] = tc.n
                    for cc in range(NCK):
                        xs = xnfm[:, T * cc + 512 * tq:T * cc + 512 * (tq + 1)]
                        lastu = te.matmul(ctx.psb[bu][:],
                                          wu_sb[:, FF * cc + 128 * fc:FF * cc + 128 * (fc + 1)],
                                          xs, start=(cc == 0), stop=(cc == NCK - 1))
                    tc.inc(lastu)
                ctx.mark("mo_gumm", tc)
                ctx.mark(f"mo_gumm{fc % 2}", tc)

                @block.vector
                def _(v, e=e, tq=tq, fc=fc, bu=bu):
                    ctx.wait_mark(v, f"mo_gumm{fc % 2}")
                    v.wait_ge(sc.h, sil_end[fc])
                    vc.inc(v.tensor_tensor(out=fuse[:, 512 * fc:512 * (fc + 1)],
                                           in0=ctx.psb[bu][:], in1=sil[fc % 2][:],
                                           op=ALU.mult))
                rec_fuz[fc] = vc.n
                ctx.mark(f"mo_fuz{fc % 2}", vc)
                if fc == NFC - 1:
                    ctx.mark("mo_fuseall", vc)

            @block.scalar
            def _(s, e=e, tq=tq, rec_gmm=rec_gmm, rec_fuz=rec_fuz):
                for fc in range(NFC):
                    s.wait_ge(tc.h, rec_gmm[fc])
                    if fc >= 2:
                        s.wait_ge(vc.h, rec_fuz[fc - 2])
                    sc.inc(s.activation(out=sil[fc % 2][:],
                                        in_=ctx.psb[2 * (fc % 2)][:], func=AF.Silu))
            assert sc.n == sc0m + NFC
            ctx.mark("mo_silall", sc)

            @block.tensor
            def _(te, e=e, tq=tq):
                ctx.wait_mark(te, "mo_fuseall")
                ctx.wait_mark(te, "mo_dnev")
                last = None
                for i4 in range(4):
                    for nb in range(2):
                        for fc in range(NFC):
                            last = te.matmul(
                                ctx.psb[i4 * 2 + nb][:],
                                fuse[:, 512 * fc + 128 * i4:512 * fc + 128 * (i4 + 1)],
                                wd_sb[:, D * fc + 512 * nb:D * fc + 512 * (nb + 1)],
                                start=(fc == 0), stop=(fc == NFC - 1))
                tc.inc(last)
            ctx.mark("mo_dnmm", tc)

            for i4h in range(2):
                @block.vector
                def _(v, e=e, tq=tq, i4h=i4h):
                    ctx.wait_mark(v, "mo_dnmm")
                    for i4 in (2 * i4h, 2 * i4h + 1):
                        v.wait_ge(ma4[i4].h, ma4[i4].n)
                    last = None
                    for i4 in (2 * i4h, 2 * i4h + 1):
                        ti = 4 * tq + i4
                        for nb in range(2):
                            last = v.tensor_scalar(
                                out=scal[i4][:, 512 * nb:512 * (nb + 1)],
                                in0=ctx.psb[i4 * 2 + nb][:],
                                scalar1=wfull[:, NE * ti + e:NE * ti + e + 1],
                                scalar2=None, op0=ALU.mult)
                    vc.inc(last)
                ctx.mark("mo_scev", vc)

                @block.gpsimd
                def _(g, e=e, tq=tq, i4h=i4h):
                    ctx.wait_mark(g, "mo_scev")
                    if e > 0 and tq == 0 and i4h == 0:
                        for s4 in ma4:
                            g.wait_ge(s4.h, s4.n)
                    for i4 in (2 * i4h, 2 * i4h + 1):
                        ti = 4 * tq + i4
                        op = {}
                        if e > 0:
                            op = dict(accum_op=ALU.add)
                        ma4[i4].dma(g.dma_start(
                            out=io["moeacc"][128 * ti:128 * (ti + 1), :],
                            in_=scal[i4][:], **op))
            ctx.mark("mo_dnev", vc)

    for i in range(NT):
        par = i % 2

        @block.sync
        def _(sync, i=i, par=par):
            if i == 0:
                for s4 in ma4:
                    sync.wait_ge(s4.h, s4.n)
            if i >= 2:
                ctx.wait_mark(sync, f"mo_x2v{par}")
                sync.wait_ge(mst[par].h, mst[par].n)
            lf[par].dma(sync.dma_start(out=acld[par][:],
                                       in_=io["moeacc"][128 * i:128 * (i + 1), :]))
            lf[par].dma(sync.dma_start(out=x1ld[par][:],
                                       in_=io["x1"][128 * i:128 * (i + 1), :]))
            lf[par].dma(sync.dma_start(out=s1ld[par][:],
                                       in_=io["s1"][128 * i:128 * (i + 1), :]))

        @block.scalar
        def _(s, i=i, par=par):
            s.wait_ge(lf[par].h, lf[par].n)
            if i >= 1:
                ctx.wait_mark(s, "mo_x2s")
            sc.inc(s.activation(out=gs[:], in_=acld[par][:], func=AF.Silu))
        ctx.mark("mo_gsil", sc)

        @block.vector
        def _(v, i=i, par=par):
            ctx.wait_mark(v, "mo_gsil")
            v.wait_ge(mst[par].h, mst[par].n)
            vc.inc(v.tensor_tensor(out=gs[:], in0=s1ld[par][:], in1=gs[:],
                                   op=ALU.mult))
            v.wait_ge(vc.h, vc.n)
            vc.inc(v.tensor_tensor(out=x1ld[par][:], in0=x1ld[par][:], in1=gs[:],
                                   op=ALU.add))
        ctx.mark("mo_x2s", vc)
        ctx.mark(f"mo_x2v{par}", vc)

        @block.sync
        def _(sync, i=i, par=par):
            ctx.wait_mark(sync, f"mo_x2v{par}")
            mst[par].dma(sync.dma_start(out=io["out"][128 * i:128 * (i + 1), :],
                                        in_=x1ld[par][:]))

    @block.sync
    def _(sync):
        for s2 in [mst[0], mst[1], lf[0], lf[1], ml, mg] + ma4:
            if s2.n:
                sync.wait_ge(s2.h, s2.n)
    ctx.mark("moe_done", mst[1])




def emit_delta(ctx):
    nc, block = ctx.nc, ctx.block
    io = ctx.io
    dl = ctx.sem("dk_ld")
    dg = ctx.sem("dk_gl")
    vc = ctx.sem("dk_vc")
    sc = ctx.sem("dk_sc")
    tc = ctx.sem("dk_tc")
    dst = [ctx.sem("dk_st0"), ctx.sem("dk_st1")]
    lp = [ctx.sem("dk_lp0"), ctx.sem("dk_lp1")]
    lq = [ctx.sem("dk_lq0"), ctx.sem("dk_lq1")]
    pst = [ctx.sem("dk_pst0"), ctx.sem("dk_pst1")]
    ident = ctx.const["ident"]

    xnfm = _sb(ctx, "dk_xnfm", [128, NCK * T], BF16)
    vfm = _sb(ctx, "dk_vfm", [128, NCK * T], BF16)
    HB = NCHK // 2 * DH
    betaC = _sb(ctx, "dk_betaC", [64, NCHK * DH], F32)
    vtm = [_sb(ctx, f"dk_vtm{h}", [64, DHD], BF16) for h in range(DH)]
    wpool = _sb(ctx, "dk_wpool", [128, NCK * D], BF16)
    wb_sb = _sb(ctx, "dk_wb", [128, NCK * DH], BF16)
    drow = _sb(ctx, "dk_drow", [1, DH], F32)
    lgrow = _sb(ctx, "dk_lgrow", [1, DH], F32)
    beta_tm = _sb(ctx, "dk_beta", [128, NT * DH], F32)
    onesr = _sb(ctx, "dk_onesr", [1, 128], F32)
    onesrb = _sb(ctx, "dk_onesrb", [1, 128], BF16)
    onescb = _sb(ctx, "dk_onescb", [128, 1], BF16)
    iopm = _sb(ctx, "dk_iopm", [64, 64], F32)
    iocol = _sb(ctx, "dk_iocol", [64, 1], F32)
    iorow = _sb(ctx, "dk_iorow", [1, 64], F32)
    lgc64 = _sb(ctx, "dk_lgc64", [64, DH], F32)
    lgc128 = _sb(ctx, "dk_lgc128", [128, DH], F32)
    negLs = _sb(ctx, "dk_negLs", [64, DH * 64], BF16)
    LTt = _sb(ctx, "dk_LTt", [64, DH * 64], BF16)
    de_col = _sb(ctx, "dk_decol", [64, DH], F32)
    dw_col = _sb(ctx, "dk_dwcol", [64, DH], F32)
    ctd128 = _sb(ctx, "dk_ctd", [128, DH], F32)
    de_row = _sb(ctx, "dk_derow", [1, DH * 64], F32)
    de_b = _sb(ctx, "dk_deb", [128, DH * 64], BF16)
    t64a = _sb(ctx, "dk_t64a", [64, 64], F32)
    t64b = _sb(ctx, "dk_t64b", [64, 64], F32)
    rst_row = _sb(ctx, "dk_rstrow", [1, 512], BF16)

    S_f = [_sb(ctx, f"dk_Sf{h}", [128, DHD], F32) for h in range(DH)]
    S_b = [_sb(ctx, f"dk_Sb{h}", [128, DHD], BF16) for h in range(DH)]
    kk_s = [_sb(ctx, f"dk_kks{h}", [64, 64], F32) for h in range(DH)]
    A0 = [_sb(ctx, f"dk_A0{h}", [64, 64], BF16) for h in range(DH)]
    atT = [_sb(ctx, f"dk_atT{h}", [64, 64], BF16) for h in range(DH)]
    Xb = [_sb(ctx, f"dk_X{h}", [64, 64], BF16) for h in range(DH)]
    Pb = [_sb(ctx, f"dk_P{h}", [64, 64], BF16) for h in range(DH)]
    PTs = [_sb(ctx, f"dk_PT{h}", [64, 64], BF16) for h in range(DH)]
    ATs = [_sb(ctx, f"dk_AT{h}", [64, 64], BF16) for h in range(DH)]
    ktm = [_sb(ctx, f"dk_ktm{h}", [64, DHD], BF16) for h in range(DH)]
    kbde = [_sb(ctx, f"dk_kbde{h}", [64, DHD], BF16) for h in range(DH)]
    kdw = [_sb(ctx, f"dk_kdw{h}", [64, DHD], BF16) for h in range(DH)]
    kdeT = [_sb(ctx, f"dk_kdeT{h}", [128, 64], BF16) for h in range(DH)]
    Yv_s = [_sb(ctx, f"dk_Yv{h}", [64, DHD], F32) for h in range(DH)]
    kcd = [_sb(ctx, f"dk_kcd{h}", [64, DHD], BF16) for h in range(DH)]
    kcdT = [_sb(ctx, f"dk_kcdT{h}", [128, 64], BF16) for h in range(DH)]
    vnew = [_sb(ctx, f"dk_vnew{h}", [64, DHD], BF16) for h in range(DH)]
    o_s = [_sb(ctx, f"dk_os{h}", [64, DHD], BF16) for h in range(DH)]
    oTst = [[_sb(ctx, f"dk_oTst{p}_{h}", [128, 64], BF16) for h in range(DH)]
            for p in range(2)]
    xnld = [_sb(ctx, f"dk_xnld{i}", [128, D], BF16) for i in range(2)]
    oTld = [_sb(ctx, f"dk_oTld{i}", [128, 512], BF16) for i in range(2)]
    s1t = [_sb(ctx, f"dk_s1t{i}", [128, D], BF16) for i in range(2)]

    def kk_ps(h):
        return ctx.psbt[2].ap()[0:64, 64 * h:64 * (h + 1)]
    def nwx_ps(h):
        return ctx.psbt[0].ap()[0:64, 64 * h:64 * (h + 1)]
    def nwp_ps(h):
        return ctx.psbt[0].ap()[0:64, 512 + 64 * h:512 + 64 * (h + 1)]
    def yv_ps(h):
        return ctx.psbt[1].ap()[0:64, 128 * h:128 * (h + 1)]
    def o_ps(h):
        return ctx.psbt[2].ap()[0:64, 128 * h:128 * (h + 1)]
    def dW_ps(h):
        return ctx.psbt[0].ap()[:, 128 * h:128 * (h + 1)]
    def ktr_ps(h):
        return ctx.pbf(6)[0:64, 128 * h:128 * h + 128]
    def vtr_ps(h):
        return ctx.pbf(6)[0:64, 128 * h:128 * h + 128]
    def ptr_ps(h):
        return ctx.pbf(6)[0:64, 64 * h:64 * (h + 1)]
    def ttr_ps(h):
        return ctx.pbf(7)[:, 64 * h:64 * (h + 1)]

    @block.sync
    def _(sync):
        ctx.wait_mark(sync, "ln1_done")
        for cc in range(NCK):
            dl.dma(sync.dma_start(out=xnfm[:, T * cc:T * (cc + 1)],
                                  in_=io["xn1_fm"][128 * cc:128 * (cc + 1), :]))
        dl.dma(sync.dma_start(out=drow[:], in_=io["decay"]))

    @block.gpsimd
    def _(g):
        for cc in range(NCK):
            dg.dma(g.dma_start(out=wpool[:, D * cc:D * (cc + 1)],
                               in_=io["WwT"][128 * cc:128 * (cc + 1), :]))
            dg.dma(g.dma_start(out=wb_sb[:, DH * cc:DH * (cc + 1)],
                               in_=io["WbT"][128 * cc:128 * (cc + 1), :]))
        dio = ctx.sem("dk_io")
        g.iota(iopm[:], pattern=[[-1, 64]], base=0, channel_multiplier=1,
               allow_small_or_imprecise_dtypes=True)
        g.iota(iocol[:], pattern=[[0, 1]], base=0, channel_multiplier=1,
               allow_small_or_imprecise_dtypes=True)
        dio.inc(g.iota(iorow[:], pattern=[[1, 64]], base=0, channel_multiplier=0,
                       allow_small_or_imprecise_dtypes=True))
        ctx.mark("dk_iota", dio)

    @block.vector
    def _(v):
        v.memset(onesr[:], 1.0)
        v.memset(onesrb[:], 1.0)
        vc.inc(v.memset(onescb[:], 1.0))
        for h in range(DH):
            v.memset(S_f[h][:], 0.0)
            vc.inc(v.memset(S_b[h][:], 0.0))
    ctx.mark("dk_ones", vc)

    @block.scalar
    def _(s):
        s.wait_ge(dl.h, dl.n)
        sc.inc(s.activation(out=lgrow[:], in_=drow[:], func=AF.Sigmoid))
        s.wait_ge(sc.h, sc.n)
        sc.inc(s.activation(out=lgrow[:], in_=lgrow[:], func=AF.Ln))
    ctx.mark("dk_lg", sc)

    @block.tensor
    def _(te):
        ctx.wait_mark(te, "dk_lg")
        ctx.wait_mark(te, "dk_ones")
        te.matmul(ctx.psb[0][0:64, 0:DH], onesr[:, 0:64], lgrow[:], start=True, stop=True)
        tc.inc(te.matmul(ctx.psb[1][:, 0:DH], onesr[:], lgrow[:], start=True, stop=True))
    ctx.mark("dk_lgmm", tc)

    @block.scalar
    def _(s):
        ctx.wait_mark(s, "dk_lgmm")
        s.activation(out=lgc64[:], in_=ctx.psb[0][0:64, 0:DH], func=AF.Copy)
        sc.inc(s.activation(out=lgc128[:], in_=ctx.psb[1][:, 0:DH], func=AF.Copy))
    ctx.mark("dk_lgc", sc)

    for h in range(DH):
        @block.vector
        def _(v, h=h):
            if h == 0:
                ctx.wait_mark(v, "dk_lgc")
                ctx.wait_mark(v, "dk_iota")
            else:
                ctx.wait_mark(v, f"dk_lmask{h - 1}")
            vc.inc(v.tensor_scalar(out=t64a[:], in0=iopm[:], scalar1=lgc64[:, h:h + 1],
                                   scalar2=None, op0=ALU.mult))
            v.wait_ge(vc.h, vc.n)
            vc.inc(v.tensor_scalar(out=t64b[:], in0=t64a[:], scalar1=-1.0,
                                   scalar2=None, op0=ALU.mult))
            v.wait_ge(vc.h, vc.n)
            v.tensor_scalar(out=de_col[:, h:h + 1], in0=iocol[:], scalar1=1.0,
                            scalar2=lgc64[:, h:h + 1], op0=ALU.add, op1=ALU.mult)
            v.tensor_scalar(out=dw_col[:, h:h + 1], in0=iocol[:], scalar1=63.0,
                            scalar2=lgc64[:, h:h + 1], op0=ALU.subtract,
                            op1=ALU.mult)
            v.tensor_scalar(out=ctd128[:, h:h + 1], in0=lgc128[:, h:h + 1],
                            scalar1=64.0, scalar2=None, op0=ALU.mult)
            vc.inc(v.tensor_scalar(out=de_row[:, 64 * h:64 * (h + 1)], in0=iorow[:],
                                   scalar1=1.0, scalar2=lgrow[:, h:h + 1],
                                   op0=ALU.add, op1=ALU.mult))
        ctx.mark(f"dk_dd{h}", vc)

        @block.scalar
        def _(s, h=h):
            ctx.wait_mark(s, f"dk_dd{h}")
            s.activation(out=t64a[:], in_=t64a[:], func=AF.Exp)
            s.activation(out=t64b[:], in_=t64b[:], func=AF.Exp)
            s.activation(out=de_col[:, h:h + 1], in_=de_col[:, h:h + 1], func=AF.Exp)
            s.activation(out=dw_col[:, h:h + 1], in_=dw_col[:, h:h + 1], func=AF.Exp,
                         scale=-1.0)
            s.activation(out=ctd128[:, h:h + 1], in_=ctd128[:, h:h + 1], func=AF.Exp)
            sc.inc(s.activation(out=de_row[:, 64 * h:64 * (h + 1)],
                                in_=de_row[:, 64 * h:64 * (h + 1)], func=AF.Exp))
        ctx.mark(f"dk_exp{h}", sc)

        @block.gpsimd
        def _(g, h=h):
            ctx.wait_mark(g, f"dk_exp{h}")
            gio = ctx.sem("dk_gio")
            g.affine_select(negLs[:, 64 * h:64 * (h + 1)], t64a[:],
                            pattern=[[-1, 64]], compare_op=ALU.is_gt, fill=0.0,
                            base=0, channel_multiplier=1)
            gio.inc(g.affine_select(LTt[:, 64 * h:64 * (h + 1)], t64b[:],
                                    pattern=[[1, 64]], compare_op=ALU.is_ge,
                                    fill=0.0, base=0, channel_multiplier=-1))
        ctx.mark(f"dk_lmask{h}", ctx.sems["dk_gio"])

        @block.vector
        def _(v2, h=h):
            ctx.wait_mark(v2, f"dk_lmask{h}")
            vc.inc(v2.tensor_scalar(out=negLs[:, 64 * h:64 * (h + 1)],
                                    in0=negLs[:, 64 * h:64 * (h + 1)],
                                    scalar1=-1.0, scalar2=None, op0=ALU.mult))
        ctx.mark(f"dk_negl{h}", vc)
    ctx.mark("dk_consts", vc)

    @block.tensor
    def _(te):
        ctx.wait_mark(te, f"dk_exp{DH - 1}")
        ctx.wait_mark(te, "dk_consts")
        last = None
        for h in range(DH):
            last = te.matmul(ctx.psb[2][:, 0:448] if False else
                             ctx.psb[2 + h // 4][:, 64 * (h % 4):64 * (h % 4 + 1)],
                             onesr[:], de_row[:, 64 * h:64 * (h + 1)],
                             start=True, stop=True)
        tc.inc(last)
    ctx.mark("dk_debmm", tc)

    @block.scalar
    def _(s):
        ctx.wait_mark(s, "dk_debmm")
        s.activation(out=de_b[:, 0:256], in_=ctx.psb[2][:, 0:256], func=AF.Copy)
        sc.inc(s.activation(out=de_b[:, 256:512], in_=ctx.psb[3][:, 0:256], func=AF.Copy))
    ctx.mark("dk_deb", sc)

    for i in range(NT):
        @block.tensor
        def _(te, i=i):
            te.wait_ge(dl.h, dl.n)
            te.wait_ge(dg.h, dg.n)
            ctx.wait_mark(te, "dk_deb")
            if i >= 2:
                ctx.wait_mark(te, f"dk_bt{i % 2}")
            last = None
            for cc in range(NCK):
                last = te.matmul(ctx.psb[4 + i % 2][:, 0:DH],
                                 xnfm[:, T * cc + 128 * i:T * cc + 128 * (i + 1)],
                                 wb_sb[:, DH * cc:DH * (cc + 1)],
                                 start=(cc == 0), stop=(cc == NCK - 1))
            tc.inc(last)
        ctx.mark(f"dk_btm{i % 2}", tc)

        @block.scalar
        def _(s, i=i):
            ctx.wait_mark(s, f"dk_btm{i % 2}")
            sc.inc(s.activation(out=beta_tm[:, DH * i:DH * (i + 1)],
                                in_=ctx.psb[4 + i % 2][:, 0:DH], func=AF.Sigmoid))
        ctx.mark(f"dk_bt{i % 2}", sc)
    ctx.mark("dk_btall", sc)

    @block.sync
    def _(sync):
        ctx.wait_mark(sync, "dk_btall")
        bsem = ctx.sem("dk_bc")
        src_lo = beta_tm.ap()[0:64, :]
        src_hi = beta_tm.ap()[64:128, :]
        bsem.dma(sync.dma_start(out=betaC[:, 0:HB], in_=src_lo))
        bsem.dma(sync.dma_start(out=betaC[:, HB:2 * HB], in_=src_hi))
        ctx.mark("dk_betaC", bsem)

    for dc in range(NCK):
        for tq in range(T // 512):
            @block.tensor
            def _(te, dc=dc, tq=tq):
                te.wait_ge(dl.h, dl.n)
                te.wait_ge(dg.h, dg.n)
                ctx.wait_mark(te, "dk_deb")
                ctx.wait_mark(te, "dk_btall")
                idx = dc * (T // 512) + tq
                if idx >= 2:
                    ctx.wait_mark(te, f"dk_vfe{idx % 2}")
                last = None
                for cc in range(NCK):
                    last = te.matmul(
                        ctx.psb[idx % 2][:],
                        wpool[:, D * cc + 128 * dc:D * cc + 128 * (dc + 1)],
                        xnfm[:, T * cc + 512 * tq:T * cc + 512 * (tq + 1)],
                        start=(cc == 0), stop=(cc == NCK - 1))
                tc.inc(last)
            ctx.mark(f"dk_vfm{(dc * (T // 512) + tq) % 2}", tc)

            @block.vector
            def _(v, dc=dc, tq=tq):
                idx = dc * (T // 512) + tq
                ctx.wait_mark(v, f"dk_vfm{idx % 2}")
                vc.inc(v.tensor_copy(
                    vfm[:, T * dc + 512 * tq:T * dc + 512 * (tq + 1)],
                    ctx.psb[idx % 2][:]))
            ctx.mark(f"dk_vfe{(dc * (T // 512) + tq) % 2}", vc)
    ctx.mark("dk_vball", vc)

    for h in range(DH):
        for tq in range(T // 512):
            @block.vector
            def _(v, h=h, tq=tq):
                ctx.wait_mark(v, "dk_vball")
                if h > 0 or tq > 0:
                    ctx.wait_mark(v, "dk_knsq")
                vc.inc(v.scalar_tensor_tensor(
                    out=s1t[0][:, 0:512], in0=xnfm[:, T * h + 512 * tq:T * h + 512 * (tq + 1)],
                    scalar=1.0, in1=xnfm[:, T * h + 512 * tq:T * h + 512 * (tq + 1)],
                    op0=ALU.mult, op1=ALU.mult))
            ctx.mark("dk_knsq", vc)

            @block.tensor
            def _(te, h=h, tq=tq):
                ctx.wait_mark(te, "dk_knsq")
                if h > 0 or tq > 0:
                    ctx.wait_mark(te, "dk_knrs")
                tc.inc(te.matmul(ctx.psb[6][0:1, :], onescb[:],
                                 s1t[0][:, 0:512], start=True, stop=True))
            ctx.mark("dk_knmm", tc)

            @block.scalar
            def _(s, h=h, tq=tq):
                ctx.wait_mark(s, "dk_knmm")
                sc.inc(s.activation(out=rst_row[:], in_=ctx.psb[6][0:1, :],
                                     func=AF.Ln))
                s.wait_ge(sc.h, sc.n)
                sc.inc(s.activation(out=rst_row[:], in_=rst_row[:], func=AF.Exp,
                                    scale=-0.5))
            ctx.mark("dk_knr", sc)

            @block.tensor
            def _(te, h=h, tq=tq):
                ctx.wait_mark(te, "dk_knr")
                tc.inc(te.matmul(ctx.psb[7][:], onesrb[:], rst_row[:],
                                 start=True, stop=True))
            ctx.mark("dk_knrs", tc)

            @block.vector
            def _(v, h=h, tq=tq):
                ctx.wait_mark(v, "dk_knrs")
                vc.inc(v.tensor_tensor(
                    out=xnfm[:, T * h + 512 * tq:T * h + 512 * (tq + 1)],
                    in0=xnfm[:, T * h + 512 * tq:T * h + 512 * (tq + 1)],
                    in1=ctx.psb[7][:], op=ALU.mult))
            ctx.mark("dk_knorm", vc)
    ctx.mark("dk_kall", vc)

    ctx.marks["dk_e1"] = (vc.h, 0)
    ctx.marks["dk_e7"] = ctx.marks["dk_ones"]
    ctx.marks["dk_e8"] = (vc.h, 0)
    ctx.marks["dk_e4p"] = (vc.h, 0)
    for c in range(NCHK):
        ti, hf = c // 2, c % 2
        r64 = slice(64 * hf, 64 * (hf + 1))

        def kslice(h, c=c):
            return xnfm[:, T * h + 64 * c:T * h + 64 * (c + 1)]

        @block.tensor
        def _(te, c=c):
            if c == 0:
                ctx.wait_mark(te, "dk_kall")
                ctx.wait_mark(te, "dk_vball")
                ctx.wait_mark(te, "prelude_ld")
            ctx.wait_mark(te, "dk_e1")
            ctx.wait_mark(te, "dk_betaC")
            last = None
            for h in range(DH):
                te.matmul(kk_ps(h), kslice(h), kslice(h), start=True, stop=True)
                last = te.transpose(ktr_ps(h), kslice(h), ident[:])
            tc.inc(last)
        ctx.mark("dk_t1", tc)

        @block.vector
        def _(v, c=c, ti=ti, hf=hf):
            ctx.wait_mark(v, "dk_t1")
            for h in range(DH):
                v.tensor_copy(kk_s[h][:], kk_ps(h))
                vc.inc(v.tensor_copy(ktm[h][:], ktr_ps(h)))
        ctx.mark("dk_e1k", vc)

        @block.tensor
        def _(te, c=c):
            ctx.wait_mark(te, "dk_e1k")
            last = None
            for h in range(DH):
                last = te.transpose(vtr_ps(h),
                                    vfm[:, T * h + 64 * c:T * h + 64 * (c + 1)],
                                    ident[:])
            tc.inc(last)
        ctx.mark("dk_t1v", tc)

        @block.vector
        def _(v, c=c, ti=ti, hf=hf):
            ctx.wait_mark(v, "dk_t1v")
            for h in range(DH):
                bcol0 = betaC.ap()[:, HB * hf + DH * ti + h:HB * hf + DH * ti + h + 1]
                vc.inc(v.tensor_scalar(out=vtm[h][:], in0=vtr_ps(h), scalar1=bcol0,
                                       scalar2=None, op0=ALU.mult))
            v.wait_ge(vc.h, vc.n)
            last = None
            for h in range(DH):
                bcol = betaC.ap()[:, HB * hf + DH * ti + h:HB * hf + DH * ti + h + 1]
                v.scalar_tensor_tensor(out=A0[h][:], in0=kk_s[h][:], scalar=bcol,
                                       in1=negLs[:, 64 * h:64 * (h + 1)],
                                       op0=ALU.mult, op1=ALU.mult)
                v.tensor_tensor(out=atT[h][:], in0=kk_s[h][:],
                                in1=LTt[:, 64 * h:64 * (h + 1)], op=ALU.mult)
                v.tensor_scalar(out=kbde[h][:], in0=ktm[h][:], scalar1=bcol,
                                scalar2=de_col[:, h:h + 1], op0=ALU.mult, op1=ALU.mult)
                v.tensor_scalar(out=kdw[h][:], in0=ktm[h][:],
                                scalar1=dw_col[:, h:h + 1], scalar2=None, op0=ALU.mult)
                last = v.tensor_tensor(out=kdeT[h][:], in0=kslice(h),
                                       in1=de_b[:, 64 * h:64 * (h + 1)], op=ALU.mult)
            vc.inc(last)
            v.wait_ge(vc.h, vc.n)
            for h in range(DH):
                vc.inc(v.tensor_copy(Xb[h][:], ident[0:64, 0:64]))
            v.wait_ge(vc.h, vc.n)
            for h in range(DH):
                vc.inc(v.tensor_copy(Pb[h][:], A0[h][:]))
        ctx.mark("dk_e1", vc)

        NR = 6
        for r in range(NR):
            @block.tensor
            def _(te, c=c, r=r):
                ctx.wait_mark(te, "dk_e1" if r == 0 else f"dk_ne{r - 1}")
                last = None
                for h in range(DH):
                    last = te.transpose(ptr_ps(h), Pb[h][:], ident[0:64, 0:64])
                tc.inc(last)
            ctx.mark(f"dk_ntr{r}", tc)

            @block.vector
            def _(v, c=c, r=r):
                ctx.wait_mark(v, f"dk_ntr{r}")
                for h in range(DH):
                    vc.inc(v.tensor_copy(PTs[h][:], ptr_ps(h)))
            ctx.mark(f"dk_nev{r}", vc)

            @block.tensor
            def _(te, c=c, r=r):
                ctx.wait_mark(te, f"dk_nev{r}")
                last = None
                for h in range(DH):
                    last = te.matmul(nwx_ps(h), PTs[h][:], Xb[h][:],
                                     start=True, stop=True)
                    if r < NR - 1:
                        last = te.matmul(nwp_ps(h), PTs[h][:], Pb[h][:],
                                         start=True, stop=True)
                tc.inc(last)
            ctx.mark(f"dk_nmm{r}", tc)

            @block.vector
            def _(v, c=c, r=r):
                ctx.wait_mark(v, f"dk_nmm{r}")
                for h in range(DH):
                    vc.inc(v.tensor_tensor(out=Xb[h][:], in0=Xb[h][:],
                                           in1=nwx_ps(h), op=ALU.add))
                if r < NR - 1:
                    v.wait_ge(vc.h, vc.n)
                    for h in range(DH):
                        vc.inc(v.tensor_copy(Pb[h][:], nwp_ps(h)))
            ctx.mark(f"dk_ne{r}", vc)

        @block.tensor
        def _(te, c=c):
            ctx.wait_mark(te, f"dk_ne{NR - 1}")
            last = None
            for h in range(DH):
                last = te.transpose(ptr_ps(h), Xb[h][:], ident[0:64, 0:64])
            tc.inc(last)
        ctx.mark("dk_t3", tc)

        @block.vector
        def _(v, c=c):
            ctx.wait_mark(v, "dk_t3")
            for h in range(DH):
                vc.inc(v.tensor_copy(ATs[h][:], ptr_ps(h)))
        ctx.mark("dk_e3", vc)

        @block.tensor
        def _(te, c=c, ti=ti, hf=hf):
            ctx.wait_mark(te, "dk_e3")
            ctx.wait_mark(te, "dk_e4p")
            last = None
            for h in range(DH):
                te.matmul(yv_ps(h), ATs[h][:], vtm[h][:], start=True, stop=True)
                last = te.matmul(o_ps(h), ATs[h][:], kbde[h][:], start=True, stop=True)
            tc.inc(last)
        ctx.mark("dk_t4", tc)

        @block.vector
        def _(v, c=c):
            ctx.wait_mark(v, "dk_t4")
            for h in range(DH):
                v.tensor_copy(Yv_s[h][:], yv_ps(h))
                vc.inc(v.tensor_copy(kcd[h][:], o_ps(h)))
        ctx.mark("dk_e4", vc)
        ctx.mark("dk_e4p", vc)

        @block.tensor
        def _(te, c=c):
            ctx.wait_mark(te, "dk_e4")
            last = None
            for h in range(DH):
                last = te.transpose(ttr_ps(h), kcd[h][:], ident[0:64, 0:64])
            tc.inc(last)
        ctx.mark("dk_t5", tc)

        @block.vector
        def _(v, c=c):
            ctx.wait_mark(v, "dk_t5")
            for h in range(DH):
                vc.inc(v.tensor_copy(kcdT[h][:], ttr_ps(h)))
        ctx.mark("dk_e5", vc)

        @block.tensor
        def _(te, c=c):
            ctx.wait_mark(te, "dk_e5")
            ctx.wait_mark(te, "dk_e7")
            last = None
            for h in range(DH):
                last = te.matmul(o_ps(h), kcdT[h][:], S_b[h][:], start=True, stop=True)
            tc.inc(last)
        ctx.mark("dk_t6", tc)

        @block.vector
        def _(v, c=c):
            ctx.wait_mark(v, "dk_t6")
            for h in range(DH):
                vc.inc(v.tensor_tensor(out=vnew[h][:], in0=Yv_s[h][:], in1=o_ps(h),
                                       op=ALU.subtract))
        ctx.mark("dk_e6", vc)

        @block.tensor
        def _(te, c=c):
            ctx.wait_mark(te, "dk_e6")
            last = None
            for h in range(DH):
                te.matmul(yv_ps(h), kdeT[h][:], S_b[h][:], start=True, stop=False)
                te.matmul(yv_ps(h), atT[h][:], vnew[h][:], start=False, stop=True)
                last = te.matmul(dW_ps(h), kdw[h][:], vnew[h][:], start=True, stop=True)
            tc.inc(last)
        ctx.mark("dk_t7", tc)

        @block.vector
        def _(v, c=c):
            ctx.wait_mark(v, "dk_t7")
            for h in range(DH):
                vc.inc(v.tensor_copy(o_s[h][:], yv_ps(h)))
            v.wait_ge(vc.h, vc.n)
            for h in range(DH):
                vc.inc(v.scalar_tensor_tensor(out=S_f[h][:], in0=S_f[h][:],
                                              scalar=ctd128[:, h:h + 1], in1=dW_ps(h),
                                              op0=ALU.mult, op1=ALU.add))
            v.wait_ge(vc.h, vc.n)
            for h in range(DH):
                vc.inc(v.tensor_copy(S_b[h][:], S_f[h][:]))
        ctx.mark("dk_e7", vc)

        @block.tensor
        def _(te, c=c):
            ctx.wait_mark(te, "dk_e7")
            ctx.wait_mark(te, "dk_e8")
            last = None
            for h in range(DH):
                last = te.transpose(ttr_ps(h), o_s[h][:], ident[0:64, 0:64])
            tc.inc(last)
        ctx.mark("dk_t8", tc)

        @block.vector
        def _(v, c=c):
            ctx.wait_mark(v, "dk_t8")
            v.wait_ge(dst[c % 2].h, dst[c % 2].n)
            for h in range(DH):
                vc.inc(v.tensor_copy(oTst[c % 2][h][:], ttr_ps(h)))
        ctx.mark("dk_e8", vc)

        @block.sync
        def _(sync, c=c):
            ctx.wait_mark(sync, "dk_e8")
            sync.wait_ge(dst[c % 2].h, dst[c % 2].n)
            for h in range(DH):
                dst[c % 2].dma(sync.dma_start(
                    out=io["oT1h"][DHD * h:DHD * (h + 1), 64 * c:64 * (c + 1)],
                    in_=oTst[c % 2][h][:]))

    @block.gpsimd
    def _(g):
        g.wait_ge(dg.h, dg.n)
        ctx.wait_mark(g, "dk_vball")
        for cc in range(NCK):
            dg.dma(g.dma_start(out=wpool[:, D * cc:D * (cc + 1)],
                               in_=io["WrT"][128 * cc:128 * (cc + 1), :]))
    ctx.mark("dk_wr", dg)

    for tg in range(T // 512):
        for oc in range(NCK):
            @block.sync
            def _(sync, tg=tg, oc=oc):
                if tg == 0 and oc == 0:
                    sync.wait_ge(dst[0].h, dst[0].n)
                    sync.wait_ge(dst[1].h, dst[1].n)
                idx = tg * NCK + oc
                if idx >= 2:
                    ctx.wait_mark(sync, f"dk_pj{oc % 2}")
                lp[oc % 2].dma(sync.dma_start(
                    out=oTld[oc % 2][:],
                    in_=io["oT1h"][128 * oc:128 * (oc + 1), 512 * tg:512 * (tg + 1)]))

            @block.tensor
            def _(te, tg=tg, oc=oc):
                te.wait_ge(lp[oc % 2].h, lp[oc % 2].n)
                if tg == 0 and oc == 0:
                    ctx.wait_mark(te, "dk_wr")
                if oc == 0:
                    ctx.wait_mark(te, "dk_pjev")
                last = None
                for i4 in range(4):
                    for nb in range(2):
                        last = te.matmul(
                            ctx.psb[i4 * 2 + nb][:],
                            oTld[oc % 2][:, 128 * i4:128 * (i4 + 1)],
                            wpool[:, D * oc + 512 * nb:D * oc + 512 * (nb + 1)],
                            start=(oc == 0), stop=(oc == NCK - 1))
                tc.inc(last)
            ctx.mark(f"dk_pj{oc % 2}", tc)
        ctx.mark("dk_pjmm", tc)

        for i4 in range(4):
            @block.sync
            def _(sync, tg=tg, i4=i4):
                i = 4 * tg + i4
                if i >= 2:
                    ctx.wait_mark(sync, f"dk_s1v{i4 % 2}")
                lq[i4 % 2].dma(sync.dma_start(
                    out=xnld[i4 % 2][:], in_=io["xn1_tm"][128 * i:128 * (i + 1), :]))

            @block.vector
            def _(v, tg=tg, i4=i4):
                if i4 == 0:
                    ctx.wait_mark(v, "dk_pjmm")
                v.wait_ge(lq[i4 % 2].h, lq[i4 % 2].n)
                v.wait_ge(pst[i4 % 2].h, pst[i4 % 2].n)
                last = None
                for nb in range(2):
                    last = v.tensor_tensor(
                        out=s1t[i4 % 2][:, 512 * nb:512 * (nb + 1)],
                        in0=ctx.psb[i4 * 2 + nb][:],
                        in1=xnld[i4 % 2][:, 512 * nb:512 * (nb + 1)], op=ALU.add)
                vc.inc(last)
            if i4 == 3:
                ctx.mark("dk_pjev", vc)
            ctx.mark(f"dk_s1v{i4 % 2}", vc)

            @block.sync
            def _(sync, tg=tg, i4=i4):
                ctx.wait_mark(sync, f"dk_s1v{i4 % 2}")
                i = 4 * tg + i4
                pst[i4 % 2].dma(sync.dma_start(
                    out=io["s1"][128 * i:128 * (i + 1), :], in_=s1t[i4 % 2][:]))

    ctx.mark("delta_done", pst[1])


TENSORS = {
    "x": ([T, D], F32),
    "qkv_wT": ([D, 3 * D], F32), "qkv_b": ([1, 3 * D], F32),
    "proj_wT": ([D, D], F32), "proj_b": ([1, D], F32),
    "wq_peerT": ([D, 2 * D], F32), "keysT": ([PH, 2, PDK, NKEYS], F32),
    "norm_w": ([1, D], F32),
    "exp_down": ([NEXP, D], F32), "exp_up": ([NEXP, D], F32),
    "tblbf": ([NEXP, 2 * D], BF16),
    "WwT": ([D, D], F32), "WrT": ([D, D], F32), "WbT": ([D, DH], F32),
    "decay": ([1, DH], F32),
    "router": ([D, NE], F32),
    "Wg": ([NE, D, FF], F32), "Wu": ([NE, D, FF], F32), "Wd": ([NE, FF, D], F32),
    "rope_cos": ([128, (T // 128) * 32], F32), "rope_sin": ([128, (T // 128) * 32], F32),
    "diag_masks": ([128, 4 * 512], BF16), "ident_bf": ([128, 128], BF16),
    "qkv_wg": ([NGRP * NCK * 128, 3 * 64 * GH], F32),
    "xn_tm": ([T, D], BF16), "xn_fm": ([D, T], BF16),
    "s": ([T, D], BF16), "oTh": ([D, T], BF16),
    "x1": ([T, D], F32),
    "xn1_tm": ([T, D], BF16), "xn1_fm": ([D, T], BF16),
    "s1": ([T, D], BF16), "moeacc": ([T, D], F32), "oT1h": ([D, T], BF16),
    "out": ([T, D], F32),
}

STAGES = {
    "ln0": dict(reads=["x", "ident_bf"], writes=["xn_tm", "xn_fm"],
                emit=lambda ctx: emit_ln(ctx, "x", "xn_tm", "xn_fm", "ln0")),
    "ln1": dict(reads=["x1", "ident_bf"], writes=["xn1_tm", "xn1_fm"],
                emit=lambda ctx: emit_ln(ctx, "x1", "xn1_tm", "xn1_fm", "ln1")),
    "attn0": dict(reads=["xn_fm", "qkv_wg", "qkv_b", "proj_wT", "proj_b",
                         "rope_cos", "rope_sin", "diag_masks", "ident_bf",
                         "exp_down", "exp_up"],
                  writes=["s", "tblbf"], scratch=["oTh"], emit=emit_attn0),
    "peer": dict(reads=["xn_tm", "xn_fm", "wq_peerT", "keysT", "norm_w",
                        "tblbf", "x", "s", "ident_bf"],
                 writes=["x1"], emit=emit_peer),
    "moe": dict(reads=["xn1_fm", "router", "Wg", "Wu", "Wd", "x1", "s1",
                       "ident_bf"],
                writes=["out"], scratch=["moeacc"], emit=emit_moe),
    "delta": dict(reads=["xn1_fm", "xn1_tm", "WwT", "WbT", "WrT", "decay",
                         "ident_bf"],
                  writes=["s1"], scratch=["oT1h"], emit=emit_delta),
}


def build_program(stage_names):
    stages = [STAGES[s] for s in stage_names]
    reads = set()
    writes = set()
    scratch = set()
    for s in stages:
        reads.update(s["reads"])
        writes.update(s["writes"])
        scratch.update(s.get("scratch", []))
    fused_all = "out" in writes and len(stage_names) > 1

    stack = ExitStack()
    nc = bass.Bass("TRN2", target_bir_lowering=False, debug=False)
    io = {}
    in_names = []
    out_names = []
    for name in sorted(reads | writes | scratch):
        shape, dt = TENSORS[name]
        if name in scratch:
            kind = "Internal"
        elif name in writes:
            if fused_all and name != "out":
                kind = "Internal"
            else:
                kind = "ExternalOutput"
                out_names.append(name)
        else:
            kind = "ExternalInput"
            in_names.append(name)
        io[name] = nc.dram_tensor(name, shape, dt, kind=kind).ap()

    ctx = Ctx(nc, stack, io)
    with stack:
        with nc.Block() as block:
            ctx.block = block
            emit_prelude(ctx)
            for si, s in enumerate(stages):
                if si > 0:
                    def _bar(eng):
                        for sem in ctx.sems.values():
                            if sem.n > 0:
                                eng.wait_ge(sem.h, sem.n)
                    block.sync(_bar)
                    block.gpsimd(_bar)
                    block.vector(_bar)
                    block.scalar(_bar)
                    block.tensor(_bar)
                with ExitStack() as scope:
                    ctx.scope = scope
                    s["emit"](ctx)
                ctx.scope = stack

            @block.sync
            def _(sync):
                for sem in ctx.sems.values():
                    if sem.n > 0:
                        sync.wait_ge(sem.h, sem.n)
    return nc, in_names, out_names


def run_stages(stage_names, per_core, core_ids=None):
    core_ids = core_ids or list(range(len(per_core)))
    nc, in_names, out_names = build_program(stage_names)
    in_maps = [{n: pc[n] for n in in_names} for pc in per_core]
    res = run_bass_kernel_spmd(nc, in_maps, core_ids=core_ids)
    for pc, r in zip(per_core, res.results):
        for n in out_names:
            pc[n] = r[n]
    return per_core


PIPELINE = ["ln0", "attn0", "peer", "ln1", "delta", "moe"]
FUSED = True


def kernel(**inputs):
    w = host_prepare(inputs)
    xfull = np.asarray(inputs["L0_attn_x"])
    per_core = [dict(w, x=np.ascontiguousarray(xfull[i])) for i in range(B)]
    if FUSED:
        run_stages(PIPELINE, per_core)
    else:
        for st in PIPELINE:
            run_stages([st], per_core)
    return np.stack([pc["out"] for pc in per_core], axis=0)


# version 05
# speedup vs baseline: 1.1852x; 1.0281x over previous
import math
from contextlib import ExitStack

import numpy as np
import ml_dtypes

import concourse.bass as bass
import concourse.mybir as mybir
from concourse.bass import IndirectOffsetOnAxis
from concourse.bass_utils import run_bass_kernel_spmd

F32 = mybir.dt.float32
BF16 = mybir.dt.bfloat16
I32 = mybir.dt.int32
U32 = mybir.dt.uint32
AF = mybir.ActivationFunctionType
ALU = mybir.AluOpType
AX = mybir.AxisListType

B = 8
T = 4096
D = 1024
NT = T // 128
NCK = D // 128

AH = 16
AHD = 64
PH = 8
PDK = 128
PK = 8
NKEYS = 256
NEXP = NKEYS * NKEYS
DH = 8
DHD = 128
CH = 64
NCHK = T // CH
NE = 8
FF = 2048

BF = ml_dtypes.bfloat16

QKS = 1.0 / math.sqrt(8.0)
NGRP = 4
GH = AH // NGRP



def host_prepare(inp):
    c = lambda a: np.ascontiguousarray(a)
    w = {}
    w["qkv_wT"] = c(np.asarray(inp["L0_attn_qkv_w"]).T)
    w["qkv_b"] = c(np.asarray(inp["L0_attn_qkv_b"]).reshape(1, 3 * D))
    w["proj_wT"] = c(np.asarray(inp["L0_attn_proj_w"]).T)
    w["proj_b"] = c(np.asarray(inp["L0_attn_proj_b"]).reshape(1, D))
    w["wq_peerT"] = c(np.asarray(inp["L0_ffn_w_q"]).T)
    w["keysT"] = c(np.asarray(inp["L0_ffn_keys"]).transpose(0, 2, 3, 1))
    w["norm_w"] = c(np.asarray(inp["L0_ffn_norm_w"]).reshape(1, D))
    w["exp_down"] = c(np.asarray(inp["L0_ffn_expert_down"]))
    w["exp_up"] = c(np.asarray(inp["L0_ffn_expert_up"]))
    w["WwT"] = c(np.asarray(inp["L1_attn_W_write"]).T)
    w["WrT"] = c(np.asarray(inp["L1_attn_W_read"]).T)
    w["WbT"] = c(np.asarray(inp["L1_attn_W_beta"]).T)
    w["decay"] = c(np.asarray(inp["L1_attn_decay"]).reshape(1, DH))
    w["router"] = c(np.asarray(inp["L1_ffn_router_kernel_DE"]))
    w["Wg"] = c(np.asarray(inp["L1_ffn_kernel_gating_EDF"]))
    w["Wu"] = c(np.asarray(inp["L1_ffn_kernel_up_EDF"]))
    w["Wd"] = c(np.asarray(inp["L1_ffn_kernel_down_EFD"]))

    qkvT = w["qkv_wT"]
    gw = np.zeros((NGRP, NCK, 128, 3 * 64 * GH), dtype=np.float32)
    for g in range(NGRP):
        cols = np.r_[64 * GH * g:64 * GH * (g + 1),
                     D + 64 * GH * g:D + 64 * GH * (g + 1),
                     2 * D + 64 * GH * g:2 * D + 64 * GH * (g + 1)]
        for cc in range(NCK):
            gw[g, cc] = qkvT[128 * cc:128 * (cc + 1), cols]
    w["qkv_wg"] = c(gw.reshape(NGRP * NCK * 128, 3 * 64 * GH))

    half = AHD // 2
    freqs = 1.0 / 10000.0 ** (np.arange(half, dtype=np.float64) / half)
    ang = np.arange(T, dtype=np.float64)[:, None] * freqs[None, :]
    cosf = np.cos(ang).astype(np.float32)
    sinf = np.sin(ang).astype(np.float32)
    w["rope_cos"] = c(cosf.reshape(T // 128, 128, half).transpose(1, 0, 2).reshape(128, -1))
    w["rope_sin"] = c(sinf.reshape(T // 128, 128, half).transpose(1, 0, 2).reshape(128, -1))
    masks = np.zeros((4, 128, 512), dtype=BF)
    kk, qq = np.arange(128)[:, None], np.arange(512)[None, :]
    for m in range(4):
        masks[m] = (kk <= qq - 128 * m).astype(BF)
    w["diag_masks"] = c(masks.transpose(1, 0, 2).reshape(128, 4 * 512))
    w["ident_bf"] = np.eye(128, dtype=BF)
    return w



class Sem:
    __slots__ = ("h", "n")

    def __init__(self, h):
        self.h = h
        self.n = 0

    def inc(self, instr, k=1):
        instr.then_inc(self.h, k)
        self.n += k
        return instr

    def dma(self, instr):
        return self.inc(instr, 16)


class Ctx:
    def __init__(self, nc, stack, io):
        self.nc = nc
        self.stack = stack
        self.scope = stack
        self.io = io
        self.block = None
        self.sems = {}
        self.marks = {}
        self.psbt = [nc.alloc_psum_tensor(f"psbt{i}", [128, 1024], F32)
                     for i in range(4)]
        self.psb = [self.psbt[i // 2].ap()[:, 512 * (i % 2):512 * (i % 2 + 1)]
                    for i in range(8)]
        self.const = {}

    def sem(self, name):
        if name not in self.sems:
            h = self.stack.enter_context(self.nc.semaphore(name))
            self.sems[name] = Sem(h)
        return self.sems[name]

    def mark(self, name, sem=None, n=None):
        self.marks[name] = (sem.h, sem.n if n is None else n)

    def wait_mark(self, eng, name):
        if name not in self.marks:
            return
        h, n = self.marks[name]
        if n > 0:
            eng.wait_ge(h, n)

    def pbf(self, i):
        return self.psb[i].bitcast(BF16)


def _sb(ctx, name, shape, dtype=F32):
    return ctx.scope.enter_context(ctx.nc.sbuf_tensor(name, shape, dtype))



def emit_prelude(ctx):
    nc, block = ctx.nc, ctx.block
    ld = ctx.sem("pre_ld")
    ident = _sb(ctx, "c_ident", [128, 128], BF16)
    ones_row = _sb(ctx, "c_ones_row", [1, 512], BF16)
    ctx.const["ident"] = ident
    ctx.const["ones_row"] = ones_row

    @block.sync
    def _(sync):
        ld.dma(sync.dma_start(out=ident[:], in_=ctx.io["ident_bf"]))

    vc = ctx.sem("pre_vc")

    @block.vector
    def _(v):
        vc.inc(v.memset(ones_row[:], 1.0))

    ctx.mark("prelude", vc)
    ctx.mark("prelude_ld", ld)



def emit_ln(ctx, x_name, xn_tm_name, xn_fm_name, tag):
    nc, block = ctx.nc, ctx.block
    x_h = ctx.io[x_name]
    xn_tm_h = ctx.io[xn_tm_name]
    xn_fm_h = ctx.io[xn_fm_name]

    ld2 = [ctx.sem(f"{tag}_ld0"), ctx.sem(f"{tag}_ld1")]
    sc = ctx.sem(f"{tag}_sc")
    vc = ctx.sem(f"{tag}_vc")
    tc = ctx.sem(f"{tag}_tc")
    ev = ctx.sem(f"{tag}_ev")
    st = ctx.sem(f"{tag}_st")

    xld = [_sb(ctx, f"{tag}_xld{i}", [128, D], F32) for i in range(2)]
    scr = _sb(ctx, f"{tag}_scr", [128, D], F32)
    scr2 = _sb(ctx, f"{tag}_scr2", [128, D], F32)
    ssum = _sb(ctx, f"{tag}_ssum", [128, 2], F32)
    mst = _sb(ctx, f"{tag}_mst", [128, 4], F32)
    xn_tm = [_sb(ctx, f"{tag}_xntm{i}", [128, D], BF16) for i in range(2)]
    xn_fm = _sb(ctx, f"{tag}_xnfm", [128, NCK * T], BF16)

    ident = ctx.const["ident"]
    vfree = {}
    tdone = {}

    for i in range(NT):
        @block.sync
        def _(sync, i=i):
            if i >= 2:
                h, n = ctx.marks[f"{tag}_vfree{i - 2}"]
                sync.wait_ge(h, n)
            ld2[i % 2].dma(sync.dma_start(out=xld[i % 2][:],
                                          in_=x_h[128 * i:128 * (i + 1), :]))

        @block.scalar
        def _(s, i=i):
            s.wait_ge(ld2[i % 2].h, ld2[i % 2].n)
            if i >= 1:
                h, n = ctx.marks[f"{tag}_vfree{i - 1}"]
                s.wait_ge(h, n)
            sc.inc(s.activation(out=scr[:], in_=xld[i % 2][:], func=AF.Copy,
                                accum_out=ssum[:, 0:1]))
            sc.inc(s.activation(out=scr2[:], in_=xld[i % 2][:], func=AF.Square,
                                accum_out=ssum[:, 1:2]))

        @block.vector
        def _(v, i=i):
            v.wait_ge(sc.h, sc.n)
            vc.inc(v.tensor_scalar(out=mst[:, 0:1], in0=ssum[:, 0:1], scalar1=1.0 / D,
                                   scalar2=None, op0=ALU.mult))
            v.wait_ge(vc.h, vc.n)
            vc.inc(v.tensor_scalar(out=mst[:, 3:4], in0=mst[:, 0:1], scalar1=mst[:, 0:1],
                                   scalar2=-1e-5, op0=ALU.mult, op1=ALU.add))
            v.wait_ge(vc.h, vc.n)
            vc.inc(v.tensor_scalar(out=mst[:, 1:2], in0=ssum[:, 1:2], scalar1=1.0 / D,
                                   scalar2=mst[:, 3:4], op0=ALU.mult, op1=ALU.subtract))
        ctx.mark(f"{tag}_var{i}", vc)

        @block.scalar
        def _(s, i=i):
            v_h, v_n = ctx.marks[f"{tag}_var{i}"]
            s.wait_ge(v_h, v_n)
            sc.inc(s.activation(out=mst[:, 3:4], in_=mst[:, 1:2], func=AF.Ln))
            s.wait_ge(sc.h, sc.n)
            sc.inc(s.activation(out=mst[:, 2:3], in_=mst[:, 3:4], func=AF.Exp, scale=-0.5))

        @block.vector
        def _(v, i=i):
            v.wait_ge(sc.h, sc.n)
            if i >= 2:
                h, n = ctx.marks[f"{tag}_tdone{i - 2}"]
                v.wait_ge(h, n)
                v.wait_ge(st.h, st.n)
            vc.inc(v.tensor_scalar(out=xn_tm[i % 2][:], in0=xld[i % 2][:],
                                   scalar1=mst[:, 0:1], scalar2=mst[:, 2:3],
                                   op0=ALU.subtract, op1=ALU.mult))
        ctx.mark(f"{tag}_vfree{i}", vc)
        ctx.mark(f"{tag}_xntm{i}", vc)

        @block.sync
        def _(sync, i=i):
            h, n = ctx.marks[f"{tag}_xntm{i}"]
            sync.wait_ge(h, n)
            st.dma(sync.dma_start(out=xn_tm_h[128 * i:128 * (i + 1), :], in_=xn_tm[i % 2][:]))

        @block.tensor
        def _(te, i=i):
            h, n = ctx.marks[f"{tag}_xntm{i}"]
            te.wait_ge(h, n)
            ctx.wait_mark(te, "prelude_ld")
            for cc in range(NCK):
                k = i * NCK + cc
                if k >= 4:
                    te.wait_ge(ev.h, k - 3)
                tc.inc(te.transpose(ctx.pbf(k % 4)[:, 0:128],
                                    xn_tm[i % 2][:, 128 * cc:128 * (cc + 1)], ident[:]))

        @block.vector
        def _(v, i=i):
            for cc in range(NCK):
                k = i * NCK + cc
                v.wait_ge(tc.h, k + 1)
                ev.inc(v.tensor_copy(xn_fm[:, T * cc + 128 * i: T * cc + 128 * (i + 1)],
                                     ctx.pbf(k % 4)[:, 0:128]))
        ctx.mark(f"{tag}_tdone{i}", ev)

    @block.sync
    def _(sync):
        sync.wait_ge(ev.h, NT * NCK)
        for cc in range(NCK):
            st.dma(sync.dma_start(out=xn_fm_h[128 * cc:128 * (cc + 1), :],
                                  in_=xn_fm[:, T * cc:T * (cc + 1)]))
        sync.wait_ge(st.h, st.n)

    ctx.mark(f"{tag}_done", st)






def emit_av(ctx, block, pend, tc, vx, PT):
    hh, j, kg, nk, par, had_mask = pend

    @block.tensor
    def _(te):
        ctx.wait_mark(te, f"at_expg{par}")
        if had_mask:
            ctx.wait_mark(te, f"at_maskg{par}")
        last = None
        for idx, k in enumerate(kg):
            last = te.matmul(
                ctx.psb[2][0:65, :],
                vx.ap()[:, 65 * (GH * k + hh):65 * (GH * k + hh) + 65],
                PT[3 * par + idx][:],
                start=(k == 0), stop=(k == nk - 1))
        tc.inc(last)
    ctx.mark(f"at_avg{par}", tc)
    ctx.mark("at_avgrp", tc)


def emit_attn0(ctx):
    nc, block = ctx.nc, ctx.block
    io = ctx.io
    ld = ctx.sem("at_ld")
    gl = ctx.sem("at_gl")
    vc = ctx.sem("at_vc")
    sc = ctx.sem("at_sc")
    tc = ctx.sem("at_tc")
    st = ctx.sem("at_st")
    sto = [ctx.sem("at_sto0"), ctx.sem("at_sto1")]
    lb = ctx.sem("at_lb")
    sts = [ctx.sem("at_sts0"), ctx.sem("at_sts1")]
    lo = [ctx.sem("at_lo0"), ctx.sem("at_lo1")]

    ident = ctx.const["ident"]

    xnfm = _sb(ctx, "at_xnfm", [128, NCK * T], BF16)
    wpool = _sb(ctx, "at_wpool", [128, NCK * D], BF16)
    cos_sb = _sb(ctx, "at_cos", [128, NT * 32], BF16)
    sin_sb = _sb(ctx, "at_sin", [128, NT * 32], BF16)
    masks = _sb(ctx, "at_masks", [128, 4 * 512], BF16)
    bias_bg = _sb(ctx, "at_biasbg", [128, 3 * 64 * GH], F32)
    projb_b = _sb(ctx, "at_projbb", [128, D], F32)
    brow_g = _sb(ctx, "at_browg", [1, 3 * 64 * GH], BF16)
    pbrow = _sb(ctx, "at_pbrow", [1, D], BF16)
    onesr = _sb(ctx, "at_onesr", [1, 128], BF16)
    sclr = _sb(ctx, "at_sclr", [1, 128], BF16)
    ones64 = _sb(ctx, "at_ones64", [1, 64], BF16)

    qTx = [_sb(ctx, f"at_qTx{h}", [64, T], BF16) for h in range(GH)]
    kTx = [_sb(ctx, f"at_kTx{h}", [64, T], BF16) for h in range(GH)]
    vx = _sb(ctx, "at_vx", [128, NT * GH * 65], BF16)
    oTst = [_sb(ctx, f"at_oTst{i}", [64, 512], BF16) for i in range(2)]
    qkvf = _sb(ctx, "at_qkvf", [128, 3 * 64 * GH], F32)
    qk_bf = _sb(ctx, "at_qkbf", [128, 2 * 64 * GH], BF16)
    rsc = _sb(ctx, "at_rsc", [128, 4 * 32 * GH], F32)
    qn2 = _sb(ctx, "at_qn2", [128, 2 * GH], F32)
    qn2m = _sb(ctx, "at_qn2m", [128, 2 * GH], F32)
    qn2mb = _sb(ctx, "at_qn2mb", [128, 2 * GH], BF16)
    cneg = _sb(ctx, "at_cneg", [128, GH], F32)
    qm4 = _sb(ctx, "at_qm4", [4, 1], F32)
    km4 = _sb(ctx, "at_km4", [4, 1], F32)
    c4s = _sb(ctx, "at_c4s", [4, 1], F32)
    c4b = _sb(ctx, "at_c4b", [4, 1], BF16)
    crow = _sb(ctx, "at_crow", [1, GH], BF16)
    onesrb = _sb(ctx, "at_onesrb", [1, 128], BF16)
    PT = [_sb(ctx, f"at_PT{i}", [128, 512], BF16) for i in range(6)]
    rz = _sb(ctx, "at_rz", [1, 512], BF16)
    zb = _sb(ctx, "at_zb", [64, 512], F32)
    oTld = [_sb(ctx, f"at_oTld{i}", [128, 512], BF16) for i in range(2)]
    s_tm = [_sb(ctx, f"at_stm{i}", [128, D], BF16) for i in range(2)]

    @block.sync
    def _(sync):
        ctx.wait_mark(sync, "ln0_done")
        for cc in range(NCK):
            ld.dma(sync.dma_start(out=xnfm[:, T * cc:T * (cc + 1)],
                                  in_=io["xn_fm"][128 * cc:128 * (cc + 1), :]))
        ld.dma(sync.dma_start(out=masks[:], in_=io["diag_masks"]))



    @block.vector
    def _(v):
        v.memset(onesr[:], 1.0)
        v.memset(sclr[:], QKS)
        v.memset(onesrb[:], 1.0)
        vc.inc(v.memset(ones64[:], 1.0))
    ctx.mark("at_memsets", vc)

    @block.gpsimd
    def _(g):
        gl.dma(g.dma_start(out=cos_sb[:], in_=io["rope_cos"]))
        gl.dma(g.dma_start(out=sin_sb[:], in_=io["rope_sin"]))
        gl.dma(g.dma_start(out=pbrow[:], in_=io["proj_b"]))


    @block.tensor
    def _(te):
        te.wait_ge(ld.h, ld.n)
        te.wait_ge(gl.h, gl.n)
        ctx.wait_mark(te, "at_memsets")
        te.matmul(ctx.psb[6][:], onesr[:], pbrow[:, 0:512], start=True, stop=True)
        tc.inc(te.matmul(ctx.psb[7][:], onesr[:], pbrow[:, 512:1024],
                         start=True, stop=True))
    ctx.mark("at_biasmm", tc)

    @block.scalar
    def _(s):
        ctx.wait_mark(s, "at_biasmm")
        s.activation(out=projb_b[:, 0:512], in_=ctx.psb[6][:], func=AF.Copy)
        sc.inc(s.activation(out=projb_b[:, 512:1024], in_=ctx.psb[7][:], func=AF.Copy))
    ctx.mark("at_bias", sc)

    ctx.mark("at_coresc", sc)
    ctx.mark("at_corevc", vc)
    ctx.mark("at_coretc", tc)

    for name in ("at_epivc", "at_avgrp", "at_pjevac", "at_expg0", "at_expg1",
                 "at_avg0", "at_avg1", "at_maskg0", "at_maskg1"):
        ctx.marks[name] = (vc.h, 0)

    for g in range(NGRP):
        qof, kof, vof = 64 * GH * g, D + 64 * GH * g, 2 * D + 64 * GH * g

        @block.gpsimd
        def _(gq, g=g):
            gq.wait_ge(gl.h, gl.n)
            if g > 0:
                ctx.wait_mark(gq, "at_qkvmm")
            base = g * NCK * 128
            for cc in range(NCK):
                gl.dma(gq.dma_start(
                    out=wpool[:, 768 * cc:768 * (cc + 1)],
                    in_=io["qkv_wg"][base + 128 * cc:base + 128 * (cc + 1), :]))
        ctx.mark("at_gw", gl)

        @block.vector
        def _(v, g=g):
            ctx.wait_mark(v, "at_coretc")
            v.memset(qn2m[:], 0.0)
            vc.inc(v.memset(
                vx.ap().rearrange("p (s c) -> p s c", c=65)[:, :, 64:65], 1.0))
        ctx.mark("at_vxones", vc)

        @block.sync
        def _(sync, g=g):
            if g > 0:
                ctx.wait_mark(sync, "at_gbias")
            pass

        @block.gpsimd
        def _(gq2, g=g):
            gq2.wait_ge(lb.h, lb.n)
            for r, off in ((0, qof), (1, kof), (2, vof)):
                lb.dma(gq2.dma_start(out=brow_g[:, 256 * r:256 * (r + 1)],
                                     in_=io["qkv_b"][0:1, off:off + 256]))
        ctx.mark("at_gbrow", lb)

        @block.tensor
        def _(te, g=g):
            te.wait_ge(ld.h, ld.n)
            te.wait_ge(lb.h, lb.n)
            ctx.wait_mark(te, "at_memsets")
            ctx.wait_mark(te, "at_coresc")
            ctx.wait_mark(te, "at_corevc")
            te.matmul(ctx.psb[0][:, 0:256], sclr[:], brow_g[:, 0:256],
                      start=True, stop=True)
            te.matmul(ctx.psb[1][:, 0:256], sclr[:], brow_g[:, 256:512],
                      start=True, stop=True)
            tc.inc(te.matmul(ctx.psb[2][:, 0:256], onesr[:], brow_g[:, 512:768],
                             start=True, stop=True))
        ctx.mark("at_gbmm", tc)

        @block.scalar
        def _(s, g=g):
            ctx.wait_mark(s, "at_gbmm")
            s.activation(out=bias_bg[:, 0:256], in_=ctx.psb[0][:, 0:256], func=AF.Copy)
            s.activation(out=bias_bg[:, 256:512], in_=ctx.psb[1][:, 0:256], func=AF.Copy)
            sc.inc(s.activation(out=bias_bg[:, 512:768], in_=ctx.psb[2][:, 0:256],
                                func=AF.Copy))
        ctx.mark("at_gbias", sc)

        for i in range(NT):
            @block.tensor
            def _(te, g=g, i=i):
                if i == 0:
                    te.wait_ge(gl.h, gl.n)
                    ctx.wait_mark(te, "at_gbias")
                else:
                    ctx.wait_mark(te, "at_qkvevac")
                mm = None
                for cc in range(NCK):
                    xn_ci = xnfm[:, T * cc + 128 * i: T * cc + 128 * (i + 1)]
                    st0, sp = (cc == 0), (cc == NCK - 1)
                    te.matmul(ctx.psb[0][:, 0:256], xn_ci,
                              wpool[:, 768 * cc + 0:768 * cc + 256],
                              start=st0, stop=sp)
                    te.matmul(ctx.psb[1][:, 0:256], xn_ci,
                              wpool[:, 768 * cc + 256:768 * cc + 512],
                              start=st0, stop=sp)
                    mm = te.matmul(ctx.psb[2][:, 0:256], xn_ci,
                                   wpool[:, 768 * cc + 512:768 * cc + 768],
                                   start=st0, stop=sp)
                tc.inc(mm)
            ctx.mark("at_qkvmm", tc)

            @block.vector
            def _(v, g=g, i=i):
                ctx.wait_mark(v, "at_qkvmm")
                ctx.wait_mark(v, "at_vxones")
                if i >= 1 or g > 0:
                    ctx.wait_mark(v, "at_trtile")
                v.scalar_tensor_tensor(
                    out=qkvf[:, 0:256], in0=ctx.psb[0][:, 0:256], scalar=QKS,
                    in1=bias_bg[:, 0:256], op0=ALU.mult, op1=ALU.add)
                v.scalar_tensor_tensor(
                    out=qkvf[:, 256:512], in0=ctx.psb[1][:, 0:256], scalar=QKS,
                    in1=bias_bg[:, 256:512], op0=ALU.mult, op1=ALU.add)
                vslots = vx.ap().rearrange("p (s c) -> p s c", c=65)[
                    :, GH * i:GH * (i + 1), 0:64]
                vc.inc(v.scalar_tensor_tensor(
                    out=vslots,
                    in0=ctx.psb[2][:, 0:256].rearrange("p (s c) -> p s c", c=64),
                    scalar=1.0,
                    in1=bias_bg.ap()[:, 512:768].rearrange("p (s c) -> p s c", c=64),
                    op0=ALU.mult, op1=ALU.add))
            ctx.mark("at_qkvevac", vc)

            @block.vector
            def _(v, g=g, i=i):
                for base in (0, 256):
                    src = qkvf.ap()[:, base:base + 256].rearrange(
                        "p (h d) -> p h d", h=GH)
                    dst = qk_bf.ap()[:, base:base + 256].rearrange(
                        "p (h d) -> p h d", h=GH)
                    r = [rsc.ap()[:, 128 * j:128 * (j + 1)].rearrange(
                        "p (h d) -> p h d", h=GH) for j in range(4)]
                    cos_t = cos_sb.ap()[:, 32 * i:32 * (i + 1)].unsqueeze(
                        1).to_broadcast([128, GH, 32])
                    sin_t = sin_sb.ap()[:, 32 * i:32 * (i + 1)].unsqueeze(
                        1).to_broadcast([128, GH, 32])
                    tops, bots = src[:, :, 0:32], src[:, :, 32:64]
                    v.wait_ge(vc.h, vc.n)
                    v.tensor_tensor(out=r[0], in0=tops, in1=cos_t, op=ALU.mult)
                    v.tensor_tensor(out=r[1], in0=bots, in1=sin_t, op=ALU.mult)
                    v.tensor_tensor(out=r[2], in0=tops, in1=sin_t, op=ALU.mult)
                    vc.inc(v.tensor_tensor(out=r[3], in0=bots, in1=cos_t, op=ALU.mult))
                    v.wait_ge(vc.h, vc.n)
                    v.tensor_tensor(out=dst[:, :, 0:32], in0=r[0], in1=r[1],
                                    op=ALU.subtract)
                    vc.inc(v.tensor_tensor(out=dst[:, :, 32:64], in0=r[2], in1=r[3],
                                           op=ALU.add))
                v.wait_ge(vc.h, vc.n)
                last = None
                for hh in range(2 * GH):
                    last = v.scalar_tensor_tensor(
                        out=rsc[:, 64 * hh:64 * (hh + 1)],
                        in0=qkvf[:, 64 * hh:64 * (hh + 1)], scalar=1.0,
                        in1=qkvf[:, 64 * hh:64 * (hh + 1)],
                        op0=ALU.mult, op1=ALU.mult, accum_out=qn2[:, hh:hh + 1])
                vc.inc(last)
                v.wait_ge(vc.h, vc.n)
                vc.inc(v.tensor_tensor(out=qn2m[:], in0=qn2m[:], in1=qn2[:],
                                       op=ALU.max))
            ctx.mark("at_rope", vc)

            for t in range(GH):
                @block.tensor
                def _(te, g=g, i=i, t=t):
                    if t == 0:
                        ctx.wait_mark(te, "at_rope")
                        if g == 0 and i == 0:
                            ctx.wait_mark(te, "prelude_ld")
                    slot = (i * GH + t) % 4
                    if g > 0 or i * GH + t >= 4:
                        ctx.wait_mark(te, f"at_trev{slot}")
                    tc.inc(te.transpose(ctx.pbf(4 + slot)[:, 0:128],
                                        qk_bf[:, 128 * t:128 * (t + 1)], ident[:]))
                ctx.mark("at_tr", tc)
                ctx.mark("at_trtile", tc)

                @block.vector
                def _(v, g=g, i=i, t=t):
                    ctx.wait_mark(v, "at_tr")
                    pv = ctx.pbf(4 + (i * GH + t) % 4)
                    if t < 2:
                        dst0, dst1 = qTx[2 * t], qTx[2 * t + 1]
                    else:
                        dst0, dst1 = kTx[2 * (t - 2)], kTx[2 * (t - 2) + 1]
                    v.tensor_copy(dst0[:, 128 * i:128 * (i + 1)], pv[0:64, 0:128])
                    vc.inc(v.tensor_copy(dst1[:, 128 * i:128 * (i + 1)],
                                         pv[64:128, 0:128]))
                ctx.mark(f"at_trev{(i * GH + t) % 4}", vc)
        ctx.mark("at_trall", vc)

        @block.vector
        def _(v, g=g):
            ctx.wait_mark(v, "at_rope")
            vc.inc(v.tensor_copy(qn2mb[:], qn2m[:]))
        ctx.mark("at_cn0", vc)

        @block.tensor
        def _(te, g=g):
            ctx.wait_mark(te, "at_cn0")
            ctx.wait_mark(te, f"at_trev0")
            ctx.wait_mark(te, f"at_trev1")
            te.transpose(ctx.pbf(4)[0:GH, 0:128], qn2mb[:, 0:GH], ident[:])
            tc.inc(te.transpose(ctx.pbf(5)[0:GH, 0:128], qn2mb[:, GH:2 * GH], ident[:]))
        ctx.mark("at_cn1", tc)

        @block.vector
        def _(v, g=g):
            ctx.wait_mark(v, "at_cn1")
            v.tensor_reduce(out=qm4[:], in_=ctx.pbf(4)[0:GH, 0:128], axis=AX.X,
                            op=ALU.max)
            vc.inc(v.tensor_reduce(out=km4[:], in_=ctx.pbf(5)[0:GH, 0:128],
                                   axis=AX.X, op=ALU.max))
            v.wait_ge(vc.h, vc.n)
            vc.inc(v.tensor_tensor(out=c4s[:], in0=qm4[:], in1=km4[:], op=ALU.add))
            v.wait_ge(vc.h, vc.n)
            vc.inc(v.tensor_scalar(out=c4b[:], in0=c4s[:], scalar1=-0.5,
                                   scalar2=None, op0=ALU.mult))
        ctx.mark("at_cn2", vc)

        @block.tensor
        def _(te, g=g):
            ctx.wait_mark(te, "at_cn2")
            tc.inc(te.transpose(ctx.pbf(6)[0:1, 0:GH], c4b[:], ident[0:GH, 0:GH]))
        ctx.mark("at_cn3", tc)

        @block.vector
        def _(v, g=g):
            ctx.wait_mark(v, "at_cn3")
            vc.inc(v.tensor_copy(crow[:], ctx.pbf(6)[0:1, 0:GH]))
        ctx.mark("at_cn4", vc)

        @block.tensor
        def _(te, g=g):
            ctx.wait_mark(te, "at_cn4")
            tc.inc(te.matmul(ctx.psb[7][:, 0:GH], onesrb[:], crow[:],
                             start=True, stop=True))
        ctx.mark("at_cn5", tc)

        @block.scalar
        def _(s, g=g):
            ctx.wait_mark(s, "at_cn5")
            ctx.wait_mark(s, "at_bias")
            sc.inc(s.activation(out=cneg[:], in_=ctx.psb[7][:, 0:GH], func=AF.Copy))
        ctx.mark("at_cn", sc)

        core_items = []
        for hh in range(GH):
            for j in range(T // 512):
                nk = 4 * j + 4
                groups = [list(range(k0, min(k0 + 3, nk))) for k0 in range(0, nk, 3)]
                for gi, kg in enumerate(groups):
                    core_items.append(("grp", hh, j, kg, nk))
                core_items.append(("epi", hh, j))

        sc0 = sc.n
        exp_end = [sc0 + idx + 1 for idx in range(len(core_items))]
        rec_mmS = [0] * len(core_items)
        rec_avmarried = None
        rec_av = [0] * len(core_items)
        rec_avlast = [0] * len(core_items)
        rec_mask = [0] * len(core_items)
        rec_oTn = [0] * len(core_items)
        rec_zb = [0] * len(core_items)

        gpar = 0
        pend = None
        first_grp = True
        lastpar_exp = {0: 0, 1: 0}
        lastpar_av = {0: 0, 1: 0}
        last_epi_idx = -1

        last_oTn_cell = [0]

        def do_av(pidx):
            hhp, jp, kgp, nkp, parp, maskp = core_pend[pidx]

            @block.tensor
            def _(te):
                te.wait_ge(sc.h, exp_end[pidx])
                if maskp:
                    te.wait_ge(vc.h, rec_mask[pidx])
                if kgp[0] == 0 and last_oTn_cell[0]:
                    te.wait_ge(vc.h, last_oTn_cell[0])
                last = None
                for idx, k in enumerate(kgp):
                    last = te.matmul(
                        ctx.psb[2][0:65, :],
                        vx.ap()[:, 65 * (GH * k + hhp):65 * (GH * k + hhp) + 65],
                        PT[3 * parp + idx][:],
                        start=(k == 0), stop=(k == nkp - 1))
                tc.inc(last)
            rec_av[pidx] = tc.n

        core_pend = {}
        for idx, item in enumerate(core_items):
            if item[0] == "grp":
                _, hh, j, kg, nk = item
                par = gpar % 2
                gpar += 1
                bankset = (0, 1, 4) if par == 0 else (5, 6, 7)
                qsl = qTx[hh][:, 512 * j:512 * (j + 1)]

                @block.tensor
                def _(te, kg=kg, par=par, bankset=bankset, qsl=qsl, hh=hh,
                      first=first_grp):
                    if first:
                        ctx.wait_mark(te, "at_cn")
                        ctx.wait_mark(te, "at_trall")
                        ctx.wait_mark(te, "at_epivc")
                    if lastpar_exp[par]:
                        te.wait_ge(sc.h, lastpar_exp[par])
                    last = None
                    for i2, k in enumerate(kg):
                        last = te.matmul(ctx.psb[bankset[i2]][:],
                                         kTx[hh][:, 128 * k:128 * (k + 1)], qsl,
                                         start=True, stop=True)
                    tc.inc(last)
                rec_mmS[idx] = tc.n
                first_grp = False
                lastpar_exp[par] = exp_end[idx]

                masked = [(i2, k) for i2, k in enumerate(kg) if k >= 4 * j]
                if masked:
                    @block.vector
                    def _(v, j=j, par=par, masked=masked, idx=idx):
                        v.wait_ge(sc.h, exp_end[idx])
                        last = None
                        for i2, k in masked:
                            m = k - 4 * j
                            last = v.tensor_tensor(
                                out=PT[3 * par + i2][:], in0=PT[3 * par + i2][:],
                                in1=masks[:, 512 * m:512 * (m + 1)], op=ALU.mult)
                        vc.inc(last)
                    rec_mask[idx] = vc.n

                core_pend[idx] = (hh, j, kg, nk, par, bool(masked))
                if pend is not None:
                    do_av(pend)
                pend = idx
            else:
                _, hh, j = item
                if pend is not None:
                    do_av(pend)
                    rec_avlast[idx] = tc.n
                    pend = None

                @block.tensor
                def _(te, idx=idx):
                    te.wait_ge(sc.h, exp_end[idx])
                    tc.inc(te.matmul(ctx.psb[3][0:64, :], ones64[:], rz[:],
                                     start=True, stop=True))
                rec_zb[idx] = tc.n

                @block.vector
                def _(v, j=j, idx=idx):
                    v.wait_ge(tc.h, rec_zb[idx])
                    vc.inc(v.reciprocal(zb[:], ctx.psb[3][0:64, :]))
                    v.wait_ge(vc.h, vc.n)
                    v.wait_ge(sto[j % 2].h, sto[j % 2].n)
                    vc.inc(v.tensor_tensor(
                        out=oTst[j % 2][:], in0=ctx.psb[2][0:64, :], in1=zb[:],
                        op=ALU.mult))
                rec_oTn[idx] = vc.n
                last_oTn_cell[0] = vc.n
                ctx.mark("at_epivc", vc)

                @block.sync
                def _(sync, g=g, hh=hh, j=j, idx=idx):
                    sync.wait_ge(vc.h, rec_oTn[idx])
                    sync.wait_ge(sto[j % 2].h, sto[j % 2].n)
                    gh = g * GH + hh
                    sto[j % 2].dma(sync.dma_start(
                        out=io["oTh"][64 * gh:64 * (gh + 1), 512 * j:512 * (j + 1)],
                        in_=oTst[j % 2][:]))
                last_epi_idx = idx

        @block.scalar
        def _(s, items=list(enumerate(core_items)), rec_mmS=rec_mmS,
              rec_av=rec_av, rec_avlast=rec_avlast, rec_oTn=rec_oTn):
            lp_av = {0: 0, 1: 0}
            prev_oTn = 0
            for idx, item in items:
                if item[0] == "grp":
                    hh2, j2, kg2, nk2 = item[1], item[2], item[3], item[4]
                    par2 = core_pend[idx][4]
                    s.wait_ge(tc.h, rec_mmS[idx])
                    if lp_av[par2]:
                        s.wait_ge(tc.h, rec_av[lp_av[par2]])
                    lp_av[par2] = idx
                    bankset2 = (0, 1, 4) if par2 == 0 else (5, 6, 7)
                    last = None
                    for i2, k in enumerate(kg2):
                        last = s.activation(out=PT[3 * par2 + i2][:],
                                            in_=ctx.psb[bankset2[i2]][:],
                                            func=AF.Exp, bias=cneg[:, hh2:hh2 + 1])
                    sc.inc(last)
                else:
                    s.wait_ge(tc.h, rec_avlast[idx])
                    if prev_oTn:
                        s.wait_ge(vc.h, prev_oTn)
                    sc.inc(s.activation(out=rz[:], in_=ctx.psb[2][64:65, :],
                                        func=AF.Copy))
                    prev_oTn = rec_oTn[idx]
            assert sc.n == sc0 + len(items)

        ctx.mark("at_coresc", sc)
        ctx.mark("at_corevc", vc)
        ctx.mark("at_coretc", tc)

    @block.gpsimd
    def _(gq):
        ctx.wait_mark(gq, "at_qkvmm")
        for cc in range(NCK):
            gl.dma(gq.dma_start(out=wpool[:, D * cc:D * (cc + 1)],
                                in_=io["proj_wT"][128 * cc:128 * (cc + 1), :]))
    ctx.mark("at_pw", gl)

    tb = ctx.sem("at_tb")

    @block.gpsimd
    def _(g):
        CHUNK = 2048
        for r0 in range(0, NEXP, CHUNK):
            tb.dma(g.dma_start(out=io["tblbf"][r0:r0 + CHUNK, 0:D],
                               in_=io["exp_down"][r0:r0 + CHUNK, :]))
            tb.dma(g.dma_start(out=io["tblbf"][r0:r0 + CHUNK, D:2 * D],
                               in_=io["exp_up"][r0:r0 + CHUNK, :]))

    for tg in range(T // 512):
        for cc in range(NCK):
            @block.sync
            def _(sync, tg=tg, cc=cc):
                if tg == 0 and cc == 0:
                    sync.wait_ge(sto[0].h, sto[0].n)
                    sync.wait_ge(sto[1].h, sto[1].n)
                idx = tg * NCK + cc
                if idx >= 2:
                    ctx.wait_mark(sync, f"at_pjld{cc % 2}")
                lo[cc % 2].dma(sync.dma_start(out=oTld[cc % 2][:],
                                      in_=io["oTh"][128 * cc:128 * (cc + 1),
                                                    512 * tg:512 * (tg + 1)]))

            @block.tensor
            def _(te, tg=tg, cc=cc):
                te.wait_ge(lo[cc % 2].h, lo[cc % 2].n)
                if tg == 0 and cc == 0:
                    ctx.wait_mark(te, "at_pw")
                if cc == 0:
                    ctx.wait_mark(te, "at_pjevac")
                last = None
                for i4 in range(4):
                    for nb in range(2):
                        last = te.matmul(
                            ctx.psb[i4 * 2 + nb][:],
                            oTld[cc % 2][:, 128 * i4:128 * (i4 + 1)],
                            wpool[:, D * cc + 512 * nb:D * cc + 512 * (nb + 1)],
                            start=(cc == 0), stop=(cc == NCK - 1))
                tc.inc(last)
            ctx.mark(f"at_pjld{cc % 2}", tc)
        ctx.mark("at_pjmm", tc)

        for i4 in range(4):
            @block.vector
            def _(v, tg=tg, i4=i4):
                if i4 == 0:
                    ctx.wait_mark(v, "at_pjmm")
                v.wait_ge(sts[i4 % 2].h, sts[i4 % 2].n)
                last = None
                for nb in range(2):
                    last = v.tensor_tensor(
                        out=s_tm[i4 % 2][:, 512 * nb:512 * (nb + 1)],
                        in0=ctx.psb[i4 * 2 + nb][:],
                        in1=projb_b[:, 512 * nb:512 * (nb + 1)], op=ALU.add)
                vc.inc(last)
            if i4 == 3:
                ctx.mark("at_pjevac", vc)
            ctx.mark("at_pjev1", vc)

            @block.sync
            def _(sync, tg=tg, i4=i4):
                ctx.wait_mark(sync, "at_pjev1")
                sts[i4 % 2].dma(sync.dma_start(
                    out=io["s"][512 * tg + 128 * i4:512 * tg + 128 * (i4 + 1), :],
                    in_=s_tm[i4 % 2][:]))

    @block.sync
    def _(sync):
        sync.wait_ge(st.h, st.n)
        for s2 in (sto[0], sto[1], sts[0], sts[1], lo[0], lo[1], lb, gl, ld):
            if s2.n:
                sync.wait_ge(s2.h, s2.n)
    ctx.mark("attn0_done", sts[1])




def emit_peer(ctx):
    nc, block = ctx.nc, ctx.block
    io = ctx.io
    pl = ctx.sem("pe_ld")
    pg = ctx.sem("pe_gl")
    lx = [ctx.sem("pe_lx0"), ctx.sem("pe_lx1")]
    gd = [ctx.sem(f"pe_gd{i}") for i in range(4)]
    gu = [ctx.sem(f"pe_gu{i}") for i in range(4)]
    vc = ctx.sem("pe_vc")
    sc = ctx.sem("pe_sc")
    tc = ctx.sem("pe_tc")
    stp = [ctx.sem("pe_st0"), ctx.sem("pe_st1")]
    ident = ctx.const["ident"]

    xnfm = _sb(ctx, "pe_xnfm", [128, NCK * T], BF16)
    wq = _sb(ctx, "pe_wq", [128, NCK * 2 * D], BF16)
    keys = _sb(ctx, "pe_keys", [128, 2 * PH * NKEYS], BF16)
    nrow = _sb(ctx, "pe_nrow", [1, D], F32)
    normw_b = _sb(ctx, "pe_normwb", [128, D], F32)
    normw_c = _sb(ctx, "pe_normwc", [128, NCK], F32)
    onesr = _sb(ctx, "pe_onesr", [1, 128], F32)
    qp = _sb(ctx, "pe_qp", [128, 16 * 512], BF16)
    sims = _sb(ctx, "pe_sims", [128, 16 * NKEYS], F32)
    xntm2 = [_sb(ctx, f"pe_xntm{i}", [128, D], BF16) for i in range(2)]
    xt2 = [_sb(ctx, f"pe_xt{i}", [128, D], BF16) for i in range(2)]
    xld2 = [_sb(ctx, f"pe_xld{i}", [128, D], F32) for i in range(2)]
    sld2 = [_sb(ctx, f"pe_sld{i}", [128, D], BF16) for i in range(2)]
    x1t2 = [_sb(ctx, f"pe_x1t{i}", [128, D], F32) for i in range(2)]
    s12 = _sb(ctx, "pe_s12", [128, 2 * PH * PK], F32)
    i12 = _sb(ctx, "pe_i12", [128, 2 * PH * PK], U32)
    ascr = _sb(ctx, "pe_ascr", [128, PH * PK * PK], F32)
    fs = _sb(ctx, "pe_fs", [128, PH * PK], F32)
    fsr = _sb(ctx, "pe_fsr", [128, PH * PK], F32)
    pk_t = _sb(ctx, "pe_pk", [128, PH * PK], U32)
    ab = _sb(ctx, "pe_ab", [128, 2 * PH * PK], U32)
    eq = _sb(ctx, "pe_eq", [128, PH * PK * PK], U32)
    isel = _sb(ctx, "pe_isel", [128, 2 * PH * PK], U32)
    fi = _sb(ctx, "pe_fi", [128, PH * PK], U32)
    iota8 = _sb(ctx, "pe_iota8", [128, PK], I32)
    hid = _sb(ctx, "pe_hid", [128, PH * PK], F32)
    hgel = _sb(ctx, "pe_hgel", [128, PH * PK], F32)
    coef = _sb(ctx, "pe_coef", [128, PH * PK], F32)
    Gdn = [_sb(ctx, f"pe_gdn{i}", [128, D], BF16) for i in range(4)]
    Gup = [_sb(ctx, f"pe_gup{i}", [128, D], BF16) for i in range(4)]
    diag = [_sb(ctx, f"pe_diag{i}", [128, 128], BF16) for i in range(2)]
    gsil = _sb(ctx, "pe_gsil", [128, D], F32)
    sqs = _sb(ctx, "pe_sqs", [128, D], BF16)
    stat = _sb(ctx, "pe_stat", [128, 4], F32)

    NSLOT = PH * PK

    @block.sync
    def _(sync):
        ctx.wait_mark(sync, "ln0_done")
        ctx.wait_mark(sync, "attn0_done")
        for cc in range(NCK):
            pl.dma(sync.dma_start(out=xnfm[:, T * cc:T * (cc + 1)],
                                  in_=io["xn_fm"][128 * cc:128 * (cc + 1), :]))
        pl.dma(sync.dma_start(out=nrow[:], in_=io["norm_w"]))
        pl.dma(sync.dma_start(
            out=normw_c[:],
            in_=io["norm_w"].rearrange("o (c p) -> p (o c)", p=128),
            allow_slow_non_contiguous=True))

    @block.gpsimd
    def _(g):
        for cc in range(NCK):
            pg.dma(g.dma_start(out=wq[:, 2 * D * cc:2 * D * (cc + 1)],
                               in_=io["wq_peerT"][128 * cc:128 * (cc + 1), :]))
        for h in range(PH):
            for s_ in range(2):
                pg.dma(g.dma_start(
                    out=keys[:, 256 * (2 * h + s_):256 * (2 * h + s_ + 1)],
                    in_=io["keysT"][h, s_]))
        pio = ctx.sem("pe_io")
        pio.inc(g.iota(iota8[:], pattern=[[1, PK]], base=0, channel_multiplier=0))
        ctx.mark("pe_iota", pio)

    @block.vector
    def _(v):
        vc.inc(v.memset(onesr[:], 1.0))
    ctx.mark("pe_ones", vc)

    @block.tensor
    def _(te):
        te.wait_ge(pl.h, pl.n)
        ctx.wait_mark(te, "pe_ones")
        te.matmul(ctx.psb[0][:], onesr[:], nrow[:, 0:512], start=True, stop=True)
        tc.inc(te.matmul(ctx.psb[1][:], onesr[:], nrow[:, 512:1024],
                         start=True, stop=True))
    ctx.mark("pe_nwmm", tc)

    @block.scalar
    def _(s):
        ctx.wait_mark(s, "pe_nwmm")
        s.activation(out=normw_b[:, 0:512], in_=ctx.psb[0][:], func=AF.Copy)
        sc.inc(s.activation(out=normw_b[:, 512:1024], in_=ctx.psb[1][:], func=AF.Copy))
    ctx.mark("pe_nwb", sc)

    @block.vector
    def _(v):
        v.wait_ge(pg.h, pg.n)
        v.wait_ge(pl.h, pl.n)
        last = None
        for cc in range(NCK):
            last = v.tensor_scalar(out=wq[:, 2 * D * cc:2 * D * (cc + 1)],
                                   in0=wq[:, 2 * D * cc:2 * D * (cc + 1)],
                                   scalar1=normw_c[:, cc:cc + 1], scalar2=None,
                                   op0=ALU.mult)
        vc.inc(last)
    ctx.mark("pe_wqf", vc)

    ctx.marks["pe_simsdone"] = (tc.h, 0)
    ctx.marks["pe_silu"] = (sc.h, 0)
    ctx.marks["pe_fin"] = (vc.h, 0)
    ctx.marks["pe_topk1"] = (vc.h, 0)

    for sg in range(T // 512):
        for qfc in range(16):
            @block.tensor
            def _(te, sg=sg, qfc=qfc):
                if qfc == 0:
                    ctx.wait_mark(te, "pe_wqf")
                    ctx.wait_mark(te, "pe_nwb")
                    ctx.wait_mark(te, "pe_simsdone")
                    ctx.wait_mark(te, "pe_silu")
                if qfc >= 2:
                    ctx.wait_mark(te, f"pe_qpev{qfc % 2}")
                last = None
                for cc in range(NCK):
                    last = te.matmul(
                        ctx.psb[qfc % 2][:],
                        wq[:, 2 * D * cc + 128 * qfc:2 * D * cc + 128 * (qfc + 1)],
                        xnfm[:, T * cc + 512 * sg:T * cc + 512 * (sg + 1)],
                        start=(cc == 0), stop=(cc == NCK - 1))
                tc.inc(last)
            ctx.mark(f"pe_qpmm{qfc % 2}", tc)

            @block.vector
            def _(v, sg=sg, qfc=qfc):
                ctx.wait_mark(v, f"pe_qpmm{qfc % 2}")
                vc.inc(v.tensor_copy(qp[:, 512 * qfc:512 * (qfc + 1)],
                                     ctx.psb[qfc % 2][:]))
            ctx.mark(f"pe_qpev{qfc % 2}", vc)
        ctx.mark("pe_qpall", vc)

        for it in range(4):
            i = sg * 4 + it
            par = i % 2

            @block.sync
            def _(sync, i=i, par=par):
                if i >= 2:
                    ctx.wait_mark(sync, f"pe_finv{par}")
                lx[par].dma(sync.dma_start(out=xntm2[par][:],
                                           in_=io["xn_tm"][128 * i:128 * (i + 1), :]))
                lx[par].dma(sync.dma_start(out=xld2[par][:],
                                           in_=io["x"][128 * i:128 * (i + 1), :]))
                lx[par].dma(sync.dma_start(out=sld2[par][:],
                                           in_=io["s"][128 * i:128 * (i + 1), :]))

            @block.scalar
            def _(s, i=i, par=par):
                s.wait_ge(lx[par].h, lx[par].n)
                if i >= 1:
                    ctx.wait_mark(s, "pe_xt")
                sc.inc(s.activation(out=sqs[:], in_=xntm2[par][:], func=AF.Square,
                                    accum_out=stat[:, 0:1]))
            ctx.mark("pe_ssq", sc)

            @block.vector
            def _(v, i=i):
                ctx.wait_mark(v, "pe_ssq")
                vc.inc(v.tensor_scalar(out=stat[:, 1:2], in0=stat[:, 0:1],
                                       scalar1=1.0 / D, scalar2=1e-6,
                                       op0=ALU.mult, op1=ALU.add))
            ctx.mark("pe_var", vc)

            @block.scalar
            def _(s, i=i):
                ctx.wait_mark(s, "pe_var")
                sc.inc(s.activation(out=stat[:, 2:3], in_=stat[:, 1:2], func=AF.Ln))
                s.wait_ge(sc.h, sc.n)
                sc.inc(s.activation(out=stat[:, 3:4], in_=stat[:, 2:3], func=AF.Exp,
                                    scale=-0.5))
            ctx.mark("pe_rstd", sc)

            @block.vector
            def _(v, i=i, par=par):
                ctx.wait_mark(v, "pe_rstd")
                ctx.wait_mark(v, "pe_nwb")
                if i >= 2:
                    ctx.wait_mark(v, f"pe_dots{par}")
                vc.inc(v.scalar_tensor_tensor(out=xt2[par][:], in0=xntm2[par][:],
                                              scalar=stat[:, 3:4], in1=normw_b[:],
                                              op0=ALU.mult, op1=ALU.mult))
            ctx.mark("pe_xt", vc)

            for hs in range(16):
                @block.tensor
                def _(te, i=i, it=it, hs=hs):
                    if hs == 0:
                        ctx.wait_mark(te, "pe_qpall")
                        ctx.wait_mark(te, "pe_topk1")
                    if hs >= 2:
                        ctx.wait_mark(te, f"pe_simev{hs % 2}")
                    tc.inc(te.matmul(
                        ctx.psb[2 + hs % 2][:, 0:256],
                        qp[:, 512 * hs + 128 * it:512 * hs + 128 * (it + 1)],
                        keys[:, 256 * hs:256 * (hs + 1)],
                        start=True, stop=True))
                ctx.mark(f"pe_simmm{hs % 2}", tc)
                if hs == 15:
                    ctx.mark("pe_simsdone", tc)

                @block.vector
                def _(v, i=i, hs=hs):
                    ctx.wait_mark(v, f"pe_simmm{hs % 2}")
                    ctx.wait_mark(v, "pe_rstd")
                    vc.inc(v.tensor_scalar(out=sims[:, 256 * hs:256 * (hs + 1)],
                                           in0=ctx.psb[2 + hs % 2][:, 0:256],
                                           scalar1=stat[:, 3:4], scalar2=None,
                                           op0=ALU.mult))
                ctx.mark(f"pe_simev{hs % 2}", vc)
            ctx.mark("pe_simsall", vc)

            @block.vector
            def _(v, i=i):
                ctx.wait_mark(v, "pe_simsall")
                last = None
                for hs in range(16):
                    last = v.max(out=s12[:, PK * hs:PK * (hs + 1)],
                                 in_=sims[:, 256 * hs:256 * (hs + 1)])
                vc.inc(last)
                v.wait_ge(vc.h, vc.n)
                for hs in range(16):
                    last = v.max_index(out=i12[:, PK * hs:PK * (hs + 1)],
                                       in_max=s12[:, PK * hs:PK * (hs + 1)],
                                       in_values=sims[:, 256 * hs:256 * (hs + 1)])
                vc.inc(last)
            ctx.mark("pe_topk1", vc)

            @block.vector
            def _(v, i=i):
                v.wait_ge(vc.h, vc.n)
                ctx.wait_mark(v, "pe_iota")
                s12r = s12.ap().rearrange("p (h s e) -> p h s e", h=PH, s=2)
                s1b = s12r[:, :, 0:1, :].transpose([0, 1, 3, 2]).to_broadcast(
                    [128, PH, PK, PK])
                s2b = s12r[:, :, 1:2, :].to_broadcast([128, PH, PK, PK])
                vc.inc(v.tensor_tensor(
                    out=ascr.ap().rearrange("p (h a b) -> p h a b", h=PH, a=PK),
                    in0=s1b, in1=s2b, op=ALU.add))
                v.wait_ge(vc.h, vc.n)
                last = None
                for h in range(PH):
                    last = v.max(out=fs[:, PK * h:PK * (h + 1)],
                                 in_=ascr[:, 64 * h:64 * (h + 1)])
                vc.inc(last)
                v.wait_ge(vc.h, vc.n)
                for h in range(PH):
                    last = v.max_index(out=pk_t[:, PK * h:PK * (h + 1)],
                                       in_max=fs[:, PK * h:PK * (h + 1)],
                                       in_values=ascr[:, 64 * h:64 * (h + 1)])
                vc.inc(last)
                v.wait_ge(vc.h, vc.n)
                v.tensor_scalar(out=ab[:, 0:NSLOT], in0=pk_t[:], scalar1=3,
                                scalar2=None, op0=ALU.logical_shift_right)
                vc.inc(v.tensor_scalar(out=ab[:, NSLOT:2 * NSLOT], in0=pk_t[:],
                                       scalar1=7, scalar2=None, op0=ALU.bitwise_and))
                v.wait_ge(vc.h, vc.n)
                i12r = i12.ap().rearrange("p (h s e) -> p h s e", h=PH, s=2)
                iob = iota8.ap().bitcast(U32).unsqueeze(1).unsqueeze(1).to_broadcast(
                    [128, PH, PK, PK])
                for half in range(2):
                    abr = ab.ap()[:, NSLOT * half:NSLOT * (half + 1)].rearrange(
                        "p (h s) -> p h s", h=PH).unsqueeze(3).to_broadcast(
                        [128, PH, PK, PK])
                    vc.inc(v.tensor_tensor(
                        out=eq.ap().rearrange("p (h a b) -> p h a b", h=PH, a=PK),
                        in0=abr, in1=iob, op=ALU.is_equal))
                    v.wait_ge(vc.h, vc.n)
                    ivals = i12r[:, :, half:half + 1, :].to_broadcast(
                        [128, PH, PK, PK])
                    vc.inc(v.tensor_tensor(
                        out=eq.ap().rearrange("p (h a b) -> p h a b", h=PH, a=PK),
                        in0=eq.ap().rearrange("p (h a b) -> p h a b", h=PH, a=PK),
                        in1=ivals, op=ALU.mult))
                    v.wait_ge(vc.h, vc.n)
                    with nc.allow_low_precision(reason="u32 index reduce"):
                        vc.inc(v.tensor_reduce(
                            out=isel.ap()[:, NSLOT * half:NSLOT * (half + 1)].rearrange(
                                "p (h s) -> p h s", h=PH),
                            in_=eq.ap().rearrange("p (h a b) -> p h a b", h=PH, a=PK),
                            axis=AX.X, op=ALU.add))
                    v.wait_ge(vc.h, vc.n)
                vc.inc(v.scalar_tensor_tensor(
                    out=fi[:], in0=isel[:, 0:NSLOT], scalar=NKEYS,
                    in1=isel[:, NSLOT:2 * NSLOT],
                    op0=ALU.mult, op1=ALU.add))
                vc.inc(v.tensor_scalar(out=fsr[:], in0=fs[:], scalar1=0.0,
                                       scalar2=None, op0=ALU.max))
            ctx.mark("pe_fi", vc)

            vdots0 = vc.n
            @block.gpsimd
            def _(g, i=i, vdots0=vdots0):
                ctx.wait_mark(g, "pe_fi")
                for j in range(NSLOT):
                    if j >= 4:
                        g.wait_ge(vc.h, vdots0 + (j - 4) + 1)
                    gd[j % 4].dma(g.indirect_dma_start(
                        out=Gdn[j % 4][:], out_offset=None, in_=io["tblbf"],
                        in_offset=IndirectOffsetOnAxis(
                            ap=fi.ap()[:, j:j + 1].bitcast(I32), axis=0)))

            @block.vector
            def _(v, i=i, par=par, vdots0=vdots0):
                ctx.wait_mark(v, "pe_xt")
                for j in range(NSLOT):
                    v.wait_ge(gd[j % 4].h, 16 * (NSLOT // 4) * i + 16 * (j // 4 + 1)
                              if False else gd[j % 4].n - 16 * (NSLOT // 4) + 16 * (j // 4 + 1))
                    vc.inc(v.scalar_tensor_tensor(
                        out=Gdn[j % 4][:], in0=Gdn[j % 4][:], scalar=1.0,
                        in1=xt2[par][:], op0=ALU.mult, op1=ALU.mult,
                        accum_out=hid[:, j:j + 1]))
            assert vc.n == vdots0 + NSLOT
            ctx.mark(f"pe_dots{par}", vc)
            ctx.mark("pe_dots", vc)

            @block.scalar
            def _(s, i=i):
                ctx.wait_mark(s, "pe_dots")
                sc.inc(s.activation(out=hgel[:], in_=hid[:], func=AF.Gelu))
            ctx.mark("pe_gelu", sc)

            @block.vector
            def _(v, i=i):
                ctx.wait_mark(v, "pe_gelu")
                vc.inc(v.tensor_tensor(out=coef[:], in0=hgel[:], in1=fsr[:],
                                       op=ALU.mult))
            ctx.mark("pe_coef", vc)

            tcu0 = tc.n
            vcu0 = vc.n
            @block.gpsimd
            def _(g, i=i, tcu0=tcu0):
                ctx.wait_mark(g, "pe_fi")
                g.wait_ge(tc.h, tcu0)
                for j in range(NSLOT):
                    if j >= 4:
                        g.wait_ge(tc.h, tcu0 + (j - 4) + 1)
                    gu[j % 4].dma(g.indirect_dma_start(
                        out=Gup[j % 4][:], out_offset=None, in_=io["tblbf"],
                        in_offset=IndirectOffsetOnAxis(
                            ap=fi.ap()[:, j:j + 1].bitcast(I32), axis=0),
                        element_offset=D))
            gu_base = [gu[r].n - 16 * (NSLOT // 4) for r in range(4)]

            @block.vector
            def _(v, i=i, tcu0=tcu0):
                ctx.wait_mark(v, "pe_coef")
                ctx.wait_mark(v, "prelude_ld")
                for j in range(NSLOT):
                    if j >= 2:
                        v.wait_ge(tc.h, tcu0 + (j - 2) + 1)
                    vc.inc(v.tensor_scalar(out=diag[j % 2][:], in0=ident[:],
                                           scalar1=coef[:, j:j + 1], scalar2=None,
                                           op0=ALU.mult))
            assert vc.n == vcu0 + NSLOT

            @block.tensor
            def _(te, i=i, tcu0=tcu0, vcu0=vcu0, gu_base=gu_base):
                ctx.wait_mark(te, "pe_silu")
                for j in range(NSLOT):
                    te.wait_ge(vc.h, vcu0 + j + 1)
                    te.wait_ge(gu[j % 4].h, gu_base[j % 4] + 16 * (j // 4 + 1))
                    te.matmul(ctx.psb[0][:], diag[j % 2][:], Gup[j % 4][:, 0:512],
                              start=(j == 0), stop=(j == NSLOT - 1))
                    mm = te.matmul(ctx.psb[1][:], diag[j % 2][:],
                                   Gup[j % 4][:, 512:1024],
                                   start=(j == 0), stop=(j == NSLOT - 1))
                    tc.inc(mm)
            assert tc.n == tcu0 + NSLOT
            ctx.mark("pe_upmm", tc)

            @block.scalar
            def _(s, i=i):
                ctx.wait_mark(s, "pe_upmm")
                ctx.wait_mark(s, "pe_fin")
                sc.inc(s.activation(out=gsil[:], in_=ctx.psbt[0].ap(), func=AF.Silu))
            ctx.mark("pe_silu", sc)

            @block.vector
            def _(v, i=i, par=par):
                ctx.wait_mark(v, "pe_silu")
                v.wait_ge(lx[par].h, lx[par].n)
                v.wait_ge(stp[par].h, stp[par].n)
                vc.inc(v.tensor_tensor(out=gsil[:], in0=sld2[par][:], in1=gsil[:],
                                       op=ALU.mult))
                v.wait_ge(vc.h, vc.n)
                vc.inc(v.tensor_tensor(out=x1t2[par][:], in0=xld2[par][:],
                                       in1=gsil[:], op=ALU.add))
            ctx.mark("pe_fin", vc)
            ctx.mark(f"pe_finv{par}", vc)

            @block.sync
            def _(sync, i=i, par=par):
                ctx.wait_mark(sync, f"pe_finv{par}")
                stp[par].dma(sync.dma_start(
                    out=io["x1"][128 * i:128 * (i + 1), :], in_=x1t2[par][:]))

    @block.sync
    def _(sync):
        for s2 in [stp[0], stp[1], lx[0], lx[1], pl, pg] + gd + gu:
            if s2.n:
                sync.wait_ge(s2.h, s2.n)
    ctx.mark("peer_done", stp[1])




def emit_moe(ctx):
    nc, block = ctx.nc, ctx.block
    io = ctx.io
    ml = ctx.sem("mo_ld")
    mg = ctx.sem("mo_gl")
    ma4 = [ctx.sem(f"mo_ma{i}") for i in range(4)]
    lf = [ctx.sem("mo_lf0"), ctx.sem("mo_lf1")]
    vc = ctx.sem("mo_vc")
    sc = ctx.sem("mo_sc")
    tc = ctx.sem("mo_tc")
    mst = [ctx.sem("mo_st0"), ctx.sem("mo_st1")]

    xnfm = _sb(ctx, "mo_xnfm", [128, NCK * T], BF16)
    rtr = _sb(ctx, "mo_rtr", [128, NCK * NE], BF16)
    wg_sb = _sb(ctx, "mo_wg", [128, NCK * FF], BF16)
    wu_sb = _sb(ctx, "mo_wu", [128, NCK * FF], BF16)
    wd_sb = _sb(ctx, "mo_wd", [128, (FF // 128) * D], BF16)
    fuse = _sb(ctx, "mo_fuse", [128, (FF // 128) * 512], BF16)
    sil = [_sb(ctx, f"mo_sil{i}", [128, 512], BF16) for i in range(2)]
    wfull = _sb(ctx, "mo_wfull", [128, NT * NE], F32)
    rsc = _sb(ctx, "mo_rsc", [128, 2 * NE + 8], F32)
    iota8 = _sb(ctx, "mo_iota8", [128, NE], I32)
    pv = _sb(ctx, "mo_pv", [128, 8], F32)
    pidx = _sb(ctx, "mo_pidx", [128, 8], U32)
    eqf = _sb(ctx, "mo_eqf", [128, NE], F32)
    scal = [_sb(ctx, f"mo_scal{i}", [128, D], F32) for i in range(4)]
    acld = scal
    gs = scal[2]
    x1ld = [scal[3], _sb(ctx, "mo_x1ld1", [128, D], F32)]
    s1ld = [_sb(ctx, f"mo_s1ld{i}", [128, D], BF16) for i in range(2)]

    NFC = FF // 128

    @block.sync
    def _(sync):
        ctx.wait_mark(sync, "ln1_done")
        for cc in range(NCK):
            ml.dma(sync.dma_start(out=xnfm[:, T * cc:T * (cc + 1)],
                                  in_=io["xn1_fm"][128 * cc:128 * (cc + 1), :]))

    @block.gpsimd
    def _(g):
        for cc in range(NCK):
            mg.dma(g.dma_start(out=rtr[:, NE * cc:NE * (cc + 1)],
                               in_=io["router"][128 * cc:128 * (cc + 1), :]))
        mio = ctx.sem("mo_io")
        mio.inc(g.iota(iota8[:], pattern=[[1, NE]], base=0, channel_multiplier=0))
        ctx.mark("mo_iota", mio)

    ctx.marks["mo_wfv"] = (vc.h, 0)
    ctx.marks["mo_sm2"] = (sc.h, 0)
    for i in range(NT):
        @block.tensor
        def _(te, i=i):
            te.wait_ge(ml.h, ml.n)
            te.wait_ge(mg.h, 16 * NCK)
            ctx.wait_mark(te, "mo_wfv")
            last = None
            for cc in range(NCK):
                last = te.matmul(ctx.psb[4][:, 0:NE],
                                 xnfm[:, T * cc + 128 * i:T * cc + 128 * (i + 1)],
                                 rtr[:, NE * cc:NE * (cc + 1)],
                                 start=(cc == 0), stop=(cc == NCK - 1))
            tc.inc(last)
        ctx.mark("mo_rmm", tc)

        @block.vector
        def _(v, i=i):
            ctx.wait_mark(v, "mo_rmm")
            ctx.wait_mark(v, "mo_sm2")
            vc.inc(v.tensor_reduce(out=rsc[:, 0:1], in_=ctx.psb[4][:, 0:NE],
                                   axis=AX.X, op=ALU.max))
            v.wait_ge(vc.h, vc.n)
            vc.inc(v.tensor_scalar(out=rsc[:, 8:8 + NE], in0=ctx.psb[4][:, 0:NE],
                                   scalar1=rsc[:, 0:1], scalar2=None,
                                   op0=ALU.subtract))
        ctx.mark("mo_sm1", vc)

        @block.scalar
        def _(s, i=i):
            ctx.wait_mark(s, "mo_sm1")
            sc.inc(s.activation(out=rsc[:, 8:8 + NE], in_=rsc[:, 8:8 + NE],
                                func=AF.Exp, accum_out=rsc[:, 1:2]))
        ctx.mark("mo_sm2", sc)

        @block.vector
        def _(v, i=i):
            ctx.wait_mark(v, "mo_sm2")
            ctx.wait_mark(v, "mo_iota")
            vc.inc(v.reciprocal(rsc[:, 2:3], rsc[:, 1:2]))
            v.wait_ge(vc.h, vc.n)
            vc.inc(v.tensor_scalar(out=rsc[:, 8:8 + NE], in0=rsc[:, 8:8 + NE],
                                   scalar1=rsc[:, 2:3], scalar2=None, op0=ALU.mult))
            v.wait_ge(vc.h, vc.n)
            vc.inc(v.max(out=pv[:], in_=rsc[:, 8:8 + NE]))
            v.wait_ge(vc.h, vc.n)
            vc.inc(v.max_index(out=pidx[:], in_max=pv[:], in_values=rsc[:, 8:8 + NE]))
            v.wait_ge(vc.h, vc.n)
            vc.inc(v.tensor_tensor(out=rsc[:, 3:4], in0=pv[:, 1:2], in1=pv[:, 0:1],
                                   op=ALU.subtract))
        ctx.mark("mo_nw1", vc)

        @block.scalar
        def _(s, i=i):
            ctx.wait_mark(s, "mo_nw1")
            sc.inc(s.activation(out=rsc[:, 4:5], in_=rsc[:, 3:4], func=AF.Exp))
        ctx.mark("mo_nw2", sc)

        @block.vector
        def _(v, i=i):
            ctx.wait_mark(v, "mo_nw2")
            vc.inc(v.tensor_scalar(out=rsc[:, 5:6], in0=rsc[:, 4:5], scalar1=1.0,
                                   scalar2=None, op0=ALU.add))
            v.wait_ge(vc.h, vc.n)
            vc.inc(v.reciprocal(rsc[:, 6:7], rsc[:, 5:6]))
            v.wait_ge(vc.h, vc.n)
            vc.inc(v.tensor_tensor(out=rsc[:, 7:8], in0=rsc[:, 4:5], in1=rsc[:, 6:7],
                                   op=ALU.mult))
            v.wait_ge(vc.h, vc.n)
            vc.inc(v.tensor_tensor(out=eqf[:],
                                   in0=iota8.ap().bitcast(U32),
                                   in1=pidx.ap()[:, 0:1].to_broadcast([128, NE]),
                                   op=ALU.is_equal))
            v.wait_ge(vc.h, vc.n)
            vc.inc(v.tensor_scalar(out=wfull[:, NE * i:NE * (i + 1)], in0=eqf[:],
                                   scalar1=rsc[:, 6:7], scalar2=None, op0=ALU.mult))
            v.wait_ge(vc.h, vc.n)
            vc.inc(v.tensor_tensor(out=eqf[:],
                                   in0=iota8.ap().bitcast(U32),
                                   in1=pidx.ap()[:, 1:2].to_broadcast([128, NE]),
                                   op=ALU.is_equal))
            v.wait_ge(vc.h, vc.n)
            vc.inc(v.scalar_tensor_tensor(out=wfull[:, NE * i:NE * (i + 1)],
                                          in0=eqf[:], scalar=rsc[:, 7:8],
                                          in1=wfull[:, NE * i:NE * (i + 1)],
                                          op0=ALU.mult, op1=ALU.add))
        ctx.mark("mo_wfv", vc)
    ctx.mark("mo_wfall", vc)

    ctx.marks["mo_dnev"] = (vc.h, 0)
    ctx.marks["mo_fuz0"] = (vc.h, 0)
    ctx.marks["mo_fuz1"] = (vc.h, 0)
    ctx.marks["mo_sil0"] = (sc.h, 0)
    ctx.marks["mo_sil1"] = (sc.h, 0)
    for e in range(NE):
        @block.gpsimd
        def _(g, e=e):
            g.wait_ge(mg.h, mg.n)
            if e > 0:
                ctx.wait_mark(g, "mo_gumm")
            for cc in range(NCK):
                mg.dma(g.dma_start(out=wg_sb[:, FF * cc:FF * (cc + 1)],
                                   in_=io["Wg"][e, 128 * cc:128 * (cc + 1), :]))
                mg.dma(g.dma_start(out=wu_sb[:, FF * cc:FF * (cc + 1)],
                                   in_=io["Wu"][e, 128 * cc:128 * (cc + 1), :]))
            if e > 0:
                ctx.wait_mark(g, "mo_dnmm")
            for fc in range(NFC):
                mg.dma(g.dma_start(out=wd_sb[:, D * fc:D * (fc + 1)],
                                   in_=io["Wd"][e, 128 * fc:128 * (fc + 1), :]))
        ctx.mark("mo_wld", mg)

        for tq in range(T // 512):
            sc0m = sc.n
            sil_end = [sc0m + fc + 1 for fc in range(NFC)]
            rec_gmm = [0] * NFC
            rec_fuz = [0] * NFC
            for fc in range(NFC):
                bg, bu = 2 * (fc % 2), 1 + 2 * (fc % 2)
                @block.tensor
                def _(te, e=e, tq=tq, fc=fc, bg=bg, bu=bu):
                    if fc == 0 and tq == 0:
                        te.wait_ge(mg.h, mg.n)
                        ctx.wait_mark(te, "mo_wfall")
                    if fc < 2:
                        ctx.wait_mark(te, "mo_dnev")
                    if fc >= 2:
                        te.wait_ge(sc.h, sil_end[fc - 2])
                        te.wait_ge(vc.h, rec_fuz[fc - 2])
                    lastg = lastu = None
                    for cc in range(NCK):
                        xs = xnfm[:, T * cc + 512 * tq:T * cc + 512 * (tq + 1)]
                        lastg = te.matmul(ctx.psb[bg][:],
                                          wg_sb[:, FF * cc + 128 * fc:FF * cc + 128 * (fc + 1)],
                                          xs, start=(cc == 0), stop=(cc == NCK - 1))
                    tc.inc(lastg)
                    rec_gmm[fc] = tc.n
                    for cc in range(NCK):
                        xs = xnfm[:, T * cc + 512 * tq:T * cc + 512 * (tq + 1)]
                        lastu = te.matmul(ctx.psb[bu][:],
                                          wu_sb[:, FF * cc + 128 * fc:FF * cc + 128 * (fc + 1)],
                                          xs, start=(cc == 0), stop=(cc == NCK - 1))
                    tc.inc(lastu)
                ctx.mark("mo_gumm", tc)
                ctx.mark(f"mo_gumm{fc % 2}", tc)

                @block.vector
                def _(v, e=e, tq=tq, fc=fc, bu=bu):
                    ctx.wait_mark(v, f"mo_gumm{fc % 2}")
                    v.wait_ge(sc.h, sil_end[fc])
                    vc.inc(v.tensor_tensor(out=fuse[:, 512 * fc:512 * (fc + 1)],
                                           in0=ctx.psb[bu][:], in1=sil[fc % 2][:],
                                           op=ALU.mult))
                rec_fuz[fc] = vc.n
                ctx.mark(f"mo_fuz{fc % 2}", vc)
                if fc == NFC - 1:
                    ctx.mark("mo_fuseall", vc)

            @block.scalar
            def _(s, e=e, tq=tq, rec_gmm=rec_gmm, rec_fuz=rec_fuz):
                for fc in range(NFC):
                    s.wait_ge(tc.h, rec_gmm[fc])
                    if fc >= 2:
                        s.wait_ge(vc.h, rec_fuz[fc - 2])
                    sc.inc(s.activation(out=sil[fc % 2][:],
                                        in_=ctx.psb[2 * (fc % 2)][:], func=AF.Silu))
            assert sc.n == sc0m + NFC
            ctx.mark("mo_silall", sc)

            @block.tensor
            def _(te, e=e, tq=tq):
                ctx.wait_mark(te, "mo_fuseall")
                ctx.wait_mark(te, "mo_dnev")
                last = None
                for i4 in range(4):
                    for nb in range(2):
                        for fc in range(NFC):
                            last = te.matmul(
                                ctx.psb[i4 * 2 + nb][:],
                                fuse[:, 512 * fc + 128 * i4:512 * fc + 128 * (i4 + 1)],
                                wd_sb[:, D * fc + 512 * nb:D * fc + 512 * (nb + 1)],
                                start=(fc == 0), stop=(fc == NFC - 1))
                tc.inc(last)
            ctx.mark("mo_dnmm", tc)

            for i4h in range(2):
                @block.vector
                def _(v, e=e, tq=tq, i4h=i4h):
                    ctx.wait_mark(v, "mo_dnmm")
                    for i4 in (2 * i4h, 2 * i4h + 1):
                        v.wait_ge(ma4[i4].h, ma4[i4].n)
                    last = None
                    for i4 in (2 * i4h, 2 * i4h + 1):
                        ti = 4 * tq + i4
                        for nb in range(2):
                            last = v.tensor_scalar(
                                out=scal[i4][:, 512 * nb:512 * (nb + 1)],
                                in0=ctx.psb[i4 * 2 + nb][:],
                                scalar1=wfull[:, NE * ti + e:NE * ti + e + 1],
                                scalar2=None, op0=ALU.mult)
                    vc.inc(last)
                ctx.mark("mo_scev", vc)

                @block.gpsimd
                def _(g, e=e, tq=tq, i4h=i4h):
                    ctx.wait_mark(g, "mo_scev")
                    if e > 0 and tq == 0 and i4h == 0:
                        for s4 in ma4:
                            g.wait_ge(s4.h, s4.n)
                    for i4 in (2 * i4h, 2 * i4h + 1):
                        ti = 4 * tq + i4
                        op = {}
                        if e > 0:
                            op = dict(accum_op=ALU.add)
                        ma4[i4].dma(g.dma_start(
                            out=io["moeacc"][128 * ti:128 * (ti + 1), :],
                            in_=scal[i4][:], **op))
            ctx.mark("mo_dnev", vc)

    for i in range(NT):
        par = i % 2

        @block.sync
        def _(sync, i=i, par=par):
            if i == 0:
                for s4 in ma4:
                    sync.wait_ge(s4.h, s4.n)
            if i >= 2:
                ctx.wait_mark(sync, f"mo_x2v{par}")
                sync.wait_ge(mst[par].h, mst[par].n)
            lf[par].dma(sync.dma_start(out=acld[par][:],
                                       in_=io["moeacc"][128 * i:128 * (i + 1), :]))
            lf[par].dma(sync.dma_start(out=x1ld[par][:],
                                       in_=io["x1"][128 * i:128 * (i + 1), :]))
            lf[par].dma(sync.dma_start(out=s1ld[par][:],
                                       in_=io["s1"][128 * i:128 * (i + 1), :]))

        @block.scalar
        def _(s, i=i, par=par):
            s.wait_ge(lf[par].h, lf[par].n)
            if i >= 1:
                ctx.wait_mark(s, "mo_x2s")
            sc.inc(s.activation(out=gs[:], in_=acld[par][:], func=AF.Silu))
        ctx.mark("mo_gsil", sc)

        @block.vector
        def _(v, i=i, par=par):
            ctx.wait_mark(v, "mo_gsil")
            v.wait_ge(mst[par].h, mst[par].n)
            vc.inc(v.tensor_tensor(out=gs[:], in0=s1ld[par][:], in1=gs[:],
                                   op=ALU.mult))
            v.wait_ge(vc.h, vc.n)
            vc.inc(v.tensor_tensor(out=x1ld[par][:], in0=x1ld[par][:], in1=gs[:],
                                   op=ALU.add))
        ctx.mark("mo_x2s", vc)
        ctx.mark(f"mo_x2v{par}", vc)

        @block.sync
        def _(sync, i=i, par=par):
            ctx.wait_mark(sync, f"mo_x2v{par}")
            mst[par].dma(sync.dma_start(out=io["out"][128 * i:128 * (i + 1), :],
                                        in_=x1ld[par][:]))

    @block.sync
    def _(sync):
        for s2 in [mst[0], mst[1], lf[0], lf[1], ml, mg] + ma4:
            if s2.n:
                sync.wait_ge(s2.h, s2.n)
    ctx.mark("moe_done", mst[1])




def emit_delta(ctx):
    nc, block = ctx.nc, ctx.block
    io = ctx.io
    dl = ctx.sem("dk_ld")
    dg = ctx.sem("dk_gl")
    vc = ctx.sem("dk_vc")
    sc = ctx.sem("dk_sc")
    tc = ctx.sem("dk_tc")
    dst = [ctx.sem("dk_st0"), ctx.sem("dk_st1")]
    lp = [ctx.sem("dk_lp0"), ctx.sem("dk_lp1")]
    lq = [ctx.sem("dk_lq0"), ctx.sem("dk_lq1")]
    pst = [ctx.sem("dk_pst0"), ctx.sem("dk_pst1")]
    ident = ctx.const["ident"]

    xnfm = _sb(ctx, "dk_xnfm", [128, NCK * T], BF16)
    vfm = _sb(ctx, "dk_vfm", [128, NCK * T], BF16)
    HB = NCHK // 2 * DH
    betaC = _sb(ctx, "dk_betaC", [64, NCHK * DH], F32)
    vtm = [_sb(ctx, f"dk_vtm{h}", [64, DHD], BF16) for h in range(DH)]
    wpool = _sb(ctx, "dk_wpool", [128, NCK * D], BF16)
    wb_sb = _sb(ctx, "dk_wb", [128, NCK * DH], BF16)
    drow = _sb(ctx, "dk_drow", [1, DH], F32)
    lgrow = _sb(ctx, "dk_lgrow", [1, DH], F32)
    beta_tm = _sb(ctx, "dk_beta", [128, NT * DH], F32)
    onesr = _sb(ctx, "dk_onesr", [1, 128], F32)
    onesrb = _sb(ctx, "dk_onesrb", [1, 128], BF16)
    onescb = _sb(ctx, "dk_onescb", [128, 1], BF16)
    iopm = _sb(ctx, "dk_iopm", [64, 64], F32)
    iocol = _sb(ctx, "dk_iocol", [64, 1], F32)
    iorow = _sb(ctx, "dk_iorow", [1, 64], F32)
    lgc64 = _sb(ctx, "dk_lgc64", [64, DH], F32)
    lgc128 = _sb(ctx, "dk_lgc128", [128, DH], F32)
    negLs = _sb(ctx, "dk_negLs", [64, DH * 64], BF16)
    LTt = _sb(ctx, "dk_LTt", [64, DH * 64], BF16)
    de_col = _sb(ctx, "dk_decol", [64, DH], F32)
    dw_col = _sb(ctx, "dk_dwcol", [64, DH], F32)
    ctd128 = _sb(ctx, "dk_ctd", [128, DH], F32)
    de_row = _sb(ctx, "dk_derow", [1, DH * 64], F32)
    de_b = _sb(ctx, "dk_deb", [128, DH * 64], BF16)
    t64a = _sb(ctx, "dk_t64a", [64, 64], F32)
    t64b = _sb(ctx, "dk_t64b", [64, 64], F32)
    rst_row = _sb(ctx, "dk_rstrow", [1, 512], BF16)

    S_f = [_sb(ctx, f"dk_Sf{h}", [128, DHD], F32) for h in range(DH)]
    S_b = [_sb(ctx, f"dk_Sb{h}", [128, DHD], BF16) for h in range(DH)]
    kk_s = [_sb(ctx, f"dk_kks{h}", [64, 64], F32) for h in range(DH)]
    A0 = [_sb(ctx, f"dk_A0{h}", [64, 64], BF16) for h in range(DH)]
    atT = [_sb(ctx, f"dk_atT{h}", [64, 64], BF16) for h in range(DH)]
    Xb = [_sb(ctx, f"dk_X{h}", [64, 64], BF16) for h in range(DH)]
    Pb = [_sb(ctx, f"dk_P{h}", [64, 64], BF16) for h in range(DH)]
    PTs = [_sb(ctx, f"dk_PT{h}", [64, 64], BF16) for h in range(DH)]
    ATs = [_sb(ctx, f"dk_AT{h}", [64, 64], BF16) for h in range(DH)]
    ktm = [_sb(ctx, f"dk_ktm{h}", [64, DHD], BF16) for h in range(DH)]
    kbde = [_sb(ctx, f"dk_kbde{h}", [64, DHD], BF16) for h in range(DH)]
    kdw = [_sb(ctx, f"dk_kdw{h}", [64, DHD], BF16) for h in range(DH)]
    kdeT = [_sb(ctx, f"dk_kdeT{h}", [128, 64], BF16) for h in range(DH)]
    Yv_s = [_sb(ctx, f"dk_Yv{h}", [64, DHD], F32) for h in range(DH)]
    kcd = [_sb(ctx, f"dk_kcd{h}", [64, DHD], BF16) for h in range(DH)]
    kcdT = [_sb(ctx, f"dk_kcdT{h}", [128, 64], BF16) for h in range(DH)]
    vnew = [_sb(ctx, f"dk_vnew{h}", [64, DHD], BF16) for h in range(DH)]
    o_s = [_sb(ctx, f"dk_os{h}", [64, DHD], BF16) for h in range(DH)]
    oTst = [[_sb(ctx, f"dk_oTst{p}_{h}", [128, 64], BF16) for h in range(DH)]
            for p in range(2)]
    xnld = [_sb(ctx, f"dk_xnld{i}", [128, D], BF16) for i in range(2)]
    oTld = [_sb(ctx, f"dk_oTld{i}", [128, 512], BF16) for i in range(2)]
    s1t = [_sb(ctx, f"dk_s1t{i}", [128, D], BF16) for i in range(2)]

    def kk_ps(h):
        return ctx.psbt[2].ap()[0:64, 64 * h:64 * (h + 1)]
    def nwx_ps(h):
        return ctx.psbt[0].ap()[0:64, 64 * h:64 * (h + 1)]
    def nwp_ps(h):
        return ctx.psbt[0].ap()[0:64, 512 + 64 * h:512 + 64 * (h + 1)]
    def yv_ps(h):
        return ctx.psbt[1].ap()[0:64, 128 * h:128 * (h + 1)]
    def o_ps(h):
        return ctx.psbt[2].ap()[0:64, 128 * h:128 * (h + 1)]
    def dW_ps(h):
        return ctx.psbt[0].ap()[:, 128 * h:128 * (h + 1)]
    def ktr_ps(h):
        return ctx.pbf(6)[0:64, 128 * h:128 * h + 128]
    def vtr_ps(h):
        return ctx.pbf(6)[0:64, 128 * h:128 * h + 128]
    def ptr_ps(h):
        return ctx.pbf(6)[0:64, 64 * h:64 * (h + 1)]
    def ttr_ps(h):
        return ctx.pbf(7)[:, 64 * h:64 * (h + 1)]

    @block.sync
    def _(sync):
        ctx.wait_mark(sync, "ln1_done")
        for cc in range(NCK):
            dl.dma(sync.dma_start(out=xnfm[:, T * cc:T * (cc + 1)],
                                  in_=io["xn1_fm"][128 * cc:128 * (cc + 1), :]))
        dl.dma(sync.dma_start(out=drow[:], in_=io["decay"]))

    @block.gpsimd
    def _(g):
        for cc in range(NCK):
            dg.dma(g.dma_start(out=wpool[:, D * cc:D * (cc + 1)],
                               in_=io["WwT"][128 * cc:128 * (cc + 1), :]))
            dg.dma(g.dma_start(out=wb_sb[:, DH * cc:DH * (cc + 1)],
                               in_=io["WbT"][128 * cc:128 * (cc + 1), :]))
        dio = ctx.sem("dk_io")
        g.iota(iopm[:], pattern=[[-1, 64]], base=0, channel_multiplier=1,
               allow_small_or_imprecise_dtypes=True)
        g.iota(iocol[:], pattern=[[0, 1]], base=0, channel_multiplier=1,
               allow_small_or_imprecise_dtypes=True)
        dio.inc(g.iota(iorow[:], pattern=[[1, 64]], base=0, channel_multiplier=0,
                       allow_small_or_imprecise_dtypes=True))
        ctx.mark("dk_iota", dio)

    @block.vector
    def _(v):
        v.memset(onesr[:], 1.0)
        v.memset(onesrb[:], 1.0)
        vc.inc(v.memset(onescb[:], 1.0))
        for h in range(DH):
            v.memset(S_f[h][:], 0.0)
            vc.inc(v.memset(S_b[h][:], 0.0))
    ctx.mark("dk_ones", vc)

    @block.scalar
    def _(s):
        s.wait_ge(dl.h, dl.n)
        sc.inc(s.activation(out=lgrow[:], in_=drow[:], func=AF.Sigmoid))
        s.wait_ge(sc.h, sc.n)
        sc.inc(s.activation(out=lgrow[:], in_=lgrow[:], func=AF.Ln))
    ctx.mark("dk_lg", sc)

    @block.tensor
    def _(te):
        ctx.wait_mark(te, "dk_lg")
        ctx.wait_mark(te, "dk_ones")
        te.matmul(ctx.psb[0][0:64, 0:DH], onesr[:, 0:64], lgrow[:], start=True, stop=True)
        tc.inc(te.matmul(ctx.psb[1][:, 0:DH], onesr[:], lgrow[:], start=True, stop=True))
    ctx.mark("dk_lgmm", tc)

    @block.scalar
    def _(s):
        ctx.wait_mark(s, "dk_lgmm")
        s.activation(out=lgc64[:], in_=ctx.psb[0][0:64, 0:DH], func=AF.Copy)
        sc.inc(s.activation(out=lgc128[:], in_=ctx.psb[1][:, 0:DH], func=AF.Copy))
    ctx.mark("dk_lgc", sc)

    for h in range(DH):
        @block.vector
        def _(v, h=h):
            if h == 0:
                ctx.wait_mark(v, "dk_lgc")
                ctx.wait_mark(v, "dk_iota")
            else:
                ctx.wait_mark(v, f"dk_lmask{h - 1}")
            vc.inc(v.tensor_scalar(out=t64a[:], in0=iopm[:], scalar1=lgc64[:, h:h + 1],
                                   scalar2=None, op0=ALU.mult))
            v.wait_ge(vc.h, vc.n)
            vc.inc(v.tensor_scalar(out=t64b[:], in0=t64a[:], scalar1=-1.0,
                                   scalar2=None, op0=ALU.mult))
            v.wait_ge(vc.h, vc.n)
            v.tensor_scalar(out=de_col[:, h:h + 1], in0=iocol[:], scalar1=1.0,
                            scalar2=lgc64[:, h:h + 1], op0=ALU.add, op1=ALU.mult)
            v.tensor_scalar(out=dw_col[:, h:h + 1], in0=iocol[:], scalar1=63.0,
                            scalar2=lgc64[:, h:h + 1], op0=ALU.subtract,
                            op1=ALU.mult)
            v.tensor_scalar(out=ctd128[:, h:h + 1], in0=lgc128[:, h:h + 1],
                            scalar1=64.0, scalar2=None, op0=ALU.mult)
            vc.inc(v.tensor_scalar(out=de_row[:, 64 * h:64 * (h + 1)], in0=iorow[:],
                                   scalar1=1.0, scalar2=lgrow[:, h:h + 1],
                                   op0=ALU.add, op1=ALU.mult))
        ctx.mark(f"dk_dd{h}", vc)

        @block.scalar
        def _(s, h=h):
            ctx.wait_mark(s, f"dk_dd{h}")
            s.activation(out=t64a[:], in_=t64a[:], func=AF.Exp)
            s.activation(out=t64b[:], in_=t64b[:], func=AF.Exp)
            s.activation(out=de_col[:, h:h + 1], in_=de_col[:, h:h + 1], func=AF.Exp)
            s.activation(out=dw_col[:, h:h + 1], in_=dw_col[:, h:h + 1], func=AF.Exp,
                         scale=-1.0)
            s.activation(out=ctd128[:, h:h + 1], in_=ctd128[:, h:h + 1], func=AF.Exp)
            sc.inc(s.activation(out=de_row[:, 64 * h:64 * (h + 1)],
                                in_=de_row[:, 64 * h:64 * (h + 1)], func=AF.Exp))
        ctx.mark(f"dk_exp{h}", sc)

        @block.gpsimd
        def _(g, h=h):
            ctx.wait_mark(g, f"dk_exp{h}")
            gio = ctx.sem("dk_gio")
            g.affine_select(negLs[:, 64 * h:64 * (h + 1)], t64a[:],
                            pattern=[[-1, 64]], compare_op=ALU.is_gt, fill=0.0,
                            base=0, channel_multiplier=1)
            gio.inc(g.affine_select(LTt[:, 64 * h:64 * (h + 1)], t64b[:],
                                    pattern=[[1, 64]], compare_op=ALU.is_ge,
                                    fill=0.0, base=0, channel_multiplier=-1))
        ctx.mark(f"dk_lmask{h}", ctx.sems["dk_gio"])

        @block.vector
        def _(v2, h=h):
            ctx.wait_mark(v2, f"dk_lmask{h}")
            vc.inc(v2.tensor_scalar(out=negLs[:, 64 * h:64 * (h + 1)],
                                    in0=negLs[:, 64 * h:64 * (h + 1)],
                                    scalar1=-1.0, scalar2=None, op0=ALU.mult))
        ctx.mark(f"dk_negl{h}", vc)
    ctx.mark("dk_consts", vc)

    @block.tensor
    def _(te):
        ctx.wait_mark(te, f"dk_exp{DH - 1}")
        ctx.wait_mark(te, "dk_consts")
        last = None
        for h in range(DH):
            last = te.matmul(ctx.psb[2][:, 0:448] if False else
                             ctx.psb[2 + h // 4][:, 64 * (h % 4):64 * (h % 4 + 1)],
                             onesr[:], de_row[:, 64 * h:64 * (h + 1)],
                             start=True, stop=True)
        tc.inc(last)
    ctx.mark("dk_debmm", tc)

    @block.scalar
    def _(s):
        ctx.wait_mark(s, "dk_debmm")
        s.activation(out=de_b[:, 0:256], in_=ctx.psb[2][:, 0:256], func=AF.Copy)
        sc.inc(s.activation(out=de_b[:, 256:512], in_=ctx.psb[3][:, 0:256], func=AF.Copy))
    ctx.mark("dk_deb", sc)

    for i in range(NT):
        @block.tensor
        def _(te, i=i):
            te.wait_ge(dl.h, dl.n)
            te.wait_ge(dg.h, dg.n)
            ctx.wait_mark(te, "dk_deb")
            if i >= 2:
                ctx.wait_mark(te, f"dk_bt{i % 2}")
            last = None
            for cc in range(NCK):
                last = te.matmul(ctx.psb[4 + i % 2][:, 0:DH],
                                 xnfm[:, T * cc + 128 * i:T * cc + 128 * (i + 1)],
                                 wb_sb[:, DH * cc:DH * (cc + 1)],
                                 start=(cc == 0), stop=(cc == NCK - 1))
            tc.inc(last)
        ctx.mark(f"dk_btm{i % 2}", tc)

        @block.scalar
        def _(s, i=i):
            ctx.wait_mark(s, f"dk_btm{i % 2}")
            sc.inc(s.activation(out=beta_tm[:, DH * i:DH * (i + 1)],
                                in_=ctx.psb[4 + i % 2][:, 0:DH], func=AF.Sigmoid))
        ctx.mark(f"dk_bt{i % 2}", sc)
    ctx.mark("dk_btall", sc)

    @block.sync
    def _(sync):
        ctx.wait_mark(sync, "dk_btall")
        bsem = ctx.sem("dk_bc")
        src_lo = beta_tm.ap()[0:64, :]
        src_hi = beta_tm.ap()[64:128, :]
        bsem.dma(sync.dma_start(out=betaC[:, 0:HB], in_=src_lo))
        bsem.dma(sync.dma_start(out=betaC[:, HB:2 * HB], in_=src_hi))
        ctx.mark("dk_betaC", bsem)

    for dc in range(NCK):
        for tq in range(T // 512):
            @block.tensor
            def _(te, dc=dc, tq=tq):
                te.wait_ge(dl.h, dl.n)
                te.wait_ge(dg.h, dg.n)
                ctx.wait_mark(te, "dk_deb")
                ctx.wait_mark(te, "dk_btall")
                idx = dc * (T // 512) + tq
                if idx >= 2:
                    ctx.wait_mark(te, f"dk_vfe{idx % 2}")
                last = None
                for cc in range(NCK):
                    last = te.matmul(
                        ctx.psb[idx % 2][:],
                        wpool[:, D * cc + 128 * dc:D * cc + 128 * (dc + 1)],
                        xnfm[:, T * cc + 512 * tq:T * cc + 512 * (tq + 1)],
                        start=(cc == 0), stop=(cc == NCK - 1))
                tc.inc(last)
            ctx.mark(f"dk_vfm{(dc * (T // 512) + tq) % 2}", tc)

            @block.vector
            def _(v, dc=dc, tq=tq):
                idx = dc * (T // 512) + tq
                ctx.wait_mark(v, f"dk_vfm{idx % 2}")
                vc.inc(v.tensor_copy(
                    vfm[:, T * dc + 512 * tq:T * dc + 512 * (tq + 1)],
                    ctx.psb[idx % 2][:]))
            ctx.mark(f"dk_vfe{(dc * (T // 512) + tq) % 2}", vc)
    ctx.mark("dk_vball", vc)

    for h in range(DH):
        for tq in range(T // 512):
            @block.vector
            def _(v, h=h, tq=tq):
                ctx.wait_mark(v, "dk_vball")
                if h > 0 or tq > 0:
                    ctx.wait_mark(v, "dk_knsq")
                vc.inc(v.scalar_tensor_tensor(
                    out=s1t[0][:, 0:512], in0=xnfm[:, T * h + 512 * tq:T * h + 512 * (tq + 1)],
                    scalar=1.0, in1=xnfm[:, T * h + 512 * tq:T * h + 512 * (tq + 1)],
                    op0=ALU.mult, op1=ALU.mult))
            ctx.mark("dk_knsq", vc)

            @block.tensor
            def _(te, h=h, tq=tq):
                ctx.wait_mark(te, "dk_knsq")
                if h > 0 or tq > 0:
                    ctx.wait_mark(te, "dk_knrs")
                tc.inc(te.matmul(ctx.psb[6][0:1, :], onescb[:],
                                 s1t[0][:, 0:512], start=True, stop=True))
            ctx.mark("dk_knmm", tc)

            @block.scalar
            def _(s, h=h, tq=tq):
                ctx.wait_mark(s, "dk_knmm")
                sc.inc(s.activation(out=rst_row[:], in_=ctx.psb[6][0:1, :],
                                     func=AF.Ln))
                s.wait_ge(sc.h, sc.n)
                sc.inc(s.activation(out=rst_row[:], in_=rst_row[:], func=AF.Exp,
                                    scale=-0.5))
            ctx.mark("dk_knr", sc)

            @block.tensor
            def _(te, h=h, tq=tq):
                ctx.wait_mark(te, "dk_knr")
                tc.inc(te.matmul(ctx.psb[7][:], onesrb[:], rst_row[:],
                                 start=True, stop=True))
            ctx.mark("dk_knrs", tc)

            @block.vector
            def _(v, h=h, tq=tq):
                ctx.wait_mark(v, "dk_knrs")
                vc.inc(v.tensor_tensor(
                    out=xnfm[:, T * h + 512 * tq:T * h + 512 * (tq + 1)],
                    in0=xnfm[:, T * h + 512 * tq:T * h + 512 * (tq + 1)],
                    in1=ctx.psb[7][:], op=ALU.mult))
            ctx.mark("dk_knorm", vc)
    ctx.mark("dk_kall", vc)

    prev_e1_cell = [0]
    ctx.marks["dk_e1"] = (vc.h, 0)
    ctx.marks["dk_e7"] = ctx.marks["dk_ones"]
    ctx.marks["dk_e8"] = (vc.h, 0)
    ctx.marks["dk_e4p"] = (vc.h, 0)
    for c in range(NCHK):
        ti, hf = c // 2, c % 2
        r64 = slice(64 * hf, 64 * (hf + 1))
        scb = sc.n
        off_e1k = scb + 1
        off_nev = [scb + 2 + 2 * r for r in range(6)]
        off_nep = [scb + 3 + 2 * r for r in range(5)]
        off_e3 = scb + 13
        off_e4 = scb + 14
        off_e5 = scb + 15
        off_e7o = scb + 16
        rec = {}

        def kslice(h, c=c):
            return xnfm[:, T * h + 64 * c:T * h + 64 * (c + 1)]

        @block.tensor
        def _(te, c=c):
            if c == 0:
                ctx.wait_mark(te, "dk_kall")
                ctx.wait_mark(te, "dk_vball")
                ctx.wait_mark(te, "prelude_ld")
            ctx.wait_mark(te, "dk_e1")
            ctx.wait_mark(te, "dk_betaC")
            last = None
            for h in range(DH):
                te.matmul(kk_ps(h), kslice(h), kslice(h), start=True, stop=True)
                last = te.transpose(ktr_ps(h), kslice(h), ident[:])
            tc.inc(last)
        ctx.mark("dk_t1", tc)

        rec["t1"] = tc.n
        ctx.marks["dk_e1k"] = (sc.h, off_e1k)

        @block.tensor
        def _(te, c=c):
            ctx.wait_mark(te, "dk_e1k")
            last = None
            for h in range(DH):
                last = te.transpose(vtr_ps(h),
                                    vfm[:, T * h + 64 * c:T * h + 64 * (c + 1)],
                                    ident[:])
            tc.inc(last)
        ctx.mark("dk_t1v", tc)

        @block.vector
        def _(v, c=c, ti=ti, hf=hf):
            ctx.wait_mark(v, "dk_t1v")
            for h in range(DH):
                bcol0 = betaC.ap()[:, HB * hf + DH * ti + h:HB * hf + DH * ti + h + 1]
                vc.inc(v.tensor_scalar(out=vtm[h][:], in0=vtr_ps(h), scalar1=bcol0,
                                       scalar2=None, op0=ALU.mult))
            v.wait_ge(vc.h, vc.n)
            last = None
            for h in range(DH):
                bcol = betaC.ap()[:, HB * hf + DH * ti + h:HB * hf + DH * ti + h + 1]
                v.scalar_tensor_tensor(out=A0[h][:], in0=kk_s[h][:], scalar=bcol,
                                       in1=negLs[:, 64 * h:64 * (h + 1)],
                                       op0=ALU.mult, op1=ALU.mult)
                v.tensor_tensor(out=atT[h][:], in0=kk_s[h][:],
                                in1=LTt[:, 64 * h:64 * (h + 1)], op=ALU.mult)
                v.tensor_scalar(out=kbde[h][:], in0=ktm[h][:], scalar1=bcol,
                                scalar2=de_col[:, h:h + 1], op0=ALU.mult, op1=ALU.mult)
                v.tensor_scalar(out=kdw[h][:], in0=ktm[h][:],
                                scalar1=dw_col[:, h:h + 1], scalar2=None, op0=ALU.mult)
                last = v.tensor_tensor(out=kdeT[h][:], in0=kslice(h),
                                       in1=de_b[:, 64 * h:64 * (h + 1)], op=ALU.mult)
            vc.inc(last)
            v.wait_ge(vc.h, vc.n)
            for h in range(DH):
                vc.inc(v.tensor_copy(Xb[h][:], ident[0:64, 0:64]))
            v.wait_ge(vc.h, vc.n)
            for h in range(DH):
                vc.inc(v.tensor_copy(Pb[h][:], A0[h][:]))
        ctx.mark("dk_e1", vc)

        NR = 6
        for r in range(NR):
            @block.tensor
            def _(te, c=c, r=r):
                ctx.wait_mark(te, "dk_e1" if r == 0 else f"dk_nep{r - 1}")
                if r > 0:
                    ctx.wait_mark(te, f"dk_ne{r - 1}")
                last = None
                for h in range(DH):
                    last = te.transpose(ptr_ps(h), Pb[h][:], ident[0:64, 0:64])
                tc.inc(last)
            ctx.mark(f"dk_ntr{r}", tc)

            rec[f"ntr{r}"] = tc.n
            ctx.marks[f"dk_nev{r}"] = (sc.h, off_nev[r])

            @block.tensor
            def _(te, c=c, r=r):
                ctx.wait_mark(te, f"dk_nev{r}")
                last = None
                for h in range(DH):
                    last = te.matmul(nwx_ps(h), PTs[h][:], Xb[h][:],
                                     start=True, stop=True)
                    if r < NR - 1:
                        last = te.matmul(nwp_ps(h), PTs[h][:], Pb[h][:],
                                         start=True, stop=True)
                tc.inc(last)
            ctx.mark(f"dk_nmm{r}", tc)

            @block.vector
            def _(v, c=c, r=r):
                ctx.wait_mark(v, f"dk_nmm{r}")
                for h in range(DH):
                    vc.inc(v.tensor_tensor(out=Xb[h][:], in0=Xb[h][:],
                                           in1=nwx_ps(h), op=ALU.add))
            ctx.mark(f"dk_ne{r}", vc)
            rec[f"nmm{r}"] = tc.n
            if r < NR - 1:
                ctx.marks[f"dk_nep{r}"] = (sc.h, off_nep[r])

        @block.tensor
        def _(te, c=c):
            ctx.wait_mark(te, f"dk_ne{NR - 1}")
            last = None
            for h in range(DH):
                last = te.transpose(ptr_ps(h), Xb[h][:], ident[0:64, 0:64])
            tc.inc(last)
        ctx.mark("dk_t3", tc)

        rec["t3"] = tc.n
        ctx.marks["dk_e3"] = (sc.h, off_e3)

        @block.tensor
        def _(te, c=c, ti=ti, hf=hf):
            ctx.wait_mark(te, "dk_e3")
            ctx.wait_mark(te, "dk_e4p")
            last = None
            for h in range(DH):
                te.matmul(yv_ps(h), ATs[h][:], vtm[h][:], start=True, stop=True)
                last = te.matmul(o_ps(h), ATs[h][:], kbde[h][:], start=True, stop=True)
            tc.inc(last)
        ctx.mark("dk_t4", tc)

        rec["t4"] = tc.n
        ctx.marks["dk_e4"] = (sc.h, off_e4)
        ctx.marks["dk_e4p"] = (sc.h, off_e4)

        @block.tensor
        def _(te, c=c):
            ctx.wait_mark(te, "dk_e4")
            last = None
            for h in range(DH):
                last = te.transpose(ttr_ps(h), kcd[h][:], ident[0:64, 0:64])
            tc.inc(last)
        ctx.mark("dk_t5", tc)

        rec["t5"] = tc.n
        ctx.marks["dk_e5"] = (sc.h, off_e5)

        @block.tensor
        def _(te, c=c):
            ctx.wait_mark(te, "dk_e5")
            ctx.wait_mark(te, "dk_e7")
            last = None
            for h in range(DH):
                last = te.matmul(o_ps(h), kcdT[h][:], S_b[h][:], start=True, stop=True)
            tc.inc(last)
        ctx.mark("dk_t6", tc)

        @block.vector
        def _(v, c=c):
            ctx.wait_mark(v, "dk_t6")
            for h in range(DH):
                vc.inc(v.tensor_tensor(out=vnew[h][:], in0=Yv_s[h][:], in1=o_ps(h),
                                       op=ALU.subtract))
        ctx.mark("dk_e6", vc)

        @block.tensor
        def _(te, c=c):
            ctx.wait_mark(te, "dk_e6")
            last = None
            for h in range(DH):
                te.matmul(yv_ps(h), kdeT[h][:], S_b[h][:], start=True, stop=False)
                te.matmul(yv_ps(h), atT[h][:], vnew[h][:], start=False, stop=True)
                last = te.matmul(dW_ps(h), kdw[h][:], vnew[h][:], start=True, stop=True)
            tc.inc(last)
        ctx.mark("dk_t7", tc)

        rec["t7"] = tc.n
        ctx.marks["dk_e7o"] = (sc.h, off_e7o)

        @block.vector
        def _(v, c=c):
            ctx.wait_mark(v, "dk_t7")
            for h in range(DH):
                vc.inc(v.scalar_tensor_tensor(out=S_f[h][:], in0=S_f[h][:],
                                              scalar=ctd128[:, h:h + 1], in1=dW_ps(h),
                                              op0=ALU.mult, op1=ALU.add))
            v.wait_ge(vc.h, vc.n)
            for h in range(DH):
                vc.inc(v.tensor_copy(S_b[h][:], S_f[h][:]))
        ctx.mark("dk_e7", vc)

        @block.tensor
        def _(te, c=c):
            ctx.wait_mark(te, "dk_e7o")
            ctx.wait_mark(te, "dk_e7")
            ctx.wait_mark(te, "dk_e8")
            last = None
            for h in range(DH):
                last = te.transpose(ttr_ps(h), o_s[h][:], ident[0:64, 0:64])
            tc.inc(last)
        ctx.mark("dk_t8", tc)

        @block.vector
        def _(v, c=c):
            ctx.wait_mark(v, "dk_t8")
            v.wait_ge(dst[c % 2].h, dst[c % 2].n)
            for h in range(DH):
                vc.inc(v.tensor_copy(oTst[c % 2][h][:], ttr_ps(h)))
        ctx.mark("dk_e8", vc)

        @block.sync
        def _(sync, c=c):
            ctx.wait_mark(sync, "dk_e8")
            sync.wait_ge(dst[c % 2].h, dst[c % 2].n)
            for h in range(DH):
                dst[c % 2].dma(sync.dma_start(
                    out=io["oT1h"][DHD * h:DHD * (h + 1), 64 * c:64 * (c + 1)],
                    in_=oTst[c % 2][h][:]))

        @block.scalar
        def _(s, c=c, rec=dict(rec), prev_e1=prev_e1_cell[0]):
            s.wait_ge(tc.h, rec["t1"])
            if prev_e1:
                s.wait_ge(vc.h, prev_e1)
            last = None
            for h in range(DH):
                s.activation(out=kk_s[h][:], in_=kk_ps(h), func=AF.Copy)
                last = s.activation(out=ktm[h][:], in_=ktr_ps(h), func=AF.Copy)
            sc.inc(last)
            for r in range(NR):
                s.wait_ge(tc.h, rec[f"ntr{r}"])
                last = None
                for h in range(DH):
                    last = s.activation(out=PTs[h][:], in_=ptr_ps(h), func=AF.Copy)
                sc.inc(last)
                if r < NR - 1:
                    s.wait_ge(tc.h, rec[f"nmm{r}"])
                    last = None
                    for h in range(DH):
                        last = s.activation(out=Pb[h][:], in_=nwp_ps(h), func=AF.Copy)
                    sc.inc(last)
            s.wait_ge(tc.h, rec["t3"])
            last = None
            for h in range(DH):
                last = s.activation(out=ATs[h][:], in_=ptr_ps(h), func=AF.Copy)
            sc.inc(last)
            s.wait_ge(tc.h, rec["t4"])
            last = None
            for h in range(DH):
                s.activation(out=Yv_s[h][:], in_=yv_ps(h), func=AF.Copy)
                last = s.activation(out=kcd[h][:], in_=o_ps(h), func=AF.Copy)
            sc.inc(last)
            s.wait_ge(tc.h, rec["t5"])
            last = None
            for h in range(DH):
                last = s.activation(out=kcdT[h][:], in_=ttr_ps(h), func=AF.Copy)
            sc.inc(last)
            s.wait_ge(tc.h, rec["t7"])
            last = None
            for h in range(DH):
                last = s.activation(out=o_s[h][:], in_=yv_ps(h), func=AF.Copy)
            sc.inc(last)
        assert sc.n == scb + 16, (sc.n, scb)
        prev_e1_cell[0] = ctx.marks["dk_e1"][1]

    @block.gpsimd
    def _(g):
        g.wait_ge(dg.h, dg.n)
        ctx.wait_mark(g, "dk_vball")
        for cc in range(NCK):
            dg.dma(g.dma_start(out=wpool[:, D * cc:D * (cc + 1)],
                               in_=io["WrT"][128 * cc:128 * (cc + 1), :]))
    ctx.mark("dk_wr", dg)

    for tg in range(T // 512):
        for oc in range(NCK):
            @block.sync
            def _(sync, tg=tg, oc=oc):
                if tg == 0 and oc == 0:
                    sync.wait_ge(dst[0].h, dst[0].n)
                    sync.wait_ge(dst[1].h, dst[1].n)
                idx = tg * NCK + oc
                if idx >= 2:
                    ctx.wait_mark(sync, f"dk_pj{oc % 2}")
                lp[oc % 2].dma(sync.dma_start(
                    out=oTld[oc % 2][:],
                    in_=io["oT1h"][128 * oc:128 * (oc + 1), 512 * tg:512 * (tg + 1)]))

            @block.tensor
            def _(te, tg=tg, oc=oc):
                te.wait_ge(lp[oc % 2].h, lp[oc % 2].n)
                if tg == 0 and oc == 0:
                    ctx.wait_mark(te, "dk_wr")
                if oc == 0:
                    ctx.wait_mark(te, "dk_pjev")
                last = None
                for i4 in range(4):
                    for nb in range(2):
                        last = te.matmul(
                            ctx.psb[i4 * 2 + nb][:],
                            oTld[oc % 2][:, 128 * i4:128 * (i4 + 1)],
                            wpool[:, D * oc + 512 * nb:D * oc + 512 * (nb + 1)],
                            start=(oc == 0), stop=(oc == NCK - 1))
                tc.inc(last)
            ctx.mark(f"dk_pj{oc % 2}", tc)
        ctx.mark("dk_pjmm", tc)

        for i4 in range(4):
            @block.sync
            def _(sync, tg=tg, i4=i4):
                i = 4 * tg + i4
                if i >= 2:
                    ctx.wait_mark(sync, f"dk_s1v{i4 % 2}")
                lq[i4 % 2].dma(sync.dma_start(
                    out=xnld[i4 % 2][:], in_=io["xn1_tm"][128 * i:128 * (i + 1), :]))

            @block.vector
            def _(v, tg=tg, i4=i4):
                if i4 == 0:
                    ctx.wait_mark(v, "dk_pjmm")
                v.wait_ge(lq[i4 % 2].h, lq[i4 % 2].n)
                v.wait_ge(pst[i4 % 2].h, pst[i4 % 2].n)
                last = None
                for nb in range(2):
                    last = v.tensor_tensor(
                        out=s1t[i4 % 2][:, 512 * nb:512 * (nb + 1)],
                        in0=ctx.psb[i4 * 2 + nb][:],
                        in1=xnld[i4 % 2][:, 512 * nb:512 * (nb + 1)], op=ALU.add)
                vc.inc(last)
            if i4 == 3:
                ctx.mark("dk_pjev", vc)
            ctx.mark(f"dk_s1v{i4 % 2}", vc)

            @block.sync
            def _(sync, tg=tg, i4=i4):
                ctx.wait_mark(sync, f"dk_s1v{i4 % 2}")
                i = 4 * tg + i4
                pst[i4 % 2].dma(sync.dma_start(
                    out=io["s1"][128 * i:128 * (i + 1), :], in_=s1t[i4 % 2][:]))

    ctx.mark("delta_done", pst[1])


TENSORS = {
    "x": ([T, D], F32),
    "qkv_wT": ([D, 3 * D], F32), "qkv_b": ([1, 3 * D], F32),
    "proj_wT": ([D, D], F32), "proj_b": ([1, D], F32),
    "wq_peerT": ([D, 2 * D], F32), "keysT": ([PH, 2, PDK, NKEYS], F32),
    "norm_w": ([1, D], F32),
    "exp_down": ([NEXP, D], F32), "exp_up": ([NEXP, D], F32),
    "tblbf": ([NEXP, 2 * D], BF16),
    "WwT": ([D, D], F32), "WrT": ([D, D], F32), "WbT": ([D, DH], F32),
    "decay": ([1, DH], F32),
    "router": ([D, NE], F32),
    "Wg": ([NE, D, FF], F32), "Wu": ([NE, D, FF], F32), "Wd": ([NE, FF, D], F32),
    "rope_cos": ([128, (T // 128) * 32], F32), "rope_sin": ([128, (T // 128) * 32], F32),
    "diag_masks": ([128, 4 * 512], BF16), "ident_bf": ([128, 128], BF16),
    "qkv_wg": ([NGRP * NCK * 128, 3 * 64 * GH], F32),
    "xn_tm": ([T, D], BF16), "xn_fm": ([D, T], BF16),
    "s": ([T, D], BF16), "oTh": ([D, T], BF16),
    "x1": ([T, D], F32),
    "xn1_tm": ([T, D], BF16), "xn1_fm": ([D, T], BF16),
    "s1": ([T, D], BF16), "moeacc": ([T, D], F32), "oT1h": ([D, T], BF16),
    "out": ([T, D], F32),
}

STAGES = {
    "ln0": dict(reads=["x", "ident_bf"], writes=["xn_tm", "xn_fm"],
                emit=lambda ctx: emit_ln(ctx, "x", "xn_tm", "xn_fm", "ln0")),
    "ln1": dict(reads=["x1", "ident_bf"], writes=["xn1_tm", "xn1_fm"],
                emit=lambda ctx: emit_ln(ctx, "x1", "xn1_tm", "xn1_fm", "ln1")),
    "attn0": dict(reads=["xn_fm", "qkv_wg", "qkv_b", "proj_wT", "proj_b",
                         "rope_cos", "rope_sin", "diag_masks", "ident_bf",
                         "exp_down", "exp_up"],
                  writes=["s", "tblbf"], scratch=["oTh"], emit=emit_attn0),
    "peer": dict(reads=["xn_tm", "xn_fm", "wq_peerT", "keysT", "norm_w",
                        "tblbf", "x", "s", "ident_bf"],
                 writes=["x1"], emit=emit_peer),
    "moe": dict(reads=["xn1_fm", "router", "Wg", "Wu", "Wd", "x1", "s1",
                       "ident_bf"],
                writes=["out"], scratch=["moeacc"], emit=emit_moe),
    "delta": dict(reads=["xn1_fm", "xn1_tm", "WwT", "WbT", "WrT", "decay",
                         "ident_bf"],
                  writes=["s1"], scratch=["oT1h"], emit=emit_delta),
}


def build_program(stage_names):
    stages = [STAGES[s] for s in stage_names]
    reads = set()
    writes = set()
    scratch = set()
    for s in stages:
        reads.update(s["reads"])
        writes.update(s["writes"])
        scratch.update(s.get("scratch", []))
    fused_all = "out" in writes and len(stage_names) > 1

    stack = ExitStack()
    nc = bass.Bass("TRN2", target_bir_lowering=False, debug=False)
    io = {}
    in_names = []
    out_names = []
    for name in sorted(reads | writes | scratch):
        shape, dt = TENSORS[name]
        if name in scratch:
            kind = "Internal"
        elif name in writes:
            if fused_all and name != "out":
                kind = "Internal"
            else:
                kind = "ExternalOutput"
                out_names.append(name)
        else:
            kind = "ExternalInput"
            in_names.append(name)
        io[name] = nc.dram_tensor(name, shape, dt, kind=kind).ap()

    ctx = Ctx(nc, stack, io)
    with stack:
        with nc.Block() as block:
            ctx.block = block
            emit_prelude(ctx)
            for si, s in enumerate(stages):
                if si > 0:
                    def _bar(eng):
                        for sem in ctx.sems.values():
                            if sem.n > 0:
                                eng.wait_ge(sem.h, sem.n)
                    block.sync(_bar)
                    block.gpsimd(_bar)
                    block.vector(_bar)
                    block.scalar(_bar)
                    block.tensor(_bar)
                with ExitStack() as scope:
                    ctx.scope = scope
                    s["emit"](ctx)
                ctx.scope = stack

            @block.sync
            def _(sync):
                for sem in ctx.sems.values():
                    if sem.n > 0:
                        sync.wait_ge(sem.h, sem.n)
    return nc, in_names, out_names


def run_stages(stage_names, per_core, core_ids=None):
    core_ids = core_ids or list(range(len(per_core)))
    nc, in_names, out_names = build_program(stage_names)
    in_maps = [{n: pc[n] for n in in_names} for pc in per_core]
    res = run_bass_kernel_spmd(nc, in_maps, core_ids=core_ids)
    for pc, r in zip(per_core, res.results):
        for n in out_names:
            pc[n] = r[n]
    return per_core


PIPELINE = ["ln0", "attn0", "peer", "ln1", "delta", "moe"]
FUSED = True


def kernel(**inputs):
    w = host_prepare(inputs)
    xfull = np.asarray(inputs["L0_attn_x"])
    per_core = [dict(w, x=np.ascontiguousarray(xfull[i])) for i in range(B)]
    if FUSED:
        run_stages(PIPELINE, per_core)
    else:
        for st in PIPELINE:
            run_stages([st], per_core)
    return np.stack([pc["out"] for pc in per_core], axis=0)
